# version 03
# baseline (speedup 1.0000x reference)
import os

import numpy as np
import ml_dtypes

import concourse.bass as bass
import concourse.mybir as mybir
from concourse.bass_utils import run_bass_kernel_spmd

F32 = mybir.dt.float32
BF16 = mybir.dt.bfloat16
F16 = mybir.dt.float16
AF = mybir.ActivationFunctionType
ALU = mybir.AluOpType
AXX = mybir.AxisListType.X
BNP = ml_dtypes.bfloat16

B, S, D = 4, 4096, 1024
T = 2048
NCORES = 8
KC = 8
EPS = 1e-5
STAGE = 3

SMCOL = {}
_nxt = [0]
for _n, _c in [("ln1_w", 8), ("ln1_b", 8), ("lns_w", 8), ("lns_b", 8),
               ("conv_b", 8), ("cw0", 8), ("cw1", 8), ("cw2", 8), ("cw3", 8),
               ("in_b", 16), ("out_b", 8), ("ln2_w", 8), ("ln2_b", 8),
               ("mlp_b1", 16), ("mlp_b2", 8), ("w_ln1", 8), ("w_ln2", 8),
               ("xp_b", 1), ("A_log", 1), ("ebias", 1)]:
    SMCOL[_n] = (_nxt[0], _c)
    _nxt[0] += _c
NSMALL = _nxt[0]


def _pack_small(dst, name, vec):
    c0, ncol = SMCOL[name]
    v = np.asarray(vec, np.float32).reshape(-1)
    if ncol == 1:
        dst[:len(v), c0] = v
    else:
        dst[:, c0:c0 + ncol] = v.reshape(ncol, 128).T


def prep_inputs(inp):
    x = np.asarray(inp["L0_attn_x"], np.float32)
    w = {k[8:]: np.asarray(v, np.float32) for k, v in inp.items() if k.startswith("L0_attn_")}
    wf = {k[7:]: np.asarray(v, np.float32) for k, v in inp.items() if k.startswith("L0_ffn_")}

    smalls = np.zeros((128, NSMALL), np.float32)
    for nm in ["ln1_w", "ln1_b", "lns_w", "lns_b", "conv_b", "in_b", "out_b",
               "ln2_w", "ln2_b", "mlp_b1", "mlp_b2", "xp_b", "A_log"]:
        _pack_small(smalls, nm, w[nm])
    for k in range(4):
        _pack_small(smalls, f"cw{k}", w["conv_w"][:, 0, k])
    _pack_small(smalls, "w_ln1", wf["w_ln1"])
    _pack_small(smalls, "w_ln2", wf["w_ln2"])
    _pack_small(smalls, "ebias", wf["expert_bias"])

    tpos = np.arange(S, dtype=np.float32)
    freqs = 1.0 / (100000.0 ** (np.arange(0, 32, 2, dtype=np.float32) / 32))
    ang = np.outer(freqs, tpos)
    cos, sin = np.cos(ang), np.sin(ang)
    csA = np.zeros((32, S), np.float32)
    csB = np.zeros((32, S), np.float32)
    csA[0::2] = cos
    csA[1::2] = cos
    csB[0::2] = -sin
    csB[1::2] = sin

    wg9 = np.ascontiguousarray(np.concatenate([wf["Wg"], wf["sWg"]], 0))
    wu9 = np.ascontiguousarray(np.concatenate([wf["Wu"], wf["sWu"]], 0))
    wd9 = np.ascontiguousarray(np.concatenate([wf["Wd"], wf["sWd"]], 0))
    rows_eff = np.concatenate([np.arange(h * 64, h * 64 + 32) for h in range(16)])
    wo_eff = np.ascontiguousarray(wf["Wo"][rows_eff])

    triu = np.triu(np.ones((128, 128), np.float32))
    zeros = np.zeros((128, 128), np.float32)
    onesm = np.ones((128, 128), np.float32)

    in_maps = []
    for c in range(NCORES):
        b0, par = c // 2, c & 1
        s0, lo = par * T, c * 512
        pad3 = np.zeros((3, D), np.float32)
        xo = np.concatenate([x[b0, s0 - 3:s0] if s0 >= 3 else pad3,
                             x[b0, s0:s0 + T]], 0).T
        x_own = np.zeros((D, 2060), np.float32)
        x_own[:, :2051] = xo
        if lo >= 3:
            xs = x[:, lo - 3:lo + 512]
        else:
            xs = np.concatenate([np.zeros((B, 3, D), np.float32), x[:, 0:512]], 1)
        x_scan = np.ascontiguousarray(xs.transpose(2, 1, 0)).reshape(D, 2060)

        cpw = np.broadcast_to(
            np.array([(c - 1 - j) if j < c else 1e6 for j in range(8)], np.float32),
            (128, 8)).copy()
        bmask = np.ones((128, 16), np.float32)
        if lo == 0:
            bmask[:, 0:12] = 0.0
        if s0 == 0:
            bmask[:, 12:15] = 0.0
        qm = np.empty((128, 32 * 128), np.float32)
        for kb in range(32):
            if par == 0:
                blk = triu if (kb % 2 == 0) else zeros
            else:
                blk = onesm if (kb % 2 == 0) else triu
            qm[:, kb * 128:(kb + 1) * 128] = blk
        own_cols = np.concatenate(
            [np.arange((2 * j + par) * 128, (2 * j + par + 1) * 128) for j in range(16)])

        parsel = np.zeros((128, 2), np.float32)
        parsel[:, par] = 1.0
        in_maps.append(dict(
            parsel=parsel,
            x_own=x_own, x_scan=x_scan,
            in_w=w["in_w"], xp_w=np.ascontiguousarray(w["xp_w"]),
            dmat=np.ascontiguousarray(w["D"]), out_w=w["out_w"],
            mlp_w1=w["mlp_w1"], mlp_w2=w["mlp_w2"],
            wq=wf["Wq"], wkva=np.ascontiguousarray(wf["Wkva"]),
            wkvb=np.ascontiguousarray(wf["Wkvb"]), wo_eff=wo_eff,
            wgate=np.ascontiguousarray(wf["Wgate"]),
            wg9=wg9, wu9=wu9, wd9=wd9,
            smalls=smalls, cpw=cpw, bmask=bmask,
            qmask=qm.astype(BNP),
            csq_a=np.ascontiguousarray(csA[:, own_cols]).astype(BNP),
            csq_b=np.ascontiguousarray(csB[:, own_cols]).astype(BNP),
            csk_a=csA.astype(BNP), csk_b=csB.astype(BNP),
        ))
    return in_maps


class MK:
    ENGS = ("tensor", "vector", "scalar", "gpsimd", "sync")

    def __init__(self):
        self.nc = bass.Bass("TRN2", target_bir_lowering=False,
                            detect_race_conditions=bool(os.environ.get("MKRACE")))
        self.ops = {e: [] for e in self.ENGS}
        self.cnt = {}
        self.sems = {}
        self._semname = []

    def newsem(self, name):
        self.cnt[name] = 0
        self._semname.append(name)

    def c(self, name):
        return (name, self.cnt.get(name, 0))

    ESEM = {"tensor": "ts", "vector": "vs", "scalar": "ss"}

    def op(self, eng, fn, waits=(), incs=()):
        waits = list(waits)
        es = self.ESEM.get(eng)
        if es is not None and self.cnt.get(es, 0) > 0 and fn.__name__ != "<nowait>":
            waits.append((es, self.cnt[es]))
        for s, k in incs:
            self.cnt[s] += k
        waits = tuple((s, v) for (s, v) in waits if v > 0)
        incs = tuple(incs)

        def run(E, fn=fn, waits=waits, incs=incs):
            for s, v in waits:
                E.wait_ge(self.sems[s], v)
            inst = fn(E)
            for s, k in incs:
                inst.then_inc(self.sems[s], k)
        self.ops[eng].append(run)
        return [(s, self.cnt[s]) for s, _ in incs]

    def dma(self, q, out, in_, waits=(), ch=0, **kw):
        eng = {"g": "gpsimd", "s": "sync"}[q]
        sem = f"d{q}{ch}"
        if sem not in self.cnt:
            self.newsem(sem)
        return self.op(eng, lambda E, o=out, i=in_, kw=kw: E.dma_start(out=o, in_=i, **kw),
                       waits=waits, incs=[(sem, 16)])[0]

    def wait(self, eng, waits):
        self.op(eng, lambda E: None, waits=waits)

    def barrier(self, engines=None):
        engines = engines or self.ENGS
        dmas = [(s, self.cnt[s]) for s in self.cnt if s.startswith("d") and self.cnt[s] > 0]
        bs = f"bar{len(self._semname)}"
        self.newsem(bs)
        n = len(engines)
        for e in engines:
            self.op(e, lambda E: E.nop(), waits=dmas, incs=[(bs, 1)])
        for e in engines:
            self.wait(e, [(bs, n)])

    def build(self):
        nc = self.nc
        import contextlib
        stack = contextlib.ExitStack()
        for name in self._semname:
            self.sems[name] = stack.enter_context(nc.semaphore(name))
        blk = stack.enter_context(nc.Block())
        for eng in self.ENGS:
            ops = self.ops[eng]
            getattr(blk, eng)(lambda E, ops=ops: [f(E) for f in ops])
        stack.close()
        return nc


def build_program(stage=STAGE, devtap=None):
    mk = MK()
    nc = mk.nc
    for s in ("ts", "vs", "ss", "cc"):
        mk.newsem(s)

    dt_in = {}
    for name, shape, dt in [
        ("x_own", [D, 2060], F32), ("x_scan", [D, 2060], F32),
        ("in_w", [D, 2048], F32), ("xp_w", [D, 16], F32), ("dmat", [16, D], F32),
        ("out_w", [D, D], F32), ("mlp_w1", [D, 2048], F32), ("mlp_w2", [2048, D], F32),
        ("wq", [D, D], F32), ("wkva", [D, 160], F32), ("wkvb", [128, 512], F32),
        ("wo_eff", [512, D], F32), ("wgate", [D, 8], F32),
        ("wg9", [9, D, 512], F32), ("wu9", [9, D, 512], F32), ("wd9", [9, 512, D], F32),
        ("smalls", [128, NSMALL], F32),
        ("cpw", [128, 8], F32), ("bmask", [128, 16], F32),
        ("parsel", [128, 2], F32),
        ("qmask", [128, 32 * 128], BF16),
        ("csq_a", [32, T], BF16), ("csq_b", [32, T], BF16),
        ("csk_a", [32, S], BF16), ("csk_b", [32, S], BF16),
    ]:
        dt_in[name] = nc.dram_tensor(name, shape, dt, kind="ExternalInput").ap()

    outp = nc.dram_tensor("outp", [128, KC, T], F32, kind="ExternalOutput").ap()

    scr = {}
    for name, shape, dt in [
        ("rows_scr", [8, 4160], F32),
        ("esnd", [128], F32), ("ercv", [8, 128], F32),
        ("xsnd", [128, 16, T], BF16), ("xrcv", [2, 128, 16, T], BF16),
        ("xbsp", [128, KC, T], F32), ("xbscan", [128, KC, T], F32),
        ("xmid", [128, KC, T], F32), ("x1own", [128, KC, T], F32),
        ("x2sp", [128, KC, T], F32), ("bo2sp", [128, KC, T], F32),
        ("cmbs", [8, T], F32),
        ("ysp", [128, 4, T], BF16),
    ]:
        scr[name] = nc.dram_tensor(name, shape, dt).ap()

    AR_A = nc.alloc_sbuf_tensor("AR_A", [128, 10304], F32).ap()
    AR_B = nc.alloc_sbuf_tensor("AR_B", [128, 10240], F32).ap()
    AR_C = nc.alloc_sbuf_tensor("AR_C", [128, 12288], F32).ap()
    WS = nc.alloc_sbuf_tensor("WS", [128, 9216], F32).ap()
    ROWS = nc.alloc_sbuf_tensor("ROWS", [128, 6180], F32).ap()
    SM = nc.alloc_sbuf_tensor("SM", [128, NSMALL], F32).ap()
    CPW = nc.alloc_sbuf_tensor("CPW", [128, 8], F32).ap()
    BMASK = nc.alloc_sbuf_tensor("BMASK", [128, 16], F32).ap()
    PARSEL = nc.alloc_sbuf_tensor("PARSEL", [128, 2], F32).ap()
    MISC = nc.alloc_sbuf_tensor("MISC", [128, 2176], F32).ap()
    SC = nc.alloc_sbuf_tensor("SC", [128, 2048], F32).ap()

    pb = [nc.alloc_psum_tensor(f"pb{i}", [128, 512], F32).ap() for i in range(8)]

    def smcol(name, k=0, rows=128):
        c0, _ = SMCOL[name]
        return SM[0:rows, c0 + k:c0 + k + 1]

    ones_bf = MISC[:, 0:4].bitcast(BF16)[:, 0:1]
    gcol_bf = MISC[:, 4:8].bitcast(BF16)
    bcol_bf = MISC[:, 8:12].bitcast(BF16)
    hw_col = MISC[:, 16:32]
    xpw_bf = MISC[:, 32:96].bitcast(BF16)
    ld64 = MISC[:, 96:97]
    ld512 = MISC[:, 97:98]
    w8t = MISC[:, 98:106]
    carry = MISC[:, 106:107]
    ecol = MISC[:, 107:108]
    ca8 = MISC[:, 108:116]
    xpbq = MISC[:, 116:117]
    eps_col = MISC[:, 117:118]
    mrow_bf = MISC[0:1, 128:1152].bitcast(BF16)
    gw_bf = MISC[0:1, 1152:2176].bitcast(BF16)
    hsr_bf = MISC[32:48, 128:1152].bitcast(BF16)
    dmat_bf = MISC[32:48, 1152:1664].bitcast(BF16)

    u64q = SC[:, 0:512]
    hsq = SC[:, 512:1024]
    dk_rep = SC[:, 1024:1536]
    p512 = SC[:, 1536:2048]

    def rslot(i, n=2060):
        return ROWS[0:1, i * 2060:i * 2060 + n]

    mk.dma("s", SM[:], dt_in["smalls"], ch=0)
    mk.dma("s", CPW[:], dt_in["cpw"], ch=0)
    mk.dma("s", PARSEL[:], dt_in["parsel"], ch=0)
    t_fix = mk.dma("s", BMASK[:], dt_in["bmask"], ch=0)
    mk.op("vector", lambda E: E.memset(MISC[:, 96:128], 0.0), incs=[("vs", 1)])
    mk.op("vector", lambda E: E.memset(ones_bf, 1.0), incs=[("vs", 1)])
    mk.op("vector", lambda E: E.memset(eps_col, EPS), incs=[("vs", 1)])
    v_ones = mk.c("vs")

    def stats(src_dram_fn, ncols, with_mean=True, fused2=False, extra_wait=(),
              ship_row0=0, ship_off=0, tbuf=None, tbase=0):
        s1, s2, t2 = rslot(0, ncols), rslot(1, ncols), rslot(2, ncols)
        ta = AR_A if tbuf is None else tbuf
        xbf = [ta[:, tbase + 0:tbase + 1030].bitcast(BF16),
               ta[:, tbase + 1030:tbase + 2060].bitcast(BF16)]
        sqb = ta[:, tbase + 2060:tbase + 3090].bitcast(BF16)
        ent_ts, ent_ss, ent_vs = mk.c("ts"), mk.c("ss"), mk.c("vs")
        ranges = [(0, min(ncols, 2048))]
        if ncols > 2048:
            ranges.append((2048, ncols - 2048))
        read_done = {}
        sq_done = {}
        for (c0, cn) in ranges:
            FCH = [(f0, min(512, cn - f0)) for f0 in range(0, cn, 512)]
            for kc in range(KC):
                bb = xbf[kc % 2][:, 0:cn]
                war = ([read_done[kc - 2]] if kc - 2 in read_done else [])
                tok = mk.dma("g", bb, src_dram_fn(kc)[:, c0:c0 + cn], ch=kc % 2,
                             waits=war
                             + ([ent_ts, ent_ss, ent_vs] if (kc <= 1 and c0 == 0) else [])
                             + (list(extra_wait) if (kc <= 1 and c0 == 0) else []))
                if with_mean:
                    for fi, (f0, fw) in enumerate(FCH):
                        mk.op("tensor", lambda E, bb=bb, f0=f0, fw=fw, fi=fi, kc=kc:
                              E.matmul(pb[4 + fi][0:1, 0:fw], ones_bf, bb[:, f0:f0 + fw],
                                       start=(kc == 0), stop=(kc == KC - 1)),
                              waits=([tok, v_ones, ent_vs] if kc == 0 else [tok]),
                              incs=[("ts", 1)])
                war_s = ([sq_done[kc - 1]] if kc - 1 in sq_done else [])
                mk.op("scalar", lambda E, bb=bb, cn=cn: E.activation(sqb[:, 0:cn], bb, AF.Square),
                      waits=[tok] + war_s, incs=[("ss", 1)])
                s_sq = mk.c("ss")
                for fi, (f0, fw) in enumerate(FCH):
                    mk.op("tensor", lambda E, f0=f0, fw=fw, fi=fi, kc=kc:
                          E.matmul(pb[fi][0:1, 0:fw], ones_bf, sqb[:, f0:f0 + fw],
                                   start=(kc == 0), stop=(kc == KC - 1)),
                          waits=([s_sq, v_ones, ent_vs] if kc == 0 else [s_sq]),
                          incs=[("ts", 1)])
                sq_done[kc] = mk.c("ts")
                read_done[kc] = mk.c("ts")
            read_done, sq_done = {}, {}
            t_stats = mk.c("ts")
            for fi, (f0, fw) in enumerate(FCH):
                if with_mean:
                    mk.op("vector", lambda E, f0=f0, fw=fw, fi=fi, c0=c0:
                          E.tensor_copy(s1[:, c0 + f0:c0 + f0 + fw], pb[4 + fi][0:1, 0:fw]),
                          waits=[t_stats], incs=[("vs", 1)])
                mk.op("vector", lambda E, f0=f0, fw=fw, fi=fi, c0=c0:
                      E.tensor_copy(s2[:, c0 + f0:c0 + f0 + fw], pb[fi][0:1, 0:fw]),
                      waits=[t_stats], incs=[("vs", 1)])
            if len(ranges) > 1 and c0 == 0:
                mk.wait("gpsimd", [mk.c("vs")])
        c1 = 1.0 / 1024.0
        out = {}
        if with_mean:
            mk.op("vector", lambda E: E.tensor_scalar_mul(s1, s1, c1), incs=[("vs", 1)])
            mk.op("vector", lambda E: E.tensor_mul(t2, s1, s1), incs=[("vs", 1)])
            mk.op("vector", lambda E: E.scalar_tensor_tensor(s2, s2, c1, t2, ALU.mult, ALU.subtract),
                  incs=[("vs", 1)])
        else:
            mk.op("vector", lambda E: E.tensor_scalar_mul(s2, s2, c1), incs=[("vs", 1)])
        v_v = mk.c("vs")
        if fused2:
            mk.op("vector", lambda E: E.tensor_scalar_add(t2, s2, EPS), waits=[v_v], incs=[("vs", 1)])
            mk.op("vector", lambda E: E.reciprocal(t2, t2), incs=[("vs", 1)])
            mk.op("vector", lambda E: E.tensor_mul(t2, s2, t2), incs=[("vs", 1)])
            v_t2 = mk.c("vs")
            mk.op("scalar", lambda E: E.activation(t2, t2, AF.Sqrt, bias=eps_col[0:1, :]),
                  waits=[v_t2], incs=[("ss", 1)])
            s_t2 = mk.c("ss")
            mk.op("vector", lambda E: E.reciprocal(t2, t2), waits=[s_t2], incs=[("vs", 1)])
        mk.op("scalar", lambda E: E.activation(s2, s2, AF.Sqrt, bias=eps_col[0:1, :]),
              waits=[mk.c("vs")], incs=[("ss", 1)])
        s_r = mk.c("ss")
        mk.op("vector", lambda E: E.reciprocal(s2, s2), waits=[s_r], incs=[("vs", 1)])
        v_r = mk.c("vs")
        out["tok_r"] = mk.dma("s", scr["rows_scr"][ship_row0 + 0, ship_off:ship_off + ncols], s2, waits=[v_r], ch=1)
        if fused2:
            mk.op("vector", lambda E: E.tensor_mul(t2, s2, t2), incs=[("vs", 1)])
            v_g = mk.c("vs")
            out["tok_g"] = mk.dma("s", scr["rows_scr"][ship_row0 + 1, ship_off:ship_off + ncols], t2, waits=[v_g], ch=2)
        if with_mean:
            mk.op("vector", lambda E: E.tensor_mul(s2, s1, s2), waits=[out["tok_r"]], incs=[("vs", 1)])
            v_mr = mk.c("vs")
            out["tok_mr"] = mk.dma("s", scr["rows_scr"][ship_row0 + 2, ship_off:ship_off + ncols], s2, waits=[v_mr], ch=3)
            if fused2:
                mk.op("vector", lambda E: E.tensor_mul(t2, s1, t2),
                      waits=[out["tok_g"]], incs=[("vs", 1)])
                v_mg = mk.c("vs")
                out["tok_mg"] = mk.dma("s", scr["rows_scr"][ship_row0 + 3, ship_off:ship_off + ncols], t2, waits=[v_mg], ch=4)
        return out

    def bcast(dst, slot, ncols, tok, q="g", off=0):
        return mk.dma(q, dst, scr["rows_scr"][slot, off:off + ncols].partition_broadcast(128),
                      waits=[tok], ch=2 + slot % 4)

    def load_mrow(tok_mr, ncols=T, slot=2):
        mrl = rslot(2, ncols)
        tok = mk.dma("s", mrl, scr["rows_scr"][slot:slot + 1, 0:ncols], waits=[tok_mr], ch=5)
        mk.op("vector", lambda E: E.tensor_scalar_mul(mrow_bf[:, 0:ncols], mrl, -1.0),
              waits=[tok], incs=[("vs", 1)])
        return mk.c("vs")

    def gw_hw(w_bf, nout, gamma_name, beta_name, bias_name, w_tok):
        c0 = SMCOL[gamma_name][0]
        mk.op("vector", lambda E, c0=c0: E.tensor_copy(gcol_bf, SM[:, c0:c0 + 8]),
              waits=[t_fix], incs=[("vs", 1)])
        c0b = SMCOL[beta_name][0]
        mk.op("vector", lambda E, c0b=c0b: E.tensor_copy(bcol_bf, SM[:, c0b:c0b + 8]),
              incs=[("vs", 1)])
        v_cols = mk.c("vs")
        for f0 in range(0, nout, 512):
            for kc in range(KC):
                mk.op("tensor", lambda E, f0=f0, kc=kc:
                      E.matmul(pb[5][0:1, 0:512], gcol_bf[:, kc:kc + 1],
                               w_bf[:, kc, f0:f0 + 512], start=(kc == 0), stop=(kc == KC - 1)),
                      waits=([v_cols, w_tok, mk.c("vs")] if kc == 0 else []), incs=[("ts", 1)])
            t_gw = mk.c("ts")
            mk.op("vector", lambda E, f0=f0: E.tensor_copy(gw_bf[:, f0:f0 + 512], pb[5][0:1, 0:512]),
                  waits=[t_gw], incs=[("vs", 1)])
        v_gw = mk.c("vs")
        for mc in range(nout // 128):
            for kc in range(KC):
                mk.op("tensor", lambda E, mc=mc, kc=kc:
                      E.matmul(pb[4][:, 0:1], w_bf[:, kc, mc * 128:mc * 128 + 128],
                               bcol_bf[:, kc:kc + 1], start=(kc == 0), stop=(kc == KC - 1)),
                      waits=([v_cols, w_tok, mk.c("vs")] if kc == 0 else []),
                      incs=[("ts", 1)])
            t_hw = mk.c("ts")
            mk.op("vector", lambda E, mc=mc:
                  E.tensor_add(hw_col[:, mc:mc + 1], pb[4][:, 0:1], smcol(bias_name, mc)),
                  waits=[t_hw], incs=[("vs", 1)])
        return v_gw, mk.c("vs")

    def m_chain(which):
        xsrc = dt_in["x_own" if which == "own" else "x_scan"]
        spill = scr["xbsp" if which == "own" else "xbscan"]
        nb = 3 if which == "own" else 12
        shift = 1 if which == "own" else 4
        ncols = T + nb
        bm = BMASK[:, 12:12 + nb] if which == "own" else BMASK[:, 0:nb]

        st = stats(lambda kc: xsrc[kc * 128:(kc + 1) * 128, :], ncols,
                   with_mean=True, fused2=True)
        bc_r = AR_B[:, 0:ncols]
        bc_g = AR_B[:, 2080:2080 + ncols]
        bc_mr = AR_B[:, 4160:4160 + ncols]
        bc_mg = AR_B[:, 6240:6240 + ncols]
        tb_r = bcast(bc_r, 0, ncols, st["tok_r"])
        tb_g = bcast(bc_g, 1, ncols, st["tok_g"])
        tb_mr = bcast(bc_mr, 2, ncols, st["tok_mr"])
        tb_mg = bcast(bc_mg, 3, ncols, st["tok_mg"])

        spill_toks, chain_toks = {}, {}
        spill0 = mk.c("dg6")
        for kc in range(KC):
            xs_c = AR_A[:, (kc % 2) * 2064:(kc % 2) * 2064 + ncols]
            xn_c = AR_A[:, 4128 + (kc % 2) * 2064:4128 + (kc % 2) * 2064 + ncols]
            xb_c = AR_A[:, 8256:8256 + T]
            war = ([chain_toks[kc - 2]] if kc - 2 in chain_toks else [tb_mg])
            tok_x = mk.dma("s", xs_c, xsrc[kc * 128:(kc + 1) * 128, 0:ncols], waits=war, ch=6 + kc % 2)
            mk.op("vector", lambda E, xn_c=xn_c, xs_c=xs_c: E.tensor_mul(xn_c, xs_c, bc_g),
                  waits=[tok_x] + ([tb_g] if kc == 0 else []), incs=[("vs", 1)])
            mk.op("vector", lambda E, xn_c=xn_c: E.tensor_sub(xn_c, xn_c, bc_mg),
                  incs=[("vs", 1)])
            mk.op("vector", lambda E, xn_c=xn_c, kc=kc:
                  E.tensor_scalar(xn_c, xn_c, smcol("ln1_w", kc), smcol("ln1_b", kc),
                                  ALU.mult, ALU.add),
                  waits=([t_fix] if kc == 0 else []), incs=[("vs", 1)])
            mk.op("vector", lambda E, xn_c=xn_c, bm=bm:
                  E.tensor_mul(xn_c[:, 0:nb], xn_c[:, 0:nb], bm), incs=[("vs", 1)])
            warb = ([spill_toks[kc - 1]] if kc - 1 in spill_toks else [spill0])
            mk.op("vector", lambda E, xb_c=xb_c, xs_c=xs_c, nb=nb:
                  E.tensor_mul(xb_c, xs_c[:, nb:nb + T], bc_r[:, nb:nb + T]),
                  waits=([tb_r] if kc == 0 else []) + warb, incs=[("vs", 1)])
            mk.op("vector", lambda E, xb_c=xb_c, nb=nb:
                  E.tensor_sub(xb_c, xb_c, bc_mr[:, nb:nb + T]),
                  waits=([tb_mr] if kc == 0 else []), incs=[("vs", 1)])
            for k in range(4):
                mk.op("vector", lambda E, xb_c=xb_c, xn_c=xn_c, k=k, kc=kc, shift=shift:
                      E.scalar_tensor_tensor(xb_c, xn_c[:, k * shift:k * shift + T],
                                             smcol(f"cw{k}", kc), xb_c, ALU.mult, ALU.add),
                      incs=[("vs", 1)])
            mk.op("vector", lambda E, xb_c=xb_c, kc=kc:
                  E.tensor_scalar_add(xb_c, xb_c, smcol("conv_b", kc)), incs=[("vs", 1)])
            v_xb = mk.c("vs")
            chain_toks[kc] = v_xb
            spill_toks[kc] = mk.dma("g", spill[:, kc, :], xb_c, waits=[v_xb], ch=6)
        return spill_toks[KC - 1]

    tok_xb_own = m_chain("own")
    mk.barrier()
    tok_xb_scan = m_chain("scan")
    mk.barrier()

    stb = stats(lambda kc: scr["xbscan"][:, kc, :], T, extra_wait=[tok_xb_scan])
    bc_r2 = AR_B[:, 0:T]
    tb_r2 = bcast(bc_r2, 0, T, stb["tok_r"])
    v_mrow = load_mrow(stb["tok_mr"])

    inw_bf = WS[:, 0:8192].bitcast(BF16).rearrange("p (k m) -> p k m", k=KC)
    tok_inw = None
    for kc in range(KC):
        tok_inw = mk.dma("g", inw_bf[:, kc, :], dt_in["in_w"][kc * 128:(kc + 1) * 128, :], ch=7)
    v_gw, v_hw = gw_hw(inw_bf, 2048, "lns_w", "lns_b", "in_b", tok_inw)

    sp_bf = AR_B[:, 2048:2048 + 8192].bitcast(BF16).rearrange("p (k m) -> p k m", k=KC)
    z_f16 = AR_A[:, 0:8192].bitcast(F16).rearrange("p (k m) -> p k m", k=KC)
    rhs_b = [AR_C[:, 0:2048].bitcast(BF16).rearrange("p (k m) -> p k m", k=KC),
             AR_C[:, 2048:4096].bitcast(BF16).rearrange("p (k m) -> p k m", k=KC)]
    xb_f = [AR_C[:, 4096:8192], AR_C[:, 8192:12288]]

    evac_toks, rhsuse_toks = {}, {}
    for ft in range(4):
        rb, xf = rhs_b[ft % 2], xb_f[ft % 2]
        war = ([rhsuse_toks[ft - 2]] if ft - 2 in rhsuse_toks else [tok_xb_scan])
        tok_xf = mk.dma("s", xf.rearrange("p (k m) -> p k m", k=KC),
                        scr["xbscan"][:, :, ft * 512:(ft + 1) * 512],
                        waits=war, ch=6 + ft % 2)
        for kc in range(KC):
            mk.op("vector", lambda E, rb=rb, xf=xf, kc=kc, ft=ft:
                  E.scalar_tensor_tensor(rb[:, kc, :], xf[:, kc * 512:(kc + 1) * 512],
                                         smcol("lns_w", kc), bc_r2[:, ft * 512:(ft + 1) * 512],
                                         ALU.mult, ALU.mult),
                  waits=([tok_xf, tb_r2] if kc == 0 else [tok_xf] if kc == 1 else []),
                  incs=[("vs", 1)])
        v_rhs = mk.c("vs")
        for mc in range(16):
            bank = pb[mc % 2]
            key = ft * 16 + mc
            war_t = ([evac_toks[key - 2]] if key - 2 in evac_toks else [])
            for kc in range(KC):
                mk.op("tensor", lambda E, bank=bank, mc=mc, kc=kc, rb=rb:
                      E.matmul(bank[:, 0:512], inw_bf[:, kc, mc * 128:mc * 128 + 128],
                               rb[:, kc, :], start=(kc == 0), stop=False),
                      waits=(([v_rhs] + war_t + ([tok_inw, v_gw, v_mrow, v_hw] if key == 0 else []))
                             if kc == 0 else []),
                      incs=[("ts", 1)])
            mk.op("tensor", lambda E, bank=bank, mc=mc, ft=ft:
                  E.matmul(bank[:, 0:512], gw_bf[:, mc * 128:mc * 128 + 128],
                           mrow_bf[:, ft * 512:(ft + 1) * 512], start=False, stop=True),
                  incs=[("ts", 1)])
            t_mm = mk.c("ts")
            if mc < 8:
                mk.op("scalar", lambda E, bank=bank, mc=mc, ft=ft:
                      E.activation(sp_bf[:, mc, ft * 512:(ft + 1) * 512], bank[:, 0:512],
                                   AF.Silu, bias=hw_col[:, mc:mc + 1]),
                      waits=[t_mm], incs=[("ss", 1)])
            else:
                mk.op("scalar", lambda E, bank=bank, mc=mc, ft=ft:
                      E.activation(z_f16[:, mc - 8, ft * 512:(ft + 1) * 512], bank[:, 0:512],
                                   AF.Sigmoid, bias=hw_col[:, mc:mc + 1]),
                      waits=[t_mm], incs=[("ss", 1)])
            evac_toks[key] = mk.c("ss")
        rhsuse_toks[ft] = mk.c("ts")
    s_inw = mk.c("ss")

    tok_xpw = mk.dma("g", xpw_bf.rearrange("p (k n) -> p k n", k=KC),
                 dt_in["xp_w"].rearrange("(k p) n -> p k n", p=128), ch=9)
    mk.op("vector", lambda E: E.memset(u64q, 0.0), incs=[("vs", 1)])
    v_u0 = mk.c("vs")
    for ft in range(4):
        for kc in range(KC):
            mk.op("tensor", lambda E, ft=ft, kc=kc:
                  E.matmul(pb[2][0:16, 0:512], xpw_bf[:, kc * 16:kc * 16 + 16],
                           sp_bf[:, kc, ft * 512:(ft + 1) * 512],
                           start=(kc == 0), stop=(kc == KC - 1)),
                  waits=([tok_xpw, s_inw, mk.c("vs")] if kc == 0 else []), incs=[("ts", 1)])
        t_u = mk.c("ts")
        for b in range(4):
            mk.op("vector", lambda E, ft=ft, b=b:
                  E.tensor_copy(u64q[32 * b:32 * b + 16, 128 * ft:128 * ft + 128],
                                pb[2][0:16, b:512:4]),
                  waits=([t_u, v_u0] if b == 0 else []), incs=[("vs", 1)])
    v_u = mk.c("vs")
    mk.op("scalar", lambda E: E.activation(ld64[0:16, :], smcol("A_log", rows=16), AF.Exp),
          waits=[t_fix], incs=[("ss", 1)])
    s_ld = mk.c("ss")
    mk.op("vector", lambda E: E.tensor_scalar_mul(ld64[0:16, :], ld64[0:16, :], -1.0),
          waits=[s_ld], incs=[("vs", 1)])
    mk.op("vector", lambda E: E.tensor_copy(xpbq[0:16, :], smcol("xp_b", rows=16)), incs=[("vs", 1)])
    for q in (32, 64, 96):
        mk.op("vector", lambda E, q=q: E.tensor_copy(ld64[q:q + 16, :], ld64[0:16, :]),
              incs=[("vs", 1)])
        mk.op("vector", lambda E, q=q: E.tensor_copy(xpbq[q:q + 16, :], xpbq[0:16, :]),
              incs=[("vs", 1)])
    mk.op("vector", lambda E: E.tensor_scalar_mul(ld512, ld64, 512.0), incs=[("vs", 1)])
    v_ld = mk.c("vs")
    mk.op("vector", lambda E: E.memset(dk_rep, 1.0), incs=[("vs", 1)])
    mk.op("vector", lambda E: E.tensor_tensor_scan(p512, dk_rep, dk_rep, 0.0, ALU.mult, ALU.add),
          incs=[("vs", 1)])
    v_iota = mk.c("vs")
    mk.op("scalar", lambda E: E.activation(dk_rep, p512, AF.Exp, bias=ld64, scale=0.0),
          waits=[v_ld, v_iota], incs=[("ss", 1)])
    mk.op("scalar", lambda E: E.activation(p512, p512, AF.Exp, bias=0.0, scale=ld64),
          waits=[mk.c("ss")], incs=[("ss", 1)])
    s_dk = mk.c("ss")
    mk.op("vector", lambda E: E.tensor_scalar_add(u64q, u64q, xpbq), waits=[v_u, v_ld],
          incs=[("vs", 1)])
    v_u64 = mk.c("vs")
    mk.op("vector", lambda E: E.tensor_tensor_scan(hsq, dk_rep, u64q, 0.0, ALU.mult, ALU.add),
          waits=[v_u64, s_dk], incs=[("vs", 1)])
    mk.op("vector", lambda E: E.tensor_copy(ecol, hsq[:, 511:512]), incs=[("vs", 1)])
    v_e = mk.c("vs")
    tok_e = mk.dma("s", scr["esnd"][0:128], ecol[:, 0:1], waits=[v_e], ch=5)
    mk.op("gpsimd", lambda E: E.collective_compute(
        "AllGather", ALU.bypass, replica_groups=[list(range(8))],
        ins=[scr["esnd"][:].opt()], outs=[scr["ercv"][:].opt()]),
        waits=[tok_e], incs=[("cc", 1)])
    cc_e = mk.c("cc")
    tok_ca = mk.dma("g", ca8[:, :], scr["ercv"].transpose([1, 0]),
                    waits=[cc_e], allow_slow_non_contiguous=True, ch=8)
    mk.op("scalar", lambda E: E.activation(w8t, CPW[:], AF.Exp, bias=0.0, scale=ld512),
          waits=[v_ld, t_fix], incs=[("ss", 1)])
    s_w8 = mk.c("ss")
    mk.op("vector", lambda E: E.tensor_mul(w8t, w8t, ca8), waits=[s_w8, tok_ca], incs=[("vs", 1)])
    mk.op("vector", lambda E: E.reduce_sum(carry, w8t, axis=AXX), incs=[("vs", 1)])
    mk.op("vector", lambda E: E.scalar_tensor_tensor(hsq, p512, carry, hsq, ALU.mult, ALU.add),
          incs=[("vs", 1)])
    v_hs = mk.c("vs")
    for b in range(4):
        mk.op("vector", lambda E, b=b: E.tensor_copy(hsr_bf[:, b::4], hsq[32 * b:32 * b + 16, :]),
              waits=([v_hs] if b == 0 else []), incs=[("vs", 1)])
    v_hsr = mk.c("vs")

    tok_d = mk.dma("g", dmat_bf, dt_in["dmat"], ch=9)
    outw_bf = WS[:, 4608:4608 + 4096].bitcast(BF16).rearrange("p (k m) -> p k m", k=KC)
    tok_ow = None
    t_inw_done = mk.c("ts")
    for kc in range(KC):
        tok_ow = mk.dma("g", outw_bf[:, kc, :], dt_in["out_w"][kc * 128:(kc + 1) * 128, :],
                        waits=([t_inw_done] if kc == 0 else []), ch=7)

    ysd_bf = AR_C[:, 0:8192].bitcast(BF16).rearrange("p (k m) -> p k m", k=KC)
    xbo_f = [AR_C[:, 8192:12288]]
    ysd_toks = {}
    for ft in range(4):
        for mc in range(8):
            bank = pb[mc % 2]
            key = ft * 8 + mc
            war = ([ysd_toks[key - 2]] if key - 2 in ysd_toks else [])
            mk.op("tensor", lambda E, bank=bank, mc=mc, ft=ft:
                  E.matmul(bank[:, 0:512], dmat_bf[:, mc * 128:mc * 128 + 128],
                           hsr_bf[:, ft * 512:(ft + 1) * 512], start=True, stop=True),
                  waits=([v_hsr, tok_d] if key == 0 else []) + war, incs=[("ts", 1)])
            t_ysd = mk.c("ts")
            mk.op("vector", lambda E, bank=bank, mc=mc, ft=ft:
                  E.scalar_tensor_tensor(ysd_bf[:, mc, ft * 512:(ft + 1) * 512],
                                         bank[:, 0:512], 1.0,
                                         z_f16[:, mc, ft * 512:(ft + 1) * 512],
                                         ALU.mult, ALU.mult),
                  waits=[t_ysd], incs=[("vs", 1)])
            ysd_toks[key] = mk.c("vs")
    v_ysd = mk.c("vs")

    xm_toks = {}
    for ft in range(4):
        xbo = xbo_f[0]
        war = ([xm_toks[ft - 1]] if ft - 1 in xm_toks else [])
        tok_xbo = mk.dma("s", xbo.rearrange("p (k m) -> p k m", k=KC),
                         scr["xbsp"][:, :, ft * 512:(ft + 1) * 512],
                         waits=([tok_xb_own] if ft == 0 else []) + war, ch=6)
        xbo_k = xbo.rearrange("p (k m) -> p k m", k=KC)
        for mc in range(8):
            bank = pb[2 + mc % 2]
            key = ft * 8 + mc
            for kc in range(KC):
                mk.op("tensor", lambda E, bank=bank, mc=mc, kc=kc, ft=ft:
                      E.matmul(bank[:, 0:512], outw_bf[:, kc, mc * 128:mc * 128 + 128],
                               ysd_bf[:, kc, ft * 512:(ft + 1) * 512],
                               start=(kc == 0), stop=(kc == KC - 1)),
                      waits=(([v_ysd, tok_ow] if key == 0 else []) +
                             ([mk.c("vs")] if kc == 0 and key >= 2 else []))
                      if kc == 0 else [],
                      incs=[("ts", 1)])
            t_xm = mk.c("ts")
            mk.op("vector", lambda E, bank=bank, mc=mc, xbo_k=xbo_k:
                  E.scalar_tensor_tensor(xbo_k[:, mc, :], bank[:, 0:512], smcol("out_b", mc),
                                         xbo_k[:, mc, :], ALU.add, ALU.add),
                  waits=[t_xm, tok_xbo], incs=[("vs", 1)])
        v_xm = mk.c("vs")
        xm_toks[ft] = mk.dma("g",
                             scr["xmid"][:, :, ft * 512:(ft + 1) * 512],
                             xbo.rearrange("p (k m) -> p k m", k=KC), waits=[v_xm], ch=6)
    tok_xmid = xm_toks[3]

    if stage >= 2:
        mk.barrier()
        stc = stats(lambda kc: scr["xmid"][:, kc, :], T)
        bc_r3 = AR_B[:, 0:T]
        tb_r3 = bcast(bc_r3, 0, T, stc["tok_r"])
        v_mrow3 = load_mrow(stc["tok_mr"])

        w1_bf = WS[:, 0:8192].bitcast(BF16).rearrange("p (k m) -> p k m", k=KC)
        tok_w1 = None
        t_ws_free = mk.c("ts")
        for kc in range(KC):
            tok_w1 = mk.dma("g", w1_bf[:, kc, :], dt_in["mlp_w1"][kc * 128:(kc + 1) * 128, :],
                            ch=7, waits=([t_ws_free] if kc == 0 else []))
        v_gw3, v_hw3 = gw_hw(w1_bf, 2048, "ln2_w", "ln2_b", "mlp_b1", tok_w1)

        w2c_bf = [WS[:, 8192:8704].bitcast(BF16), WS[:, 8704:9216].bitcast(BF16)]
        hdn_bf = AR_A[:, 0:4096].bitcast(BF16).rearrange("p (k m) -> p k m", k=16)
        rhs_c = [AR_C[:, 0:2048].bitcast(BF16).rearrange("p (k m) -> p k m", k=KC),
                 AR_C[:, 2048:4096].bitcast(BF16).rearrange("p (k m) -> p k m", k=KC)]
        xm_f = [AR_C[:, 4096:8192], AR_C[:, 8192:12288]]

        prev_spill, rhsuse = {}, {}
        hdn_war = None
        for ft in range(4):
            rb, xf = rhs_c[ft % 2], xm_f[ft % 2]
            war = ([rhsuse[ft - 2]] if ft - 2 in rhsuse else [])
            warsp = ([prev_spill[ft - 2]] if ft - 2 in prev_spill else [])
            tok_xf = mk.dma("s", xf.rearrange("p (k m) -> p k m", k=KC),
                            scr["xmid"][:, :, ft * 512:(ft + 1) * 512],
                            waits=war + warsp, ch=6 + ft % 2)
            for kc in range(KC):
                mk.op("vector", lambda E, rb=rb, xf=xf, kc=kc, ft=ft:
                      E.scalar_tensor_tensor(rb[:, kc, :], xf[:, kc * 512:(kc + 1) * 512],
                                             smcol("ln2_w", kc), bc_r3[:, ft * 512:(ft + 1) * 512],
                                             ALU.mult, ALU.mult),
                      waits=([tok_xf, tb_r3] if kc == 0 else []), incs=[("vs", 1)])
            v_rhs = mk.c("vs")
            for mc in range(16):
                bank = pb[mc % 2]
                key = ft * 16 + mc
                war_t = ([mk.c("ss")] if mc >= 2 else ([hdn_war] if hdn_war else []))
                for kc in range(KC):
                    mk.op("tensor", lambda E, bank=bank, mc=mc, kc=kc, rb=rb:
                          E.matmul(bank[:, 0:512], w1_bf[:, kc, mc * 128:mc * 128 + 128],
                                   rb[:, kc, :], start=(kc == 0), stop=False),
                          waits=(([v_rhs] + war_t +
                                  ([tok_w1, v_gw3, v_mrow3, v_hw3] if key == 0 else []))
                                 if kc == 0 else []),
                          incs=[("ts", 1)])
                mk.op("tensor", lambda E, bank=bank, mc=mc, ft=ft:
                      E.matmul(bank[:, 0:512], gw_bf[:, mc * 128:mc * 128 + 128],
                               mrow_bf[:, ft * 512:(ft + 1) * 512], start=False, stop=True),
                      incs=[("ts", 1)])
                t_mm = mk.c("ts")
                mk.op("scalar", lambda E, bank=bank, mc=mc:
                      E.activation(hdn_bf[:, mc, :], bank[:, 0:512], AF.Gelu,
                                   bias=hw_col[:, mc:mc + 1]),
                      waits=[t_mm], incs=[("ss", 1)])
            s_hdn = mk.c("ss")
            tokw2, w2war = {}, {}
            for kc in range(16):
                w2b = w2c_bf[kc % 2]
                war2 = ([w2war[kc - 2]] if kc - 2 in w2war else [mk.c("ts")])
                tokw2[kc] = mk.dma("g", w2b, dt_in["mlp_w2"][kc * 128:(kc + 1) * 128, :],
                                   ch=9 + kc % 2, waits=war2)
                for mc in range(8):
                    mk.op("tensor", lambda E, kc=kc, mc=mc, w2b=w2b:
                          E.matmul(pb[mc][:, 0:512], w2b[:, mc * 128:mc * 128 + 128],
                                   hdn_bf[:, kc, :], start=(kc == 0), stop=(kc == 15)),
                          waits=(([tokw2[kc]] + ([s_hdn] if kc == 0 else [])) if mc == 0 else []),
                          incs=[("ts", 1)])
                w2war[kc] = mk.c("ts")
            t_w2 = mk.c("ts")
            hdn_war = t_w2
            xm_k = xf.rearrange("p (k m) -> p k m", k=KC)
            for mc in range(8):
                mk.op("vector", lambda E, mc=mc, xm_k=xm_k:
                      E.scalar_tensor_tensor(xm_k[:, mc, :], pb[mc][:, 0:512], smcol("mlp_b2", mc),
                                             xm_k[:, mc, :], ALU.add, ALU.add),
                      waits=([t_w2] if mc == 0 else []), incs=[("vs", 1)])
            v_bo = mk.c("vs")
            prev_spill[ft] = mk.dma("g", scr["xmid"][:, :, ft * 512:(ft + 1) * 512],
                                    xm_k, waits=[v_bo], ch=6 + 5 * (ft % 2))
            rhsuse[ft] = mk.c("ts")
        mk.barrier()

        std = stats(lambda kc: scr["xmid"][:, kc, :], T)
        bc_r4 = AR_B[:, 0:T]
        bc_mr4 = AR_B[:, 2080:2080 + T]
        tb_r4 = bcast(bc_r4, 0, T, std["tok_r"])
        tb_mr4 = bcast(bc_mr4, 2, T, std["tok_mr"])
        bo_c = [AR_C[:, 0:4096], AR_C[:, 4096:8192]]
        xo_c = AR_C[:, 8192:12288]
        hic = AR_A[:, 0:2048].bitcast(BF16).rearrange("p (k m) -> p k m", k=KC)
        loc = AR_A[:, 2048:4096].bitcast(BF16).rearrange("p (k m) -> p k m", k=KC)
        hif = AR_A[:, 4096:4608]
        prev = {}
        tok_hi = tok_lo = None
        for ft in range(4):
            boc = bo_c[ft % 2]
            war = ([prev[ft - 2]] if ft - 2 in prev else [])
            warx = ([prev[ft - 1]] if ft - 1 in prev else [])
            tok_bo = mk.dma("s", boc.rearrange("p (k m) -> p k m", k=KC),
                            scr["xmid"][:, :, ft * 512:(ft + 1) * 512],
                            waits=war, ch=6 + ft % 2)
            tok_xo = mk.dma("g", xo_c.rearrange("p (k m) -> p k m", k=KC),
                            dt_in["x_own"].rearrange("(k p) m -> p k m", p=128)[:, :, 3 + ft * 512:3 + (ft + 1) * 512],
                            waits=warx, ch=8)
            boc_k = boc.rearrange("p (k m) -> p k m", k=KC)
            xoc_k = xo_c.rearrange("p (k m) -> p k m", k=KC)
            sl = slice(ft * 512, (ft + 1) * 512)
            war_hl = [tok_hi, tok_lo] if tok_hi else []
            for kc in range(KC):
                mk.op("vector", lambda E, boc_k=boc_k, kc=kc, sl=sl:
                      E.tensor_mul(boc_k[:, kc, :], boc_k[:, kc, :], bc_r4[:, sl]),
                      waits=([tok_bo, tb_r4] if kc == 0 else []), incs=[("vs", 1)])
                mk.op("vector", lambda E, boc_k=boc_k, kc=kc, sl=sl:
                      E.tensor_sub(boc_k[:, kc, :], boc_k[:, kc, :], bc_mr4[:, sl]),
                      waits=([tb_mr4] if kc == 0 else []), incs=[("vs", 1)])
                mk.op("vector", lambda E, boc_k=boc_k, xoc_k=xoc_k, kc=kc:
                      E.tensor_add(boc_k[:, kc, :], boc_k[:, kc, :], xoc_k[:, kc, :]),
                      waits=([tok_xo] if kc == 0 else []), incs=[("vs", 1)])
                mk.op("vector", lambda E, boc_k=boc_k, kc=kc:
                      E.tensor_copy(hic[:, kc, :], boc_k[:, kc, :]),
                      waits=(war_hl if kc == 0 else []), incs=[("vs", 1)])
                mk.op("vector", lambda E, kc=kc:
                      E.tensor_copy(hif, hic[:, kc, :]), incs=[("vs", 1)])
                mk.op("vector", lambda E, boc_k=boc_k, kc=kc:
                      E.tensor_sub(loc[:, kc, :], boc_k[:, kc, :], hif), incs=[("vs", 1)])
            v_x1 = mk.c("vs")
            tok_hi = mk.dma("g", scr["xsnd"][:, 0:8, ft * 512:(ft + 1) * 512], hic,
                            waits=[v_x1], ch=6 + 5 * (ft % 2))
            tok_lo = mk.dma("g", scr["xsnd"][:, 8:16, ft * 512:(ft + 1) * 512], loc,
                            waits=[v_x1], ch=12 + ft % 2)
            prev[ft] = tok_lo
        mk.op("gpsimd", lambda E: E.collective_compute(
            "AllGather", ALU.bypass, replica_groups=[[0, 1], [2, 3], [4, 5], [6, 7]],
            ins=[scr["xsnd"][:].opt()], outs=[scr["xrcv"][:].opt()]),
            waits=[mk.c("dg6"), mk.c("dg11"), mk.c("dg12"), mk.c("dg13")], incs=[("cc", 1)])
        cc_x1 = mk.c("cc")

        t0 = AR_A[:, 0:1024]
        srcbuf = AR_A[:, 1024:1024 + 2048].bitcast(BF16).rearrange("p (v k m) -> p v k m", v=4, k=KC)
        x1o_c = AR_A[:, 3072 + 1024:3072 + 1024 + 1024]
        x1o_c = AR_A[:, 4096:5120]
        prevass = {}
        for j in range(16):
            g_ev, g_od = 2 * j, 2 * j + 1
            war = ([prevass[j - 1]] if j - 1 in prevass else [cc_x1])
            toks = []
            for vi, (g, hl) in enumerate([(g_ev, 0), (g_ev, 1), (g_od, 0), (g_od, 1)]):
                half, locc = g // 16, (g % 16) * 128
                toks.append(mk.dma("s", srcbuf[:, vi, :, :],
                                   scr["xrcv"][half, :, hl * 8:hl * 8 + 8, locc:locc + 128],
                                   waits=war, ch=(6, 7, 14, 15)[vi]))
            tsel = toks[-1]
            x1k = x1o_c.rearrange("p (k m) -> p k m", k=KC)
            mk.op("vector", lambda E, j=j:
                  E.tensor_scalar(x1o_c, srcbuf[:, 0, :, :].rearrange("p k m -> p (k m)"),
                                  PARSEL[:, 0:1], None, ALU.mult),
                  waits=toks + ([prevass[j - 1]] if j - 1 in prevass else []), incs=[("vs", 1)])
            mk.op("vector", lambda E, j=j:
                  E.scalar_tensor_tensor(x1o_c, srcbuf[:, 1, :, :].rearrange("p k m -> p (k m)"),
                                         PARSEL[:, 0:1], x1o_c, ALU.mult, ALU.add),
                  incs=[("vs", 1)])
            mk.op("vector", lambda E, j=j:
                  E.scalar_tensor_tensor(x1o_c, srcbuf[:, 2, :, :].rearrange("p k m -> p (k m)"),
                                         PARSEL[:, 1:2], x1o_c, ALU.mult, ALU.add),
                  incs=[("vs", 1)])
            mk.op("vector", lambda E, j=j:
                  E.scalar_tensor_tensor(x1o_c, srcbuf[:, 3, :, :].rearrange("p k m -> p (k m)"),
                                         PARSEL[:, 1:2], x1o_c, ALU.mult, ALU.add),
                  incs=[("vs", 1)])
            v_sel = mk.c("vs")
            prevass[j] = mk.dma("g", scr["x1own"][:, :, j * 128:(j + 1) * 128],
                                x1o_c.rearrange("p (k m) -> p k m", k=KC),
                                waits=[v_sel], ch=8)
        tok_x1own = prevass[15]

    if stage >= 3:
        mk.barrier()
        H = 16
        steo = stats(lambda kc: scr["x1own"][:, kc, :], T, fused2=True, ship_row0=4)
        bc_g5o = AR_B[:, 6144:6144 + T]
        tb_g5o = bcast(bc_g5o, 5, T, steo["tok_g"])
        v_mrow5 = load_mrow(steo["tok_mg"], slot=7)

        wq_bf = WS[:, 0:4096].bitcast(BF16).rearrange("p (k m) -> p k m", k=KC)
        kva_bf = WS[:, 4096:4736].bitcast(BF16).rearrange("p (k m) -> p k m", k=KC)
        kvb_bf = WS[:, 4736:4992].bitcast(BF16)
        wo_bf = WS[:, 4992:7040].bitcast(BF16).rearrange("p (k m) -> p k m", k=4)
        wgate_f = WS[:, 7040:7104].rearrange("p (k m) -> p k m", k=KC)
        tok_wq = tok_kva = tok_kvb = tok_wo = tok_wg8 = None
        for kc in range(KC):
            tok_wq = mk.dma("g", wq_bf[:, kc, :], dt_in["wq"][kc * 128:(kc + 1) * 128, :], ch=7)
        for kc in range(KC):
            tok_kva = mk.dma("g", kva_bf[:, kc, :], dt_in["wkva"][kc * 128:(kc + 1) * 128, :], ch=9)
        tok_kvb = mk.dma("g", kvb_bf, dt_in["wkvb"], ch=10)
        for kc in range(4):
            tok_wo = mk.dma("g", wo_bf[:, kc, :], dt_in["wo_eff"][kc * 128:(kc + 1) * 128, :], ch=8)
        tok_wg8 = mk.dma("s", wgate_f, dt_in["wgate"].rearrange("(k p) m -> p k m", p=128), ch=14)
        csqa_v = AR_B[:, 2048:3072].bitcast(BF16)
        csqb_v = AR_B[:, 3072:4096].bitcast(BF16)
        for rb in (32, 96):
            tok_csq = mk.dma("s", csqa_v[rb:rb + 32, :], dt_in["csq_a"], ch=15)
            tok_csq = mk.dma("s", csqb_v[rb:rb + 32, :], dt_in["csq_b"], ch=15)
        cska_bf = AR_B[:, 6144:8192].bitcast(BF16)[32:64, :]
        cskb_bf = AR_B[:, 8192:10240].bitcast(BF16)[32:64, :]
        maskq_bf = AR_B[:, 0:2048].bitcast(BF16)
        tok_mask = mk.dma("s", maskq_bf, dt_in["qmask"], ch=14)

        c0w = SMCOL["w_ln1"][0]
        mk.op("vector", lambda E: E.tensor_copy(gcol_bf, SM[:, c0w:c0w + 8]), incs=[("vs", 1)])
        v_g1 = mk.c("vs")
        for i, f0 in enumerate(range(0, 1024, 512)):
            for kc in range(KC):
                mk.op("tensor", lambda E, f0=f0, kc=kc:
                      E.matmul(pb[5][0:1, 0:512], gcol_bf[:, kc:kc + 1],
                               wq_bf[:, kc, f0:f0 + 512], start=(kc == 0), stop=(kc == KC - 1)),
                      waits=([v_g1, tok_wq, mk.c("vs")] if kc == 0 else []), incs=[("ts", 1)])
            t_gw = mk.c("ts")
            mk.op("vector", lambda E, f0=f0: E.tensor_copy(gw_bf[:, f0:f0 + 512], pb[5][0:1, 0:512]),
                  waits=[t_gw], incs=[("vs", 1)])
        for kc in range(KC):
            mk.op("tensor", lambda E, kc=kc:
                  E.matmul(pb[5][0:1, 0:160], gcol_bf[:, kc:kc + 1],
                           kva_bf[:, kc, :], start=(kc == 0), stop=(kc == KC - 1)),
                  waits=([tok_kva, mk.c("vs")] if kc == 0 else []), incs=[("ts", 1)])
        t_gk = mk.c("ts")
        mk.op("vector", lambda E: E.tensor_copy(gw_bf[:, 1024:1184], pb[5][0:1, 0:160]),
              waits=[t_gk], incs=[("vs", 1)])
        v_gwq = mk.c("vs")

        qp_bf = AR_A[:, 0:8192].bitcast(BF16)
        lat_bf = AR_A[:, 8192:10240].bitcast(BF16)
        xoc2 = [AR_C[:, 0:4096], AR_C[:, 4096:8192]]
        rhsq = AR_C[:, 8448:10496].bitcast(BF16).rearrange("p (k m) -> p k m", k=KC)
        ev_tok = {}
        for ft in range(4):
            xoc = xoc2[ft % 2]
            war = ([ev_tok[ft - 2]] if ft - 2 in ev_tok else [])
            tok_xo = mk.dma("s", xoc.rearrange("p (k m) -> p k m", k=KC),
                            scr["x1own"][:, :, ft * 512:(ft + 1) * 512],
                            waits=war, ch=6 + ft % 2)
            xok = xoc.rearrange("p (k m) -> p k m", k=KC)
            for kc in range(KC):
                mk.op("vector", lambda E, xok=xok, kc=kc, ft=ft:
                      E.scalar_tensor_tensor(rhsq[:, kc, :], xok[:, kc, :],
                                             smcol("w_ln1", kc), bc_g5o[:, ft * 512:(ft + 1) * 512],
                                             ALU.mult, ALU.mult),
                      waits=([tok_xo, tb_g5o, mk.c("ts")] if kc == 0 else []), incs=[("vs", 1)])
            v_rq = mk.c("vs")
            for pr in range(8):
                bank = pb[pr % 2]
                key = ft * 8 + pr
                war_t = ([ev_tok.get(("q", key - 2), ("ss", 0))] if key >= 2 else [])
                for kc in range(KC):
                    mk.op("tensor", lambda E, bank=bank, pr=pr, kc=kc:
                          E.matmul(bank[:, 0:512], wq_bf[:, kc, pr * 128:pr * 128 + 128],
                                   rhsq[:, kc, :], start=(kc == 0), stop=False),
                          waits=(([v_rq] + war_t + ([v_gwq, v_mrow5] if key == 0 else []))
                                 if kc == 0 else []),
                          incs=[("ts", 1)])
                mk.op("tensor", lambda E, bank=bank, pr=pr, ft=ft:
                      E.matmul(bank[:, 0:512], gw_bf[:, pr * 128:pr * 128 + 128],
                               mrow_bf[:, ft * 512:(ft + 1) * 512], start=False, stop=True),
                      incs=[("ts", 1)])
                t_q = mk.c("ts")
                mk.op("scalar", lambda E, bank=bank, pr=pr, ft=ft:
                      E.activation(qp_bf[:, pr * 2048 + ft * 512:pr * 2048 + (ft + 1) * 512],
                                   bank[:, 0:512], AF.Copy, scale=0.125),
                      waits=[t_q], incs=[("ss", 1)])
                ev_tok[("q", key)] = mk.c("ss")
            ev_tok[ft] = mk.c("ts")
        s_qdone = mk.c("ss")

        rtmp = AR_B[:, 4096:5120].bitcast(BF16)
        for pr in range(8):
            cols = slice(pr * 2048, (pr + 1) * 2048)
            for base in (32, 96):
                rt = rtmp[base:base + 32, :]
                mk.op("vector", lambda E, cols=cols, base=base, rt=rt:
                      E.stream_shuffle(rt, qp_bf[base:base + 32, cols],
                                       [i ^ 1 for i in range(32)]),
                      waits=([s_qdone, tok_csq] if (pr == 0 and base == 32) else []),
                      incs=[("vs", 1)])
                mk.op("vector", lambda E, rt=rt, base=base:
                      E.tensor_mul(rt, rt, csqb_v[base:base + 32, :]),
                      incs=[("vs", 1)])
                mk.op("vector", lambda E, cols=cols, base=base:
                      E.tensor_mul(qp_bf[base:base + 32, cols], qp_bf[base:base + 32, cols],
                                   csqa_v[base:base + 32, :]),
                      incs=[("vs", 1)])
                mk.op("vector", lambda E, cols=cols, base=base, rt=rt:
                      E.tensor_add(qp_bf[base:base + 32, cols], qp_bf[base:base + 32, cols], rt),
                      incs=[("vs", 1)])
        v_qrope = mk.c("vs")

        stg = {}
        for half in range(2):
            stg[half] = stats(lambda kc, half=half: scr["xrcv"][half, :, kc, :], T,
                              fused2=True, ship_off=half * T, tbuf=AR_C, tbase=4096)
        bc_g5g = [AR_B[:, 6144:6144 + T], AR_B[:, 8192:8192 + T]]
        tb_g5g = [bcast(bc_g5g[0], 1, T, stg[0]["tok_g"], off=0),
                  bcast(bc_g5g[1], 1, T, stg[1]["tok_g"], off=T)]
        mrowG_bf = ROWS[0:1, 0:2048].bitcast(BF16)
        mg_f = rslot(2, T)
        for half in range(2):
            tokm = mk.dma("s", mg_f, scr["rows_scr"][3:4, half * T:(half + 1) * T],
                          waits=[stg[half]["tok_mg"], mk.c("vs")], ch=5)
            mk.op("vector", lambda E, half=half:
                  E.tensor_scalar_mul(mrowG_bf[:, half * T:(half + 1) * T], mg_f, -1.0),
                  waits=[tokm], incs=[("vs", 1)])
        v_mrowG = mk.c("vs")

        krp_bf = ROWS[32:64, 0:2048].bitcast(BF16)
        khc = [AR_C[:, 0:2048].bitcast(BF16).rearrange("p (k m) -> p k m", k=KC),
               AR_C[:, 2048:4096].bitcast(BF16).rearrange("p (k m) -> p k m", k=KC)]
        rhk = AR_C[:, 8448:10496].bitcast(BF16).rearrange("p (k m) -> p k m", k=KC)
        kv_tok = {}
        for ftg in range(8):
            half, ftl = ftg // 4, ftg % 4
            hc = khc[ftg % 2]
            war = ([kv_tok[ftg - 2]] if ftg - 2 in kv_tok else [v_qrope])
            tok_h = mk.dma("s", hc, scr["xrcv"][half, :, 0:8, ftl * 512:(ftl + 1) * 512],
                           waits=war, ch=6 + ftg % 2)
            for kc in range(KC):
                mk.op("vector", lambda E, hc=hc, kc=kc, ftg=ftg:
                      E.scalar_tensor_tensor(rhk[:, kc, :], hc[:, kc, :],
                                             smcol("w_ln1", kc),
                                             bc_g5g[ftg // 4][:, (ftg % 4) * 512:(ftg % 4 + 1) * 512],
                                             ALU.mult, ALU.mult),
                      waits=([tok_h] + ([tb_g5g[0], tb_g5g[1], mk.c("ts")] if kc == 0 else []))
                      if kc <= 0 else [],
                      incs=[("vs", 1)])
            v_rk = mk.c("vs")
            for kc in range(KC):
                mk.op("tensor", lambda E, kc=kc, ftg=ftg:
                      E.matmul(pb[ftg % 2][:, 0:512], kva_bf[:, kc, 0:128],
                               rhk[:, kc, :], start=(kc == 0), stop=False),
                      waits=([v_rk, v_mrowG] if kc == 0 else []), incs=[("ts", 1)])
            mk.op("tensor", lambda E, ftg=ftg:
                  E.matmul(pb[ftg % 2][:, 0:512], gw_bf[:, 1024:1152],
                           mrowG_bf[:, ftg * 512:(ftg + 1) * 512], start=False, stop=True),
                  incs=[("ts", 1)])
            t_lat = mk.c("ts")
            for kc in range(KC):
                mk.op("tensor", lambda E, kc=kc, ftg=ftg:
                      E.matmul(pb[2 + ftg % 2][0:32, 0:512], kva_bf[:, kc, 128:160],
                               rhk[:, kc, :], start=(kc == 0), stop=False),
                      incs=[("ts", 1)])
            mk.op("tensor", lambda E, ftg=ftg:
                  E.matmul(pb[2 + ftg % 2][0:32, 0:512], gw_bf[:, 1152:1184],
                           mrowG_bf[:, ftg * 512:(ftg + 1) * 512], start=False, stop=True),
                  incs=[("ts", 1)])
            t_krp = mk.c("ts")
            mk.op("scalar", lambda E, ftg=ftg:
                  E.activation(lat_bf[:, ftg * 512:(ftg + 1) * 512], pb[ftg % 2][:, 0:512], AF.Copy),
                  waits=[t_lat], incs=[("ss", 1)])
            mk.op("scalar", lambda E, ftg=ftg:
                  E.activation(krp_bf[:, ftg * 512:(ftg + 1) * 512], pb[2 + ftg % 2][0:32, 0:512], AF.Copy),
                  waits=[t_krp], incs=[("ss", 1)])
            kv_tok[ftg] = mk.c("ss")
        s_ckv = mk.c("ss")

        tok_csk = mk.dma("s", cska_bf, dt_in["csk_a"], waits=[s_ckv], ch=15)
        tok_csk = mk.dma("s", cskb_bf, dt_in["csk_b"], ch=15)
        rtm2 = AR_B[:, 4096:6144].bitcast(BF16)[32:64, :]
        mk.op("vector", lambda E: E.stream_shuffle(rtm2[:, 0:2048], krp_bf[:, 0:2048],
                                                   [i ^ 1 for i in range(32)]),
              waits=[s_ckv, tok_csk], incs=[("vs", 1)])
        mk.op("vector", lambda E: E.stream_shuffle(rtm2[:, 2048:4096], krp_bf[:, 2048:4096],
                                                   [i ^ 1 for i in range(32)]),
              incs=[("vs", 1)])
        mk.op("vector", lambda E: E.tensor_mul(krp_bf, krp_bf, cska_bf), incs=[("vs", 1)])
        mk.op("vector", lambda E: E.tensor_mul(rtm2, rtm2, cskb_bf), incs=[("vs", 1)])
        mk.op("vector", lambda E: E.tensor_add(krp_bf, krp_bf, rtm2), incs=[("vs", 1)])
        v_rotk = mk.c("vs")

        vp_bf = AR_C[:, 0:8448].bitcast(BF16)
        mk.op("vector", lambda E: E.memset(vp_bf, 1.0), waits=[mk.c("ts")], incs=[("vs", 1)])
        v_vp0 = mk.c("vs")
        for kb in range(32):
            bank = pb[kb % 2]
            war = [mk.c("vs")] if kb >= 2 else [v_vp0]
            mk.op("tensor", lambda E, kb=kb, bank=bank:
                  E.matmul(bank[:, 0:512], lat_bf[:, kb * 128:(kb + 1) * 128],
                           kvb_bf, start=True, stop=True),
                  waits=[s_ckv, tok_kvb] + war, incs=[("ts", 1)])
            t_v = mk.c("ts")
            mk.op("vector", lambda E, kb=kb, bank=bank:
                  E.tensor_copy(vp_bf[:, kb * 528:kb * 528 + 528].rearrange("p (h v) -> p h v", h=16)[:, :, 0:32],
                                bank[:, 0:512].rearrange("p (h v) -> p h v", h=16)),
                  waits=[t_v], incs=[("vs", 1)])
        v_vp = mk.c("vs")

        kb_buf = AR_C[:, 8448:10496].bitcast(BF16)
        p_buf = AR_B[:, 0:0]
        p_buf = AR_C[:, 10240 - 2048 - 0:0] if False else None
        pB = AR_B[:, 4096:5120].bitcast(BF16)
        yh_bf = AR_B[0:32, 5120:6144].bitcast(BF16)
        lb_sb = AR_B[0:32, 6144:6400].bitcast(BF16)
        linv_bf = ROWS[0:1, 0:256].bitcast(BF16)
        ones32_bf = MISC[0:1, 2048:2064].bitcast(BF16)
        mk.op("vector", lambda E: E.memset(ones32_bf, 1.0), incs=[("vs", 1)])
        mk.op("vector", lambda E: E.tensor_copy(kb_buf[32:64, :], krp_bf),
              waits=[v_rotk], incs=[("vs", 1)])
        mk.op("vector", lambda E: E.tensor_copy(kb_buf[96:128, :], krp_bf),
              incs=[("vs", 1)])
        v_o32 = mk.c("vs")

        kbuild_tok = {}
        att_done = {}
        yspill = {}
        prev_tlb = None
        probe1 = isinstance(devtap, str) and devtap.startswith("dbg_attk")
        att_H = 1 if probe1 else H
        att_KB = int(devtap[8:]) if probe1 else 32
        if probe1:
            mk.op("vector", lambda E: E.memset(SC[:, 0:2048], 0.0), incs=[("vs", 1)])
            vz = mk.c("vs")
            ysflat = scr["ysp"].rearrange("p k t -> p (k t)")
            mk.dma("g", ysflat[:, 0:4096], SC[:, 0:2048].bitcast(BF16), waits=[vz], ch=6)
            mk.dma("g", ysflat[:, 4096:8192], SC[:, 0:2048].bitcast(BF16), ch=6)
        for h in range(att_H):
            pbase = 64 * (h % 2)
            war_kb = ([att_done[h - 2]] if h - 2 in att_done else [])
            cp_done = {}
            for ftk in range(8):
                bank = pb[6 if ftk % 2 == 0 else 7]
                war_cp = ([cp_done[ftk - 2]] if ftk - 2 in cp_done else [])
                mk.op("tensor", lambda E, h=h, ftk=ftk, bank=bank:
                      E.matmul(bank[0:32, 0:512], kvb_bf[:, 32 * h:32 * h + 32],
                               lat_bf[:, ftk * 512:(ftk + 1) * 512], start=True, stop=True),
                      waits=([s_ckv, v_rotk, mk.c("vs")] + war_kb if ftk == 0 else war_cp),
                      incs=[("ts", 1)])
                t_kn = mk.c("ts")
                mk.op("vector", lambda E, h=h, ftk=ftk, bank=bank, pbase=pbase:
                      E.tensor_copy(kb_buf[pbase:pbase + 32, ftk * 512:(ftk + 1) * 512],
                                    bank[0:32, 0:512]),
                      waits=[t_kn], incs=[("vs", 1)])
                cp_done[ftk] = mk.c("vs")
            kbuild_tok[h] = mk.c("vs")

            pr = h // 2
            qcols0 = pr * 2048
            exp_done = {}
            avused = {}
            for kb in range(att_KB):
                jmin = kb // 2
                for fc in range(jmin // 4, 4):
                    n_inv = min(max(jmin - 4 * fc, 0), 4)
                    qk_bank = pb[(kb + fc) % 2]
                    mk.op("tensor", lambda E, h=h, kb=kb, fc=fc, qk_bank=qk_bank, pbase=pbase, qcols0=qcols0:
                          E.matmul(qk_bank[:, 0:512],
                                   kb_buf[pbase:pbase + 64, kb * 128:(kb + 1) * 128],
                                   qp_bf[pbase:pbase + 64, qcols0 + fc * 512:qcols0 + (fc + 1) * 512],
                                   start=True, stop=True),
                          waits=(([kbuild_tok[h], v_qrope, v_vp, v_o32, tok_mask] if kb == 0 and fc == jmin // 4 else []) +
                                 ([exp_done[(kb - 2, fc)]] if (kb - 2, fc) in exp_done else []) +
                                 ([mk.c("vs")] if kb == 0 and fc == jmin // 4 and h >= 1 else [])),
                          incs=[("ts", 1)])
                    t_s = mk.c("ts")
                    pc = pB[:, fc * 512:(fc + 1) * 512]
                    war_p = ([avused[(kb - 1, fc)]] if (kb - 1, fc) in avused else [])
                    mk.op("scalar", lambda E, pc=pc, qk_bank=qk_bank:
                          E.activation(pc, qk_bank[:, 0:512], AF.Exp),
                          waits=[t_s] + war_p, incs=[("ss", 1)])
                    s_p = mk.c("ss")
                    v_after_mask = None
                    if n_inv > 0:
                        mk.op("vector", lambda E, pc=pc, n_inv=n_inv:
                              E.memset(pc[:, 0:n_inv * 128], 0.0),
                              waits=[s_p], incs=[("vs", 1)])
                        v_after_mask = mk.c("vs")
                    if 4 * fc <= jmin < 4 * fc + 4:
                        off = (jmin - 4 * fc) * 128
                        mk.op("vector", lambda E, pc=pc, off=off, kb=kb:
                              E.tensor_mul(pc[:, off:off + 128], pc[:, off:off + 128],
                                           maskq_bf[:, kb * 128:(kb + 1) * 128]),
                              waits=[s_p], incs=[("vs", 1)])
                        v_after_mask = mk.c("vs")
                    avw = [s_p] if v_after_mask is None else [v_after_mask]
                    mk.op("tensor", lambda E, h=h, kb=kb, fc=fc, pc=pc:
                          E.matmul(pb[2 + fc][0:33, 0:512],
                                   vp_bf[:, kb * 528 + 33 * h:kb * 528 + 33 * h + 33],
                                   pc, start=(kb == 0),
                                   stop=(kb == (att_KB - 1 if probe1 and att_KB < 32
                                                else 8 * fc + 7))),
                          waits=avw, incs=[("ts", 1)])
                    avused[(kb, fc)] = mk.c("ts")
                    exp_done[(kb, fc)] = mk.c("ts")
            for fc in range(4):
                t_av = avused[(att_KB - 1 if probe1 and att_KB < 32 else 8 * fc + 7, fc)]
                def _recip(E, fc=fc):
                    with mk.nc.allow_low_precision(reason="softmax denom as bf16 operand"):
                        return E.reciprocal(linv_bf[:, 0:512], pb[2 + fc][32:33, 0:512])
                mk.op("vector", _recip,
                      waits=[t_av] + ([prev_tlb] if prev_tlb is not None else []),
                      incs=[("vs", 1)])
                v_li = mk.c("vs")
                mk.op("tensor", lambda E:
                      E.matmul(pb[6][0:32, 0:512], ones32_bf, linv_bf[:, 0:512],
                               start=True, stop=True),
                      waits=[v_li, mk.c("vs")], incs=[("ts", 1)])
                t_lb = mk.c("ts")
                prev_tlb = t_lb
                mk.op("vector", lambda E:
                      E.tensor_copy(lb_sb, pb[6][0:32, 0:512]),
                      waits=[t_lb], incs=[("vs", 1)])
                mk.op("vector", lambda E, fc=fc:
                      E.tensor_mul(yh_bf[:, fc * 512:(fc + 1) * 512], pb[2 + fc][0:32, 0:512], lb_sb),
                      waits=([yspill[h - 1]] if fc == 0 and h >= 1 else []),
                      incs=[("vs", 1)])
            v_y = mk.c("vs")
            yspill[h] = mk.dma("g", scr["ysp"][32 * (h % 4):32 * (h % 4) + 32, h // 4, :],
                               yh_bf, waits=[v_y], ch=6 + h % 2)
            att_done[h] = mk.c("ts")
        mk.barrier()
        if probe1:
            mk.op("vector", lambda E: E.tensor_copy(AR_B[0:33, 6912:7424], pb[2][0:33, 0:512]),
                  incs=[("vs", 1)])
            mk.barrier()
            mk.dma("g", scr["xmid"][:, 5, 0:1024], AR_B[:, 4096:5120], ch=6)
            mk.dma("g", scr["xmid"][0:32, 6, 0:1024], AR_B[0:32, 5120:6144], ch=6)
            mk.dma("g", scr["xmid"][0:33, 7, 0:512], AR_B[0:33, 6912:7424], ch=6)
            mk.dma("g", scr["xbsp"].rearrange("p k t -> p (k t)")[:, 0:10240],
                   AR_A[:, 0:10240], ch=6)
            mk.dma("g", scr["xbscan"].rearrange("p k t -> p (k t)")[:, 0:8448],
                   AR_C[:, 0:8448], ch=6)
            mk.dma("g", scr["xmid"][0:32, 0, :], ROWS[32:64, 0:2048], ch=6)
            mk.dma("g", scr["xmid"][:, 3, :], AR_B[:, 0:2048], ch=6)
            mk.barrier()
        if devtap == "dbg_att":
            mk.dma("g", scr["xbsp"].rearrange("p k t -> p (k t)")[:, 0:10240],
                   AR_A[:, 0:10240], ch=6)
            mk.dma("g", scr["xbscan"].rearrange("p k t -> p (k t)")[:, 0:8448],
                   AR_C[:, 0:8448], ch=6)
            mk.dma("g", scr["xmid"][0:32, 0, :], ROWS[32:64, 0:2048], ch=6)
            mk.dma("g", scr["xmid"][:, 2, :], bc_g5o, ch=6)
            mk.dma("g", scr["xmid"][:, 3, :], AR_B[:, 0:2048], ch=6)
            mk.dma("g", scr["xmid"][:, 4, :], AR_C[:, 8448:10496], ch=6)
            mk.barrier()

        bc_r5o = AR_B[:, 6144:6144 + T]
        bc_mr5o = AR_B[:, 8192:8192 + T]
        tb_r5o = bcast(bc_r5o, 4, T, steo["tok_r"])
        tb_mr5o = bcast(bc_mr5o, 6, T, steo["tok_mr"])
        yb = [AR_C[:, 0:1024].bitcast(BF16).rearrange("p (k m) -> p k m", k=4),
              AR_C[:, 1024:2048].bitcast(BF16).rearrange("p (k m) -> p k m", k=4)]
        x1c2 = [AR_C[:, 2048:6144], AR_C[:, 6144:10240]]
        x2c = AR_A[:, 0:4096]
        x2tok = {}
        for ft in range(4):
            ybf, x1c = yb[ft % 2], x1c2[ft % 2]
            war = ([x2tok[ft - 2]] if ft - 2 in x2tok else [])
            tok_y = mk.dma("s", ybf, scr["ysp"][:, :, ft * 512:(ft + 1) * 512],
                           waits=war, ch=6 + ft % 2)
            tok_x1 = mk.dma("s", x1c.rearrange("p (k m) -> p k m", k=KC),
                            scr["x1own"][:, :, ft * 512:(ft + 1) * 512],
                            waits=war, ch=14 + ft % 2)
            x1k = x1c.rearrange("p (k m) -> p k m", k=KC)
            x2k = x2c.rearrange("p (k m) -> p k m", k=KC)
            for mc in range(8):
                bank = pb[mc % 2]
                for kc in range(4):
                    mk.op("tensor", lambda E, bank=bank, mc=mc, kc=kc, ybf=ybf:
                          E.matmul(bank[:, 0:512], wo_bf[:, kc, mc * 128:mc * 128 + 128],
                                   ybf[:, kc, :], start=(kc == 0), stop=(kc == 3)),
                          waits=(([tok_y, tok_wo] + ([mk.c("vs")] if ft + mc > 0 else []))
                                 if kc == 0 else []),
                          incs=[("ts", 1)])
                t_wo = mk.c("ts")
                sl = slice(ft * 512, (ft + 1) * 512)
                mk.op("vector", lambda E, x1k=x1k, x2k=x2k, mc=mc, sl=sl:
                      E.tensor_mul(x2k[:, mc, :], x1k[:, mc, :], bc_r5o[:, sl]),
                      waits=[tok_x1] + ([tb_r5o, tb_mr5o] if ft == 0 and mc == 0 else [])
                      + ([x2tok[ft - 1]] if mc == 0 and ft - 1 in x2tok else []),
                      incs=[("vs", 1)])
                mk.op("vector", lambda E, x2k=x2k, mc=mc, sl=sl:
                      E.tensor_sub(x2k[:, mc, :], x2k[:, mc, :], bc_mr5o[:, sl]),
                      incs=[("vs", 1)])
                mk.op("vector", lambda E, x2k=x2k, mc=mc, bank=bank:
                      E.tensor_add(x2k[:, mc, :], x2k[:, mc, :], bank[:, 0:512]),
                      waits=[t_wo], incs=[("vs", 1)])
            v_x2 = mk.c("vs")
            x2tok[ft] = mk.dma("g", scr["x2sp"][:, :, ft * 512:(ft + 1) * 512], x2k,
                               waits=[v_x2], ch=11 + ft % 2)
        mk.barrier()

        stf = stats(lambda kc: scr["x2sp"][:, kc, :], T, with_mean=False)
        bc_rr = AR_C[:, 0:2048]
        tb_rr = bcast(bc_rr, 0, T, stf["tok_r"])
        gate8 = MISC[0:8, 128:2176]
        rhsm = AR_B[:, 0:8192].bitcast(BF16).rearrange("p (k m) -> p k m", k=KC)
        x2l = [AR_C[:, 4096:8192], AR_C[:, 8192:12288]]
        gtok = {}
        for ft in range(4):
            x2b = x2l[ft % 2]
            war = ([gtok[ft - 2]] if ft - 2 in gtok else [])
            tok_x2 = mk.dma("s", x2b.rearrange("p (k m) -> p k m", k=KC),
                            scr["x2sp"][:, :, ft * 512:(ft + 1) * 512],
                            waits=war, ch=6 + ft % 2)
            x2bk = x2b.rearrange("p (k m) -> p k m", k=KC)
            for kc in range(KC):
                mk.op("vector", lambda E, x2bk=x2bk, kc=kc:
                      E.tensor_scalar(x2bk[:, kc, :], x2bk[:, kc, :], smcol("w_ln2", kc), None,
                                      ALU.mult),
                      waits=([tok_x2] if kc == 0 else []), incs=[("vs", 1)])
            v_tp = mk.c("vs")
            for kc in range(KC):
                mk.op("tensor", lambda E, x2bk=x2bk, kc=kc, ft=ft:
                      E.matmul(pb[4][0:8, 0:512], wgate_f[:, kc, :], x2bk[:, kc, :],
                               start=(kc == 0), stop=(kc == KC - 1)),
                      waits=([v_tp, tok_wg8, mk.c("vs")] if kc == 0 else []), incs=[("ts", 1)])
            t_g8 = mk.c("ts")
            mk.op("vector", lambda E, ft=ft:
                  E.tensor_copy(gate8[:, ft * 512:(ft + 1) * 512], pb[4][0:8, 0:512]),
                  waits=[t_g8], incs=[("vs", 1)])
            for kc in range(KC):
                mk.op("vector", lambda E, x2bk=x2bk, kc=kc, ft=ft:
                      E.tensor_mul(rhsm[:, kc, ft * 512:(ft + 1) * 512], x2bk[:, kc, :],
                                   bc_rr[:, ft * 512:(ft + 1) * 512]),
                      waits=([tb_rr] if kc == 0 and ft == 0 else []), incs=[("vs", 1)])
            gtok[ft] = mk.c("vs")
        v_rhsm = mk.c("vs")

        sG = AR_A[0:8, 0:2048]
        sT8, sT32 = AR_A[0:8, 2048:4096], AR_A[0:32, 2048:4096]
        sM8, sM32 = AR_A[0:8, 4096:6144], AR_A[0:32, 4096:6144]
        sK8 = AR_A[0:8, 6144:8192]
        sC8, sC32 = AR_A[0:8, 8192:10240], AR_A[0:32, 8192:10240]
        sF1 = AR_C[0:8, 8448:10496]
        sE8 = AR_B[0:8, 8192:10240]
        ebcol = smcol("ebias", rows=8)
        mk.op("vector", lambda E: E.tensor_mul(sG, gate8, bc_rr[0:8, :]), waits=[v_rhsm, mk.c("ts")],
              incs=[("vs", 1)])
        mk.op("vector", lambda E: E.tensor_scalar_add(sG, sG, ebcol), incs=[("vs", 1)])
        mk.op("scalar", lambda E: E.activation(sG, sG, AF.Exp), waits=[mk.c("vs")], incs=[("ss", 1)])
        s_e8 = mk.c("ss")
        mk.wait("vector", [s_e8])
        shufs = {k: [((i + k) % 8) if i < 8 else i for i in range(32)] for k in (1, 2, 4)}

        def fold(op):
            for k in (4, 2, 1):
                mk.op("vector", lambda E, k=k: E.stream_shuffle(sT32, sM32, shufs[k]),
                      incs=[("vs", 1)])
                mk.op("vector", lambda E, op=op: E.tensor_tensor(sM8, sM8, sT8, op),
                      incs=[("vs", 1)])

        def cumsum8():
            for k in (1, 2, 4):
                mask = [((i - k) % 8) if i < 8 else i for i in range(32)]
                mk.op("vector", lambda E, mask=mask: E.stream_shuffle(sT32, sC32, mask),
                      incs=[("vs", 1)])
                mk.op("vector", lambda E, k=k: E.memset(sT8[0:k, :], 0.0), incs=[("vs", 1)])
                mk.op("vector", lambda E: E.tensor_add(sC8, sC8, sT8), incs=[("vs", 1)])
        mk.op("vector", lambda E: E.tensor_copy(sM8, sG), incs=[("vs", 1)])
        fold(ALU.max)
        mk.op("vector", lambda E: E.tensor_tensor(sK8, sG, sM8, ALU.is_equal), incs=[("vs", 1)])
        mk.op("vector", lambda E: E.tensor_copy(sC8, sK8), incs=[("vs", 1)])
        cumsum8()
        mk.op("vector", lambda E: E.tensor_single_scalar(sF1, sC8, 1.0, ALU.is_equal),
              incs=[("vs", 1)])
        mk.op("vector", lambda E: E.tensor_mul(sF1, sF1, sK8), incs=[("vs", 1)])
        mk.op("vector", lambda E: E.tensor_scalar(sE8, sF1, -1.0, 1.0, ALU.mult, ALU.add),
              incs=[("vs", 1)])
        mk.op("vector", lambda E: E.tensor_mul(sE8, sE8, sG), incs=[("vs", 1)])
        mk.op("vector", lambda E: E.tensor_copy(sM8, sE8), incs=[("vs", 1)])
        fold(ALU.max)
        mk.op("vector", lambda E: E.tensor_tensor(sK8, sE8, sM8, ALU.is_equal), incs=[("vs", 1)])
        mk.op("vector", lambda E: E.tensor_copy(sC8, sK8), incs=[("vs", 1)])
        cumsum8()
        mk.op("vector", lambda E: E.tensor_single_scalar(sE8, sC8, 1.0, ALU.is_equal),
              incs=[("vs", 1)])
        mk.op("vector", lambda E: E.tensor_mul(sE8, sE8, sK8), incs=[("vs", 1)])
        mk.op("vector", lambda E: E.tensor_add(sK8, sF1, sE8), incs=[("vs", 1)])
        mk.op("vector", lambda E: E.tensor_mul(sK8, sK8, sG), incs=[("vs", 1)])
        mk.op("vector", lambda E: E.tensor_copy(sM8, sK8), incs=[("vs", 1)])
        fold(ALU.add)
        mk.op("vector", lambda E: E.reciprocal(sM8, sM8), incs=[("vs", 1)])
        mk.op("vector", lambda E: E.tensor_mul(sK8, sK8, sM8), incs=[("vs", 1)])
        v_comb = mk.c("vs")
        tok_comb = mk.dma("s", scr["cmbs"][:], sK8, waits=[v_comb], ch=5)

        wd_flat = dt_in["wd9"].rearrange("e i o -> (e i) o")
        h_all = AR_A[:, 0:9216].bitcast(BF16).rearrange("p (c m) -> p c m", c=36)
        g_sb = AR_C[:, 2048:3072].bitcast(BF16).rearrange("p (c m) -> p c m", c=4)
        gu_bf = AR_C[:, 3072:3584].bitcast(BF16).rearrange("p (c m) -> p c m", c=2)[:, 0, :]
        gu_bf = AR_C[:, 3072:3328].bitcast(BF16)
        cmb_b = [AR_C[:, 3328:3584].bitcast(BF16), AR_C[:, 3584:3840].bitcast(BF16)]
        wg_b = [WS[:, 0:2048].bitcast(BF16).rearrange("p (k m) -> p k m", k=KC),
                WS[:, 4096:6144].bitcast(BF16).rearrange("p (k m) -> p k m", k=KC)]
        wu_b = [WS[:, 2048:4096].bitcast(BF16).rearrange("p (k m) -> p k m", k=KC),
                WS[:, 6144:8192].bitcast(BF16).rearrange("p (k m) -> p k m", k=KC)]
        wd_b = [WS[:, 8192:8704].bitcast(BF16), WS[:, 8704:9216].bitcast(BF16)]
        bo2c = AR_C[:, 8192:12288]
        x2rl = AR_C[:, 4096:8192]
        bo2tok = {}
        wtok = {}
        wuse = {}
        cb_use = {}
        for tq in range(4):
            tsl = slice(tq * 512, (tq + 1) * 512)
            for e in range(9):
                bi = (tq * 9 + e) % 2
                war_w = ([wuse[(tq * 9 + e) - 2]] if (tq * 9 + e) - 2 in wuse else [mk.c("ts")])
                tg = tu = None
                if tq == 0 or True:
                    for kc in range(KC):
                        tg = mk.dma("g", wg_b[bi][:, kc, :],
                                    dt_in["wg9"][e, kc * 128:(kc + 1) * 128, :],
                                    waits=(war_w if kc == 0 else []), ch=7 + bi)
                    for kc in range(KC):
                        tu = mk.dma("g", wu_b[bi][:, kc, :],
                                    dt_in["wu9"][e, kc * 128:(kc + 1) * 128, :], ch=9 + bi)
                tok_cb = None
                if e < 8:
                    war_cb = ([cb_use[bi]] if bi in cb_use else
                              ([tok_comb] if tq == 0 and e == 0 else []))
                    tok_cb = mk.dma("g", cmb_b[bi], scr["cmbs"][e, tsl].partition_broadcast(128),
                                    waits=war_cb, ch=12 + bi)
                for mc in range(4):
                    for kc in range(KC):
                        mk.op("tensor", lambda E, mc=mc, kc=kc, bi=bi, tsl=tsl:
                              E.matmul(pb[4][:, 0:512], wg_b[bi][:, kc, mc * 128:mc * 128 + 128],
                                       rhsm[:, kc, tsl], start=(kc == 0), stop=(kc == KC - 1)),
                              waits=(([tg, v_rhsm, mk.c("ss")] if kc == 0 else [])
                                     if True else []),
                              incs=[("ts", 1)])
                    t_gm = mk.c("ts")
                    mk.op("scalar", lambda E, mc=mc:
                          E.activation(g_sb[:, mc, :], pb[4][:, 0:512], AF.Silu),
                          waits=[t_gm], incs=[("ss", 1)])
                s_gs = mk.c("ss")
                for mc in range(4):
                    for kc in range(KC):
                        mk.op("tensor", lambda E, mc=mc, kc=kc, bi=bi, tsl=tsl:
                              E.matmul(pb[5][:, 0:512], wu_b[bi][:, kc, mc * 128:mc * 128 + 128],
                                       rhsm[:, kc, tsl], start=(kc == 0), stop=(kc == KC - 1)),
                              waits=([tu, mk.c("vs")] if kc == 0 else []),
                              incs=[("ts", 1)])
                    t_um = mk.c("ts")
                    mk.op("vector", lambda E, mc=mc:
                          E.scalar_tensor_tensor(gu_bf, pb[5][:, 0:512], 1.0, g_sb[:, mc, :],
                                                 ALU.mult, ALU.mult),
                          waits=[t_um, s_gs], incs=[("vs", 1)])
                    ch = 4 * e + mc
                    if e < 8:
                        mk.op("vector", lambda E, ch=ch, bi=bi:
                              E.tensor_mul(h_all[:, ch, :], gu_bf, cmb_b[bi]),
                              waits=([tok_cb] if mc == 0 else []), incs=[("vs", 1)])
                    else:
                        mk.op("vector", lambda E, ch=ch:
                              E.tensor_copy(h_all[:, ch, :], gu_bf), incs=[("vs", 1)])
                if e < 8:
                    cb_use[bi] = mk.c("vs")
                wuse[tq * 9 + e] = mk.c("ts")
            v_hall = mk.c("vs")
            wdtok = {}
            for kc in range(36):
                wdb = wd_b[kc % 2]
                war_d = ([wdtok[kc - 2]] if kc - 2 in wdtok else [mk.c("ts")])
                td = mk.dma("g", wdb, wd_flat[kc * 128:(kc + 1) * 128, :],
                            waits=war_d, ch=14 + kc % 2)
                for mc in range(8):
                    mk.op("tensor", lambda E, kc=kc, mc=mc, wdb=wdb:
                          E.matmul(pb[mc][:, 0:512], wdb[:, mc * 128:mc * 128 + 128],
                                   h_all[:, kc, :], start=(kc == 0), stop=(kc == 35)),
                          waits=(([td] + ([v_hall] if kc == 0 else [])) if mc == 0 else []),
                          incs=[("ts", 1)])
                wdtok[kc] = mk.c("ts")
            t_rt = mk.c("ts")
            war_x2 = ([bo2tok[tq - 1]] if tq - 1 in bo2tok else [])
            tok_x2r = mk.dma("s", x2rl.rearrange("p (k m) -> p k m", k=KC),
                             scr["x2sp"][:, :, tsl], waits=war_x2, ch=6)
            bo2k = bo2c.rearrange("p (k m) -> p k m", k=KC)
            x2rk = x2rl.rearrange("p (k m) -> p k m", k=KC)
            for mc in range(8):
                mk.op("vector", lambda E, mc=mc, bo2k=bo2k, x2rk=x2rk:
                      E.tensor_add(bo2k[:, mc, :], pb[mc][:, 0:512], x2rk[:, mc, :]),
                      waits=[t_rt] + ([tok_x2r] if mc == 0 else []), incs=[("vs", 1)])
            v_bo2 = mk.c("vs")
            bo2tok[tq] = mk.dma("g", scr["bo2sp"][:, :, tsl], bo2k, waits=[v_bo2], ch=11)
        mk.barrier()

        stgf = stats(lambda kc: scr["bo2sp"][:, kc, :], T)
        bc_r7 = AR_B[:, 0:T]
        bc_mr7 = AR_B[:, 2080:2080 + T]
        tb_r7 = bcast(bc_r7, 0, T, stgf["tok_r"])
        tb_mr7 = bcast(bc_mr7, 2, T, stgf["tok_mr"])
        boc2 = [AR_C[:, 0:4096], AR_C[:, 4096:8192]]
        x1f = AR_C[:, 8192:12288]
        ftok = {}
        for ft in range(4):
            bc2 = boc2[ft % 2]
            war = ([ftok[ft - 2]] if ft - 2 in ftok else [])
            warx = ([ftok[ft - 1]] if ft - 1 in ftok else [])
            tok_b = mk.dma("s", bc2.rearrange("p (k m) -> p k m", k=KC),
                           scr["bo2sp"][:, :, ft * 512:(ft + 1) * 512],
                           waits=war, ch=6 + ft % 2)
            tok_x1f = mk.dma("s", x1f.rearrange("p (k m) -> p k m", k=KC),
                             scr["x1own"][:, :, ft * 512:(ft + 1) * 512],
                             waits=warx, ch=14)
            b2k = bc2.rearrange("p (k m) -> p k m", k=KC)
            x1fk = x1f.rearrange("p (k m) -> p k m", k=KC)
            sl = slice(ft * 512, (ft + 1) * 512)
            for kc in range(KC):
                mk.op("vector", lambda E, b2k=b2k, kc=kc, sl=sl:
                      E.tensor_mul(b2k[:, kc, :], b2k[:, kc, :], bc_r7[:, sl]),
                      waits=([tok_b, tb_r7] if kc == 0 else []), incs=[("vs", 1)])
                mk.op("vector", lambda E, b2k=b2k, kc=kc, sl=sl:
                      E.tensor_sub(b2k[:, kc, :], b2k[:, kc, :], bc_mr7[:, sl]),
                      waits=([tb_mr7] if kc == 0 else []), incs=[("vs", 1)])
                mk.op("vector", lambda E, b2k=b2k, x1fk=x1fk, kc=kc:
                      E.tensor_add(b2k[:, kc, :], b2k[:, kc, :], x1fk[:, kc, :]),
                      waits=([tok_x1f] if kc == 0 else []), incs=[("vs", 1)])
            v_f = mk.c("vs")
            ftok[ft] = mk.dma("s", outp[:, :, ft * 512:(ft + 1) * 512], b2k,
                              waits=[v_f], ch=0)

    if devtap == "dbg_misc":
        mk.barrier()
        mk.dma("g", scr["x2sp"][:, 0, :], MISC[:, 0:2048], ch=6)
        mk.dma("g", scr["x2sp"][:, 1, :], AR_B[:, 0:2048], ch=6)
        devtap = "x2sp"
    if devtap in ("dbg_sp", "dbg_z", "dbg_scan", "dbg_ysd"):
        mk.barrier()
        if devtap == "dbg_sp":
            for kc in range(KC):
                mk.dma("g", scr["x2sp"][:, kc, :], sp_bf[:, kc, :], ch=6)
        elif devtap == "dbg_z":
            for kc in range(KC):
                mk.dma("g", scr["x2sp"][:, kc, :], z_f16[:, kc, :], ch=6)
        elif devtap == "dbg_scan":
            mk.dma("g", scr["x2sp"][:, 0, 0:512], u64q, ch=6)
            mk.dma("g", scr["x2sp"][:, 1, 0:512], hsq, ch=6)
            mk.dma("g", scr["x2sp"][:, 2, 0:512], dk_rep, ch=6)
            mk.dma("g", scr["x2sp"][:, 3, 0:512], p512, ch=6)
            mk.dma("g", scr["x2sp"][:, 4, 0:1024], hsr_bf, ch=6)
        elif devtap == "dbg_ysd":
            for kc in range(KC):
                mk.dma("g", scr["x2sp"][:, kc, :], ysd_bf[:, kc, :], ch=6)
        devtap = "x2sp"

    mk.barrier()
    if devtap == "dbg_att" or (isinstance(devtap, str) and devtap.startswith("dbg_attk")):
        devtap = None
    if devtap or stage < 3:
        src = scr[devtap] if devtap else (scr["xmid"] if stage == 1 else scr["x1own"])
        for kc in range(KC):
            mk.dma("s", outp[:, kc, :], src[:, kc, :], ch=0)
    mk.wait("sync", [mk.c("ds0")])
    mk.wait("gpsimd", [mk.c("ds0")])
    return mk.build()


_PROG_CACHE = {}


def get_program(stage=STAGE, devtap=None):
    key = (stage, devtap)
    if key not in _PROG_CACHE:
        _PROG_CACHE[key] = build_program(stage, devtap)
    return _PROG_CACHE[key]


def kernel(**inputs):
    in_maps = prep_inputs(inputs)
    nc = get_program()
    res = run_bass_kernel_spmd(nc, in_maps, core_ids=list(range(NCORES)))
    out = np.zeros((B, S, D), np.float32)
    for c in range(NCORES):
        b0, par = c // 2, c & 1
        o = np.asarray(res.results[c]["outp"], np.float32)
        o = o.transpose(1, 0, 2).reshape(D, T)
        for j in range(16):
            g = 2 * j + par
            out[b0, g * 128:(g + 1) * 128, :] = o[:, j * 128:(j + 1) * 128].T
    return out


# version 04
# speedup vs baseline: 1.1550x; 1.1550x over previous
import os

import numpy as np
import ml_dtypes

import concourse.bass as bass
import concourse.mybir as mybir
from concourse.bass_utils import run_bass_kernel_spmd

F32 = mybir.dt.float32
BF16 = mybir.dt.bfloat16
F16 = mybir.dt.float16
AF = mybir.ActivationFunctionType
ALU = mybir.AluOpType
AXX = mybir.AxisListType.X
BNP = ml_dtypes.bfloat16

B, S, D = 4, 4096, 1024
T = 2048
NCORES = 8
KC = 8
EPS = 1e-5
STAGE = 3

SMCOL = {}
_nxt = [0]
for _n, _c in [("ln1_w", 8), ("ln1_b", 8), ("lns_w", 8), ("lns_b", 8),
               ("conv_b", 8), ("cw0", 8), ("cw1", 8), ("cw2", 8), ("cw3", 8),
               ("in_b", 16), ("out_b", 8), ("ln2_w", 8), ("ln2_b", 8),
               ("mlp_b1", 16), ("mlp_b2", 8), ("w_ln1", 8), ("w_ln2", 8),
               ("xp_b", 1), ("A_log", 1), ("ebias", 1)]:
    SMCOL[_n] = (_nxt[0], _c)
    _nxt[0] += _c
NSMALL = _nxt[0]


def _pack_small(dst, name, vec):
    c0, ncol = SMCOL[name]
    v = np.asarray(vec, np.float32).reshape(-1)
    if ncol == 1:
        dst[:len(v), c0] = v
    else:
        dst[:, c0:c0 + ncol] = v.reshape(ncol, 128).T


def prep_inputs(inp):
    x = np.asarray(inp["L0_attn_x"], np.float32)
    w = {k[8:]: np.asarray(v, np.float32) for k, v in inp.items() if k.startswith("L0_attn_")}
    wf = {k[7:]: np.asarray(v, np.float32) for k, v in inp.items() if k.startswith("L0_ffn_")}

    smalls = np.zeros((128, NSMALL), np.float32)
    for nm in ["ln1_w", "ln1_b", "lns_w", "lns_b", "conv_b", "in_b", "out_b",
               "ln2_w", "ln2_b", "mlp_b1", "mlp_b2", "xp_b", "A_log"]:
        _pack_small(smalls, nm, w[nm])
    for k in range(4):
        _pack_small(smalls, f"cw{k}", w["conv_w"][:, 0, k])
    _pack_small(smalls, "w_ln1", wf["w_ln1"])
    _pack_small(smalls, "w_ln2", wf["w_ln2"])
    _pack_small(smalls, "ebias", wf["expert_bias"])

    tpos = np.arange(S, dtype=np.float32)
    freqs = 1.0 / (100000.0 ** (np.arange(0, 32, 2, dtype=np.float32) / 32))
    ang = np.outer(freqs, tpos)
    cos, sin = np.cos(ang), np.sin(ang)
    csA = np.zeros((32, S), np.float32)
    csB = np.zeros((32, S), np.float32)
    csA[0::2] = cos
    csA[1::2] = cos
    csB[0::2] = -sin
    csB[1::2] = sin

    wg9 = np.ascontiguousarray(np.concatenate([wf["Wg"], wf["sWg"]], 0))
    wu9 = np.ascontiguousarray(np.concatenate([wf["Wu"], wf["sWu"]], 0))
    wd9 = np.ascontiguousarray(np.concatenate([wf["Wd"], wf["sWd"]], 0))
    rows_eff = np.concatenate([np.arange(h * 64, h * 64 + 32) for h in range(16)])
    wo_eff = np.ascontiguousarray(wf["Wo"][rows_eff])

    triu = np.triu(np.ones((128, 128), np.float32))
    zeros = np.zeros((128, 128), np.float32)
    onesm = np.ones((128, 128), np.float32)

    in_maps = []
    for c in range(NCORES):
        b0, par = c // 2, c & 1
        s0, lo = par * T, c * 512
        pad3 = np.zeros((3, D), np.float32)
        xo = np.concatenate([x[b0, s0 - 3:s0] if s0 >= 3 else pad3,
                             x[b0, s0:s0 + T]], 0).T
        x_own = np.zeros((D, 2060), np.float32)
        x_own[:, :2051] = xo
        if lo >= 3:
            xs = x[:, lo - 3:lo + 512]
        else:
            xs = np.concatenate([np.zeros((B, 3, D), np.float32), x[:, 0:512]], 1)
        x_scan = np.ascontiguousarray(xs.transpose(2, 1, 0)).reshape(D, 2060)

        cpw = np.broadcast_to(
            np.array([(c - 1 - j) if j < c else 1e6 for j in range(8)], np.float32),
            (128, 8)).copy()
        bmask = np.ones((128, 16), np.float32)
        if lo == 0:
            bmask[:, 0:12] = 0.0
        if s0 == 0:
            bmask[:, 12:15] = 0.0
        qm = np.empty((128, 32 * 128), np.float32)
        for kb in range(32):
            if par == 0:
                blk = triu if (kb % 2 == 0) else zeros
            else:
                blk = onesm if (kb % 2 == 0) else triu
            qm[:, kb * 128:(kb + 1) * 128] = blk
        own_cols = np.concatenate(
            [np.arange((2 * j + par) * 128, (2 * j + par + 1) * 128) for j in range(16)])

        parsel = np.zeros((128, 2), np.float32)
        parsel[:, par] = 1.0
        in_maps.append(dict(
            parsel=parsel,
            x_own=x_own, x_scan=x_scan,
            in_w=w["in_w"], xp_w=np.ascontiguousarray(w["xp_w"]),
            dmat=np.ascontiguousarray(w["D"]), out_w=w["out_w"],
            mlp_w1=w["mlp_w1"], mlp_w2=w["mlp_w2"],
            wq=wf["Wq"], wkva=np.ascontiguousarray(wf["Wkva"]),
            wkvb=np.ascontiguousarray(wf["Wkvb"]), wo_eff=wo_eff,
            wgate=np.ascontiguousarray(wf["Wgate"]),
            wg9=wg9, wu9=wu9, wd9=wd9,
            smalls=smalls, cpw=cpw, bmask=bmask,
            qmask=qm.astype(BNP),
            csq_a=np.ascontiguousarray(csA[:, own_cols]).astype(BNP),
            csq_b=np.ascontiguousarray(csB[:, own_cols]).astype(BNP),
            csk_a=csA.astype(BNP), csk_b=csB.astype(BNP),
        ))
    return in_maps


class MK:
    ENGS = ("tensor", "vector", "scalar", "gpsimd", "sync")

    def __init__(self):
        self.nc = bass.Bass("TRN2", target_bir_lowering=False,
                            detect_race_conditions=bool(os.environ.get("MKRACE")))
        self.ops = {e: [] for e in self.ENGS}
        self.cnt = {}
        self.sems = {}
        self._semname = []

    def newsem(self, name):
        self.cnt[name] = 0
        self._semname.append(name)

    def c(self, name):
        return (name, self.cnt.get(name, 0))

    ESEM = {"tensor": "ts", "vector": "vs", "scalar": "ss"}

    def op(self, eng, fn, waits=(), incs=()):
        waits = list(waits)
        es = self.ESEM.get(eng)
        if es is not None and self.cnt.get(es, 0) > 0 and fn.__name__ != "<nowait>":
            waits.append((es, self.cnt[es]))
        for s, k in incs:
            self.cnt[s] += k
        waits = tuple((s, v) for (s, v) in waits if v > 0)
        incs = tuple(incs)

        def run(E, fn=fn, waits=waits, incs=incs):
            for s, v in waits:
                E.wait_ge(self.sems[s], v)
            inst = fn(E)
            for s, k in incs:
                inst.then_inc(self.sems[s], k)
        self.ops[eng].append(run)
        return [(s, self.cnt[s]) for s, _ in incs]

    def dma(self, q, out, in_, waits=(), ch=0, **kw):
        eng = {"g": "gpsimd", "s": "sync"}[q]
        sem = f"d{q}{ch}"
        if sem not in self.cnt:
            self.newsem(sem)
        return self.op(eng, lambda E, o=out, i=in_, kw=kw: E.dma_start(out=o, in_=i, **kw),
                       waits=waits, incs=[(sem, 16)])[0]

    def wait(self, eng, waits):
        self.op(eng, lambda E: None, waits=waits)

    def barrier(self, engines=None):
        engines = engines or self.ENGS
        dmas = [(s, self.cnt[s]) for s in self.cnt if s.startswith("d") and self.cnt[s] > 0]
        bs = f"bar{len(self._semname)}"
        self.newsem(bs)
        n = len(engines)
        for e in engines:
            self.op(e, lambda E: E.nop(), waits=dmas, incs=[(bs, 1)])
        for e in engines:
            self.wait(e, [(bs, n)])

    def build(self):
        nc = self.nc
        import contextlib
        stack = contextlib.ExitStack()
        for name in self._semname:
            self.sems[name] = stack.enter_context(nc.semaphore(name))
        blk = stack.enter_context(nc.Block())
        for eng in self.ENGS:
            ops = self.ops[eng]
            getattr(blk, eng)(lambda E, ops=ops: [f(E) for f in ops])
        stack.close()
        return nc


def build_program(stage=STAGE, devtap=None):
    mk = MK()
    nc = mk.nc
    for s in ("ts", "vs", "ss", "cc"):
        mk.newsem(s)

    dt_in = {}
    for name, shape, dt in [
        ("x_own", [D, 2060], F32), ("x_scan", [D, 2060], F32),
        ("in_w", [D, 2048], F32), ("xp_w", [D, 16], F32), ("dmat", [16, D], F32),
        ("out_w", [D, D], F32), ("mlp_w1", [D, 2048], F32), ("mlp_w2", [2048, D], F32),
        ("wq", [D, D], F32), ("wkva", [D, 160], F32), ("wkvb", [128, 512], F32),
        ("wo_eff", [512, D], F32), ("wgate", [D, 8], F32),
        ("wg9", [9, D, 512], F32), ("wu9", [9, D, 512], F32), ("wd9", [9, 512, D], F32),
        ("smalls", [128, NSMALL], F32),
        ("cpw", [128, 8], F32), ("bmask", [128, 16], F32),
        ("parsel", [128, 2], F32),
        ("qmask", [128, 32 * 128], BF16),
        ("csq_a", [32, T], BF16), ("csq_b", [32, T], BF16),
        ("csk_a", [32, S], BF16), ("csk_b", [32, S], BF16),
    ]:
        dt_in[name] = nc.dram_tensor(name, shape, dt, kind="ExternalInput").ap()

    outp = nc.dram_tensor("outp", [128, KC, T], F32, kind="ExternalOutput").ap()

    scr = {}
    for name, shape, dt in [
        ("rows_scr", [8, 4160], F32),
        ("esnd", [128], F32), ("ercv", [8, 128], F32),
        ("xsnd", [128, 16, T], BF16), ("xrcv", [2, 128, 16, T], BF16),
        ("xbsp", [128, KC, T], F32), ("xbscan", [128, KC, T], F32),
        ("xmid", [128, KC, T], F32), ("x1own", [128, KC, T], F32),
        ("x2sp", [128, KC, T], F32), ("bo2sp", [128, KC, T], F32),
        ("cmbs", [8, T], F32),
        ("ysp", [128, 4, T], BF16),
    ]:
        scr[name] = nc.dram_tensor(name, shape, dt).ap()

    AR_A = nc.alloc_sbuf_tensor("AR_A", [128, 10304], F32).ap()
    AR_B = nc.alloc_sbuf_tensor("AR_B", [128, 10240], F32).ap()
    AR_C = nc.alloc_sbuf_tensor("AR_C", [128, 12288], F32).ap()
    WS = nc.alloc_sbuf_tensor("WS", [128, 9216], F32).ap()
    ROWS = nc.alloc_sbuf_tensor("ROWS", [128, 6180], F32).ap()
    SM = nc.alloc_sbuf_tensor("SM", [128, NSMALL], F32).ap()
    CPW = nc.alloc_sbuf_tensor("CPW", [128, 8], F32).ap()
    BMASK = nc.alloc_sbuf_tensor("BMASK", [128, 16], F32).ap()
    PARSEL = nc.alloc_sbuf_tensor("PARSEL", [128, 2], F32).ap()
    MISC = nc.alloc_sbuf_tensor("MISC", [128, 2176], F32).ap()
    SC = nc.alloc_sbuf_tensor("SC", [128, 2048], F32).ap()

    pb = [nc.alloc_psum_tensor(f"pb{i}", [128, 512], F32).ap() for i in range(8)]

    def smcol(name, k=0, rows=128):
        c0, _ = SMCOL[name]
        return SM[0:rows, c0 + k:c0 + k + 1]

    ones_bf = MISC[:, 0:4].bitcast(BF16)[:, 0:1]
    gcol_bf = MISC[:, 4:8].bitcast(BF16)
    bcol_bf = MISC[:, 8:12].bitcast(BF16)
    hw_col = MISC[:, 16:32]
    xpw_bf = MISC[:, 32:96].bitcast(BF16)
    ld64 = MISC[:, 96:97]
    ld512 = MISC[:, 97:98]
    w8t = MISC[:, 98:106]
    carry = MISC[:, 106:107]
    ecol = MISC[:, 107:108]
    ca8 = MISC[:, 108:116]
    xpbq = MISC[:, 116:117]
    eps_col = MISC[:, 117:118]
    mrow_bf = MISC[0:1, 128:1152].bitcast(BF16)
    gw_bf = MISC[0:1, 1152:2176].bitcast(BF16)
    hsr_bf = MISC[32:48, 128:1152].bitcast(BF16)
    dmat_bf = MISC[32:48, 1152:1664].bitcast(BF16)

    u64q = SC[:, 0:512]
    hsq = SC[:, 512:1024]
    dk_rep = SC[:, 1024:1536]
    p512 = SC[:, 1536:2048]

    def rslot(i, n=2060):
        return ROWS[0:1, i * 2060:i * 2060 + n]

    mk.dma("s", SM[:], dt_in["smalls"], ch=0)
    mk.dma("s", CPW[:], dt_in["cpw"], ch=0)
    mk.dma("s", PARSEL[:], dt_in["parsel"], ch=0)
    t_fix = mk.dma("s", BMASK[:], dt_in["bmask"], ch=0)
    mk.op("vector", lambda E: E.memset(MISC[:, 96:128], 0.0), incs=[("vs", 1)])
    mk.op("vector", lambda E: E.memset(ones_bf, 1.0), incs=[("vs", 1)])
    mk.op("vector", lambda E: E.memset(eps_col, EPS), incs=[("vs", 1)])
    v_ones = mk.c("vs")

    def stats(src_dram_fn, ncols, with_mean=True, fused2=False, extra_wait=(),
              ship_row0=0, ship_off=0, tbuf=None, tbase=0):
        s1, s2, t2 = rslot(0, ncols), rslot(1, ncols), rslot(2, ncols)
        ta = AR_A if tbuf is None else tbuf
        xbf = [ta[:, tbase + 0:tbase + 1030].bitcast(BF16),
               ta[:, tbase + 1030:tbase + 2060].bitcast(BF16)]
        sqb = ta[:, tbase + 2060:tbase + 3090].bitcast(BF16)
        ent_ts, ent_ss, ent_vs = mk.c("ts"), mk.c("ss"), mk.c("vs")
        ranges = [(0, min(ncols, 2048))]
        if ncols > 2048:
            ranges.append((2048, ncols - 2048))
        read_done = {}
        sq_done = {}
        for (c0, cn) in ranges:
            FCH = [(f0, min(512, cn - f0)) for f0 in range(0, cn, 512)]
            for kc in range(KC):
                bb = xbf[kc % 2][:, 0:cn]
                war = ([read_done[kc - 2]] if kc - 2 in read_done else [])
                tok = mk.dma("g", bb, src_dram_fn(kc)[:, c0:c0 + cn], ch=kc % 2,
                             waits=war
                             + ([ent_ts, ent_ss, ent_vs] if (kc <= 1 and c0 == 0) else [])
                             + (list(extra_wait) if (kc <= 1 and c0 == 0) else []))
                if with_mean:
                    for fi, (f0, fw) in enumerate(FCH):
                        mk.op("tensor", lambda E, bb=bb, f0=f0, fw=fw, fi=fi, kc=kc:
                              E.matmul(pb[4 + fi][0:1, 0:fw], ones_bf, bb[:, f0:f0 + fw],
                                       start=(kc == 0), stop=(kc == KC - 1)),
                              waits=([tok, v_ones, ent_vs] if kc == 0 else [tok]),
                              incs=[("ts", 1)])
                war_s = ([sq_done[kc - 1]] if kc - 1 in sq_done else [])
                mk.op("scalar", lambda E, bb=bb, cn=cn: E.activation(sqb[:, 0:cn], bb, AF.Square),
                      waits=[tok] + war_s, incs=[("ss", 1)])
                s_sq = mk.c("ss")
                for fi, (f0, fw) in enumerate(FCH):
                    mk.op("tensor", lambda E, f0=f0, fw=fw, fi=fi, kc=kc:
                          E.matmul(pb[fi][0:1, 0:fw], ones_bf, sqb[:, f0:f0 + fw],
                                   start=(kc == 0), stop=(kc == KC - 1)),
                          waits=([s_sq, v_ones, ent_vs] if kc == 0 else [s_sq]),
                          incs=[("ts", 1)])
                sq_done[kc] = mk.c("ts")
                read_done[kc] = mk.c("ts")
            read_done, sq_done = {}, {}
            t_stats = mk.c("ts")
            for fi, (f0, fw) in enumerate(FCH):
                if with_mean:
                    mk.op("vector", lambda E, f0=f0, fw=fw, fi=fi, c0=c0:
                          E.tensor_copy(s1[:, c0 + f0:c0 + f0 + fw], pb[4 + fi][0:1, 0:fw]),
                          waits=[t_stats], incs=[("vs", 1)])
                mk.op("vector", lambda E, f0=f0, fw=fw, fi=fi, c0=c0:
                      E.tensor_copy(s2[:, c0 + f0:c0 + f0 + fw], pb[fi][0:1, 0:fw]),
                      waits=[t_stats], incs=[("vs", 1)])
            if len(ranges) > 1 and c0 == 0:
                mk.wait("gpsimd", [mk.c("vs")])
        c1 = 1.0 / 1024.0
        out = {}
        if with_mean:
            mk.op("vector", lambda E: E.tensor_scalar_mul(s1, s1, c1), incs=[("vs", 1)])
            mk.op("vector", lambda E: E.tensor_mul(t2, s1, s1), incs=[("vs", 1)])
            mk.op("vector", lambda E: E.scalar_tensor_tensor(s2, s2, c1, t2, ALU.mult, ALU.subtract),
                  incs=[("vs", 1)])
        else:
            mk.op("vector", lambda E: E.tensor_scalar_mul(s2, s2, c1), incs=[("vs", 1)])
        v_v = mk.c("vs")
        if fused2:
            mk.op("vector", lambda E: E.tensor_scalar_add(t2, s2, EPS), waits=[v_v], incs=[("vs", 1)])
            mk.op("vector", lambda E: E.reciprocal(t2, t2), incs=[("vs", 1)])
            mk.op("vector", lambda E: E.tensor_mul(t2, s2, t2), incs=[("vs", 1)])
            v_t2 = mk.c("vs")
            mk.op("scalar", lambda E: E.activation(t2, t2, AF.Sqrt, bias=eps_col[0:1, :]),
                  waits=[v_t2], incs=[("ss", 1)])
            s_t2 = mk.c("ss")
            mk.op("vector", lambda E: E.reciprocal(t2, t2), waits=[s_t2], incs=[("vs", 1)])
        mk.op("scalar", lambda E: E.activation(s2, s2, AF.Sqrt, bias=eps_col[0:1, :]),
              waits=[mk.c("vs")], incs=[("ss", 1)])
        s_r = mk.c("ss")
        mk.op("vector", lambda E: E.reciprocal(s2, s2), waits=[s_r], incs=[("vs", 1)])
        v_r = mk.c("vs")
        out["tok_r"] = mk.dma("s", scr["rows_scr"][ship_row0 + 0, ship_off:ship_off + ncols], s2, waits=[v_r], ch=1)
        if fused2:
            mk.op("vector", lambda E: E.tensor_mul(t2, s2, t2), incs=[("vs", 1)])
            v_g = mk.c("vs")
            out["tok_g"] = mk.dma("s", scr["rows_scr"][ship_row0 + 1, ship_off:ship_off + ncols], t2, waits=[v_g], ch=2)
        if with_mean:
            mk.op("vector", lambda E: E.tensor_mul(s2, s1, s2), waits=[out["tok_r"]], incs=[("vs", 1)])
            v_mr = mk.c("vs")
            out["tok_mr"] = mk.dma("s", scr["rows_scr"][ship_row0 + 2, ship_off:ship_off + ncols], s2, waits=[v_mr], ch=3)
            if fused2:
                mk.op("vector", lambda E: E.tensor_mul(t2, s1, t2),
                      waits=[out["tok_g"]], incs=[("vs", 1)])
                v_mg = mk.c("vs")
                out["tok_mg"] = mk.dma("s", scr["rows_scr"][ship_row0 + 3, ship_off:ship_off + ncols], t2, waits=[v_mg], ch=4)
        return out

    def bcast(dst, slot, ncols, tok, q="g", off=0):
        return mk.dma(q, dst, scr["rows_scr"][slot, off:off + ncols].partition_broadcast(128),
                      waits=[tok], ch=2 + slot % 4)

    def load_mrow(tok_mr, ncols=T, slot=2):
        mrl = rslot(2, ncols)
        tok = mk.dma("s", mrl, scr["rows_scr"][slot:slot + 1, 0:ncols], waits=[tok_mr], ch=5)
        mk.op("vector", lambda E: E.tensor_scalar_mul(mrow_bf[:, 0:ncols], mrl, -1.0),
              waits=[tok], incs=[("vs", 1)])
        return mk.c("vs")

    def gw_hw(w_bf, nout, gamma_name, beta_name, bias_name, w_tok):
        c0 = SMCOL[gamma_name][0]
        mk.op("vector", lambda E, c0=c0: E.tensor_copy(gcol_bf, SM[:, c0:c0 + 8]),
              waits=[t_fix], incs=[("vs", 1)])
        c0b = SMCOL[beta_name][0]
        mk.op("vector", lambda E, c0b=c0b: E.tensor_copy(bcol_bf, SM[:, c0b:c0b + 8]),
              incs=[("vs", 1)])
        v_cols = mk.c("vs")
        for f0 in range(0, nout, 512):
            for kc in range(KC):
                mk.op("tensor", lambda E, f0=f0, kc=kc:
                      E.matmul(pb[5][0:1, 0:512], gcol_bf[:, kc:kc + 1],
                               w_bf[:, kc, f0:f0 + 512], start=(kc == 0), stop=(kc == KC - 1)),
                      waits=([v_cols, w_tok, mk.c("vs")] if kc == 0 else []), incs=[("ts", 1)])
            t_gw = mk.c("ts")
            mk.op("vector", lambda E, f0=f0: E.tensor_copy(gw_bf[:, f0:f0 + 512], pb[5][0:1, 0:512]),
                  waits=[t_gw], incs=[("vs", 1)])
        v_gw = mk.c("vs")
        for mc in range(nout // 128):
            for kc in range(KC):
                mk.op("tensor", lambda E, mc=mc, kc=kc:
                      E.matmul(pb[4][:, 0:1], w_bf[:, kc, mc * 128:mc * 128 + 128],
                               bcol_bf[:, kc:kc + 1], start=(kc == 0), stop=(kc == KC - 1)),
                      waits=([v_cols, w_tok, mk.c("vs")] if kc == 0 else []),
                      incs=[("ts", 1)])
            t_hw = mk.c("ts")
            mk.op("vector", lambda E, mc=mc:
                  E.tensor_add(hw_col[:, mc:mc + 1], pb[4][:, 0:1], smcol(bias_name, mc)),
                  waits=[t_hw], incs=[("vs", 1)])
        return v_gw, mk.c("vs")

    def m_chain(which):
        xsrc = dt_in["x_own" if which == "own" else "x_scan"]
        spill = scr["xbsp" if which == "own" else "xbscan"]
        nb = 3 if which == "own" else 12
        shift = 1 if which == "own" else 4
        ncols = T + nb
        bm = BMASK[:, 12:12 + nb] if which == "own" else BMASK[:, 0:nb]

        st = stats(lambda kc: xsrc[kc * 128:(kc + 1) * 128, :], ncols,
                   with_mean=True, fused2=True)
        bc_r = AR_B[:, 0:ncols]
        bc_g = AR_B[:, 2080:2080 + ncols]
        bc_mr = AR_B[:, 4160:4160 + ncols]
        bc_mg = AR_B[:, 6240:6240 + ncols]
        tb_r = bcast(bc_r, 0, ncols, st["tok_r"])
        tb_g = bcast(bc_g, 1, ncols, st["tok_g"])
        tb_mr = bcast(bc_mr, 2, ncols, st["tok_mr"])
        tb_mg = bcast(bc_mg, 3, ncols, st["tok_mg"])

        spill_toks, chain_toks = {}, {}
        spill0 = mk.c("dg6")
        for kc in range(KC):
            xs_c = AR_A[:, (kc % 2) * 2064:(kc % 2) * 2064 + ncols]
            xn_c = AR_A[:, 4128 + (kc % 2) * 2064:4128 + (kc % 2) * 2064 + ncols]
            xb_c = AR_A[:, 8256:8256 + T]
            war = ([chain_toks[kc - 2]] if kc - 2 in chain_toks else [tb_mg])
            tok_x = mk.dma("s", xs_c, xsrc[kc * 128:(kc + 1) * 128, 0:ncols], waits=war, ch=6 + kc % 2)
            mk.op("vector", lambda E, xn_c=xn_c, xs_c=xs_c: E.tensor_mul(xn_c, xs_c, bc_g),
                  waits=[tok_x] + ([tb_g] if kc == 0 else []), incs=[("vs", 1)])
            mk.op("vector", lambda E, xn_c=xn_c: E.tensor_sub(xn_c, xn_c, bc_mg),
                  incs=[("vs", 1)])
            mk.op("vector", lambda E, xn_c=xn_c, kc=kc:
                  E.tensor_scalar(xn_c, xn_c, smcol("ln1_w", kc), smcol("ln1_b", kc),
                                  ALU.mult, ALU.add),
                  waits=([t_fix] if kc == 0 else []), incs=[("vs", 1)])
            mk.op("vector", lambda E, xn_c=xn_c, bm=bm:
                  E.tensor_mul(xn_c[:, 0:nb], xn_c[:, 0:nb], bm), incs=[("vs", 1)])
            warb = ([spill_toks[kc - 1]] if kc - 1 in spill_toks else [spill0])
            mk.op("vector", lambda E, xb_c=xb_c, xs_c=xs_c, nb=nb:
                  E.tensor_mul(xb_c, xs_c[:, nb:nb + T], bc_r[:, nb:nb + T]),
                  waits=([tb_r] if kc == 0 else []) + warb, incs=[("vs", 1)])
            mk.op("vector", lambda E, xb_c=xb_c, nb=nb:
                  E.tensor_sub(xb_c, xb_c, bc_mr[:, nb:nb + T]),
                  waits=([tb_mr] if kc == 0 else []), incs=[("vs", 1)])
            for k in range(4):
                mk.op("vector", lambda E, xb_c=xb_c, xn_c=xn_c, k=k, kc=kc, shift=shift:
                      E.scalar_tensor_tensor(xb_c, xn_c[:, k * shift:k * shift + T],
                                             smcol(f"cw{k}", kc), xb_c, ALU.mult, ALU.add),
                      incs=[("vs", 1)])
            mk.op("vector", lambda E, xb_c=xb_c, kc=kc:
                  E.tensor_scalar_add(xb_c, xb_c, smcol("conv_b", kc)), incs=[("vs", 1)])
            v_xb = mk.c("vs")
            chain_toks[kc] = v_xb
            spill_toks[kc] = mk.dma("g", spill[:, kc, :], xb_c, waits=[v_xb], ch=6)
        return spill_toks[KC - 1]

    tok_xb_own = m_chain("own")
    mk.barrier()
    tok_xb_scan = m_chain("scan")
    mk.barrier()

    stb = stats(lambda kc: scr["xbscan"][:, kc, :], T, extra_wait=[tok_xb_scan])
    bc_r2 = AR_B[:, 0:T]
    tb_r2 = bcast(bc_r2, 0, T, stb["tok_r"])
    v_mrow = load_mrow(stb["tok_mr"])

    inw_bf = WS[:, 0:8192].bitcast(BF16).rearrange("p (k m) -> p k m", k=KC)
    tok_inw = None
    for kc in range(KC):
        tok_inw = mk.dma("g", inw_bf[:, kc, :], dt_in["in_w"][kc * 128:(kc + 1) * 128, :], ch=7)
    v_gw, v_hw = gw_hw(inw_bf, 2048, "lns_w", "lns_b", "in_b", tok_inw)

    sp_bf = AR_B[:, 2048:2048 + 8192].bitcast(BF16).rearrange("p (k m) -> p k m", k=KC)
    z_f16 = AR_A[:, 0:8192].bitcast(F16).rearrange("p (k m) -> p k m", k=KC)
    rhs_b = [AR_C[:, 0:2048].bitcast(BF16).rearrange("p (k m) -> p k m", k=KC),
             AR_C[:, 2048:4096].bitcast(BF16).rearrange("p (k m) -> p k m", k=KC)]
    xb_f = [AR_C[:, 4096:8192], AR_C[:, 8192:12288]]

    evac_toks, rhsuse_toks = {}, {}
    for ft in range(4):
        rb, xf = rhs_b[ft % 2], xb_f[ft % 2]
        war = ([rhsuse_toks[ft - 2]] if ft - 2 in rhsuse_toks else [tok_xb_scan])
        tok_xf = mk.dma("s", xf.rearrange("p (k m) -> p k m", k=KC),
                        scr["xbscan"][:, :, ft * 512:(ft + 1) * 512],
                        waits=war, ch=6 + ft % 2)
        for kc in range(KC):
            mk.op("vector", lambda E, rb=rb, xf=xf, kc=kc, ft=ft:
                  E.scalar_tensor_tensor(rb[:, kc, :], xf[:, kc * 512:(kc + 1) * 512],
                                         smcol("lns_w", kc), bc_r2[:, ft * 512:(ft + 1) * 512],
                                         ALU.mult, ALU.mult),
                  waits=([tok_xf, tb_r2] if kc == 0 else [tok_xf] if kc == 1 else []),
                  incs=[("vs", 1)])
        v_rhs = mk.c("vs")
        for mc in range(16):
            bank = pb[mc % 2]
            key = ft * 16 + mc
            war_t = ([evac_toks[key - 2]] if key - 2 in evac_toks else [])
            for kc in range(KC):
                mk.op("tensor", lambda E, bank=bank, mc=mc, kc=kc, rb=rb:
                      E.matmul(bank[:, 0:512], inw_bf[:, kc, mc * 128:mc * 128 + 128],
                               rb[:, kc, :], start=(kc == 0), stop=False),
                      waits=(([v_rhs] + war_t + ([tok_inw, v_gw, v_mrow, v_hw] if key == 0 else []))
                             if kc == 0 else []),
                      incs=[("ts", 1)])
            mk.op("tensor", lambda E, bank=bank, mc=mc, ft=ft:
                  E.matmul(bank[:, 0:512], gw_bf[:, mc * 128:mc * 128 + 128],
                           mrow_bf[:, ft * 512:(ft + 1) * 512], start=False, stop=True),
                  incs=[("ts", 1)])
            t_mm = mk.c("ts")
            if mc < 8:
                mk.op("scalar", lambda E, bank=bank, mc=mc, ft=ft:
                      E.activation(sp_bf[:, mc, ft * 512:(ft + 1) * 512], bank[:, 0:512],
                                   AF.Silu, bias=hw_col[:, mc:mc + 1]),
                      waits=[t_mm], incs=[("ss", 1)])
            else:
                mk.op("scalar", lambda E, bank=bank, mc=mc, ft=ft:
                      E.activation(z_f16[:, mc - 8, ft * 512:(ft + 1) * 512], bank[:, 0:512],
                                   AF.Sigmoid, bias=hw_col[:, mc:mc + 1]),
                      waits=[t_mm], incs=[("ss", 1)])
            evac_toks[key] = mk.c("ss")
        rhsuse_toks[ft] = mk.c("ts")
    s_inw = mk.c("ss")

    tok_xpw = mk.dma("g", xpw_bf.rearrange("p (k n) -> p k n", k=KC),
                 dt_in["xp_w"].rearrange("(k p) n -> p k n", p=128), ch=9)
    mk.op("vector", lambda E: E.memset(u64q, 0.0), incs=[("vs", 1)])
    v_u0 = mk.c("vs")
    for ft in range(4):
        for kc in range(KC):
            mk.op("tensor", lambda E, ft=ft, kc=kc:
                  E.matmul(pb[2][0:16, 0:512], xpw_bf[:, kc * 16:kc * 16 + 16],
                           sp_bf[:, kc, ft * 512:(ft + 1) * 512],
                           start=(kc == 0), stop=(kc == KC - 1)),
                  waits=([tok_xpw, s_inw, mk.c("vs")] if kc == 0 else []), incs=[("ts", 1)])
        t_u = mk.c("ts")
        for b in range(4):
            mk.op("vector", lambda E, ft=ft, b=b:
                  E.tensor_copy(u64q[32 * b:32 * b + 16, 128 * ft:128 * ft + 128],
                                pb[2][0:16, b:512:4]),
                  waits=([t_u, v_u0] if b == 0 else []), incs=[("vs", 1)])
    v_u = mk.c("vs")
    mk.op("scalar", lambda E: E.activation(ld64[0:16, :], smcol("A_log", rows=16), AF.Exp),
          waits=[t_fix], incs=[("ss", 1)])
    s_ld = mk.c("ss")
    mk.op("vector", lambda E: E.tensor_scalar_mul(ld64[0:16, :], ld64[0:16, :], -1.0),
          waits=[s_ld], incs=[("vs", 1)])
    mk.op("vector", lambda E: E.tensor_copy(xpbq[0:16, :], smcol("xp_b", rows=16)), incs=[("vs", 1)])
    for q in (32, 64, 96):
        mk.op("vector", lambda E, q=q: E.tensor_copy(ld64[q:q + 16, :], ld64[0:16, :]),
              incs=[("vs", 1)])
        mk.op("vector", lambda E, q=q: E.tensor_copy(xpbq[q:q + 16, :], xpbq[0:16, :]),
              incs=[("vs", 1)])
    mk.op("vector", lambda E: E.tensor_scalar_mul(ld512, ld64, 512.0), incs=[("vs", 1)])
    v_ld = mk.c("vs")
    mk.op("vector", lambda E: E.memset(dk_rep, 1.0), incs=[("vs", 1)])
    mk.op("vector", lambda E: E.tensor_tensor_scan(p512, dk_rep, dk_rep, 0.0, ALU.mult, ALU.add),
          incs=[("vs", 1)])
    v_iota = mk.c("vs")
    mk.op("scalar", lambda E: E.activation(dk_rep, p512, AF.Exp, bias=ld64, scale=0.0),
          waits=[v_ld, v_iota], incs=[("ss", 1)])
    mk.op("scalar", lambda E: E.activation(p512, p512, AF.Exp, bias=0.0, scale=ld64),
          waits=[mk.c("ss")], incs=[("ss", 1)])
    s_dk = mk.c("ss")
    mk.op("vector", lambda E: E.tensor_scalar_add(u64q, u64q, xpbq), waits=[v_u, v_ld],
          incs=[("vs", 1)])
    v_u64 = mk.c("vs")
    mk.op("vector", lambda E: E.tensor_tensor_scan(hsq, dk_rep, u64q, 0.0, ALU.mult, ALU.add),
          waits=[v_u64, s_dk], incs=[("vs", 1)])
    mk.op("vector", lambda E: E.tensor_copy(ecol, hsq[:, 511:512]), incs=[("vs", 1)])
    v_e = mk.c("vs")
    tok_e = mk.dma("s", scr["esnd"][0:128], ecol[:, 0:1], waits=[v_e], ch=5)
    mk.op("gpsimd", lambda E: E.collective_compute(
        "AllGather", ALU.bypass, replica_groups=[list(range(8))],
        ins=[scr["esnd"][:].opt()], outs=[scr["ercv"][:].opt()]),
        waits=[tok_e], incs=[("cc", 1)])
    cc_e = mk.c("cc")
    tok_ca = mk.dma("g", ca8[:, :], scr["ercv"].transpose([1, 0]),
                    waits=[cc_e], allow_slow_non_contiguous=True, ch=8)
    mk.op("scalar", lambda E: E.activation(w8t, CPW[:], AF.Exp, bias=0.0, scale=ld512),
          waits=[v_ld, t_fix], incs=[("ss", 1)])
    s_w8 = mk.c("ss")
    mk.op("vector", lambda E: E.tensor_mul(w8t, w8t, ca8), waits=[s_w8, tok_ca], incs=[("vs", 1)])
    mk.op("vector", lambda E: E.reduce_sum(carry, w8t, axis=AXX), incs=[("vs", 1)])
    mk.op("vector", lambda E: E.scalar_tensor_tensor(hsq, p512, carry, hsq, ALU.mult, ALU.add),
          incs=[("vs", 1)])
    v_hs = mk.c("vs")
    for b in range(4):
        mk.op("vector", lambda E, b=b: E.tensor_copy(hsr_bf[:, b::4], hsq[32 * b:32 * b + 16, :]),
              waits=([v_hs] if b == 0 else []), incs=[("vs", 1)])
    v_hsr = mk.c("vs")

    tok_d = mk.dma("g", dmat_bf, dt_in["dmat"], ch=9)
    outw_bf = WS[:, 4608:4608 + 4096].bitcast(BF16).rearrange("p (k m) -> p k m", k=KC)
    tok_ow = None
    t_inw_done = mk.c("ts")
    for kc in range(KC):
        tok_ow = mk.dma("g", outw_bf[:, kc, :], dt_in["out_w"][kc * 128:(kc + 1) * 128, :],
                        waits=([t_inw_done] if kc == 0 else []), ch=7)

    ysd_bf = AR_C[:, 0:8192].bitcast(BF16).rearrange("p (k m) -> p k m", k=KC)
    xbo_f = [AR_C[:, 8192:12288]]
    ysd_toks = {}
    for ft in range(4):
        for mc in range(8):
            bank = pb[mc % 2]
            key = ft * 8 + mc
            war = ([ysd_toks[key - 2]] if key - 2 in ysd_toks else [])
            mk.op("tensor", lambda E, bank=bank, mc=mc, ft=ft:
                  E.matmul(bank[:, 0:512], dmat_bf[:, mc * 128:mc * 128 + 128],
                           hsr_bf[:, ft * 512:(ft + 1) * 512], start=True, stop=True),
                  waits=([v_hsr, tok_d] if key == 0 else []) + war, incs=[("ts", 1)])
            t_ysd = mk.c("ts")
            mk.op("vector", lambda E, bank=bank, mc=mc, ft=ft:
                  E.scalar_tensor_tensor(ysd_bf[:, mc, ft * 512:(ft + 1) * 512],
                                         bank[:, 0:512], 1.0,
                                         z_f16[:, mc, ft * 512:(ft + 1) * 512],
                                         ALU.mult, ALU.mult),
                  waits=[t_ysd], incs=[("vs", 1)])
            ysd_toks[key] = mk.c("vs")
    v_ysd = mk.c("vs")

    xm_toks = {}
    for ft in range(4):
        xbo = xbo_f[0]
        war = ([xm_toks[ft - 1]] if ft - 1 in xm_toks else [])
        tok_xbo = mk.dma("s", xbo.rearrange("p (k m) -> p k m", k=KC),
                         scr["xbsp"][:, :, ft * 512:(ft + 1) * 512],
                         waits=([tok_xb_own] if ft == 0 else []) + war, ch=6)
        xbo_k = xbo.rearrange("p (k m) -> p k m", k=KC)
        for mc in range(8):
            bank = pb[2 + mc % 2]
            key = ft * 8 + mc
            for kc in range(KC):
                mk.op("tensor", lambda E, bank=bank, mc=mc, kc=kc, ft=ft:
                      E.matmul(bank[:, 0:512], outw_bf[:, kc, mc * 128:mc * 128 + 128],
                               ysd_bf[:, kc, ft * 512:(ft + 1) * 512],
                               start=(kc == 0), stop=(kc == KC - 1)),
                      waits=(([v_ysd, tok_ow] if key == 0 else []) +
                             ([mk.c("vs")] if kc == 0 and key >= 2 else []))
                      if kc == 0 else [],
                      incs=[("ts", 1)])
            t_xm = mk.c("ts")
            mk.op("vector", lambda E, bank=bank, mc=mc, xbo_k=xbo_k:
                  E.scalar_tensor_tensor(xbo_k[:, mc, :], bank[:, 0:512], smcol("out_b", mc),
                                         xbo_k[:, mc, :], ALU.add, ALU.add),
                  waits=[t_xm, tok_xbo], incs=[("vs", 1)])
        v_xm = mk.c("vs")
        xm_toks[ft] = mk.dma("g",
                             scr["xmid"][:, :, ft * 512:(ft + 1) * 512],
                             xbo.rearrange("p (k m) -> p k m", k=KC), waits=[v_xm], ch=6)
    tok_xmid = xm_toks[3]

    if stage >= 2:
        mk.barrier()
        stc = stats(lambda kc: scr["xmid"][:, kc, :], T)
        bc_r3 = AR_B[:, 0:T]
        tb_r3 = bcast(bc_r3, 0, T, stc["tok_r"])
        v_mrow3 = load_mrow(stc["tok_mr"])

        w1_bf = WS[:, 0:8192].bitcast(BF16).rearrange("p (k m) -> p k m", k=KC)
        tok_w1 = None
        t_ws_free = mk.c("ts")
        for kc in range(KC):
            tok_w1 = mk.dma("g", w1_bf[:, kc, :], dt_in["mlp_w1"][kc * 128:(kc + 1) * 128, :],
                            ch=7, waits=([t_ws_free] if kc == 0 else []))
        v_gw3, v_hw3 = gw_hw(w1_bf, 2048, "ln2_w", "ln2_b", "mlp_b1", tok_w1)

        w2c_bf = [WS[:, 8192:8704].bitcast(BF16), WS[:, 8704:9216].bitcast(BF16)]
        hdn_bf = AR_A[:, 0:4096].bitcast(BF16).rearrange("p (k m) -> p k m", k=16)
        rhs_c = [AR_C[:, 0:2048].bitcast(BF16).rearrange("p (k m) -> p k m", k=KC),
                 AR_C[:, 2048:4096].bitcast(BF16).rearrange("p (k m) -> p k m", k=KC)]
        xm_f = [AR_C[:, 4096:8192], AR_C[:, 8192:12288]]

        prev_spill, rhsuse = {}, {}
        hdn_war = None
        for ft in range(4):
            rb, xf = rhs_c[ft % 2], xm_f[ft % 2]
            war = ([rhsuse[ft - 2]] if ft - 2 in rhsuse else [])
            warsp = ([prev_spill[ft - 2]] if ft - 2 in prev_spill else [])
            tok_xf = mk.dma("s", xf.rearrange("p (k m) -> p k m", k=KC),
                            scr["xmid"][:, :, ft * 512:(ft + 1) * 512],
                            waits=war + warsp, ch=6 + ft % 2)
            for kc in range(KC):
                mk.op("vector", lambda E, rb=rb, xf=xf, kc=kc, ft=ft:
                      E.scalar_tensor_tensor(rb[:, kc, :], xf[:, kc * 512:(kc + 1) * 512],
                                             smcol("ln2_w", kc), bc_r3[:, ft * 512:(ft + 1) * 512],
                                             ALU.mult, ALU.mult),
                      waits=([tok_xf, tb_r3] if kc == 0 else []), incs=[("vs", 1)])
            v_rhs = mk.c("vs")
            for mc in range(16):
                bank = pb[mc % 2]
                key = ft * 16 + mc
                war_t = ([mk.c("ss")] if mc >= 2 else ([hdn_war] if hdn_war else []))
                for kc in range(KC):
                    mk.op("tensor", lambda E, bank=bank, mc=mc, kc=kc, rb=rb:
                          E.matmul(bank[:, 0:512], w1_bf[:, kc, mc * 128:mc * 128 + 128],
                                   rb[:, kc, :], start=(kc == 0), stop=False),
                          waits=(([v_rhs] + war_t +
                                  ([tok_w1, v_gw3, v_mrow3, v_hw3] if key == 0 else []))
                                 if kc == 0 else []),
                          incs=[("ts", 1)])
                mk.op("tensor", lambda E, bank=bank, mc=mc, ft=ft:
                      E.matmul(bank[:, 0:512], gw_bf[:, mc * 128:mc * 128 + 128],
                               mrow_bf[:, ft * 512:(ft + 1) * 512], start=False, stop=True),
                      incs=[("ts", 1)])
                t_mm = mk.c("ts")
                mk.op("scalar", lambda E, bank=bank, mc=mc:
                      E.activation(hdn_bf[:, mc, :], bank[:, 0:512], AF.Gelu,
                                   bias=hw_col[:, mc:mc + 1]),
                      waits=[t_mm], incs=[("ss", 1)])
            s_hdn = mk.c("ss")
            tokw2, w2war = {}, {}
            for kc in range(16):
                w2b = w2c_bf[kc % 2]
                war2 = ([w2war[kc - 2]] if kc - 2 in w2war else [mk.c("ts")])
                tokw2[kc] = mk.dma("g", w2b, dt_in["mlp_w2"][kc * 128:(kc + 1) * 128, :],
                                   ch=9 + kc % 2, waits=war2)
                for mc in range(8):
                    mk.op("tensor", lambda E, kc=kc, mc=mc, w2b=w2b:
                          E.matmul(pb[mc][:, 0:512], w2b[:, mc * 128:mc * 128 + 128],
                                   hdn_bf[:, kc, :], start=(kc == 0), stop=(kc == 15)),
                          waits=(([tokw2[kc]] + ([s_hdn] if kc == 0 else [])) if mc == 0 else []),
                          incs=[("ts", 1)])
                w2war[kc] = mk.c("ts")
            t_w2 = mk.c("ts")
            hdn_war = t_w2
            xm_k = xf.rearrange("p (k m) -> p k m", k=KC)
            for mc in range(8):
                mk.op("vector", lambda E, mc=mc, xm_k=xm_k:
                      E.scalar_tensor_tensor(xm_k[:, mc, :], pb[mc][:, 0:512], smcol("mlp_b2", mc),
                                             xm_k[:, mc, :], ALU.add, ALU.add),
                      waits=([t_w2] if mc == 0 else []), incs=[("vs", 1)])
            v_bo = mk.c("vs")
            prev_spill[ft] = mk.dma("g", scr["xmid"][:, :, ft * 512:(ft + 1) * 512],
                                    xm_k, waits=[v_bo], ch=6 + 5 * (ft % 2))
            rhsuse[ft] = mk.c("ts")
        mk.barrier()

        std = stats(lambda kc: scr["xmid"][:, kc, :], T)
        bc_r4 = AR_B[:, 0:T]
        bc_mr4 = AR_B[:, 2080:2080 + T]
        tb_r4 = bcast(bc_r4, 0, T, std["tok_r"])
        tb_mr4 = bcast(bc_mr4, 2, T, std["tok_mr"])
        bo_c = [AR_C[:, 0:4096], AR_C[:, 4096:8192]]
        xo_c = AR_C[:, 8192:12288]
        hic = AR_A[:, 0:2048].bitcast(BF16).rearrange("p (k m) -> p k m", k=KC)
        loc = AR_A[:, 2048:4096].bitcast(BF16).rearrange("p (k m) -> p k m", k=KC)
        hif = AR_A[:, 4096:4608]
        prev = {}
        tok_hi = tok_lo = None
        for ft in range(4):
            boc = bo_c[ft % 2]
            war = ([prev[ft - 2]] if ft - 2 in prev else [])
            warx = ([prev[ft - 1]] if ft - 1 in prev else [])
            tok_bo = mk.dma("s", boc.rearrange("p (k m) -> p k m", k=KC),
                            scr["xmid"][:, :, ft * 512:(ft + 1) * 512],
                            waits=war, ch=6 + ft % 2)
            tok_xo = mk.dma("g", xo_c.rearrange("p (k m) -> p k m", k=KC),
                            dt_in["x_own"].rearrange("(k p) m -> p k m", p=128)[:, :, 3 + ft * 512:3 + (ft + 1) * 512],
                            waits=warx, ch=8)
            boc_k = boc.rearrange("p (k m) -> p k m", k=KC)
            xoc_k = xo_c.rearrange("p (k m) -> p k m", k=KC)
            sl = slice(ft * 512, (ft + 1) * 512)
            war_hl = [tok_hi, tok_lo] if tok_hi else []
            for kc in range(KC):
                mk.op("vector", lambda E, boc_k=boc_k, kc=kc, sl=sl:
                      E.tensor_mul(boc_k[:, kc, :], boc_k[:, kc, :], bc_r4[:, sl]),
                      waits=([tok_bo, tb_r4] if kc == 0 else []), incs=[("vs", 1)])
                mk.op("vector", lambda E, boc_k=boc_k, kc=kc, sl=sl:
                      E.tensor_sub(boc_k[:, kc, :], boc_k[:, kc, :], bc_mr4[:, sl]),
                      waits=([tb_mr4] if kc == 0 else []), incs=[("vs", 1)])
                mk.op("vector", lambda E, boc_k=boc_k, xoc_k=xoc_k, kc=kc:
                      E.tensor_add(boc_k[:, kc, :], boc_k[:, kc, :], xoc_k[:, kc, :]),
                      waits=([tok_xo] if kc == 0 else []), incs=[("vs", 1)])
                mk.op("vector", lambda E, boc_k=boc_k, kc=kc:
                      E.tensor_copy(hic[:, kc, :], boc_k[:, kc, :]),
                      waits=(war_hl if kc == 0 else []), incs=[("vs", 1)])
                mk.op("vector", lambda E, kc=kc:
                      E.tensor_copy(hif, hic[:, kc, :]), incs=[("vs", 1)])
                mk.op("vector", lambda E, boc_k=boc_k, kc=kc:
                      E.tensor_sub(loc[:, kc, :], boc_k[:, kc, :], hif), incs=[("vs", 1)])
            v_x1 = mk.c("vs")
            tok_hi = mk.dma("g", scr["xsnd"][:, 0:8, ft * 512:(ft + 1) * 512], hic,
                            waits=[v_x1], ch=6 + 5 * (ft % 2))
            tok_lo = mk.dma("g", scr["xsnd"][:, 8:16, ft * 512:(ft + 1) * 512], loc,
                            waits=[v_x1], ch=12 + ft % 2)
            prev[ft] = tok_lo
        mk.op("gpsimd", lambda E: E.collective_compute(
            "AllGather", ALU.bypass, replica_groups=[[0, 1], [2, 3], [4, 5], [6, 7]],
            ins=[scr["xsnd"][:].opt()], outs=[scr["xrcv"][:].opt()]),
            waits=[mk.c("dg6"), mk.c("dg11"), mk.c("dg12"), mk.c("dg13")], incs=[("cc", 1)])
        cc_x1 = mk.c("cc")

        t0 = AR_A[:, 0:1024]
        srcbuf = AR_A[:, 1024:1024 + 2048].bitcast(BF16).rearrange("p (v k m) -> p v k m", v=4, k=KC)
        x1o_c = AR_A[:, 3072 + 1024:3072 + 1024 + 1024]
        x1o_c = AR_A[:, 4096:5120]
        prevass = {}
        for j in range(16):
            g_ev, g_od = 2 * j, 2 * j + 1
            war = ([prevass[j - 1]] if j - 1 in prevass else [cc_x1])
            toks = []
            for vi, (g, hl) in enumerate([(g_ev, 0), (g_ev, 1), (g_od, 0), (g_od, 1)]):
                half, locc = g // 16, (g % 16) * 128
                toks.append(mk.dma("s", srcbuf[:, vi, :, :],
                                   scr["xrcv"][half, :, hl * 8:hl * 8 + 8, locc:locc + 128],
                                   waits=war, ch=(6, 7, 14, 15)[vi]))
            tsel = toks[-1]
            x1k = x1o_c.rearrange("p (k m) -> p k m", k=KC)
            mk.op("vector", lambda E, j=j:
                  E.tensor_scalar(x1o_c, srcbuf[:, 0, :, :].rearrange("p k m -> p (k m)"),
                                  PARSEL[:, 0:1], None, ALU.mult),
                  waits=toks + ([prevass[j - 1]] if j - 1 in prevass else []), incs=[("vs", 1)])
            mk.op("vector", lambda E, j=j:
                  E.scalar_tensor_tensor(x1o_c, srcbuf[:, 1, :, :].rearrange("p k m -> p (k m)"),
                                         PARSEL[:, 0:1], x1o_c, ALU.mult, ALU.add),
                  incs=[("vs", 1)])
            mk.op("vector", lambda E, j=j:
                  E.scalar_tensor_tensor(x1o_c, srcbuf[:, 2, :, :].rearrange("p k m -> p (k m)"),
                                         PARSEL[:, 1:2], x1o_c, ALU.mult, ALU.add),
                  incs=[("vs", 1)])
            mk.op("vector", lambda E, j=j:
                  E.scalar_tensor_tensor(x1o_c, srcbuf[:, 3, :, :].rearrange("p k m -> p (k m)"),
                                         PARSEL[:, 1:2], x1o_c, ALU.mult, ALU.add),
                  incs=[("vs", 1)])
            v_sel = mk.c("vs")
            prevass[j] = mk.dma("g", scr["x1own"][:, :, j * 128:(j + 1) * 128],
                                x1o_c.rearrange("p (k m) -> p k m", k=KC),
                                waits=[v_sel], ch=8)
        tok_x1own = prevass[15]

    if stage >= 3:
        mk.barrier()
        H = 16
        steo = stats(lambda kc: scr["x1own"][:, kc, :], T, fused2=True, ship_row0=4)
        bc_g5o = AR_B[:, 6144:6144 + T]
        tb_g5o = bcast(bc_g5o, 5, T, steo["tok_g"])
        v_mrow5 = load_mrow(steo["tok_mg"], slot=7)

        wq_bf = WS[:, 0:4096].bitcast(BF16).rearrange("p (k m) -> p k m", k=KC)
        kva_bf = WS[:, 4096:4736].bitcast(BF16).rearrange("p (k m) -> p k m", k=KC)
        kvb_bf = WS[:, 4736:4992].bitcast(BF16)
        wo_bf = WS[:, 4992:7040].bitcast(BF16).rearrange("p (k m) -> p k m", k=4)
        wgate_f = WS[:, 7040:7104].rearrange("p (k m) -> p k m", k=KC)
        tok_wq = tok_kva = tok_kvb = tok_wo = tok_wg8 = None
        for kc in range(KC):
            tok_wq = mk.dma("g", wq_bf[:, kc, :], dt_in["wq"][kc * 128:(kc + 1) * 128, :], ch=7)
        for kc in range(KC):
            tok_kva = mk.dma("g", kva_bf[:, kc, :], dt_in["wkva"][kc * 128:(kc + 1) * 128, :], ch=9)
        tok_kvb = mk.dma("g", kvb_bf, dt_in["wkvb"], ch=10)
        for kc in range(4):
            tok_wo = mk.dma("g", wo_bf[:, kc, :], dt_in["wo_eff"][kc * 128:(kc + 1) * 128, :], ch=8)
        tok_wg8 = mk.dma("s", wgate_f, dt_in["wgate"].rearrange("(k p) m -> p k m", p=128), ch=14)
        csqa_v = AR_B[:, 2048:3072].bitcast(BF16)
        csqb_v = AR_B[:, 3072:4096].bitcast(BF16)
        for rb in (32, 96):
            tok_csq = mk.dma("s", csqa_v[rb:rb + 32, :], dt_in["csq_a"], ch=15)
            tok_csq = mk.dma("s", csqb_v[rb:rb + 32, :], dt_in["csq_b"], ch=15)
        cska_bf = AR_B[:, 6144:8192].bitcast(BF16)[32:64, :]
        cskb_bf = AR_B[:, 8192:10240].bitcast(BF16)[32:64, :]
        maskq_bf = AR_B[:, 0:2048].bitcast(BF16)
        tok_mask = mk.dma("s", maskq_bf, dt_in["qmask"], ch=14)

        c0w = SMCOL["w_ln1"][0]
        mk.op("vector", lambda E: E.tensor_copy(gcol_bf, SM[:, c0w:c0w + 8]), incs=[("vs", 1)])
        v_g1 = mk.c("vs")
        for i, f0 in enumerate(range(0, 1024, 512)):
            for kc in range(KC):
                mk.op("tensor", lambda E, f0=f0, kc=kc:
                      E.matmul(pb[5][0:1, 0:512], gcol_bf[:, kc:kc + 1],
                               wq_bf[:, kc, f0:f0 + 512], start=(kc == 0), stop=(kc == KC - 1)),
                      waits=([v_g1, tok_wq, mk.c("vs")] if kc == 0 else []), incs=[("ts", 1)])
            t_gw = mk.c("ts")
            mk.op("vector", lambda E, f0=f0: E.tensor_copy(gw_bf[:, f0:f0 + 512], pb[5][0:1, 0:512]),
                  waits=[t_gw], incs=[("vs", 1)])
        for kc in range(KC):
            mk.op("tensor", lambda E, kc=kc:
                  E.matmul(pb[5][0:1, 0:160], gcol_bf[:, kc:kc + 1],
                           kva_bf[:, kc, :], start=(kc == 0), stop=(kc == KC - 1)),
                  waits=([tok_kva, mk.c("vs")] if kc == 0 else []), incs=[("ts", 1)])
        t_gk = mk.c("ts")
        mk.op("vector", lambda E: E.tensor_copy(gw_bf[:, 1024:1184], pb[5][0:1, 0:160]),
              waits=[t_gk], incs=[("vs", 1)])
        v_gwq = mk.c("vs")

        qp_bf = AR_A[:, 0:8192].bitcast(BF16)
        lat_bf = AR_A[:, 8192:10240].bitcast(BF16)
        xoc2 = [AR_C[:, 0:4096], AR_C[:, 4096:8192]]
        rhsq = AR_C[:, 8448:10496].bitcast(BF16).rearrange("p (k m) -> p k m", k=KC)
        ev_tok = {}
        for ft in range(4):
            xoc = xoc2[ft % 2]
            war = ([ev_tok[ft - 2]] if ft - 2 in ev_tok else [])
            tok_xo = mk.dma("s", xoc.rearrange("p (k m) -> p k m", k=KC),
                            scr["x1own"][:, :, ft * 512:(ft + 1) * 512],
                            waits=war, ch=6 + ft % 2)
            xok = xoc.rearrange("p (k m) -> p k m", k=KC)
            for kc in range(KC):
                mk.op("vector", lambda E, xok=xok, kc=kc, ft=ft:
                      E.scalar_tensor_tensor(rhsq[:, kc, :], xok[:, kc, :],
                                             smcol("w_ln1", kc), bc_g5o[:, ft * 512:(ft + 1) * 512],
                                             ALU.mult, ALU.mult),
                      waits=([tok_xo, tb_g5o, mk.c("ts")] if kc == 0 else []), incs=[("vs", 1)])
            v_rq = mk.c("vs")
            for pr in range(8):
                bank = pb[pr % 2]
                key = ft * 8 + pr
                war_t = ([ev_tok.get(("q", key - 2), ("ss", 0))] if key >= 2 else [])
                for kc in range(KC):
                    mk.op("tensor", lambda E, bank=bank, pr=pr, kc=kc:
                          E.matmul(bank[:, 0:512], wq_bf[:, kc, pr * 128:pr * 128 + 128],
                                   rhsq[:, kc, :], start=(kc == 0), stop=False),
                          waits=(([v_rq] + war_t + ([v_gwq, v_mrow5] if key == 0 else []))
                                 if kc == 0 else []),
                          incs=[("ts", 1)])
                mk.op("tensor", lambda E, bank=bank, pr=pr, ft=ft:
                      E.matmul(bank[:, 0:512], gw_bf[:, pr * 128:pr * 128 + 128],
                               mrow_bf[:, ft * 512:(ft + 1) * 512], start=False, stop=True),
                      incs=[("ts", 1)])
                t_q = mk.c("ts")
                mk.op("scalar", lambda E, bank=bank, pr=pr, ft=ft:
                      E.activation(qp_bf[:, pr * 2048 + ft * 512:pr * 2048 + (ft + 1) * 512],
                                   bank[:, 0:512], AF.Copy, scale=0.125),
                      waits=[t_q], incs=[("ss", 1)])
                ev_tok[("q", key)] = mk.c("ss")
            ev_tok[ft] = mk.c("ts")
        s_qdone = mk.c("ss")

        rtmp = AR_B[:, 4096:5120].bitcast(BF16)
        for pr in range(8):
            cols = slice(pr * 2048, (pr + 1) * 2048)
            for base in (32, 96):
                rt = rtmp[base:base + 32, :]
                mk.op("vector", lambda E, cols=cols, base=base, rt=rt:
                      E.stream_shuffle(rt, qp_bf[base:base + 32, cols],
                                       [i ^ 1 for i in range(32)]),
                      waits=([s_qdone, tok_csq] if (pr == 0 and base == 32) else []),
                      incs=[("vs", 1)])
                mk.op("vector", lambda E, rt=rt, base=base:
                      E.tensor_mul(rt, rt, csqb_v[base:base + 32, :]),
                      incs=[("vs", 1)])
                mk.op("vector", lambda E, cols=cols, base=base:
                      E.tensor_mul(qp_bf[base:base + 32, cols], qp_bf[base:base + 32, cols],
                                   csqa_v[base:base + 32, :]),
                      incs=[("vs", 1)])
                mk.op("vector", lambda E, cols=cols, base=base, rt=rt:
                      E.tensor_add(qp_bf[base:base + 32, cols], qp_bf[base:base + 32, cols], rt),
                      incs=[("vs", 1)])
        v_qrope = mk.c("vs")

        stg = {}
        for half in range(2):
            stg[half] = stats(lambda kc, half=half: scr["xrcv"][half, :, kc, :], T,
                              fused2=True, ship_off=half * T, tbuf=AR_C, tbase=4096)
        bc_g5g = [AR_B[:, 6144:6144 + T], AR_B[:, 8192:8192 + T]]
        tb_g5g = [bcast(bc_g5g[0], 1, T, stg[0]["tok_g"], off=0),
                  bcast(bc_g5g[1], 1, T, stg[1]["tok_g"], off=T)]
        mrowG_bf = ROWS[0:1, 0:2048].bitcast(BF16)
        mg_f = rslot(2, T)
        for half in range(2):
            tokm = mk.dma("s", mg_f, scr["rows_scr"][3:4, half * T:(half + 1) * T],
                          waits=[stg[half]["tok_mg"], mk.c("vs")], ch=5)
            mk.op("vector", lambda E, half=half:
                  E.tensor_scalar_mul(mrowG_bf[:, half * T:(half + 1) * T], mg_f, -1.0),
                  waits=[tokm], incs=[("vs", 1)])
        v_mrowG = mk.c("vs")

        krp_bf = ROWS[32:64, 0:2048].bitcast(BF16)
        khc = [AR_C[:, 0:2048].bitcast(BF16).rearrange("p (k m) -> p k m", k=KC),
               AR_C[:, 2048:4096].bitcast(BF16).rearrange("p (k m) -> p k m", k=KC)]
        rhk = AR_C[:, 8448:10496].bitcast(BF16).rearrange("p (k m) -> p k m", k=KC)
        kv_tok = {}
        for ftg in range(8):
            half, ftl = ftg // 4, ftg % 4
            hc = khc[ftg % 2]
            war = ([kv_tok[ftg - 2]] if ftg - 2 in kv_tok else [v_qrope])
            tok_h = mk.dma("s", hc, scr["xrcv"][half, :, 0:8, ftl * 512:(ftl + 1) * 512],
                           waits=war, ch=6 + ftg % 2)
            for kc in range(KC):
                mk.op("vector", lambda E, hc=hc, kc=kc, ftg=ftg:
                      E.scalar_tensor_tensor(rhk[:, kc, :], hc[:, kc, :],
                                             smcol("w_ln1", kc),
                                             bc_g5g[ftg // 4][:, (ftg % 4) * 512:(ftg % 4 + 1) * 512],
                                             ALU.mult, ALU.mult),
                      waits=([tok_h] + ([tb_g5g[0], tb_g5g[1], mk.c("ts")] if kc == 0 else []))
                      if kc <= 0 else [],
                      incs=[("vs", 1)])
            v_rk = mk.c("vs")
            for kc in range(KC):
                mk.op("tensor", lambda E, kc=kc, ftg=ftg:
                      E.matmul(pb[ftg % 2][:, 0:512], kva_bf[:, kc, 0:128],
                               rhk[:, kc, :], start=(kc == 0), stop=False),
                      waits=([v_rk, v_mrowG] if kc == 0 else []), incs=[("ts", 1)])
            mk.op("tensor", lambda E, ftg=ftg:
                  E.matmul(pb[ftg % 2][:, 0:512], gw_bf[:, 1024:1152],
                           mrowG_bf[:, ftg * 512:(ftg + 1) * 512], start=False, stop=True),
                  incs=[("ts", 1)])
            t_lat = mk.c("ts")
            for kc in range(KC):
                mk.op("tensor", lambda E, kc=kc, ftg=ftg:
                      E.matmul(pb[2 + ftg % 2][0:32, 0:512], kva_bf[:, kc, 128:160],
                               rhk[:, kc, :], start=(kc == 0), stop=False),
                      incs=[("ts", 1)])
            mk.op("tensor", lambda E, ftg=ftg:
                  E.matmul(pb[2 + ftg % 2][0:32, 0:512], gw_bf[:, 1152:1184],
                           mrowG_bf[:, ftg * 512:(ftg + 1) * 512], start=False, stop=True),
                  incs=[("ts", 1)])
            t_krp = mk.c("ts")
            mk.op("scalar", lambda E, ftg=ftg:
                  E.activation(lat_bf[:, ftg * 512:(ftg + 1) * 512], pb[ftg % 2][:, 0:512], AF.Copy),
                  waits=[t_lat], incs=[("ss", 1)])
            mk.op("scalar", lambda E, ftg=ftg:
                  E.activation(krp_bf[:, ftg * 512:(ftg + 1) * 512], pb[2 + ftg % 2][0:32, 0:512], AF.Copy),
                  waits=[t_krp], incs=[("ss", 1)])
            kv_tok[ftg] = mk.c("ss")
        s_ckv = mk.c("ss")

        tok_csk = mk.dma("s", cska_bf, dt_in["csk_a"], waits=[s_ckv], ch=15)
        tok_csk = mk.dma("s", cskb_bf, dt_in["csk_b"], ch=15)
        rtm2 = AR_B[:, 4096:6144].bitcast(BF16)[32:64, :]
        mk.op("vector", lambda E: E.stream_shuffle(rtm2[:, 0:2048], krp_bf[:, 0:2048],
                                                   [i ^ 1 for i in range(32)]),
              waits=[s_ckv, tok_csk], incs=[("vs", 1)])
        mk.op("vector", lambda E: E.stream_shuffle(rtm2[:, 2048:4096], krp_bf[:, 2048:4096],
                                                   [i ^ 1 for i in range(32)]),
              incs=[("vs", 1)])
        mk.op("vector", lambda E: E.tensor_mul(krp_bf, krp_bf, cska_bf), incs=[("vs", 1)])
        mk.op("vector", lambda E: E.tensor_mul(rtm2, rtm2, cskb_bf), incs=[("vs", 1)])
        mk.op("vector", lambda E: E.tensor_add(krp_bf, krp_bf, rtm2), incs=[("vs", 1)])
        v_rotk = mk.c("vs")

        vp_bf = AR_C[:, 0:8448].bitcast(BF16)
        mk.op("vector", lambda E: E.memset(vp_bf, 1.0), waits=[mk.c("ts")], incs=[("vs", 1)])
        v_vp0 = mk.c("vs")
        for kb in range(32):
            bank = pb[kb % 2]
            war = [mk.c("vs")] if kb >= 2 else [v_vp0]
            mk.op("tensor", lambda E, kb=kb, bank=bank:
                  E.matmul(bank[:, 0:512], lat_bf[:, kb * 128:(kb + 1) * 128],
                           kvb_bf, start=True, stop=True),
                  waits=[s_ckv, tok_kvb] + war, incs=[("ts", 1)])
            t_v = mk.c("ts")
            mk.op("vector", lambda E, kb=kb, bank=bank:
                  E.tensor_copy(vp_bf[:, kb * 528:kb * 528 + 528].rearrange("p (h v) -> p h v", h=16)[:, :, 0:32],
                                bank[:, 0:512].rearrange("p (h v) -> p h v", h=16)),
                  waits=[t_v], incs=[("vs", 1)])
        v_vp = mk.c("vs")

        kb_buf = AR_C[:, 8448:10496].bitcast(BF16)
        p_buf = AR_B[:, 0:0]
        p_buf = AR_C[:, 10240 - 2048 - 0:0] if False else None
        pB = AR_B[:, 4096:5120].bitcast(BF16)
        pB_b = AR_B[:, 6400:7424].bitcast(BF16)
        yh_bf = AR_B[0:32, 5120:6144].bitcast(BF16)
        lb_sb = AR_B[0:32, 6144:6400].bitcast(BF16)
        linv_bf = ROWS[0:1, 0:256].bitcast(BF16)
        ones32_bf = MISC[0:1, 2048:2064].bitcast(BF16)
        mk.op("vector", lambda E: E.memset(ones32_bf, 1.0), incs=[("vs", 1)])
        mk.op("vector", lambda E: E.tensor_copy(kb_buf[32:64, :], krp_bf),
              waits=[v_rotk], incs=[("vs", 1)])
        mk.op("vector", lambda E: E.tensor_copy(kb_buf[96:128, :], krp_bf),
              incs=[("vs", 1)])
        v_o32 = mk.c("vs")

        kbuild_tok = {}
        att_done = {}
        yspill = {}
        prev_tlb = None
        probe1 = isinstance(devtap, str) and devtap.startswith("dbg_attk")
        att_H = 1 if probe1 else H
        att_KB = int(devtap[8:]) if probe1 else 32
        if probe1:
            mk.op("vector", lambda E: E.memset(SC[:, 0:2048], 0.0), incs=[("vs", 1)])
            vz = mk.c("vs")
            ysflat = scr["ysp"].rearrange("p k t -> p (k t)")
            mk.dma("g", ysflat[:, 0:4096], SC[:, 0:2048].bitcast(BF16), waits=[vz], ch=6)
            mk.dma("g", ysflat[:, 4096:8192], SC[:, 0:2048].bitcast(BF16), ch=6)
        for h in range(att_H):
            pbase = 64 * (h % 2)
            war_kb = ([att_done[h - 2]] if h - 2 in att_done else [])
            cp_done = {}
            for ftk in range(8):
                bank = pb[6 if ftk % 2 == 0 else 7]
                war_cp = ([cp_done[ftk - 2]] if ftk - 2 in cp_done else [])
                mk.op("tensor", lambda E, h=h, ftk=ftk, bank=bank:
                      E.matmul(bank[0:32, 0:512], kvb_bf[:, 32 * h:32 * h + 32],
                               lat_bf[:, ftk * 512:(ftk + 1) * 512], start=True, stop=True),
                      waits=([s_ckv, v_rotk, mk.c("vs")] + war_kb if ftk == 0 else war_cp),
                      incs=[("ts", 1)])
                t_kn = mk.c("ts")
                mk.op("vector", lambda E, h=h, ftk=ftk, bank=bank, pbase=pbase:
                      E.tensor_copy(kb_buf[pbase:pbase + 32, ftk * 512:(ftk + 1) * 512],
                                    bank[0:32, 0:512]),
                      waits=[t_kn], incs=[("vs", 1)])
                cp_done[ftk] = mk.c("vs")
            kbuild_tok[h] = mk.c("vs")

            pr = h // 2
            qcols0 = pr * 2048
            exp_done = {}
            avused = {}
            for kb in range(att_KB):
                jmin = kb // 2
                fcs = list(range(jmin // 4, 4))
                pend_av = None

                def emit_qk(fc, kb=kb, jmin=jmin):
                    off0 = min(max(jmin - 4 * fc, 0), 4) * 128
                    qk_bank = pb[(kb + fc) % 2]
                    mk.op("tensor", lambda E, h=h, kb=kb, fc=fc, qk_bank=qk_bank, pbase=pbase,
                          qcols0=qcols0, off0=off0:
                          E.matmul(qk_bank[:, off0:512],
                                   kb_buf[pbase:pbase + 64, kb * 128:(kb + 1) * 128],
                                   qp_bf[pbase:pbase + 64,
                                         qcols0 + fc * 512 + off0:qcols0 + (fc + 1) * 512],
                                   start=True, stop=True),
                          waits=(([kbuild_tok[h], v_qrope, v_vp, v_o32, tok_mask] if kb == 0 and fc == fcs[0] else []) +
                                 ([exp_done[(kb - 2, fc)]] if (kb - 2, fc) in exp_done else []) +
                                 ([mk.c("vs")] if kb == 0 and fc == fcs[0] and h >= 1 else [])),
                          incs=[("ts", 1)])
                    t_s = mk.c("ts")
                    pc = (pB if kb % 2 == 0 else pB_b)[:, fc * 512:(fc + 1) * 512]
                    war_p = ([avused[(kb - 2, fc)]] if (kb - 2, fc) in avused else [])
                    mk.op("scalar", lambda E, pc=pc, qk_bank=qk_bank, off0=off0:
                          E.activation(pc[:, off0:512], qk_bank[:, off0:512], AF.Exp),
                          waits=[t_s] + war_p, incs=[("ss", 1)])
                    s_p = mk.c("ss")
                    avw = [s_p]
                    if 4 * fc <= jmin < 4 * fc + 4:
                        mk.op("vector", lambda E, pc=pc, off0=off0, kb=kb:
                              E.tensor_mul(pc[:, off0:off0 + 128], pc[:, off0:off0 + 128],
                                           maskq_bf[:, kb * 128:(kb + 1) * 128]),
                              waits=[s_p], incs=[("vs", 1)])
                        avw = [mk.c("vs")]
                    return (fc, off0, pc, avw)

                def emit_av(st, kb=kb):
                    fc, off0, pc, avw = st
                    mk.op("tensor", lambda E, h=h, kb=kb, fc=fc, pc=pc, off0=off0:
                          E.matmul(pb[2 + fc][0:33, off0:512],
                                   vp_bf[:, kb * 528 + 33 * h:kb * 528 + 33 * h + 33],
                                   pc[:, off0:512], start=(kb == 0),
                                   stop=(kb == (att_KB - 1 if probe1 and att_KB < 32
                                                else 8 * fc + 7))),
                          waits=avw, incs=[("ts", 1)])
                    avused[(kb, fc)] = mk.c("ts")
                    exp_done[(kb, fc)] = mk.c("ts")

                for fc in fcs:
                    st = emit_qk(fc)
                    if pend_av is not None:
                        emit_av(pend_av)
                    pend_av = st
                emit_av(pend_av)
            for fc in range(4):
                t_av = avused[(att_KB - 1 if probe1 and att_KB < 32 else 8 * fc + 7, fc)]
                def _recip(E, fc=fc):
                    with mk.nc.allow_low_precision(reason="softmax denom as bf16 operand"):
                        return E.reciprocal(linv_bf[:, 0:512], pb[2 + fc][32:33, 0:512])
                mk.op("vector", _recip,
                      waits=[t_av] + ([prev_tlb] if prev_tlb is not None else []),
                      incs=[("vs", 1)])
                v_li = mk.c("vs")
                mk.op("tensor", lambda E:
                      E.matmul(pb[6][0:32, 0:512], ones32_bf, linv_bf[:, 0:512],
                               start=True, stop=True),
                      waits=[v_li, mk.c("vs")], incs=[("ts", 1)])
                t_lb = mk.c("ts")
                prev_tlb = t_lb
                mk.op("vector", lambda E:
                      E.tensor_copy(lb_sb, pb[6][0:32, 0:512]),
                      waits=[t_lb], incs=[("vs", 1)])
                mk.op("vector", lambda E, fc=fc:
                      E.tensor_mul(yh_bf[:, fc * 512:(fc + 1) * 512], pb[2 + fc][0:32, 0:512], lb_sb),
                      waits=([yspill[h - 1]] if fc == 0 and h >= 1 else []),
                      incs=[("vs", 1)])
            v_y = mk.c("vs")
            yspill[h] = mk.dma("g", scr["ysp"][32 * (h % 4):32 * (h % 4) + 32, h // 4, :],
                               yh_bf, waits=[v_y], ch=6 + h % 2)
            att_done[h] = mk.c("ts")
        mk.barrier()
        if probe1:
            mk.op("vector", lambda E: E.tensor_copy(AR_B[0:33, 6912:7424], pb[2][0:33, 0:512]),
                  incs=[("vs", 1)])
            mk.barrier()
            mk.dma("g", scr["xmid"][:, 5, 0:1024], AR_B[:, 4096:5120], ch=6)
            mk.dma("g", scr["xmid"][0:32, 6, 0:1024], AR_B[0:32, 5120:6144], ch=6)
            mk.dma("g", scr["xmid"][0:33, 7, 0:512], AR_B[0:33, 6912:7424], ch=6)
            mk.dma("g", scr["xbsp"].rearrange("p k t -> p (k t)")[:, 0:10240],
                   AR_A[:, 0:10240], ch=6)
            mk.dma("g", scr["xbscan"].rearrange("p k t -> p (k t)")[:, 0:8448],
                   AR_C[:, 0:8448], ch=6)
            mk.dma("g", scr["xmid"][0:32, 0, :], ROWS[32:64, 0:2048], ch=6)
            mk.dma("g", scr["xmid"][:, 3, :], AR_B[:, 0:2048], ch=6)
            mk.barrier()
        if devtap == "dbg_att":
            mk.dma("g", scr["xbsp"].rearrange("p k t -> p (k t)")[:, 0:10240],
                   AR_A[:, 0:10240], ch=6)
            mk.dma("g", scr["xbscan"].rearrange("p k t -> p (k t)")[:, 0:8448],
                   AR_C[:, 0:8448], ch=6)
            mk.dma("g", scr["xmid"][0:32, 0, :], ROWS[32:64, 0:2048], ch=6)
            mk.dma("g", scr["xmid"][:, 2, :], bc_g5o, ch=6)
            mk.dma("g", scr["xmid"][:, 3, :], AR_B[:, 0:2048], ch=6)
            mk.dma("g", scr["xmid"][:, 4, :], AR_C[:, 8448:10496], ch=6)
            mk.barrier()

        bc_r5o = AR_B[:, 6144:6144 + T]
        bc_mr5o = AR_B[:, 8192:8192 + T]
        tb_r5o = bcast(bc_r5o, 4, T, steo["tok_r"])
        tb_mr5o = bcast(bc_mr5o, 6, T, steo["tok_mr"])
        yb = [AR_C[:, 0:1024].bitcast(BF16).rearrange("p (k m) -> p k m", k=4),
              AR_C[:, 1024:2048].bitcast(BF16).rearrange("p (k m) -> p k m", k=4)]
        x1c2 = [AR_C[:, 2048:6144], AR_C[:, 6144:10240]]
        x2c = AR_A[:, 0:4096]
        x2tok = {}
        for ft in range(4):
            ybf, x1c = yb[ft % 2], x1c2[ft % 2]
            war = ([x2tok[ft - 2]] if ft - 2 in x2tok else [])
            tok_y = mk.dma("s", ybf, scr["ysp"][:, :, ft * 512:(ft + 1) * 512],
                           waits=war, ch=6 + ft % 2)
            tok_x1 = mk.dma("s", x1c.rearrange("p (k m) -> p k m", k=KC),
                            scr["x1own"][:, :, ft * 512:(ft + 1) * 512],
                            waits=war, ch=14 + ft % 2)
            x1k = x1c.rearrange("p (k m) -> p k m", k=KC)
            x2k = x2c.rearrange("p (k m) -> p k m", k=KC)
            for mc in range(8):
                bank = pb[mc % 2]
                for kc in range(4):
                    mk.op("tensor", lambda E, bank=bank, mc=mc, kc=kc, ybf=ybf:
                          E.matmul(bank[:, 0:512], wo_bf[:, kc, mc * 128:mc * 128 + 128],
                                   ybf[:, kc, :], start=(kc == 0), stop=(kc == 3)),
                          waits=(([tok_y, tok_wo] + ([mk.c("vs")] if ft + mc > 0 else []))
                                 if kc == 0 else []),
                          incs=[("ts", 1)])
                t_wo = mk.c("ts")
                sl = slice(ft * 512, (ft + 1) * 512)
                mk.op("vector", lambda E, x1k=x1k, x2k=x2k, mc=mc, sl=sl:
                      E.tensor_mul(x2k[:, mc, :], x1k[:, mc, :], bc_r5o[:, sl]),
                      waits=[tok_x1] + ([tb_r5o, tb_mr5o] if ft == 0 and mc == 0 else [])
                      + ([x2tok[ft - 1]] if mc == 0 and ft - 1 in x2tok else []),
                      incs=[("vs", 1)])
                mk.op("vector", lambda E, x2k=x2k, mc=mc, sl=sl:
                      E.tensor_sub(x2k[:, mc, :], x2k[:, mc, :], bc_mr5o[:, sl]),
                      incs=[("vs", 1)])
                mk.op("vector", lambda E, x2k=x2k, mc=mc, bank=bank:
                      E.tensor_add(x2k[:, mc, :], x2k[:, mc, :], bank[:, 0:512]),
                      waits=[t_wo], incs=[("vs", 1)])
            v_x2 = mk.c("vs")
            x2tok[ft] = mk.dma("g", scr["x2sp"][:, :, ft * 512:(ft + 1) * 512], x2k,
                               waits=[v_x2], ch=11 + ft % 2)
        mk.barrier()

        stf = stats(lambda kc: scr["x2sp"][:, kc, :], T, with_mean=False)
        bc_rr = AR_C[:, 0:2048]
        tb_rr = bcast(bc_rr, 0, T, stf["tok_r"])
        gate8 = MISC[0:8, 128:2176]
        rhsm = AR_B[:, 0:8192].bitcast(BF16).rearrange("p (k m) -> p k m", k=KC)
        x2l = [AR_C[:, 4096:8192], AR_C[:, 8192:12288]]
        gtok = {}
        for ft in range(4):
            x2b = x2l[ft % 2]
            war = ([gtok[ft - 2]] if ft - 2 in gtok else [])
            tok_x2 = mk.dma("s", x2b.rearrange("p (k m) -> p k m", k=KC),
                            scr["x2sp"][:, :, ft * 512:(ft + 1) * 512],
                            waits=war, ch=6 + ft % 2)
            x2bk = x2b.rearrange("p (k m) -> p k m", k=KC)
            for kc in range(KC):
                mk.op("vector", lambda E, x2bk=x2bk, kc=kc:
                      E.tensor_scalar(x2bk[:, kc, :], x2bk[:, kc, :], smcol("w_ln2", kc), None,
                                      ALU.mult),
                      waits=([tok_x2] if kc == 0 else []), incs=[("vs", 1)])
            v_tp = mk.c("vs")
            for kc in range(KC):
                mk.op("tensor", lambda E, x2bk=x2bk, kc=kc, ft=ft:
                      E.matmul(pb[4][0:8, 0:512], wgate_f[:, kc, :], x2bk[:, kc, :],
                               start=(kc == 0), stop=(kc == KC - 1)),
                      waits=([v_tp, tok_wg8, mk.c("vs")] if kc == 0 else []), incs=[("ts", 1)])
            t_g8 = mk.c("ts")
            mk.op("vector", lambda E, ft=ft:
                  E.tensor_copy(gate8[:, ft * 512:(ft + 1) * 512], pb[4][0:8, 0:512]),
                  waits=[t_g8], incs=[("vs", 1)])
            for kc in range(KC):
                mk.op("vector", lambda E, x2bk=x2bk, kc=kc, ft=ft:
                      E.tensor_mul(rhsm[:, kc, ft * 512:(ft + 1) * 512], x2bk[:, kc, :],
                                   bc_rr[:, ft * 512:(ft + 1) * 512]),
                      waits=([tb_rr] if kc == 0 and ft == 0 else []), incs=[("vs", 1)])
            gtok[ft] = mk.c("vs")
        v_rhsm = mk.c("vs")

        sG = AR_A[0:8, 0:2048]
        sT8, sT32 = AR_A[0:8, 2048:4096], AR_A[0:32, 2048:4096]
        sM8, sM32 = AR_A[0:8, 4096:6144], AR_A[0:32, 4096:6144]
        sK8 = AR_A[0:8, 6144:8192]
        sC8, sC32 = AR_A[0:8, 8192:10240], AR_A[0:32, 8192:10240]
        sF1 = AR_C[0:8, 8448:10496]
        sE8 = AR_B[0:8, 8192:10240]
        ebcol = smcol("ebias", rows=8)
        mk.op("vector", lambda E: E.tensor_mul(sG, gate8, bc_rr[0:8, :]), waits=[v_rhsm, mk.c("ts")],
              incs=[("vs", 1)])
        mk.op("vector", lambda E: E.tensor_scalar_add(sG, sG, ebcol), incs=[("vs", 1)])
        mk.op("scalar", lambda E: E.activation(sG, sG, AF.Exp), waits=[mk.c("vs")], incs=[("ss", 1)])
        s_e8 = mk.c("ss")
        mk.wait("vector", [s_e8])
        shufs = {k: [((i + k) % 8) if i < 8 else i for i in range(32)] for k in (1, 2, 4)}

        def fold(op):
            for k in (4, 2, 1):
                mk.op("vector", lambda E, k=k: E.stream_shuffle(sT32, sM32, shufs[k]),
                      incs=[("vs", 1)])
                mk.op("vector", lambda E, op=op: E.tensor_tensor(sM8, sM8, sT8, op),
                      incs=[("vs", 1)])

        def cumsum8():
            for k in (1, 2, 4):
                mask = [((i - k) % 8) if i < 8 else i for i in range(32)]
                mk.op("vector", lambda E, mask=mask: E.stream_shuffle(sT32, sC32, mask),
                      incs=[("vs", 1)])
                mk.op("vector", lambda E, k=k: E.memset(sT8[0:k, :], 0.0), incs=[("vs", 1)])
                mk.op("vector", lambda E: E.tensor_add(sC8, sC8, sT8), incs=[("vs", 1)])
        mk.op("vector", lambda E: E.tensor_copy(sM8, sG), incs=[("vs", 1)])
        fold(ALU.max)
        mk.op("vector", lambda E: E.tensor_tensor(sK8, sG, sM8, ALU.is_equal), incs=[("vs", 1)])
        mk.op("vector", lambda E: E.tensor_copy(sC8, sK8), incs=[("vs", 1)])
        cumsum8()
        mk.op("vector", lambda E: E.tensor_single_scalar(sF1, sC8, 1.0, ALU.is_equal),
              incs=[("vs", 1)])
        mk.op("vector", lambda E: E.tensor_mul(sF1, sF1, sK8), incs=[("vs", 1)])
        mk.op("vector", lambda E: E.tensor_scalar(sE8, sF1, -1.0, 1.0, ALU.mult, ALU.add),
              incs=[("vs", 1)])
        mk.op("vector", lambda E: E.tensor_mul(sE8, sE8, sG), incs=[("vs", 1)])
        mk.op("vector", lambda E: E.tensor_copy(sM8, sE8), incs=[("vs", 1)])
        fold(ALU.max)
        mk.op("vector", lambda E: E.tensor_tensor(sK8, sE8, sM8, ALU.is_equal), incs=[("vs", 1)])
        mk.op("vector", lambda E: E.tensor_copy(sC8, sK8), incs=[("vs", 1)])
        cumsum8()
        mk.op("vector", lambda E: E.tensor_single_scalar(sE8, sC8, 1.0, ALU.is_equal),
              incs=[("vs", 1)])
        mk.op("vector", lambda E: E.tensor_mul(sE8, sE8, sK8), incs=[("vs", 1)])
        mk.op("vector", lambda E: E.tensor_add(sK8, sF1, sE8), incs=[("vs", 1)])
        mk.op("vector", lambda E: E.tensor_mul(sK8, sK8, sG), incs=[("vs", 1)])
        mk.op("vector", lambda E: E.tensor_copy(sM8, sK8), incs=[("vs", 1)])
        fold(ALU.add)
        mk.op("vector", lambda E: E.reciprocal(sM8, sM8), incs=[("vs", 1)])
        mk.op("vector", lambda E: E.tensor_mul(sK8, sK8, sM8), incs=[("vs", 1)])
        v_comb = mk.c("vs")
        tok_comb = mk.dma("s", scr["cmbs"][:], sK8, waits=[v_comb], ch=5)

        wd_flat = dt_in["wd9"].rearrange("e i o -> (e i) o")
        h_all = AR_A[:, 0:9216].bitcast(BF16).rearrange("p (c m) -> p c m", c=36)
        gsb2 = [AR_C[:, 2048:3072].bitcast(BF16).rearrange("p (c m) -> p c m", c=4),
                AR_C[:, 3072:4096].bitcast(BF16).rearrange("p (c m) -> p c m", c=4)]
        gu_bf = AR_C[:, 0:256].bitcast(BF16)
        cmb_b = [AR_C[:, 256:512].bitcast(BF16), AR_C[:, 512:768].bitcast(BF16)]
        wg_b = [WS[:, 0:2048].bitcast(BF16).rearrange("p (k m) -> p k m", k=KC),
                WS[:, 4096:6144].bitcast(BF16).rearrange("p (k m) -> p k m", k=KC)]
        wu_b = [WS[:, 2048:4096].bitcast(BF16).rearrange("p (k m) -> p k m", k=KC),
                WS[:, 6144:8192].bitcast(BF16).rearrange("p (k m) -> p k m", k=KC)]
        wd_b = [WS[:, 8192:8704].bitcast(BF16), WS[:, 8704:9216].bitcast(BF16)]
        bo2c = AR_C[:, 8192:12288]
        x2rl = AR_C[:, 4096:8192]
        bo2tok = {}
        wtok = {}
        wuse = {}
        cb_use = {}
        sg_done = {}
        su_done = {}
        g_banks = (pb[4], pb[6])
        u_banks = (pb[5], pb[7])
        for tq in range(4):
            tsl = slice(tq * 512, (tq + 1) * 512)
            for e in range(9):
                bi = (tq * 9 + e) % 2
                war_w = ([wuse[(tq * 9 + e) - 2]] if (tq * 9 + e) - 2 in wuse else [mk.c("ts")])
                tg = tu = None
                if tq == 0 or True:
                    for kc in range(KC):
                        tg = mk.dma("g", wg_b[bi][:, kc, :],
                                    dt_in["wg9"][e, kc * 128:(kc + 1) * 128, :],
                                    waits=(war_w if kc == 0 else []), ch=7 + bi)
                    for kc in range(KC):
                        tu = mk.dma("g", wu_b[bi][:, kc, :],
                                    dt_in["wu9"][e, kc * 128:(kc + 1) * 128, :], ch=9 + bi)
                tok_cb = None
                if e < 8:
                    war_cb = ([cb_use[bi]] if bi in cb_use else
                              ([tok_comb] if tq == 0 and e == 0 else []))
                    tok_cb = mk.dma("g", cmb_b[bi], scr["cmbs"][e, tsl].partition_broadcast(128),
                                    waits=war_cb, ch=12 + bi)
                ei = tq * 9 + e
                gsb = gsb2[ei % 2]
                for mc in range(4):
                    sg = ei * 4 + mc
                    bank = g_banks[sg % 2]
                    war_bank = ([sg_done[sg - 2]] if sg - 2 in sg_done else [])
                    for kc in range(KC):
                        mk.op("tensor", lambda E, mc=mc, kc=kc, bi=bi, tsl=tsl, bank=bank:
                              E.matmul(bank[:, 0:512], wg_b[bi][:, kc, mc * 128:mc * 128 + 128],
                                       rhsm[:, kc, tsl], start=(kc == 0), stop=(kc == KC - 1)),
                              waits=(([tg] + war_bank
                                      + ([v_rhsm] if ei == 0 and mc == 0 else [])
                                      + ([mk.c("vs")] if e == 0 and mc == 0 else []))
                                     if kc == 0 else []),
                              incs=[("ts", 1)])
                    t_gm = mk.c("ts")
                    war_gsb = ([su_done[(ei - 2) * 4 + mc]] if ei >= 2 else [])
                    mk.op("scalar", lambda E, mc=mc, bank=bank, gsb=gsb:
                          E.activation(gsb[:, mc, :], bank[:, 0:512], AF.Silu),
                          waits=[t_gm] + war_gsb, incs=[("ss", 1)])
                    sg_done[sg] = mk.c("ss")
                for mc in range(4):
                    su = ei * 4 + mc
                    bank = u_banks[su % 2]
                    war_bank = ([su_done[su - 2]] if su - 2 in su_done else [])
                    for kc in range(KC):
                        mk.op("tensor", lambda E, mc=mc, kc=kc, bi=bi, tsl=tsl, bank=bank:
                              E.matmul(bank[:, 0:512], wu_b[bi][:, kc, mc * 128:mc * 128 + 128],
                                       rhsm[:, kc, tsl], start=(kc == 0), stop=(kc == KC - 1)),
                              waits=(([tu] + war_bank
                                      + ([mk.c("vs")] if e == 0 and mc == 0 else []))
                                     if kc == 0 else []),
                              incs=[("ts", 1)])
                    t_um = mk.c("ts")
                    mk.op("vector", lambda E, mc=mc, bank=bank, gsb=gsb:
                          E.scalar_tensor_tensor(gu_bf, bank[:, 0:512], 1.0, gsb[:, mc, :],
                                                 ALU.mult, ALU.mult),
                          waits=[t_um, sg_done[ei * 4 + mc]], incs=[("vs", 1)])
                    su_done[su] = mk.c("vs")
                    ch = 4 * e + mc
                    if e < 8:
                        mk.op("vector", lambda E, ch=ch, bi=bi:
                              E.tensor_mul(h_all[:, ch, :], gu_bf, cmb_b[bi]),
                              waits=([tok_cb] if mc == 0 else []), incs=[("vs", 1)])
                    else:
                        mk.op("vector", lambda E, ch=ch:
                              E.tensor_copy(h_all[:, ch, :], gu_bf), incs=[("vs", 1)])
                if e < 8:
                    cb_use[bi] = mk.c("vs")
                wuse[tq * 9 + e] = mk.c("ts")
            v_hall = mk.c("vs")
            wdtok = {}
            for kc in range(36):
                wdb = wd_b[kc % 2]
                war_d = ([wdtok[kc - 2]] if kc - 2 in wdtok else [mk.c("ts")])
                td = mk.dma("g", wdb, wd_flat[kc * 128:(kc + 1) * 128, :],
                            waits=war_d, ch=14 + kc % 2)
                for mc in range(8):
                    mk.op("tensor", lambda E, kc=kc, mc=mc, wdb=wdb:
                          E.matmul(pb[mc][:, 0:512], wdb[:, mc * 128:mc * 128 + 128],
                                   h_all[:, kc, :], start=(kc == 0), stop=(kc == 35)),
                          waits=(([td] + ([v_hall, mk.c("ss")] if kc == 0 else []))
                                 if mc == 0 else []),
                          incs=[("ts", 1)])
                wdtok[kc] = mk.c("ts")
            t_rt = mk.c("ts")
            war_x2 = ([bo2tok[tq - 1]] if tq - 1 in bo2tok else [])
            tok_x2r = mk.dma("s", x2rl.rearrange("p (k m) -> p k m", k=KC),
                             scr["x2sp"][:, :, tsl], waits=war_x2, ch=6)
            bo2k = bo2c.rearrange("p (k m) -> p k m", k=KC)
            x2rk = x2rl.rearrange("p (k m) -> p k m", k=KC)
            for mc in range(8):
                mk.op("vector", lambda E, mc=mc, bo2k=bo2k, x2rk=x2rk:
                      E.tensor_add(bo2k[:, mc, :], pb[mc][:, 0:512], x2rk[:, mc, :]),
                      waits=[t_rt] + ([tok_x2r] if mc == 0 else []), incs=[("vs", 1)])
            v_bo2 = mk.c("vs")
            bo2tok[tq] = mk.dma("g", scr["bo2sp"][:, :, tsl], bo2k, waits=[v_bo2], ch=11)
        mk.barrier()

        stgf = stats(lambda kc: scr["bo2sp"][:, kc, :], T)
        bc_r7 = AR_B[:, 0:T]
        bc_mr7 = AR_B[:, 2080:2080 + T]
        tb_r7 = bcast(bc_r7, 0, T, stgf["tok_r"])
        tb_mr7 = bcast(bc_mr7, 2, T, stgf["tok_mr"])
        boc2 = [AR_C[:, 0:4096], AR_C[:, 4096:8192]]
        x1f = AR_C[:, 8192:12288]
        ftok = {}
        for ft in range(4):
            bc2 = boc2[ft % 2]
            war = ([ftok[ft - 2]] if ft - 2 in ftok else [])
            warx = ([ftok[ft - 1]] if ft - 1 in ftok else [])
            tok_b = mk.dma("s", bc2.rearrange("p (k m) -> p k m", k=KC),
                           scr["bo2sp"][:, :, ft * 512:(ft + 1) * 512],
                           waits=war, ch=6 + ft % 2)
            tok_x1f = mk.dma("s", x1f.rearrange("p (k m) -> p k m", k=KC),
                             scr["x1own"][:, :, ft * 512:(ft + 1) * 512],
                             waits=warx, ch=14)
            b2k = bc2.rearrange("p (k m) -> p k m", k=KC)
            x1fk = x1f.rearrange("p (k m) -> p k m", k=KC)
            sl = slice(ft * 512, (ft + 1) * 512)
            for kc in range(KC):
                mk.op("vector", lambda E, b2k=b2k, kc=kc, sl=sl:
                      E.tensor_mul(b2k[:, kc, :], b2k[:, kc, :], bc_r7[:, sl]),
                      waits=([tok_b, tb_r7] if kc == 0 else []), incs=[("vs", 1)])
                mk.op("vector", lambda E, b2k=b2k, kc=kc, sl=sl:
                      E.tensor_sub(b2k[:, kc, :], b2k[:, kc, :], bc_mr7[:, sl]),
                      waits=([tb_mr7] if kc == 0 else []), incs=[("vs", 1)])
                mk.op("vector", lambda E, b2k=b2k, x1fk=x1fk, kc=kc:
                      E.tensor_add(b2k[:, kc, :], b2k[:, kc, :], x1fk[:, kc, :]),
                      waits=([tok_x1f] if kc == 0 else []), incs=[("vs", 1)])
            v_f = mk.c("vs")
            ftok[ft] = mk.dma("s", outp[:, :, ft * 512:(ft + 1) * 512], b2k,
                              waits=[v_f], ch=0)

    if devtap == "dbg_misc":
        mk.barrier()
        mk.dma("g", scr["x2sp"][:, 0, :], MISC[:, 0:2048], ch=6)
        mk.dma("g", scr["x2sp"][:, 1, :], AR_B[:, 0:2048], ch=6)
        devtap = "x2sp"
    if devtap in ("dbg_sp", "dbg_z", "dbg_scan", "dbg_ysd"):
        mk.barrier()
        if devtap == "dbg_sp":
            for kc in range(KC):
                mk.dma("g", scr["x2sp"][:, kc, :], sp_bf[:, kc, :], ch=6)
        elif devtap == "dbg_z":
            for kc in range(KC):
                mk.dma("g", scr["x2sp"][:, kc, :], z_f16[:, kc, :], ch=6)
        elif devtap == "dbg_scan":
            mk.dma("g", scr["x2sp"][:, 0, 0:512], u64q, ch=6)
            mk.dma("g", scr["x2sp"][:, 1, 0:512], hsq, ch=6)
            mk.dma("g", scr["x2sp"][:, 2, 0:512], dk_rep, ch=6)
            mk.dma("g", scr["x2sp"][:, 3, 0:512], p512, ch=6)
            mk.dma("g", scr["x2sp"][:, 4, 0:1024], hsr_bf, ch=6)
        elif devtap == "dbg_ysd":
            for kc in range(KC):
                mk.dma("g", scr["x2sp"][:, kc, :], ysd_bf[:, kc, :], ch=6)
        devtap = "x2sp"

    mk.barrier()
    if devtap == "dbg_att" or (isinstance(devtap, str) and devtap.startswith("dbg_attk")):
        devtap = None
    if devtap or stage < 3:
        src = scr[devtap] if devtap else (scr["xmid"] if stage == 1 else scr["x1own"])
        for kc in range(KC):
            mk.dma("s", outp[:, kc, :], src[:, kc, :], ch=0)
    mk.wait("sync", [mk.c("ds0")])
    mk.wait("gpsimd", [mk.c("ds0")])
    return mk.build()


_PROG_CACHE = {}


def get_program(stage=STAGE, devtap=None):
    key = (stage, devtap)
    if key not in _PROG_CACHE:
        _PROG_CACHE[key] = build_program(stage, devtap)
    return _PROG_CACHE[key]


def kernel(**inputs):
    in_maps = prep_inputs(inputs)
    nc = get_program()
    res = run_bass_kernel_spmd(nc, in_maps, core_ids=list(range(NCORES)))
    out = np.zeros((B, S, D), np.float32)
    for c in range(NCORES):
        b0, par = c // 2, c & 1
        o = np.asarray(res.results[c]["outp"], np.float32)
        o = o.transpose(1, 0, 2).reshape(D, T)
        for j in range(16):
            g = 2 * j + par
            out[b0, g * 128:(g + 1) * 128, :] = o[:, j * 128:(j + 1) * 128].T
    return out


# version 05
# speedup vs baseline: 1.2533x; 1.0851x over previous
import os

import numpy as np
import ml_dtypes

import concourse.bass as bass
import concourse.mybir as mybir
from concourse.bass_utils import run_bass_kernel_spmd

F32 = mybir.dt.float32
BF16 = mybir.dt.bfloat16
F16 = mybir.dt.float16
AF = mybir.ActivationFunctionType
ALU = mybir.AluOpType
AXX = mybir.AxisListType.X
BNP = ml_dtypes.bfloat16

B, S, D = 4, 4096, 1024
T = 2048
NCORES = 8
KC = 8
EPS = 1e-5
STAGE = 3

SMCOL = {}
_nxt = [0]
for _n, _c in [("ln1_w", 8), ("ln1_b", 8), ("lns_w", 8), ("lns_b", 8),
               ("conv_b", 8), ("cw0", 8), ("cw1", 8), ("cw2", 8), ("cw3", 8),
               ("in_b", 16), ("out_b", 8), ("ln2_w", 8), ("ln2_b", 8),
               ("mlp_b1", 16), ("mlp_b2", 8), ("w_ln1", 8), ("w_ln2", 8),
               ("xp_b", 1), ("A_log", 1), ("ebias", 1)]:
    SMCOL[_n] = (_nxt[0], _c)
    _nxt[0] += _c
NSMALL = _nxt[0]


def _pack_small(dst, name, vec):
    c0, ncol = SMCOL[name]
    v = np.asarray(vec, np.float32).reshape(-1)
    if ncol == 1:
        dst[:len(v), c0] = v
    else:
        dst[:, c0:c0 + ncol] = v.reshape(ncol, 128).T


def prep_inputs(inp):
    x = np.asarray(inp["L0_attn_x"], np.float32)
    w = {k[8:]: np.asarray(v, np.float32) for k, v in inp.items() if k.startswith("L0_attn_")}
    wf = {k[7:]: np.asarray(v, np.float32) for k, v in inp.items() if k.startswith("L0_ffn_")}

    smalls = np.zeros((128, NSMALL), np.float32)
    for nm in ["ln1_w", "ln1_b", "lns_w", "lns_b", "conv_b", "in_b", "out_b",
               "ln2_w", "ln2_b", "mlp_b1", "mlp_b2", "xp_b", "A_log"]:
        _pack_small(smalls, nm, w[nm])
    for k in range(4):
        _pack_small(smalls, f"cw{k}", w["conv_w"][:, 0, k])
    _pack_small(smalls, "w_ln1", wf["w_ln1"])
    _pack_small(smalls, "w_ln2", wf["w_ln2"])
    _pack_small(smalls, "ebias", wf["expert_bias"])

    tpos = np.arange(S, dtype=np.float32)
    freqs = 1.0 / (100000.0 ** (np.arange(0, 32, 2, dtype=np.float32) / 32))
    ang = np.outer(freqs, tpos)
    cos, sin = np.cos(ang), np.sin(ang)
    csA = np.zeros((32, S), np.float32)
    csB = np.zeros((32, S), np.float32)
    csA[0::2] = cos
    csA[1::2] = cos
    csB[0::2] = -sin
    csB[1::2] = sin

    wg9 = np.ascontiguousarray(np.concatenate([wf["Wg"], wf["sWg"]], 0))
    wu9 = np.ascontiguousarray(np.concatenate([wf["Wu"], wf["sWu"]], 0))
    wd9 = np.ascontiguousarray(np.concatenate([wf["Wd"], wf["sWd"]], 0))
    rows_eff = np.concatenate([np.arange(h * 64, h * 64 + 32) for h in range(16)])
    wo_eff = np.ascontiguousarray(wf["Wo"][rows_eff])

    triu = np.triu(np.ones((128, 128), np.float32))
    zeros = np.zeros((128, 128), np.float32)
    onesm = np.ones((128, 128), np.float32)

    in_maps = []
    for c in range(NCORES):
        b0, par = c // 2, c & 1
        s0, lo = par * T, c * 512
        pad3 = np.zeros((3, D), np.float32)
        xo = np.concatenate([x[b0, s0 - 3:s0] if s0 >= 3 else pad3,
                             x[b0, s0:s0 + T]], 0).T
        x_own = np.zeros((D, 2060), np.float32)
        x_own[:, :2051] = xo
        if lo >= 3:
            xs = x[:, lo - 3:lo + 512]
        else:
            xs = np.concatenate([np.zeros((B, 3, D), np.float32), x[:, 0:512]], 1)
        x_scan = np.ascontiguousarray(xs.transpose(2, 1, 0)).reshape(D, 2060)

        cpw = np.broadcast_to(
            np.array([(c - 1 - j) if j < c else 1e6 for j in range(8)], np.float32),
            (128, 8)).copy()
        bmask = np.ones((128, 16), np.float32)
        if lo == 0:
            bmask[:, 0:12] = 0.0
        if s0 == 0:
            bmask[:, 12:15] = 0.0
        qm = np.empty((128, 32 * 128), np.float32)
        for kb in range(32):
            if par == 0:
                blk = triu if (kb % 2 == 0) else zeros
            else:
                blk = onesm if (kb % 2 == 0) else triu
            qm[:, kb * 128:(kb + 1) * 128] = blk
        own_cols = np.concatenate(
            [np.arange((2 * j + par) * 128, (2 * j + par + 1) * 128) for j in range(16)])

        parsel = np.zeros((128, 2), np.float32)
        parsel[:, par] = 1.0
        in_maps.append(dict(
            parsel=parsel,
            x_own=x_own, x_scan=x_scan,
            in_w=w["in_w"], xp_w=np.ascontiguousarray(w["xp_w"]),
            dmat=np.ascontiguousarray(w["D"]), out_w=w["out_w"],
            mlp_w1=w["mlp_w1"], mlp_w2=w["mlp_w2"],
            wq=wf["Wq"], wkva=np.ascontiguousarray(wf["Wkva"]),
            wkvb=np.ascontiguousarray(wf["Wkvb"]), wo_eff=wo_eff,
            wgate=np.ascontiguousarray(wf["Wgate"]),
            wg9=wg9, wu9=wu9, wd9=wd9,
            smalls=smalls, cpw=cpw, bmask=bmask,
            qmask=qm.astype(BNP),
            csq_a=np.ascontiguousarray(csA[:, own_cols]).astype(BNP),
            csq_b=np.ascontiguousarray(csB[:, own_cols]).astype(BNP),
            csk_a=csA.astype(BNP), csk_b=csB.astype(BNP),
        ))
    return in_maps


class MK:
    ENGS = ("tensor", "vector", "scalar", "gpsimd", "sync")

    def __init__(self):
        self.nc = bass.Bass("TRN2", target_bir_lowering=False,
                            detect_race_conditions=bool(os.environ.get("MKRACE")))
        self.ops = {e: [] for e in self.ENGS}
        self.cnt = {}
        self.sems = {}
        self._semname = []

    def newsem(self, name):
        self.cnt[name] = 0
        self._semname.append(name)

    def c(self, name):
        return (name, self.cnt.get(name, 0))

    ESEM = {"tensor": "ts", "vector": "vs", "scalar": "ss"}

    def op(self, eng, fn, waits=(), incs=()):
        waits = list(waits)
        es = self.ESEM.get(eng)
        if es is not None and self.cnt.get(es, 0) > 0 and fn.__name__ != "<nowait>":
            waits.append((es, self.cnt[es]))
        for s, k in incs:
            self.cnt[s] += k
        waits = tuple((s, v) for (s, v) in waits if v > 0)
        incs = tuple(incs)

        def run(E, fn=fn, waits=waits, incs=incs):
            for s, v in waits:
                E.wait_ge(self.sems[s], v)
            inst = fn(E)
            for s, k in incs:
                inst.then_inc(self.sems[s], k)
        self.ops[eng].append(run)
        return [(s, self.cnt[s]) for s, _ in incs]

    def dma(self, q, out, in_, waits=(), ch=0, **kw):
        eng = {"g": "gpsimd", "s": "sync"}[q]
        sem = f"d{q}{ch}"
        if sem not in self.cnt:
            self.newsem(sem)
        return self.op(eng, lambda E, o=out, i=in_, kw=kw: E.dma_start(out=o, in_=i, **kw),
                       waits=waits, incs=[(sem, 16)])[0]

    def wait(self, eng, waits):
        self.op(eng, lambda E: None, waits=waits)

    def barrier(self, engines=None):
        engines = engines or self.ENGS
        dmas = [(s, self.cnt[s]) for s in self.cnt if s.startswith("d") and self.cnt[s] > 0]
        bs = f"bar{len(self._semname)}"
        self.newsem(bs)
        n = len(engines)
        for e in engines:
            self.op(e, lambda E: E.nop(), waits=dmas, incs=[(bs, 1)])
        for e in engines:
            self.wait(e, [(bs, n)])

    def build(self):
        nc = self.nc
        import contextlib
        stack = contextlib.ExitStack()
        for name in self._semname:
            self.sems[name] = stack.enter_context(nc.semaphore(name))
        blk = stack.enter_context(nc.Block())
        for eng in self.ENGS:
            ops = self.ops[eng]
            getattr(blk, eng)(lambda E, ops=ops: [f(E) for f in ops])
        stack.close()
        return nc


def build_program(stage=STAGE, devtap=None):
    mk = MK()
    nc = mk.nc
    for s in ("ts", "vs", "ss", "cc"):
        mk.newsem(s)

    dt_in = {}
    for name, shape, dt in [
        ("x_own", [D, 2060], F32), ("x_scan", [D, 2060], F32),
        ("in_w", [D, 2048], F32), ("xp_w", [D, 16], F32), ("dmat", [16, D], F32),
        ("out_w", [D, D], F32), ("mlp_w1", [D, 2048], F32), ("mlp_w2", [2048, D], F32),
        ("wq", [D, D], F32), ("wkva", [D, 160], F32), ("wkvb", [128, 512], F32),
        ("wo_eff", [512, D], F32), ("wgate", [D, 8], F32),
        ("wg9", [9, D, 512], F32), ("wu9", [9, D, 512], F32), ("wd9", [9, 512, D], F32),
        ("smalls", [128, NSMALL], F32),
        ("cpw", [128, 8], F32), ("bmask", [128, 16], F32),
        ("parsel", [128, 2], F32),
        ("qmask", [128, 32 * 128], BF16),
        ("csq_a", [32, T], BF16), ("csq_b", [32, T], BF16),
        ("csk_a", [32, S], BF16), ("csk_b", [32, S], BF16),
    ]:
        dt_in[name] = nc.dram_tensor(name, shape, dt, kind="ExternalInput").ap()

    outp = nc.dram_tensor("outp", [128, KC, T], F32, kind="ExternalOutput").ap()

    scr = {}
    for name, shape, dt in [
        ("rows_scr", [8, 4160], F32),
        ("esnd", [128], F32), ("ercv", [8, 128], F32),
        ("xsnd", [128, 16, T], BF16), ("xrcv", [2, 128, 16, T], BF16),
        ("xbsp", [128, KC, T], F32), ("xbscan", [128, KC, T], F32),
        ("xmid", [128, KC, T], F32), ("x1own", [128, KC, T], F32),
        ("x2sp", [128, KC, T], F32), ("bo2sp", [128, KC, T], F32),
        ("cmbs", [8, T], F32),
        ("ysp", [128, 4, T], BF16),
    ]:
        scr[name] = nc.dram_tensor(name, shape, dt).ap()

    AR_A = nc.alloc_sbuf_tensor("AR_A", [128, 10304], F32).ap()
    AR_B = nc.alloc_sbuf_tensor("AR_B", [128, 10240], F32).ap()
    AR_C = nc.alloc_sbuf_tensor("AR_C", [128, 12288], F32).ap()
    WS = nc.alloc_sbuf_tensor("WS", [128, 9216], F32).ap()
    ROWS = nc.alloc_sbuf_tensor("ROWS", [128, 6180], F32).ap()
    SM = nc.alloc_sbuf_tensor("SM", [128, NSMALL], F32).ap()
    CPW = nc.alloc_sbuf_tensor("CPW", [128, 8], F32).ap()
    BMASK = nc.alloc_sbuf_tensor("BMASK", [128, 16], F32).ap()
    PARSEL = nc.alloc_sbuf_tensor("PARSEL", [128, 2], F32).ap()
    MISC = nc.alloc_sbuf_tensor("MISC", [128, 2176], F32).ap()
    SC = nc.alloc_sbuf_tensor("SC", [128, 2048], F32).ap()

    pb = [nc.alloc_psum_tensor(f"pb{i}", [128, 512], F32).ap() for i in range(8)]

    def smcol(name, k=0, rows=128):
        c0, _ = SMCOL[name]
        return SM[0:rows, c0 + k:c0 + k + 1]

    ones_bf = MISC[:, 0:4].bitcast(BF16)[:, 0:1]
    gcol_bf = MISC[:, 4:8].bitcast(BF16)
    bcol_bf = MISC[:, 8:12].bitcast(BF16)
    hw_col = MISC[:, 16:32]
    xpw_bf = MISC[:, 32:96].bitcast(BF16)
    ld64 = MISC[:, 96:97]
    ld512 = MISC[:, 97:98]
    w8t = MISC[:, 98:106]
    carry = MISC[:, 106:107]
    ecol = MISC[:, 107:108]
    ca8 = MISC[:, 108:116]
    xpbq = MISC[:, 116:117]
    eps_col = MISC[:, 117:118]
    eps2_col = MISC[:, 118:119]
    mrow_bf = MISC[0:1, 128:1152].bitcast(BF16)
    gw_bf = MISC[0:1, 1152:2176].bitcast(BF16)
    hsr_bf = MISC[32:48, 128:1152].bitcast(BF16)
    dmat_bf = MISC[32:48, 1152:1664].bitcast(BF16)

    u64q = SC[:, 0:512]
    hsq = SC[:, 512:1024]
    dk_rep = SC[:, 1024:1536]
    p512 = SC[:, 1536:2048]

    def rslot(i, n=2060):
        return ROWS[0:1, i * 2060:i * 2060 + n]

    mk.dma("s", SM[:], dt_in["smalls"], ch=0)
    mk.dma("s", CPW[:], dt_in["cpw"], ch=0)
    mk.dma("s", PARSEL[:], dt_in["parsel"], ch=0)
    t_fix = mk.dma("s", BMASK[:], dt_in["bmask"], ch=0)
    mk.op("vector", lambda E: E.memset(MISC[:, 96:128], 0.0), incs=[("vs", 1)])
    mk.op("vector", lambda E: E.memset(ones_bf, 1.0), incs=[("vs", 1)])
    mk.op("vector", lambda E: E.memset(eps_col, EPS), incs=[("vs", 1)])
    mk.op("vector", lambda E: E.memset(eps2_col, EPS * EPS), incs=[("vs", 1)])
    v_ones = mk.c("vs")

    def stats(src_dram_fn, ncols, with_mean=True, fused2=False, extra_wait=(),
              ship_row0=0, ship_off=0, tbuf=None, tbase=0, psum_wait=None):
        s1, s2, t2 = rslot(0, ncols), rslot(1, ncols), rslot(2, ncols)
        ta = AR_A if tbuf is None else tbuf
        xbf = [ta[:, tbase + 0:tbase + 1030].bitcast(BF16),
               ta[:, tbase + 1030:tbase + 2060].bitcast(BF16)]
        sqb2 = [ta[:, tbase + 2060:tbase + 3090].bitcast(BF16),
                ta[:, tbase + 3090:tbase + 4120].bitcast(BF16)]
        if psum_wait is None:
            psum_wait = [mk.c("vs")]
        if hasattr(mk, "_stats_prev"):
            ent_ts, ent_ss, ent_vs = mk._stats_prev
        else:
            ent_ts, ent_ss, ent_vs = mk.c("ts"), mk.c("ss"), mk.c("vs")
        ranges = [(0, min(ncols, 2048))]
        if ncols > 2048:
            ranges.append((2048, ncols - 2048))
        read_done = {}
        sq_done = {}
        for (c0, cn) in ranges:
            FCH = [(f0, min(512, cn - f0)) for f0 in range(0, cn, 512)]
            for kc in range(KC):
                bb = xbf[kc % 2][:, 0:cn]
                war = ([read_done[kc - 2]] if kc - 2 in read_done else [])
                tok = mk.dma("g", bb, src_dram_fn(kc)[:, c0:c0 + cn], ch=kc % 2,
                             waits=war
                             + ([ent_ts, ent_ss, ent_vs] if (kc <= 1 and c0 == 0) else [])
                             + (list(extra_wait) if (kc <= 1 and c0 == 0) else []))
                sqb = sqb2[kc % 2]
                if with_mean:
                    for fi, (f0, fw) in enumerate(FCH):
                        mk.op("tensor", lambda E, bb=bb, f0=f0, fw=fw, fi=fi, kc=kc:
                              E.matmul(pb[4 + fi][0:1, 0:fw], ones_bf, bb[:, f0:f0 + fw],
                                       start=(kc == 0), stop=(kc == KC - 1)),
                              waits=([tok, v_ones, ent_vs] + psum_wait if kc == 0 else [tok]),
                              incs=[("ts", 1)])
                war_s = ([sq_done[kc - 2]] if kc - 2 in sq_done else [])
                mk.op("scalar", lambda E, bb=bb, cn=cn, sqb=sqb:
                      E.activation(sqb[:, 0:cn], bb, AF.Square),
                      waits=[tok] + war_s, incs=[("ss", 1)])
                s_sq = mk.c("ss")
                for fi, (f0, fw) in enumerate(FCH):
                    mk.op("tensor", lambda E, f0=f0, fw=fw, fi=fi, kc=kc, sqb=sqb:
                          E.matmul(pb[fi][0:1, 0:fw], ones_bf, sqb[:, f0:f0 + fw],
                                   start=(kc == 0), stop=(kc == KC - 1)),
                          waits=([s_sq, v_ones, ent_vs] + psum_wait if kc == 0 else [s_sq]),
                          incs=[("ts", 1)])
                sq_done[kc] = mk.c("ts")
                read_done[kc] = mk.c("ts")
            read_done, sq_done = {}, {}
            t_stats = mk.c("ts")
            prev_ships = getattr(mk, "_stats_ships", [])
            first_evac = True
            for fi, (f0, fw) in enumerate(FCH):
                if with_mean:
                    mk.op("vector", lambda E, f0=f0, fw=fw, fi=fi, c0=c0:
                          E.tensor_copy(s1[:, c0 + f0:c0 + f0 + fw], pb[4 + fi][0:1, 0:fw]),
                          waits=[t_stats] + (list(prev_ships) if first_evac else []),
                          incs=[("vs", 1)])
                    first_evac = False
                mk.op("vector", lambda E, f0=f0, fw=fw, fi=fi, c0=c0:
                      E.tensor_copy(s2[:, c0 + f0:c0 + f0 + fw], pb[fi][0:1, 0:fw]),
                      waits=[t_stats] + (list(prev_ships) if first_evac else []),
                      incs=[("vs", 1)])
                first_evac = False
            if len(ranges) > 1 and c0 == 0:
                mk.wait("gpsimd", [mk.c("vs")])
        c1 = 1.0 / 1024.0
        out = {}
        if with_mean:
            mk.op("vector", lambda E: E.tensor_scalar_mul(s1, s1, c1), incs=[("vs", 1)])
            mk.op("vector", lambda E: E.tensor_mul(t2, s1, s1), incs=[("vs", 1)])
            mk.op("vector", lambda E: E.scalar_tensor_tensor(s2, s2, c1, t2, ALU.mult, ALU.subtract),
                  incs=[("vs", 1)])
        else:
            mk.op("vector", lambda E: E.tensor_scalar_mul(s2, s2, c1), incs=[("vs", 1)])
        v_v = mk.c("vs")
        if fused2:
            mk.op("scalar", lambda E: E.activation(t2, s2, AF.Ln, bias=eps2_col[0:1, :],
                                                   scale=1.0 + EPS),
                  waits=[v_v], incs=[("ss", 1)])
            mk.op("scalar", lambda E: E.activation(t2, t2, AF.Exp, scale=-0.5),
                  incs=[("ss", 1)])
        mk.op("scalar", lambda E: E.activation(s2, s2, AF.Ln, bias=eps_col[0:1, :]),
              waits=[v_v], incs=[("ss", 1)])
        mk.op("scalar", lambda E: E.activation(s2, s2, AF.Exp, scale=-0.5),
              incs=[("ss", 1)])
        s_r = mk.c("ss")
        out["tok_r"] = mk.dma("s", scr["rows_scr"][ship_row0 + 0, ship_off:ship_off + ncols], s2, waits=[s_r], ch=1)
        if fused2:
            out["tok_g"] = mk.dma("s", scr["rows_scr"][ship_row0 + 1, ship_off:ship_off + ncols], t2, waits=[s_r], ch=2)
        if with_mean:
            mk.op("vector", lambda E: E.tensor_mul(s2, s1, s2), waits=[out["tok_r"], s_r],
                  incs=[("vs", 1)])
            v_mr = mk.c("vs")
            out["tok_mr"] = mk.dma("s", scr["rows_scr"][ship_row0 + 2, ship_off:ship_off + ncols], s2, waits=[v_mr], ch=3)
            if fused2:
                mk.op("vector", lambda E: E.tensor_mul(t2, s1, t2),
                      waits=[out["tok_g"]], incs=[("vs", 1)])
                v_mg = mk.c("vs")
                out["tok_mg"] = mk.dma("s", scr["rows_scr"][ship_row0 + 3, ship_off:ship_off + ncols], t2, waits=[v_mg], ch=4)
        mk._stats_prev = (mk.c("ts"), mk.c("ss"), mk.c("vs"))
        mk._stats_ships = list(out.values())
        return out

    def bcast(dst, slot, ncols, tok, q="g", off=0):
        return mk.dma(q, dst, scr["rows_scr"][slot, off:off + ncols].partition_broadcast(128),
                      waits=[tok], ch=2 + slot % 4)

    def load_mrow(tok_mr, ncols=T, slot=2):
        mrl = rslot(2, ncols)
        tok = mk.dma("s", mrl, scr["rows_scr"][slot:slot + 1, 0:ncols], waits=[tok_mr], ch=5)
        mk.op("vector", lambda E: E.tensor_scalar_mul(mrow_bf[:, 0:ncols], mrl, -1.0),
              waits=[tok], incs=[("vs", 1)])
        mk._stats_ships = getattr(mk, "_stats_ships", []) + [tok]
        return mk.c("vs")

    def gw_hw(w_bf, nout, gamma_name, beta_name, bias_name, w_tok):
        c0 = SMCOL[gamma_name][0]
        mk.op("vector", lambda E, c0=c0: E.tensor_copy(gcol_bf, SM[:, c0:c0 + 8]),
              waits=[t_fix], incs=[("vs", 1)])
        c0b = SMCOL[beta_name][0]
        mk.op("vector", lambda E, c0b=c0b: E.tensor_copy(bcol_bf, SM[:, c0b:c0b + 8]),
              incs=[("vs", 1)])
        v_cols = mk.c("vs")
        for f0 in range(0, nout, 512):
            for kc in range(KC):
                mk.op("tensor", lambda E, f0=f0, kc=kc:
                      E.matmul(pb[5][0:1, 0:512], gcol_bf[:, kc:kc + 1],
                               w_bf[:, kc, f0:f0 + 512], start=(kc == 0), stop=(kc == KC - 1)),
                      waits=([v_cols, w_tok, mk.c("vs")] if kc == 0 else []), incs=[("ts", 1)])
            t_gw = mk.c("ts")
            mk.op("vector", lambda E, f0=f0: E.tensor_copy(gw_bf[:, f0:f0 + 512], pb[5][0:1, 0:512]),
                  waits=[t_gw], incs=[("vs", 1)])
        v_gw = mk.c("vs")
        for mc in range(nout // 128):
            for kc in range(KC):
                mk.op("tensor", lambda E, mc=mc, kc=kc:
                      E.matmul(pb[4][:, 0:1], w_bf[:, kc, mc * 128:mc * 128 + 128],
                               bcol_bf[:, kc:kc + 1], start=(kc == 0), stop=(kc == KC - 1)),
                      waits=([v_cols, w_tok, mk.c("vs")] if kc == 0 else []),
                      incs=[("ts", 1)])
            t_hw = mk.c("ts")
            mk.op("vector", lambda E, mc=mc:
                  E.tensor_add(hw_col[:, mc:mc + 1], pb[4][:, 0:1], smcol(bias_name, mc)),
                  waits=[t_hw], incs=[("vs", 1)])
        return v_gw, mk.c("vs")

    def m_chain(which):
        xsrc = dt_in["x_own" if which == "own" else "x_scan"]
        spill = scr["xbsp" if which == "own" else "xbscan"]
        nb = 3 if which == "own" else 12
        shift = 1 if which == "own" else 4
        ncols = T + nb
        bm = BMASK[:, 12:12 + nb] if which == "own" else BMASK[:, 0:nb]

        st = stats(lambda kc: xsrc[kc * 128:(kc + 1) * 128, :], ncols,
                   with_mean=True, fused2=True)
        bc_r = AR_B[:, 0:ncols]
        bc_g = AR_B[:, 2080:2080 + ncols]
        bc_mr = AR_B[:, 4160:4160 + ncols]
        bc_mg = AR_B[:, 6240:6240 + ncols]
        tb_r = bcast(bc_r, 0, ncols, st["tok_r"])
        tb_g = bcast(bc_g, 1, ncols, st["tok_g"])
        tb_mr = bcast(bc_mr, 2, ncols, st["tok_mr"])
        tb_mg = bcast(bc_mg, 3, ncols, st["tok_mg"])

        spill_toks, chain_toks = {}, {}
        spill0 = mk.c("dg6")
        for kc in range(KC):
            xs_c = AR_A[:, (kc % 2) * 2064:(kc % 2) * 2064 + ncols]
            xn_c = AR_A[:, 4128 + (kc % 2) * 2064:4128 + (kc % 2) * 2064 + ncols]
            xb_c = AR_A[:, 8256:8256 + T]
            war = ([chain_toks[kc - 2]] if kc - 2 in chain_toks else [tb_mg])
            tok_x = mk.dma("s", xs_c, xsrc[kc * 128:(kc + 1) * 128, 0:ncols], waits=war, ch=6 + kc % 2)
            mk.op("vector", lambda E, xn_c=xn_c, xs_c=xs_c: E.tensor_mul(xn_c, xs_c, bc_g),
                  waits=[tok_x] + ([tb_g] if kc == 0 else []), incs=[("vs", 1)])
            mk.op("vector", lambda E, xn_c=xn_c: E.tensor_sub(xn_c, xn_c, bc_mg),
                  incs=[("vs", 1)])
            mk.op("vector", lambda E, xn_c=xn_c, kc=kc:
                  E.tensor_scalar(xn_c, xn_c, smcol("ln1_w", kc), smcol("ln1_b", kc),
                                  ALU.mult, ALU.add),
                  waits=([t_fix] if kc == 0 else []), incs=[("vs", 1)])
            mk.op("vector", lambda E, xn_c=xn_c, bm=bm:
                  E.tensor_mul(xn_c[:, 0:nb], xn_c[:, 0:nb], bm), incs=[("vs", 1)])
            warb = ([spill_toks[kc - 1]] if kc - 1 in spill_toks else [spill0])
            mk.op("vector", lambda E, xb_c=xb_c, xs_c=xs_c, nb=nb:
                  E.tensor_mul(xb_c, xs_c[:, nb:nb + T], bc_r[:, nb:nb + T]),
                  waits=([tb_r] if kc == 0 else []) + warb, incs=[("vs", 1)])
            mk.op("vector", lambda E, xb_c=xb_c, nb=nb:
                  E.tensor_sub(xb_c, xb_c, bc_mr[:, nb:nb + T]),
                  waits=([tb_mr] if kc == 0 else []), incs=[("vs", 1)])
            for k in range(4):
                mk.op("vector", lambda E, xb_c=xb_c, xn_c=xn_c, k=k, kc=kc, shift=shift:
                      E.scalar_tensor_tensor(xb_c, xn_c[:, k * shift:k * shift + T],
                                             smcol(f"cw{k}", kc), xb_c, ALU.mult, ALU.add),
                      incs=[("vs", 1)])
            mk.op("vector", lambda E, xb_c=xb_c, kc=kc:
                  E.tensor_scalar_add(xb_c, xb_c, smcol("conv_b", kc)), incs=[("vs", 1)])
            v_xb = mk.c("vs")
            chain_toks[kc] = v_xb
            spill_toks[kc] = mk.dma("g", spill[:, kc, :], xb_c, waits=[v_xb], ch=6)
        return spill_toks[KC - 1]

    tok_xb_own = m_chain("own")
    mk.barrier()
    tok_xb_scan = m_chain("scan")
    mk.barrier()

    stb = stats(lambda kc: scr["xbscan"][:, kc, :], T, extra_wait=[tok_xb_scan])
    bc_r2 = AR_B[:, 0:T]
    tb_r2 = bcast(bc_r2, 0, T, stb["tok_r"])
    v_mrow = load_mrow(stb["tok_mr"])

    inw_bf = WS[:, 0:8192].bitcast(BF16).rearrange("p (k m) -> p k m", k=KC)
    tok_inw = None
    for kc in range(KC):
        tok_inw = mk.dma("g", inw_bf[:, kc, :], dt_in["in_w"][kc * 128:(kc + 1) * 128, :], ch=7)
    v_gw, v_hw = gw_hw(inw_bf, 2048, "lns_w", "lns_b", "in_b", tok_inw)

    sp_bf = AR_B[:, 2048:2048 + 8192].bitcast(BF16).rearrange("p (k m) -> p k m", k=KC)
    z_f16 = AR_A[:, 0:8192].bitcast(F16).rearrange("p (k m) -> p k m", k=KC)
    rhs_b = [AR_C[:, 0:2048].bitcast(BF16).rearrange("p (k m) -> p k m", k=KC),
             AR_C[:, 2048:4096].bitcast(BF16).rearrange("p (k m) -> p k m", k=KC)]
    xb_f = [AR_C[:, 4096:8192], AR_C[:, 8192:12288]]

    evac_toks, rhsuse_toks = {}, {}
    for ft in range(4):
        rb, xf = rhs_b[ft % 2], xb_f[ft % 2]
        war = ([rhsuse_toks[ft - 2]] if ft - 2 in rhsuse_toks else [tok_xb_scan])
        tok_xf = mk.dma("s", xf.rearrange("p (k m) -> p k m", k=KC),
                        scr["xbscan"][:, :, ft * 512:(ft + 1) * 512],
                        waits=war, ch=6 + ft % 2)
        for kc in range(KC):
            mk.op("vector", lambda E, rb=rb, xf=xf, kc=kc, ft=ft:
                  E.scalar_tensor_tensor(rb[:, kc, :], xf[:, kc * 512:(kc + 1) * 512],
                                         smcol("lns_w", kc), bc_r2[:, ft * 512:(ft + 1) * 512],
                                         ALU.mult, ALU.mult),
                  waits=([tok_xf, tb_r2] if kc == 0 else [tok_xf] if kc == 1 else []),
                  incs=[("vs", 1)])
        v_rhs = mk.c("vs")
        for mc in range(16):
            bank = pb[mc % 2]
            key = ft * 16 + mc
            war_t = ([evac_toks[key - 2]] if key - 2 in evac_toks else [])
            for kc in range(KC):
                mk.op("tensor", lambda E, bank=bank, mc=mc, kc=kc, rb=rb:
                      E.matmul(bank[:, 0:512], inw_bf[:, kc, mc * 128:mc * 128 + 128],
                               rb[:, kc, :], start=(kc == 0), stop=False),
                      waits=(([v_rhs] + war_t + ([tok_inw, v_gw, v_mrow, v_hw] if key == 0 else []))
                             if kc == 0 else []),
                      incs=[("ts", 1)])
            mk.op("tensor", lambda E, bank=bank, mc=mc, ft=ft:
                  E.matmul(bank[:, 0:512], gw_bf[:, mc * 128:mc * 128 + 128],
                           mrow_bf[:, ft * 512:(ft + 1) * 512], start=False, stop=True),
                  incs=[("ts", 1)])
            t_mm = mk.c("ts")
            if mc < 8:
                mk.op("scalar", lambda E, bank=bank, mc=mc, ft=ft:
                      E.activation(sp_bf[:, mc, ft * 512:(ft + 1) * 512], bank[:, 0:512],
                                   AF.Silu, bias=hw_col[:, mc:mc + 1]),
                      waits=[t_mm], incs=[("ss", 1)])
            else:
                mk.op("scalar", lambda E, bank=bank, mc=mc, ft=ft:
                      E.activation(z_f16[:, mc - 8, ft * 512:(ft + 1) * 512], bank[:, 0:512],
                                   AF.Sigmoid, bias=hw_col[:, mc:mc + 1]),
                      waits=[t_mm], incs=[("ss", 1)])
            evac_toks[key] = mk.c("ss")
        rhsuse_toks[ft] = mk.c("ts")
    s_inw = mk.c("ss")

    tok_xpw = mk.dma("g", xpw_bf.rearrange("p (k n) -> p k n", k=KC),
                 dt_in["xp_w"].rearrange("(k p) n -> p k n", p=128), ch=9)
    mk.op("vector", lambda E: E.memset(u64q, 0.0), incs=[("vs", 1)])
    v_u0 = mk.c("vs")
    for ft in range(4):
        for kc in range(KC):
            mk.op("tensor", lambda E, ft=ft, kc=kc:
                  E.matmul(pb[2][0:16, 0:512], xpw_bf[:, kc * 16:kc * 16 + 16],
                           sp_bf[:, kc, ft * 512:(ft + 1) * 512],
                           start=(kc == 0), stop=(kc == KC - 1)),
                  waits=([tok_xpw, s_inw, mk.c("vs")] if kc == 0 else []), incs=[("ts", 1)])
        t_u = mk.c("ts")
        for b in range(4):
            mk.op("vector", lambda E, ft=ft, b=b:
                  E.tensor_copy(u64q[32 * b:32 * b + 16, 128 * ft:128 * ft + 128],
                                pb[2][0:16, b:512:4]),
                  waits=([t_u, v_u0] if b == 0 else []), incs=[("vs", 1)])
    v_u = mk.c("vs")
    mk.op("scalar", lambda E: E.activation(ld64[0:16, :], smcol("A_log", rows=16), AF.Exp),
          waits=[t_fix], incs=[("ss", 1)])
    s_ld = mk.c("ss")
    mk.op("vector", lambda E: E.tensor_scalar_mul(ld64[0:16, :], ld64[0:16, :], -1.0),
          waits=[s_ld], incs=[("vs", 1)])
    mk.op("vector", lambda E: E.tensor_copy(xpbq[0:16, :], smcol("xp_b", rows=16)), incs=[("vs", 1)])
    for q in (32, 64, 96):
        mk.op("vector", lambda E, q=q: E.tensor_copy(ld64[q:q + 16, :], ld64[0:16, :]),
              incs=[("vs", 1)])
        mk.op("vector", lambda E, q=q: E.tensor_copy(xpbq[q:q + 16, :], xpbq[0:16, :]),
              incs=[("vs", 1)])
    mk.op("vector", lambda E: E.tensor_scalar_mul(ld512, ld64, 512.0), incs=[("vs", 1)])
    v_ld = mk.c("vs")
    mk.op("vector", lambda E: E.memset(dk_rep, 1.0), incs=[("vs", 1)])
    mk.op("vector", lambda E: E.tensor_tensor_scan(p512, dk_rep, dk_rep, 0.0, ALU.mult, ALU.add),
          incs=[("vs", 1)])
    v_iota = mk.c("vs")
    mk.op("scalar", lambda E: E.activation(dk_rep, p512, AF.Exp, bias=ld64, scale=0.0),
          waits=[v_ld, v_iota], incs=[("ss", 1)])
    mk.op("scalar", lambda E: E.activation(p512, p512, AF.Exp, bias=0.0, scale=ld64),
          waits=[mk.c("ss")], incs=[("ss", 1)])
    s_dk = mk.c("ss")
    mk.op("vector", lambda E: E.tensor_scalar_add(u64q, u64q, xpbq), waits=[v_u, v_ld],
          incs=[("vs", 1)])
    v_u64 = mk.c("vs")
    mk.op("vector", lambda E: E.tensor_tensor_scan(hsq, dk_rep, u64q, 0.0, ALU.mult, ALU.add),
          waits=[v_u64, s_dk], incs=[("vs", 1)])
    mk.op("vector", lambda E: E.tensor_copy(ecol, hsq[:, 511:512]), incs=[("vs", 1)])
    v_e = mk.c("vs")
    tok_e = mk.dma("s", scr["esnd"][0:128], ecol[:, 0:1], waits=[v_e], ch=5)
    mk.op("gpsimd", lambda E: E.collective_compute(
        "AllGather", ALU.bypass, replica_groups=[list(range(8))],
        ins=[scr["esnd"][:].opt()], outs=[scr["ercv"][:].opt()]),
        waits=[tok_e], incs=[("cc", 1)])
    cc_e = mk.c("cc")
    tok_ca = mk.dma("g", ca8[:, :], scr["ercv"].transpose([1, 0]),
                    waits=[cc_e], allow_slow_non_contiguous=True, ch=8)
    mk.op("scalar", lambda E: E.activation(w8t, CPW[:], AF.Exp, bias=0.0, scale=ld512),
          waits=[v_ld, t_fix], incs=[("ss", 1)])
    s_w8 = mk.c("ss")
    mk.op("vector", lambda E: E.tensor_mul(w8t, w8t, ca8), waits=[s_w8, tok_ca], incs=[("vs", 1)])
    mk.op("vector", lambda E: E.reduce_sum(carry, w8t, axis=AXX), incs=[("vs", 1)])
    mk.op("vector", lambda E: E.scalar_tensor_tensor(hsq, p512, carry, hsq, ALU.mult, ALU.add),
          incs=[("vs", 1)])
    v_hs = mk.c("vs")
    for b in range(4):
        mk.op("vector", lambda E, b=b: E.tensor_copy(hsr_bf[:, b::4], hsq[32 * b:32 * b + 16, :]),
              waits=([v_hs] if b == 0 else []), incs=[("vs", 1)])
    v_hsr = mk.c("vs")

    tok_d = mk.dma("g", dmat_bf, dt_in["dmat"], ch=9)
    outw_bf = WS[:, 4608:4608 + 4096].bitcast(BF16).rearrange("p (k m) -> p k m", k=KC)
    tok_ow = None
    t_inw_done = mk.c("ts")
    for kc in range(KC):
        tok_ow = mk.dma("g", outw_bf[:, kc, :], dt_in["out_w"][kc * 128:(kc + 1) * 128, :],
                        waits=([t_inw_done] if kc == 0 else []), ch=7)

    ysd_bf = AR_C[:, 0:8192].bitcast(BF16).rearrange("p (k m) -> p k m", k=KC)
    xbo_f = [AR_C[:, 8192:12288]]
    ysd_toks = {}
    for ft in range(4):
        for mc in range(8):
            bank = pb[mc % 2]
            key = ft * 8 + mc
            war = ([ysd_toks[key - 2]] if key - 2 in ysd_toks else [])
            mk.op("tensor", lambda E, bank=bank, mc=mc, ft=ft:
                  E.matmul(bank[:, 0:512], dmat_bf[:, mc * 128:mc * 128 + 128],
                           hsr_bf[:, ft * 512:(ft + 1) * 512], start=True, stop=True),
                  waits=([v_hsr, tok_d] if key == 0 else []) + war, incs=[("ts", 1)])
            t_ysd = mk.c("ts")
            mk.op("vector", lambda E, bank=bank, mc=mc, ft=ft:
                  E.scalar_tensor_tensor(ysd_bf[:, mc, ft * 512:(ft + 1) * 512],
                                         bank[:, 0:512], 1.0,
                                         z_f16[:, mc, ft * 512:(ft + 1) * 512],
                                         ALU.mult, ALU.mult),
                  waits=[t_ysd], incs=[("vs", 1)])
            ysd_toks[key] = mk.c("vs")
    v_ysd = mk.c("vs")

    xm_toks = {}
    for ft in range(4):
        xbo = xbo_f[0]
        war = ([xm_toks[ft - 1]] if ft - 1 in xm_toks else [])
        tok_xbo = mk.dma("s", xbo.rearrange("p (k m) -> p k m", k=KC),
                         scr["xbsp"][:, :, ft * 512:(ft + 1) * 512],
                         waits=([tok_xb_own] if ft == 0 else []) + war, ch=6)
        xbo_k = xbo.rearrange("p (k m) -> p k m", k=KC)
        for mc in range(8):
            bank = pb[2 + mc % 2]
            key = ft * 8 + mc
            for kc in range(KC):
                mk.op("tensor", lambda E, bank=bank, mc=mc, kc=kc, ft=ft:
                      E.matmul(bank[:, 0:512], outw_bf[:, kc, mc * 128:mc * 128 + 128],
                               ysd_bf[:, kc, ft * 512:(ft + 1) * 512],
                               start=(kc == 0), stop=(kc == KC - 1)),
                      waits=(([v_ysd, tok_ow] if key == 0 else []) +
                             ([mk.c("vs")] if kc == 0 and key >= 2 else []))
                      if kc == 0 else [],
                      incs=[("ts", 1)])
            t_xm = mk.c("ts")
            mk.op("vector", lambda E, bank=bank, mc=mc, xbo_k=xbo_k:
                  E.scalar_tensor_tensor(xbo_k[:, mc, :], bank[:, 0:512], smcol("out_b", mc),
                                         xbo_k[:, mc, :], ALU.add, ALU.add),
                  waits=[t_xm, tok_xbo], incs=[("vs", 1)])
        v_xm = mk.c("vs")
        xm_toks[ft] = mk.dma("g",
                             scr["xmid"][:, :, ft * 512:(ft + 1) * 512],
                             xbo.rearrange("p (k m) -> p k m", k=KC), waits=[v_xm], ch=6)
    tok_xmid = xm_toks[3]

    if stage >= 2:
        mk.barrier()
        stc = stats(lambda kc: scr["xmid"][:, kc, :], T)
        bc_r3 = AR_B[:, 0:T]
        tb_r3 = bcast(bc_r3, 0, T, stc["tok_r"])
        v_mrow3 = load_mrow(stc["tok_mr"])

        w1_bf = WS[:, 0:8192].bitcast(BF16).rearrange("p (k m) -> p k m", k=KC)
        tok_w1 = None
        t_ws_free = mk.c("ts")
        for kc in range(KC):
            tok_w1 = mk.dma("g", w1_bf[:, kc, :], dt_in["mlp_w1"][kc * 128:(kc + 1) * 128, :],
                            ch=7, waits=([t_ws_free] if kc == 0 else []))
        v_gw3, v_hw3 = gw_hw(w1_bf, 2048, "ln2_w", "ln2_b", "mlp_b1", tok_w1)

        w2c_bf = [WS[:, 8192:8704].bitcast(BF16), WS[:, 8704:9216].bitcast(BF16)]
        hdn_bf = AR_A[:, 0:4096].bitcast(BF16).rearrange("p (k m) -> p k m", k=16)
        rhs_c = [AR_C[:, 0:2048].bitcast(BF16).rearrange("p (k m) -> p k m", k=KC),
                 AR_C[:, 2048:4096].bitcast(BF16).rearrange("p (k m) -> p k m", k=KC)]
        xm_f = [AR_C[:, 4096:8192], AR_C[:, 8192:12288]]

        prev_spill, rhsuse = {}, {}
        hdn_war = None
        for ft in range(4):
            rb, xf = rhs_c[ft % 2], xm_f[ft % 2]
            war = ([rhsuse[ft - 2]] if ft - 2 in rhsuse else [])
            warsp = ([prev_spill[ft - 2]] if ft - 2 in prev_spill else [])
            tok_xf = mk.dma("s", xf.rearrange("p (k m) -> p k m", k=KC),
                            scr["xmid"][:, :, ft * 512:(ft + 1) * 512],
                            waits=war + warsp, ch=6 + ft % 2)
            for kc in range(KC):
                mk.op("vector", lambda E, rb=rb, xf=xf, kc=kc, ft=ft:
                      E.scalar_tensor_tensor(rb[:, kc, :], xf[:, kc * 512:(kc + 1) * 512],
                                             smcol("ln2_w", kc), bc_r3[:, ft * 512:(ft + 1) * 512],
                                             ALU.mult, ALU.mult),
                      waits=([tok_xf, tb_r3] if kc == 0 else []), incs=[("vs", 1)])
            v_rhs = mk.c("vs")
            for mc in range(16):
                bank = pb[mc % 2]
                key = ft * 16 + mc
                war_t = ([mk.c("ss")] if mc >= 2 else ([hdn_war] if hdn_war else []))
                for kc in range(KC):
                    mk.op("tensor", lambda E, bank=bank, mc=mc, kc=kc, rb=rb:
                          E.matmul(bank[:, 0:512], w1_bf[:, kc, mc * 128:mc * 128 + 128],
                                   rb[:, kc, :], start=(kc == 0), stop=False),
                          waits=(([v_rhs] + war_t +
                                  ([tok_w1, v_gw3, v_mrow3, v_hw3] if key == 0 else []))
                                 if kc == 0 else []),
                          incs=[("ts", 1)])
                mk.op("tensor", lambda E, bank=bank, mc=mc, ft=ft:
                      E.matmul(bank[:, 0:512], gw_bf[:, mc * 128:mc * 128 + 128],
                               mrow_bf[:, ft * 512:(ft + 1) * 512], start=False, stop=True),
                      incs=[("ts", 1)])
                t_mm = mk.c("ts")
                mk.op("scalar", lambda E, bank=bank, mc=mc:
                      E.activation(hdn_bf[:, mc, :], bank[:, 0:512], AF.Gelu,
                                   bias=hw_col[:, mc:mc + 1]),
                      waits=[t_mm], incs=[("ss", 1)])
            s_hdn = mk.c("ss")
            tokw2, w2war = {}, {}
            for kc in range(16):
                w2b = w2c_bf[kc % 2]
                war2 = ([w2war[kc - 2]] if kc - 2 in w2war else [mk.c("ts")])
                tokw2[kc] = mk.dma("g", w2b, dt_in["mlp_w2"][kc * 128:(kc + 1) * 128, :],
                                   ch=9 + kc % 2, waits=war2)
                for mc in range(8):
                    mk.op("tensor", lambda E, kc=kc, mc=mc, w2b=w2b:
                          E.matmul(pb[mc][:, 0:512], w2b[:, mc * 128:mc * 128 + 128],
                                   hdn_bf[:, kc, :], start=(kc == 0), stop=(kc == 15)),
                          waits=(([tokw2[kc]] + ([s_hdn] if kc == 0 else [])) if mc == 0 else []),
                          incs=[("ts", 1)])
                w2war[kc] = mk.c("ts")
            t_w2 = mk.c("ts")
            hdn_war = t_w2
            xm_k = xf.rearrange("p (k m) -> p k m", k=KC)
            for mc in range(8):
                mk.op("vector", lambda E, mc=mc, xm_k=xm_k:
                      E.scalar_tensor_tensor(xm_k[:, mc, :], pb[mc][:, 0:512], smcol("mlp_b2", mc),
                                             xm_k[:, mc, :], ALU.add, ALU.add),
                      waits=([t_w2] if mc == 0 else []), incs=[("vs", 1)])
            v_bo = mk.c("vs")
            prev_spill[ft] = mk.dma("g", scr["xmid"][:, :, ft * 512:(ft + 1) * 512],
                                    xm_k, waits=[v_bo], ch=6 + 5 * (ft % 2))
            rhsuse[ft] = mk.c("ts")
        mk.barrier()

        std = stats(lambda kc: scr["xmid"][:, kc, :], T)
        bc_r4 = AR_B[:, 0:T]
        bc_mr4 = AR_B[:, 2080:2080 + T]
        tb_r4 = bcast(bc_r4, 0, T, std["tok_r"])
        tb_mr4 = bcast(bc_mr4, 2, T, std["tok_mr"])
        bo_c = [AR_C[:, 0:4096], AR_C[:, 4096:8192]]
        xo_c = AR_C[:, 8192:12288]
        hic = AR_A[:, 0:2048].bitcast(BF16).rearrange("p (k m) -> p k m", k=KC)
        loc = AR_A[:, 2048:4096].bitcast(BF16).rearrange("p (k m) -> p k m", k=KC)
        hif = AR_A[:, 4096:4608]
        prev = {}
        tok_hi = tok_lo = None
        for ft in range(4):
            boc = bo_c[ft % 2]
            war = ([prev[ft - 2]] if ft - 2 in prev else [])
            warx = ([prev[ft - 1]] if ft - 1 in prev else [])
            tok_bo = mk.dma("s", boc.rearrange("p (k m) -> p k m", k=KC),
                            scr["xmid"][:, :, ft * 512:(ft + 1) * 512],
                            waits=war, ch=6 + ft % 2)
            tok_xo = mk.dma("g", xo_c.rearrange("p (k m) -> p k m", k=KC),
                            dt_in["x_own"].rearrange("(k p) m -> p k m", p=128)[:, :, 3 + ft * 512:3 + (ft + 1) * 512],
                            waits=warx, ch=8)
            boc_k = boc.rearrange("p (k m) -> p k m", k=KC)
            xoc_k = xo_c.rearrange("p (k m) -> p k m", k=KC)
            sl = slice(ft * 512, (ft + 1) * 512)
            war_hl = [tok_hi, tok_lo] if tok_hi else []
            for kc in range(KC):
                mk.op("vector", lambda E, boc_k=boc_k, kc=kc, sl=sl:
                      E.tensor_mul(boc_k[:, kc, :], boc_k[:, kc, :], bc_r4[:, sl]),
                      waits=([tok_bo, tb_r4] if kc == 0 else []), incs=[("vs", 1)])
                mk.op("vector", lambda E, boc_k=boc_k, kc=kc, sl=sl:
                      E.tensor_sub(boc_k[:, kc, :], boc_k[:, kc, :], bc_mr4[:, sl]),
                      waits=([tb_mr4] if kc == 0 else []), incs=[("vs", 1)])
                mk.op("vector", lambda E, boc_k=boc_k, xoc_k=xoc_k, kc=kc:
                      E.tensor_add(boc_k[:, kc, :], boc_k[:, kc, :], xoc_k[:, kc, :]),
                      waits=([tok_xo] if kc == 0 else []), incs=[("vs", 1)])
                mk.op("vector", lambda E, boc_k=boc_k, kc=kc:
                      E.tensor_copy(hic[:, kc, :], boc_k[:, kc, :]),
                      waits=(war_hl if kc == 0 else []), incs=[("vs", 1)])
                mk.op("vector", lambda E, kc=kc:
                      E.tensor_copy(hif, hic[:, kc, :]), incs=[("vs", 1)])
                mk.op("vector", lambda E, boc_k=boc_k, kc=kc:
                      E.tensor_sub(loc[:, kc, :], boc_k[:, kc, :], hif), incs=[("vs", 1)])
            v_x1 = mk.c("vs")
            tok_hi = mk.dma("g", scr["xsnd"][:, 0:8, ft * 512:(ft + 1) * 512], hic,
                            waits=[v_x1], ch=6 + 5 * (ft % 2))
            tok_lo = mk.dma("g", scr["xsnd"][:, 8:16, ft * 512:(ft + 1) * 512], loc,
                            waits=[v_x1], ch=12 + ft % 2)
            prev[ft] = tok_lo
        mk.op("gpsimd", lambda E: E.collective_compute(
            "AllGather", ALU.bypass, replica_groups=[[0, 1], [2, 3], [4, 5], [6, 7]],
            ins=[scr["xsnd"][:].opt()], outs=[scr["xrcv"][:].opt()]),
            waits=[mk.c("dg6"), mk.c("dg11"), mk.c("dg12"), mk.c("dg13")], incs=[("cc", 1)])
        cc_x1 = mk.c("cc")

        t0 = AR_A[:, 0:1024]
        srcbuf = AR_A[:, 1024:1024 + 2048].bitcast(BF16).rearrange("p (v k m) -> p v k m", v=4, k=KC)
        x1o_c = AR_A[:, 3072 + 1024:3072 + 1024 + 1024]
        x1o_c = AR_A[:, 4096:5120]
        prevass = {}
        for j in range(16):
            g_ev, g_od = 2 * j, 2 * j + 1
            war = ([prevass[j - 1]] if j - 1 in prevass else [cc_x1])
            toks = []
            for vi, (g, hl) in enumerate([(g_ev, 0), (g_ev, 1), (g_od, 0), (g_od, 1)]):
                half, locc = g // 16, (g % 16) * 128
                toks.append(mk.dma("s", srcbuf[:, vi, :, :],
                                   scr["xrcv"][half, :, hl * 8:hl * 8 + 8, locc:locc + 128],
                                   waits=war, ch=(6, 7, 14, 15)[vi]))
            tsel = toks[-1]
            x1k = x1o_c.rearrange("p (k m) -> p k m", k=KC)
            mk.op("vector", lambda E, j=j:
                  E.tensor_scalar(x1o_c, srcbuf[:, 0, :, :].rearrange("p k m -> p (k m)"),
                                  PARSEL[:, 0:1], None, ALU.mult),
                  waits=toks + ([prevass[j - 1]] if j - 1 in prevass else []), incs=[("vs", 1)])
            mk.op("vector", lambda E, j=j:
                  E.scalar_tensor_tensor(x1o_c, srcbuf[:, 1, :, :].rearrange("p k m -> p (k m)"),
                                         PARSEL[:, 0:1], x1o_c, ALU.mult, ALU.add),
                  incs=[("vs", 1)])
            mk.op("vector", lambda E, j=j:
                  E.scalar_tensor_tensor(x1o_c, srcbuf[:, 2, :, :].rearrange("p k m -> p (k m)"),
                                         PARSEL[:, 1:2], x1o_c, ALU.mult, ALU.add),
                  incs=[("vs", 1)])
            mk.op("vector", lambda E, j=j:
                  E.scalar_tensor_tensor(x1o_c, srcbuf[:, 3, :, :].rearrange("p k m -> p (k m)"),
                                         PARSEL[:, 1:2], x1o_c, ALU.mult, ALU.add),
                  incs=[("vs", 1)])
            v_sel = mk.c("vs")
            prevass[j] = mk.dma("g", scr["x1own"][:, :, j * 128:(j + 1) * 128],
                                x1o_c.rearrange("p (k m) -> p k m", k=KC),
                                waits=[v_sel], ch=8)
        tok_x1own = prevass[15]

    if stage >= 3:
        mk.barrier()
        H = 16
        steo = stats(lambda kc: scr["x1own"][:, kc, :], T, fused2=True, ship_row0=4)
        bc_g5o = AR_B[:, 6144:6144 + T]
        tb_g5o = bcast(bc_g5o, 5, T, steo["tok_g"])
        v_mrow5 = load_mrow(steo["tok_mg"], slot=7)

        wq_bf = WS[:, 0:4096].bitcast(BF16).rearrange("p (k m) -> p k m", k=KC)
        kva_bf = WS[:, 4096:4736].bitcast(BF16).rearrange("p (k m) -> p k m", k=KC)
        kvb_bf = WS[:, 4736:4992].bitcast(BF16)
        wo_bf = WS[:, 4992:7040].bitcast(BF16).rearrange("p (k m) -> p k m", k=4)
        wgate_f = WS[:, 7040:7104].rearrange("p (k m) -> p k m", k=KC)
        tok_wq = tok_kva = tok_kvb = tok_wo = tok_wg8 = None
        for kc in range(KC):
            tok_wq = mk.dma("g", wq_bf[:, kc, :], dt_in["wq"][kc * 128:(kc + 1) * 128, :], ch=7)
        for kc in range(KC):
            tok_kva = mk.dma("g", kva_bf[:, kc, :], dt_in["wkva"][kc * 128:(kc + 1) * 128, :], ch=9)
        tok_kvb = mk.dma("g", kvb_bf, dt_in["wkvb"], ch=10)
        for kc in range(4):
            tok_wo = mk.dma("g", wo_bf[:, kc, :], dt_in["wo_eff"][kc * 128:(kc + 1) * 128, :], ch=8)
        tok_wg8 = mk.dma("s", wgate_f, dt_in["wgate"].rearrange("(k p) m -> p k m", p=128), ch=14)
        csqa_v = AR_B[:, 2048:3072].bitcast(BF16)
        csqb_v = AR_B[:, 3072:4096].bitcast(BF16)
        for rb in (32, 96):
            tok_csq = mk.dma("s", csqa_v[rb:rb + 32, :], dt_in["csq_a"], ch=15)
            tok_csq = mk.dma("s", csqb_v[rb:rb + 32, :], dt_in["csq_b"], ch=15)
        cska_bf = AR_B[:, 6144:8192].bitcast(BF16)[32:64, :]
        cskb_bf = AR_B[:, 8192:10240].bitcast(BF16)[32:64, :]
        maskq_bf = AR_B[:, 0:2048].bitcast(BF16)
        tok_mask = mk.dma("s", maskq_bf, dt_in["qmask"], ch=14)

        c0w = SMCOL["w_ln1"][0]
        mk.op("vector", lambda E: E.tensor_copy(gcol_bf, SM[:, c0w:c0w + 8]), incs=[("vs", 1)])
        v_g1 = mk.c("vs")
        for i, f0 in enumerate(range(0, 1024, 512)):
            for kc in range(KC):
                mk.op("tensor", lambda E, f0=f0, kc=kc:
                      E.matmul(pb[5][0:1, 0:512], gcol_bf[:, kc:kc + 1],
                               wq_bf[:, kc, f0:f0 + 512], start=(kc == 0), stop=(kc == KC - 1)),
                      waits=([v_g1, tok_wq, mk.c("vs")] if kc == 0 else []), incs=[("ts", 1)])
            t_gw = mk.c("ts")
            mk.op("vector", lambda E, f0=f0: E.tensor_copy(gw_bf[:, f0:f0 + 512], pb[5][0:1, 0:512]),
                  waits=[t_gw], incs=[("vs", 1)])
        for kc in range(KC):
            mk.op("tensor", lambda E, kc=kc:
                  E.matmul(pb[5][0:1, 0:160], gcol_bf[:, kc:kc + 1],
                           kva_bf[:, kc, :], start=(kc == 0), stop=(kc == KC - 1)),
                  waits=([tok_kva, mk.c("vs")] if kc == 0 else []), incs=[("ts", 1)])
        t_gk = mk.c("ts")
        mk.op("vector", lambda E: E.tensor_copy(gw_bf[:, 1024:1184], pb[5][0:1, 0:160]),
              waits=[t_gk], incs=[("vs", 1)])
        v_gwq = mk.c("vs")

        qp_bf = AR_A[:, 0:8192].bitcast(BF16)
        lat_bf = AR_A[:, 8192:10240].bitcast(BF16)
        xoc2 = [AR_C[:, 0:4096], AR_C[:, 4096:8192]]
        rhsq = AR_C[:, 8448:10496].bitcast(BF16).rearrange("p (k m) -> p k m", k=KC)
        ev_tok = {}
        for ft in range(4):
            xoc = xoc2[ft % 2]
            war = ([ev_tok[ft - 2]] if ft - 2 in ev_tok else [])
            tok_xo = mk.dma("s", xoc.rearrange("p (k m) -> p k m", k=KC),
                            scr["x1own"][:, :, ft * 512:(ft + 1) * 512],
                            waits=war, ch=6 + ft % 2)
            xok = xoc.rearrange("p (k m) -> p k m", k=KC)
            for kc in range(KC):
                mk.op("vector", lambda E, xok=xok, kc=kc, ft=ft:
                      E.scalar_tensor_tensor(rhsq[:, kc, :], xok[:, kc, :],
                                             smcol("w_ln1", kc), bc_g5o[:, ft * 512:(ft + 1) * 512],
                                             ALU.mult, ALU.mult),
                      waits=([tok_xo, tb_g5o, mk.c("ts")] if kc == 0 else []), incs=[("vs", 1)])
            v_rq = mk.c("vs")
            for pr in range(8):
                bank = pb[pr % 2]
                key = ft * 8 + pr
                war_t = ([ev_tok.get(("q", key - 2), ("ss", 0))] if key >= 2 else [])
                for kc in range(KC):
                    mk.op("tensor", lambda E, bank=bank, pr=pr, kc=kc:
                          E.matmul(bank[:, 0:512], wq_bf[:, kc, pr * 128:pr * 128 + 128],
                                   rhsq[:, kc, :], start=(kc == 0), stop=False),
                          waits=(([v_rq] + war_t + ([v_gwq, v_mrow5] if key == 0 else []))
                                 if kc == 0 else []),
                          incs=[("ts", 1)])
                mk.op("tensor", lambda E, bank=bank, pr=pr, ft=ft:
                      E.matmul(bank[:, 0:512], gw_bf[:, pr * 128:pr * 128 + 128],
                               mrow_bf[:, ft * 512:(ft + 1) * 512], start=False, stop=True),
                      incs=[("ts", 1)])
                t_q = mk.c("ts")
                mk.op("scalar", lambda E, bank=bank, pr=pr, ft=ft:
                      E.activation(qp_bf[:, pr * 2048 + ft * 512:pr * 2048 + (ft + 1) * 512],
                                   bank[:, 0:512], AF.Copy, scale=0.125),
                      waits=[t_q], incs=[("ss", 1)])
                ev_tok[("q", key)] = mk.c("ss")
            ev_tok[ft] = mk.c("ts")
        s_qdone = mk.c("ss")

        rtmp = AR_B[:, 4096:5120].bitcast(BF16)
        for pr in range(8):
            cols = slice(pr * 2048, (pr + 1) * 2048)
            for base in (32, 96):
                rt = rtmp[base:base + 32, :]
                mk.op("vector", lambda E, cols=cols, base=base, rt=rt:
                      E.stream_shuffle(rt, qp_bf[base:base + 32, cols],
                                       [i ^ 1 for i in range(32)]),
                      waits=([s_qdone, tok_csq] if (pr == 0 and base == 32) else []),
                      incs=[("vs", 1)])
                mk.op("vector", lambda E, rt=rt, base=base:
                      E.tensor_mul(rt, rt, csqb_v[base:base + 32, :]),
                      incs=[("vs", 1)])
                mk.op("vector", lambda E, cols=cols, base=base:
                      E.tensor_mul(qp_bf[base:base + 32, cols], qp_bf[base:base + 32, cols],
                                   csqa_v[base:base + 32, :]),
                      incs=[("vs", 1)])
                mk.op("vector", lambda E, cols=cols, base=base, rt=rt:
                      E.tensor_add(qp_bf[base:base + 32, cols], qp_bf[base:base + 32, cols], rt),
                      incs=[("vs", 1)])
        v_qrope = mk.c("vs")

        stg = {}
        for half in range(2):
            stg[half] = stats(lambda kc, half=half: scr["xrcv"][half, :, kc, :], T,
                              fused2=True, ship_off=half * T, tbuf=AR_C, tbase=4096,
                              extra_wait=[mk.c("vs")] if half == 0 else ())
        bc_g5g = [AR_B[:, 6144:6144 + T], AR_B[:, 8192:8192 + T]]
        tb_g5g = [bcast(bc_g5g[0], 1, T, stg[0]["tok_g"], off=0),
                  bcast(bc_g5g[1], 1, T, stg[1]["tok_g"], off=T)]
        mrowG_bf = ROWS[0:1, 0:2048].bitcast(BF16)
        mg_f = rslot(2, T)
        for half in range(2):
            tokm = mk.dma("s", mg_f, scr["rows_scr"][3:4, half * T:(half + 1) * T],
                          waits=[stg[half]["tok_mg"], mk.c("vs")], ch=5)
            mk.op("vector", lambda E, half=half:
                  E.tensor_scalar_mul(mrowG_bf[:, half * T:(half + 1) * T], mg_f, -1.0),
                  waits=[tokm], incs=[("vs", 1)])
            mk._stats_ships = getattr(mk, "_stats_ships", []) + [tokm]
        v_mrowG = mk.c("vs")

        krp_bf = ROWS[32:64, 0:2048].bitcast(BF16)
        khc = [AR_C[:, 0:2048].bitcast(BF16).rearrange("p (k m) -> p k m", k=KC),
               AR_C[:, 2048:4096].bitcast(BF16).rearrange("p (k m) -> p k m", k=KC)]
        rhk = AR_C[:, 8448:10496].bitcast(BF16).rearrange("p (k m) -> p k m", k=KC)
        kv_tok = {}
        for ftg in range(8):
            half, ftl = ftg // 4, ftg % 4
            hc = khc[ftg % 2]
            war = ([kv_tok[ftg - 2]] if ftg - 2 in kv_tok else [v_qrope])
            tok_h = mk.dma("s", hc, scr["xrcv"][half, :, 0:8, ftl * 512:(ftl + 1) * 512],
                           waits=war, ch=6 + ftg % 2)
            for kc in range(KC):
                mk.op("vector", lambda E, hc=hc, kc=kc, ftg=ftg:
                      E.scalar_tensor_tensor(rhk[:, kc, :], hc[:, kc, :],
                                             smcol("w_ln1", kc),
                                             bc_g5g[ftg // 4][:, (ftg % 4) * 512:(ftg % 4 + 1) * 512],
                                             ALU.mult, ALU.mult),
                      waits=([tok_h] + ([tb_g5g[0], tb_g5g[1], mk.c("ts")] if kc == 0 else []))
                      if kc <= 0 else [],
                      incs=[("vs", 1)])
            v_rk = mk.c("vs")
            for kc in range(KC):
                mk.op("tensor", lambda E, kc=kc, ftg=ftg:
                      E.matmul(pb[ftg % 2][:, 0:512], kva_bf[:, kc, 0:128],
                               rhk[:, kc, :], start=(kc == 0), stop=False),
                      waits=([v_rk, v_mrowG] if kc == 0 else []), incs=[("ts", 1)])
            mk.op("tensor", lambda E, ftg=ftg:
                  E.matmul(pb[ftg % 2][:, 0:512], gw_bf[:, 1024:1152],
                           mrowG_bf[:, ftg * 512:(ftg + 1) * 512], start=False, stop=True),
                  incs=[("ts", 1)])
            t_lat = mk.c("ts")
            for kc in range(KC):
                mk.op("tensor", lambda E, kc=kc, ftg=ftg:
                      E.matmul(pb[2 + ftg % 2][0:32, 0:512], kva_bf[:, kc, 128:160],
                               rhk[:, kc, :], start=(kc == 0), stop=False),
                      incs=[("ts", 1)])
            mk.op("tensor", lambda E, ftg=ftg:
                  E.matmul(pb[2 + ftg % 2][0:32, 0:512], gw_bf[:, 1152:1184],
                           mrowG_bf[:, ftg * 512:(ftg + 1) * 512], start=False, stop=True),
                  incs=[("ts", 1)])
            t_krp = mk.c("ts")
            mk.op("scalar", lambda E, ftg=ftg:
                  E.activation(lat_bf[:, ftg * 512:(ftg + 1) * 512], pb[ftg % 2][:, 0:512], AF.Copy),
                  waits=[t_lat], incs=[("ss", 1)])
            mk.op("scalar", lambda E, ftg=ftg:
                  E.activation(krp_bf[:, ftg * 512:(ftg + 1) * 512], pb[2 + ftg % 2][0:32, 0:512], AF.Copy),
                  waits=[t_krp], incs=[("ss", 1)])
            kv_tok[ftg] = mk.c("ss")
        s_ckv = mk.c("ss")

        tok_csk = mk.dma("s", cska_bf, dt_in["csk_a"], waits=[s_ckv], ch=15)
        tok_csk = mk.dma("s", cskb_bf, dt_in["csk_b"], ch=15)
        rtm2 = AR_B[:, 4096:6144].bitcast(BF16)[32:64, :]
        mk.op("vector", lambda E: E.stream_shuffle(rtm2[:, 0:2048], krp_bf[:, 0:2048],
                                                   [i ^ 1 for i in range(32)]),
              waits=[s_ckv, tok_csk], incs=[("vs", 1)])
        mk.op("vector", lambda E: E.stream_shuffle(rtm2[:, 2048:4096], krp_bf[:, 2048:4096],
                                                   [i ^ 1 for i in range(32)]),
              incs=[("vs", 1)])
        mk.op("vector", lambda E: E.tensor_mul(krp_bf, krp_bf, cska_bf), incs=[("vs", 1)])
        mk.op("vector", lambda E: E.tensor_mul(rtm2, rtm2, cskb_bf), incs=[("vs", 1)])
        mk.op("vector", lambda E: E.tensor_add(krp_bf, krp_bf, rtm2), incs=[("vs", 1)])
        v_rotk = mk.c("vs")

        vp_bf = AR_C[:, 0:8448].bitcast(BF16)
        mk.op("vector", lambda E: E.memset(vp_bf, 1.0), waits=[mk.c("ts")], incs=[("vs", 1)])
        v_vp0 = mk.c("vs")
        for kb in range(32):
            bank = pb[kb % 2]
            war = [mk.c("vs")] if kb >= 2 else [v_vp0]
            mk.op("tensor", lambda E, kb=kb, bank=bank:
                  E.matmul(bank[:, 0:512], lat_bf[:, kb * 128:(kb + 1) * 128],
                           kvb_bf, start=True, stop=True),
                  waits=[s_ckv, tok_kvb] + war, incs=[("ts", 1)])
            t_v = mk.c("ts")
            mk.op("vector", lambda E, kb=kb, bank=bank:
                  E.tensor_copy(vp_bf[:, kb * 528:kb * 528 + 528].rearrange("p (h v) -> p h v", h=16)[:, :, 0:32],
                                bank[:, 0:512].rearrange("p (h v) -> p h v", h=16)),
                  waits=[t_v], incs=[("vs", 1)])
        v_vp = mk.c("vs")

        kb_buf = AR_C[:, 8448:10496].bitcast(BF16)
        p_buf = AR_B[:, 0:0]
        p_buf = AR_C[:, 10240 - 2048 - 0:0] if False else None
        pB = AR_B[:, 4096:5120].bitcast(BF16)
        pB_b = AR_B[:, 6400:7424].bitcast(BF16)
        yh_bf = AR_B[0:32, 5120:6144].bitcast(BF16)
        lb_sb = AR_B[0:32, 6144:6400].bitcast(BF16)
        linv_bf = ROWS[0:1, 0:256].bitcast(BF16)
        ones32_bf = MISC[0:1, 2048:2064].bitcast(BF16)
        mk.op("vector", lambda E: E.memset(ones32_bf, 1.0), incs=[("vs", 1)])
        mk.op("vector", lambda E: E.tensor_copy(kb_buf[32:64, :], krp_bf),
              waits=[v_rotk], incs=[("vs", 1)])
        mk.op("vector", lambda E: E.tensor_copy(kb_buf[96:128, :], krp_bf),
              incs=[("vs", 1)])
        v_o32 = mk.c("vs")

        kbuild_tok = {}
        att_done = {}
        yspill = {}
        prev_tlb = None
        probe1 = isinstance(devtap, str) and devtap.startswith("dbg_attk")
        att_H = 1 if probe1 else H
        att_KB = int(devtap[8:]) if probe1 else 32

        def att_fc0(kb):
            return (kb // 2) // 4
        if probe1:
            mk.op("vector", lambda E: E.memset(SC[:, 0:2048], 0.0), incs=[("vs", 1)])
            vz = mk.c("vs")
            ysflat = scr["ysp"].rearrange("p k t -> p (k t)")
            mk.dma("g", ysflat[:, 0:4096], SC[:, 0:2048].bitcast(BF16), waits=[vz], ch=6)
            mk.dma("g", ysflat[:, 4096:8192], SC[:, 0:2048].bitcast(BF16), ch=6)
        for h in range(att_H):
            pbase = 64 * (h % 2)
            war_kb = ([att_done[h - 2]] if h - 2 in att_done else [])
            cp_done = {}
            for ftk in range(8):
                bank = pb[6 if ftk % 2 == 0 else 7]
                war_cp = ([cp_done[ftk - 2]] if ftk - 2 in cp_done else [])
                mk.op("tensor", lambda E, h=h, ftk=ftk, bank=bank:
                      E.matmul(bank[0:32, 0:512], kvb_bf[:, 32 * h:32 * h + 32],
                               lat_bf[:, ftk * 512:(ftk + 1) * 512], start=True, stop=True),
                      waits=([s_ckv, v_rotk, mk.c("vs")] + war_kb if ftk == 0 else war_cp),
                      incs=[("ts", 1)])
                t_kn = mk.c("ts")
                mk.op("vector", lambda E, h=h, ftk=ftk, bank=bank, pbase=pbase:
                      E.tensor_copy(kb_buf[pbase:pbase + 32, ftk * 512:(ftk + 1) * 512],
                                    bank[0:32, 0:512]),
                      waits=[t_kn], incs=[("vs", 1)])
                cp_done[ftk] = mk.c("vs")
            kbuild_tok[h] = mk.c("vs")

            pr = h // 2
            qcols0 = pr * 2048
            exp_done = {}
            avused = {}
            bank_free = {0: None, 1: None}
            pend = []

            def emit_qk(kb, fc):
                jmin = kb // 2
                off0 = min(max(jmin - 4 * fc, 0), 4) * 128
                bksel = (kb + fc) % 2
                qk_bank = pb[bksel]
                war_bank = ([bank_free[bksel]] if bank_free[bksel] is not None else [])
                first = kb == 0 and fc == jmin // 4
                mk.op("tensor", lambda E, h=h, kb=kb, fc=fc, qk_bank=qk_bank, pbase=pbase,
                      qcols0=qcols0, off0=off0:
                      E.matmul(qk_bank[:, off0:512],
                               kb_buf[pbase:pbase + 64, kb * 128:(kb + 1) * 128],
                               qp_bf[pbase:pbase + 64,
                                     qcols0 + fc * 512 + off0:qcols0 + (fc + 1) * 512],
                               start=True, stop=True),
                      waits=(([kbuild_tok[h], v_qrope, v_vp, v_o32, tok_mask] if first else []) +
                             war_bank +
                             ([mk.c("vs")] if first and h >= 1 else [])),
                      incs=[("ts", 1)])
                t_s = mk.c("ts")
                pc = (pB if kb % 2 == 0 else pB_b)[:, fc * 512:(fc + 1) * 512]
                war_p = ([avused[(kb - 2, fc)]] if (kb - 2, fc) in avused else [])
                mk.op("scalar", lambda E, pc=pc, qk_bank=qk_bank, off0=off0:
                      E.activation(pc[:, off0:512], qk_bank[:, off0:512], AF.Exp),
                      waits=[t_s] + war_p, incs=[("ss", 1)])
                s_p = mk.c("ss")
                bank_free[bksel] = s_p
                avw = [s_p]
                if 4 * fc <= jmin < 4 * fc + 4:
                    mk.op("vector", lambda E, pc=pc, off0=off0, kb=kb:
                          E.tensor_mul(pc[:, off0:off0 + 128], pc[:, off0:off0 + 128],
                                       maskq_bf[:, kb * 128:(kb + 1) * 128]),
                          waits=[s_p], incs=[("vs", 1)])
                    avw = [mk.c("vs")]
                return (kb, fc, off0, pc, avw)

            def emit_av(st):
                kb, fc, off0, pc, avw = st
                mk.op("tensor", lambda E, h=h, kb=kb, fc=fc, pc=pc, off0=off0:
                      E.matmul(pb[2 + fc][0:33, off0:512],
                               vp_bf[:, kb * 528 + 33 * h:kb * 528 + 33 * h + 33],
                               pc[:, off0:512], start=(kb == 0),
                               stop=(kb == (att_KB - 1 if probe1 and att_KB < 32
                                            else 8 * fc + 7))),
                      waits=avw, incs=[("ts", 1)])
                avused[(kb, fc)] = mk.c("ts")
                exp_done[(kb, fc)] = mk.c("ts")

            for kb in range(att_KB):
                for fc in range(att_fc0(kb), 4):
                    pend.append(emit_qk(kb, fc))
                    if len(pend) > 2:
                        emit_av(pend.pop(0))
            for st in pend:
                emit_av(st)
            lrow = ROWS[0:1, 256:768]
            for fc in range(4):
                t_av = avused[(att_KB - 1 if probe1 and att_KB < 32 else 8 * fc + 7, fc)]
                mk.op("scalar", lambda E, fc=fc:
                      E.activation(lrow, pb[2 + fc][32:33, 0:512], AF.Ln),
                      waits=[t_av] + ([prev_tlb] if prev_tlb is not None else []),
                      incs=[("ss", 1)])
                mk.op("scalar", lambda E:
                      E.activation(linv_bf[:, 0:512], lrow, AF.Exp, scale=-1.0),
                      incs=[("ss", 1)])
                s_li = mk.c("ss")
                mk.op("tensor", lambda E:
                      E.matmul(pb[6][0:32, 0:512], ones32_bf, linv_bf[:, 0:512],
                               start=True, stop=True),
                      waits=[s_li, mk.c("vs")], incs=[("ts", 1)])
                t_lb = mk.c("ts")
                prev_tlb = t_lb
                mk.op("vector", lambda E:
                      E.tensor_copy(lb_sb, pb[6][0:32, 0:512]),
                      waits=[t_lb], incs=[("vs", 1)])
                mk.op("vector", lambda E, fc=fc:
                      E.tensor_mul(yh_bf[:, fc * 512:(fc + 1) * 512], pb[2 + fc][0:32, 0:512], lb_sb),
                      waits=([yspill[h - 1]] if fc == 0 and h >= 1 else []),
                      incs=[("vs", 1)])
            v_y = mk.c("vs")
            yspill[h] = mk.dma("g", scr["ysp"][32 * (h % 4):32 * (h % 4) + 32, h // 4, :],
                               yh_bf, waits=[v_y], ch=6 + h % 2)
            att_done[h] = mk.c("ts")
        mk.barrier()
        if probe1:
            mk.op("vector", lambda E: E.tensor_copy(AR_B[0:33, 6912:7424], pb[2][0:33, 0:512]),
                  incs=[("vs", 1)])
            mk.barrier()
            mk.dma("g", scr["xmid"][:, 5, 0:1024], AR_B[:, 4096:5120], ch=6)
            mk.dma("g", scr["xmid"][0:32, 6, 0:1024], AR_B[0:32, 5120:6144], ch=6)
            mk.dma("g", scr["xmid"][0:33, 7, 0:512], AR_B[0:33, 6912:7424], ch=6)
            mk.dma("g", scr["xbsp"].rearrange("p k t -> p (k t)")[:, 0:10240],
                   AR_A[:, 0:10240], ch=6)
            mk.dma("g", scr["xbscan"].rearrange("p k t -> p (k t)")[:, 0:8448],
                   AR_C[:, 0:8448], ch=6)
            mk.dma("g", scr["xmid"][0:32, 0, :], ROWS[32:64, 0:2048], ch=6)
            mk.dma("g", scr["xmid"][:, 3, :], AR_B[:, 0:2048], ch=6)
            mk.barrier()
        if devtap == "dbg_att":
            mk.dma("g", scr["xbsp"].rearrange("p k t -> p (k t)")[:, 0:10240],
                   AR_A[:, 0:10240], ch=6)
            mk.dma("g", scr["xbscan"].rearrange("p k t -> p (k t)")[:, 0:8448],
                   AR_C[:, 0:8448], ch=6)
            mk.dma("g", scr["xmid"][0:32, 0, :], ROWS[32:64, 0:2048], ch=6)
            mk.dma("g", scr["xmid"][:, 2, :], bc_g5o, ch=6)
            mk.dma("g", scr["xmid"][:, 3, :], AR_B[:, 0:2048], ch=6)
            mk.dma("g", scr["xmid"][:, 4, :], AR_C[:, 8448:10496], ch=6)
            mk.barrier()

        bc_r5o = AR_B[:, 6144:6144 + T]
        bc_mr5o = AR_B[:, 8192:8192 + T]
        tb_r5o = bcast(bc_r5o, 4, T, steo["tok_r"])
        tb_mr5o = bcast(bc_mr5o, 6, T, steo["tok_mr"])
        yb = [AR_C[:, 0:1024].bitcast(BF16).rearrange("p (k m) -> p k m", k=4),
              AR_C[:, 1024:2048].bitcast(BF16).rearrange("p (k m) -> p k m", k=4)]
        x1c2 = [AR_C[:, 2048:6144], AR_C[:, 6144:10240]]
        x2c = AR_A[:, 0:4096]
        x2tok = {}
        for ft in range(4):
            ybf, x1c = yb[ft % 2], x1c2[ft % 2]
            war = ([x2tok[ft - 2]] if ft - 2 in x2tok else [])
            tok_y = mk.dma("s", ybf, scr["ysp"][:, :, ft * 512:(ft + 1) * 512],
                           waits=war, ch=6 + ft % 2)
            tok_x1 = mk.dma("s", x1c.rearrange("p (k m) -> p k m", k=KC),
                            scr["x1own"][:, :, ft * 512:(ft + 1) * 512],
                            waits=war, ch=14 + ft % 2)
            x1k = x1c.rearrange("p (k m) -> p k m", k=KC)
            x2k = x2c.rearrange("p (k m) -> p k m", k=KC)
            for mc in range(8):
                bank = pb[mc % 2]
                for kc in range(4):
                    mk.op("tensor", lambda E, bank=bank, mc=mc, kc=kc, ybf=ybf:
                          E.matmul(bank[:, 0:512], wo_bf[:, kc, mc * 128:mc * 128 + 128],
                                   ybf[:, kc, :], start=(kc == 0), stop=(kc == 3)),
                          waits=(([tok_y, tok_wo] + ([mk.c("vs")] if ft + mc > 0 else []))
                                 if kc == 0 else []),
                          incs=[("ts", 1)])
                t_wo = mk.c("ts")
                sl = slice(ft * 512, (ft + 1) * 512)
                mk.op("vector", lambda E, x1k=x1k, x2k=x2k, mc=mc, sl=sl:
                      E.tensor_mul(x2k[:, mc, :], x1k[:, mc, :], bc_r5o[:, sl]),
                      waits=[tok_x1] + ([tb_r5o, tb_mr5o] if ft == 0 and mc == 0 else [])
                      + ([x2tok[ft - 1]] if mc == 0 and ft - 1 in x2tok else []),
                      incs=[("vs", 1)])
                mk.op("vector", lambda E, x2k=x2k, mc=mc, sl=sl:
                      E.tensor_sub(x2k[:, mc, :], x2k[:, mc, :], bc_mr5o[:, sl]),
                      incs=[("vs", 1)])
                mk.op("vector", lambda E, x2k=x2k, mc=mc, bank=bank:
                      E.tensor_add(x2k[:, mc, :], x2k[:, mc, :], bank[:, 0:512]),
                      waits=[t_wo], incs=[("vs", 1)])
            v_x2 = mk.c("vs")
            x2tok[ft] = mk.dma("g", scr["x2sp"][:, :, ft * 512:(ft + 1) * 512], x2k,
                               waits=[v_x2], ch=11 + ft % 2)
        mk.barrier()

        stf = stats(lambda kc: scr["x2sp"][:, kc, :], T, with_mean=False)
        bc_rr = AR_C[:, 0:2048]
        tb_rr = bcast(bc_rr, 0, T, stf["tok_r"])
        gate8 = MISC[0:8, 128:2176]
        rhsm = AR_B[:, 0:8192].bitcast(BF16).rearrange("p (k m) -> p k m", k=KC)
        x2l = [AR_C[:, 4096:8192], AR_C[:, 8192:12288]]
        gtok = {}
        for ft in range(4):
            x2b = x2l[ft % 2]
            war = ([gtok[ft - 2]] if ft - 2 in gtok else [])
            tok_x2 = mk.dma("s", x2b.rearrange("p (k m) -> p k m", k=KC),
                            scr["x2sp"][:, :, ft * 512:(ft + 1) * 512],
                            waits=war, ch=6 + ft % 2)
            x2bk = x2b.rearrange("p (k m) -> p k m", k=KC)
            for kc in range(KC):
                mk.op("vector", lambda E, x2bk=x2bk, kc=kc:
                      E.tensor_scalar(x2bk[:, kc, :], x2bk[:, kc, :], smcol("w_ln2", kc), None,
                                      ALU.mult),
                      waits=([tok_x2] if kc == 0 else []), incs=[("vs", 1)])
            v_tp = mk.c("vs")
            for kc in range(KC):
                mk.op("tensor", lambda E, x2bk=x2bk, kc=kc, ft=ft:
                      E.matmul(pb[4][0:8, 0:512], wgate_f[:, kc, :], x2bk[:, kc, :],
                               start=(kc == 0), stop=(kc == KC - 1)),
                      waits=([v_tp, tok_wg8, mk.c("vs")] if kc == 0 else []), incs=[("ts", 1)])
            t_g8 = mk.c("ts")
            mk.op("vector", lambda E, ft=ft:
                  E.tensor_copy(gate8[:, ft * 512:(ft + 1) * 512], pb[4][0:8, 0:512]),
                  waits=[t_g8], incs=[("vs", 1)])
            for kc in range(KC):
                mk.op("vector", lambda E, x2bk=x2bk, kc=kc, ft=ft:
                      E.tensor_mul(rhsm[:, kc, ft * 512:(ft + 1) * 512], x2bk[:, kc, :],
                                   bc_rr[:, ft * 512:(ft + 1) * 512]),
                      waits=([tb_rr] if kc == 0 and ft == 0 else []), incs=[("vs", 1)])
            gtok[ft] = mk.c("vs")
        v_rhsm = mk.c("vs")

        sG = AR_A[0:8, 0:2048]
        sT8, sT32 = AR_A[0:8, 2048:4096], AR_A[0:32, 2048:4096]
        sM8, sM32 = AR_A[0:8, 4096:6144], AR_A[0:32, 4096:6144]
        sK8 = AR_A[0:8, 6144:8192]
        sC8, sC32 = AR_A[0:8, 8192:10240], AR_A[0:32, 8192:10240]
        sF1 = AR_C[0:8, 8448:10496]
        sE8 = AR_B[0:8, 8192:10240]
        ebcol = smcol("ebias", rows=8)
        mk.op("vector", lambda E: E.tensor_mul(sG, gate8, bc_rr[0:8, :]), waits=[v_rhsm, mk.c("ts")],
              incs=[("vs", 1)])
        mk.op("vector", lambda E: E.tensor_scalar_add(sG, sG, ebcol), incs=[("vs", 1)])
        mk.op("scalar", lambda E: E.activation(sG, sG, AF.Exp), waits=[mk.c("vs")], incs=[("ss", 1)])
        s_e8 = mk.c("ss")
        mk.wait("vector", [s_e8])
        shufs = {k: [((i + k) % 8) if i < 8 else i for i in range(32)] for k in (1, 2, 4)}

        def fold(op):
            for k in (4, 2, 1):
                mk.op("vector", lambda E, k=k: E.stream_shuffle(sT32, sM32, shufs[k]),
                      incs=[("vs", 1)])
                mk.op("vector", lambda E, op=op: E.tensor_tensor(sM8, sM8, sT8, op),
                      incs=[("vs", 1)])

        def cumsum8():
            for k in (1, 2, 4):
                mask = [((i - k) % 8) if i < 8 else i for i in range(32)]
                mk.op("vector", lambda E, mask=mask: E.stream_shuffle(sT32, sC32, mask),
                      incs=[("vs", 1)])
                mk.op("vector", lambda E, k=k: E.memset(sT8[0:k, :], 0.0), incs=[("vs", 1)])
                mk.op("vector", lambda E: E.tensor_add(sC8, sC8, sT8), incs=[("vs", 1)])
        mk.op("vector", lambda E: E.tensor_copy(sM8, sG), incs=[("vs", 1)])
        fold(ALU.max)
        mk.op("vector", lambda E: E.tensor_tensor(sK8, sG, sM8, ALU.is_equal), incs=[("vs", 1)])
        mk.op("vector", lambda E: E.tensor_copy(sC8, sK8), incs=[("vs", 1)])
        cumsum8()
        mk.op("vector", lambda E: E.tensor_single_scalar(sF1, sC8, 1.0, ALU.is_equal),
              incs=[("vs", 1)])
        mk.op("vector", lambda E: E.tensor_mul(sF1, sF1, sK8), incs=[("vs", 1)])
        mk.op("vector", lambda E: E.tensor_scalar(sE8, sF1, -1.0, 1.0, ALU.mult, ALU.add),
              incs=[("vs", 1)])
        mk.op("vector", lambda E: E.tensor_mul(sE8, sE8, sG), incs=[("vs", 1)])
        mk.op("vector", lambda E: E.tensor_copy(sM8, sE8), incs=[("vs", 1)])
        fold(ALU.max)
        mk.op("vector", lambda E: E.tensor_tensor(sK8, sE8, sM8, ALU.is_equal), incs=[("vs", 1)])
        mk.op("vector", lambda E: E.tensor_copy(sC8, sK8), incs=[("vs", 1)])
        cumsum8()
        mk.op("vector", lambda E: E.tensor_single_scalar(sE8, sC8, 1.0, ALU.is_equal),
              incs=[("vs", 1)])
        mk.op("vector", lambda E: E.tensor_mul(sE8, sE8, sK8), incs=[("vs", 1)])
        mk.op("vector", lambda E: E.tensor_add(sK8, sF1, sE8), incs=[("vs", 1)])
        mk.op("vector", lambda E: E.tensor_mul(sK8, sK8, sG), incs=[("vs", 1)])
        mk.op("vector", lambda E: E.tensor_copy(sM8, sK8), incs=[("vs", 1)])
        fold(ALU.add)
        mk.op("scalar", lambda E: E.activation(sM8, sM8, AF.Ln), waits=[mk.c("vs")],
              incs=[("ss", 1)])
        mk.op("scalar", lambda E: E.activation(sM8, sM8, AF.Exp, scale=-1.0),
              incs=[("ss", 1)])
        mk.op("vector", lambda E: E.tensor_mul(sK8, sK8, sM8), waits=[mk.c("ss")],
              incs=[("vs", 1)])
        v_comb = mk.c("vs")
        tok_comb = mk.dma("s", scr["cmbs"][:], sK8, waits=[v_comb], ch=5)

        wd_flat = dt_in["wd9"].rearrange("e i o -> (e i) o")
        h_all = AR_A[:, 0:9216].bitcast(BF16).rearrange("p (c m) -> p c m", c=36)
        gsb2 = [AR_C[:, 2048:3072].bitcast(BF16).rearrange("p (c m) -> p c m", c=4),
                AR_C[:, 3072:4096].bitcast(BF16).rearrange("p (c m) -> p c m", c=4)]
        gu_bf = AR_C[:, 0:256].bitcast(BF16)
        cmb_b = [AR_C[:, 256:512].bitcast(BF16), AR_C[:, 512:768].bitcast(BF16)]
        wg_b = [WS[:, 0:2048].bitcast(BF16).rearrange("p (k m) -> p k m", k=KC),
                WS[:, 4096:6144].bitcast(BF16).rearrange("p (k m) -> p k m", k=KC)]
        wu_b = [WS[:, 2048:4096].bitcast(BF16).rearrange("p (k m) -> p k m", k=KC),
                WS[:, 6144:8192].bitcast(BF16).rearrange("p (k m) -> p k m", k=KC)]
        wd_b = [WS[:, 8192:8704].bitcast(BF16), WS[:, 8704:9216].bitcast(BF16)]
        bo2c = AR_C[:, 8192:12288]
        x2rl = AR_C[:, 4096:8192]
        bo2tok = {}
        wtok = {}
        wuse = {}
        cb_use = {}
        sg_done = {}
        su_done = {}
        g_banks = (pb[4], pb[6])
        u_banks = (pb[5], pb[7])
        for tq in range(4):
            tsl = slice(tq * 512, (tq + 1) * 512)
            for e in range(9):
                bi = (tq * 9 + e) % 2
                war_w = ([wuse[(tq * 9 + e) - 2]] if (tq * 9 + e) - 2 in wuse else [mk.c("ts")])
                tg = tu = None
                if tq == 0 or True:
                    for kc in range(KC):
                        tg = mk.dma("g", wg_b[bi][:, kc, :],
                                    dt_in["wg9"][e, kc * 128:(kc + 1) * 128, :],
                                    waits=(war_w if kc == 0 else []), ch=7 + bi)
                    for kc in range(KC):
                        tu = mk.dma("g", wu_b[bi][:, kc, :],
                                    dt_in["wu9"][e, kc * 128:(kc + 1) * 128, :], ch=9 + bi)
                tok_cb = None
                if e < 8:
                    war_cb = ([cb_use[bi]] if bi in cb_use else
                              ([tok_comb] if tq == 0 and e == 0 else []))
                    tok_cb = mk.dma("g", cmb_b[bi], scr["cmbs"][e, tsl].partition_broadcast(128),
                                    waits=war_cb, ch=12 + bi)
                ei = tq * 9 + e
                gsb = gsb2[ei % 2]
                for mc in range(4):
                    sg = ei * 4 + mc
                    bank = g_banks[sg % 2]
                    war_bank = ([sg_done[sg - 2]] if sg - 2 in sg_done else [])
                    for kc in range(KC):
                        mk.op("tensor", lambda E, mc=mc, kc=kc, bi=bi, tsl=tsl, bank=bank:
                              E.matmul(bank[:, 0:512], wg_b[bi][:, kc, mc * 128:mc * 128 + 128],
                                       rhsm[:, kc, tsl], start=(kc == 0), stop=(kc == KC - 1)),
                              waits=(([tg] + war_bank
                                      + ([v_rhsm] if ei == 0 and mc == 0 else [])
                                      + ([mk.c("vs")] if e == 0 and mc == 0 else []))
                                     if kc == 0 else []),
                              incs=[("ts", 1)])
                    t_gm = mk.c("ts")
                    war_gsb = ([su_done[(ei - 2) * 4 + mc]] if ei >= 2 else [])
                    mk.op("scalar", lambda E, mc=mc, bank=bank, gsb=gsb:
                          E.activation(gsb[:, mc, :], bank[:, 0:512], AF.Silu),
                          waits=[t_gm] + war_gsb, incs=[("ss", 1)])
                    sg_done[sg] = mk.c("ss")
                for mc in range(4):
                    su = ei * 4 + mc
                    bank = u_banks[su % 2]
                    war_bank = ([su_done[su - 2]] if su - 2 in su_done else [])
                    for kc in range(KC):
                        mk.op("tensor", lambda E, mc=mc, kc=kc, bi=bi, tsl=tsl, bank=bank:
                              E.matmul(bank[:, 0:512], wu_b[bi][:, kc, mc * 128:mc * 128 + 128],
                                       rhsm[:, kc, tsl], start=(kc == 0), stop=(kc == KC - 1)),
                              waits=(([tu] + war_bank
                                      + ([mk.c("vs")] if e == 0 and mc == 0 else []))
                                     if kc == 0 else []),
                              incs=[("ts", 1)])
                    t_um = mk.c("ts")
                    mk.op("vector", lambda E, mc=mc, bank=bank, gsb=gsb:
                          E.scalar_tensor_tensor(gu_bf, bank[:, 0:512], 1.0, gsb[:, mc, :],
                                                 ALU.mult, ALU.mult),
                          waits=[t_um, sg_done[ei * 4 + mc]], incs=[("vs", 1)])
                    su_done[su] = mk.c("vs")
                    ch = 4 * e + mc
                    if e < 8:
                        mk.op("vector", lambda E, ch=ch, bi=bi:
                              E.tensor_mul(h_all[:, ch, :], gu_bf, cmb_b[bi]),
                              waits=([tok_cb] if mc == 0 else []), incs=[("vs", 1)])
                    else:
                        mk.op("vector", lambda E, ch=ch:
                              E.tensor_copy(h_all[:, ch, :], gu_bf), incs=[("vs", 1)])
                if e < 8:
                    cb_use[bi] = mk.c("vs")
                wuse[tq * 9 + e] = mk.c("ts")
            v_hall = mk.c("vs")
            wdtok = {}
            for kc in range(36):
                wdb = wd_b[kc % 2]
                war_d = ([wdtok[kc - 2]] if kc - 2 in wdtok else [mk.c("ts")])
                td = mk.dma("g", wdb, wd_flat[kc * 128:(kc + 1) * 128, :],
                            waits=war_d, ch=14 + kc % 2)
                for mc in range(8):
                    mk.op("tensor", lambda E, kc=kc, mc=mc, wdb=wdb:
                          E.matmul(pb[mc][:, 0:512], wdb[:, mc * 128:mc * 128 + 128],
                                   h_all[:, kc, :], start=(kc == 0), stop=(kc == 35)),
                          waits=(([td] + ([v_hall, mk.c("ss")] if kc == 0 else []))
                                 if mc == 0 else []),
                          incs=[("ts", 1)])
                wdtok[kc] = mk.c("ts")
            t_rt = mk.c("ts")
            war_x2 = ([bo2tok[tq - 1]] if tq - 1 in bo2tok else [])
            tok_x2r = mk.dma("s", x2rl.rearrange("p (k m) -> p k m", k=KC),
                             scr["x2sp"][:, :, tsl], waits=war_x2, ch=6)
            bo2k = bo2c.rearrange("p (k m) -> p k m", k=KC)
            x2rk = x2rl.rearrange("p (k m) -> p k m", k=KC)
            for mc in range(8):
                mk.op("vector", lambda E, mc=mc, bo2k=bo2k, x2rk=x2rk:
                      E.tensor_add(bo2k[:, mc, :], pb[mc][:, 0:512], x2rk[:, mc, :]),
                      waits=[t_rt] + ([tok_x2r] if mc == 0 else []), incs=[("vs", 1)])
            v_bo2 = mk.c("vs")
            bo2tok[tq] = mk.dma("g", scr["bo2sp"][:, :, tsl], bo2k, waits=[v_bo2], ch=11)
        mk.barrier()

        stgf = stats(lambda kc: scr["bo2sp"][:, kc, :], T)
        bc_r7 = AR_B[:, 0:T]
        bc_mr7 = AR_B[:, 2080:2080 + T]
        tb_r7 = bcast(bc_r7, 0, T, stgf["tok_r"])
        tb_mr7 = bcast(bc_mr7, 2, T, stgf["tok_mr"])
        boc2 = [AR_C[:, 0:4096], AR_C[:, 4096:8192]]
        x1f = AR_C[:, 8192:12288]
        ftok = {}
        for ft in range(4):
            bc2 = boc2[ft % 2]
            war = ([ftok[ft - 2]] if ft - 2 in ftok else [])
            warx = ([ftok[ft - 1]] if ft - 1 in ftok else [])
            tok_b = mk.dma("s", bc2.rearrange("p (k m) -> p k m", k=KC),
                           scr["bo2sp"][:, :, ft * 512:(ft + 1) * 512],
                           waits=war, ch=6 + ft % 2)
            tok_x1f = mk.dma("s", x1f.rearrange("p (k m) -> p k m", k=KC),
                             scr["x1own"][:, :, ft * 512:(ft + 1) * 512],
                             waits=warx, ch=14)
            b2k = bc2.rearrange("p (k m) -> p k m", k=KC)
            x1fk = x1f.rearrange("p (k m) -> p k m", k=KC)
            sl = slice(ft * 512, (ft + 1) * 512)
            for kc in range(KC):
                mk.op("vector", lambda E, b2k=b2k, kc=kc, sl=sl:
                      E.tensor_mul(b2k[:, kc, :], b2k[:, kc, :], bc_r7[:, sl]),
                      waits=([tok_b, tb_r7] if kc == 0 else []), incs=[("vs", 1)])
                mk.op("vector", lambda E, b2k=b2k, kc=kc, sl=sl:
                      E.tensor_sub(b2k[:, kc, :], b2k[:, kc, :], bc_mr7[:, sl]),
                      waits=([tb_mr7] if kc == 0 else []), incs=[("vs", 1)])
                mk.op("vector", lambda E, b2k=b2k, x1fk=x1fk, kc=kc:
                      E.tensor_add(b2k[:, kc, :], b2k[:, kc, :], x1fk[:, kc, :]),
                      waits=([tok_x1f] if kc == 0 else []), incs=[("vs", 1)])
            v_f = mk.c("vs")
            ftok[ft] = mk.dma("s", outp[:, :, ft * 512:(ft + 1) * 512], b2k,
                              waits=[v_f], ch=0)

    if devtap == "dbg_misc":
        mk.barrier()
        mk.dma("g", scr["x2sp"][:, 0, :], MISC[:, 0:2048], ch=6)
        mk.dma("g", scr["x2sp"][:, 1, :], AR_B[:, 0:2048], ch=6)
        devtap = "x2sp"
    if devtap in ("dbg_sp", "dbg_z", "dbg_scan", "dbg_ysd"):
        mk.barrier()
        if devtap == "dbg_sp":
            for kc in range(KC):
                mk.dma("g", scr["x2sp"][:, kc, :], sp_bf[:, kc, :], ch=6)
        elif devtap == "dbg_z":
            for kc in range(KC):
                mk.dma("g", scr["x2sp"][:, kc, :], z_f16[:, kc, :], ch=6)
        elif devtap == "dbg_scan":
            mk.dma("g", scr["x2sp"][:, 0, 0:512], u64q, ch=6)
            mk.dma("g", scr["x2sp"][:, 1, 0:512], hsq, ch=6)
            mk.dma("g", scr["x2sp"][:, 2, 0:512], dk_rep, ch=6)
            mk.dma("g", scr["x2sp"][:, 3, 0:512], p512, ch=6)
            mk.dma("g", scr["x2sp"][:, 4, 0:1024], hsr_bf, ch=6)
        elif devtap == "dbg_ysd":
            for kc in range(KC):
                mk.dma("g", scr["x2sp"][:, kc, :], ysd_bf[:, kc, :], ch=6)
        devtap = "x2sp"

    mk.barrier()
    if devtap == "dbg_att" or (isinstance(devtap, str) and devtap.startswith("dbg_attk")):
        devtap = None
    if devtap or stage < 3:
        src = scr[devtap] if devtap else (scr["xmid"] if stage == 1 else scr["x1own"])
        for kc in range(KC):
            mk.dma("s", outp[:, kc, :], src[:, kc, :], ch=0)
    mk.wait("sync", [mk.c("ds0")])
    mk.wait("gpsimd", [mk.c("ds0")])
    return mk.build()


_PROG_CACHE = {}


def get_program(stage=STAGE, devtap=None):
    key = (stage, devtap)
    if key not in _PROG_CACHE:
        _PROG_CACHE[key] = build_program(stage, devtap)
    return _PROG_CACHE[key]


def kernel(**inputs):
    in_maps = prep_inputs(inputs)
    nc = get_program()
    res = run_bass_kernel_spmd(nc, in_maps, core_ids=list(range(NCORES)))
    out = np.zeros((B, S, D), np.float32)
    for c in range(NCORES):
        b0, par = c // 2, c & 1
        o = np.asarray(res.results[c]["outp"], np.float32)
        o = o.transpose(1, 0, 2).reshape(D, T)
        for j in range(16):
            g = 2 * j + par
            out[b0, g * 128:(g + 1) * 128, :] = o[:, j * 128:(j + 1) * 128].T
    return out


# version 06
# speedup vs baseline: 1.2803x; 1.0215x over previous
import os

import numpy as np
import ml_dtypes

import concourse.bass as bass
import concourse.mybir as mybir
from concourse.bass_utils import run_bass_kernel_spmd

F32 = mybir.dt.float32
BF16 = mybir.dt.bfloat16
F16 = mybir.dt.float16
AF = mybir.ActivationFunctionType
ALU = mybir.AluOpType
AXX = mybir.AxisListType.X
BNP = ml_dtypes.bfloat16

B, S, D = 4, 4096, 1024
T = 2048
NCORES = 8
KC = 8
EPS = 1e-5
STAGE = 3

SMCOL = {}
_nxt = [0]
for _n, _c in [("ln1_w", 8), ("ln1_b", 8), ("lns_w", 8), ("lns_b", 8),
               ("conv_b", 8), ("cw0", 8), ("cw1", 8), ("cw2", 8), ("cw3", 8),
               ("in_b", 16), ("out_b", 8), ("ln2_w", 8), ("ln2_b", 8),
               ("mlp_b1", 16), ("mlp_b2", 8), ("w_ln1", 8), ("w_ln2", 8),
               ("xp_b", 1), ("A_log", 1), ("ebias", 1)]:
    SMCOL[_n] = (_nxt[0], _c)
    _nxt[0] += _c
NSMALL = _nxt[0]


def _pack_small(dst, name, vec):
    c0, ncol = SMCOL[name]
    v = np.asarray(vec, np.float32).reshape(-1)
    if ncol == 1:
        dst[:len(v), c0] = v
    else:
        dst[:, c0:c0 + ncol] = v.reshape(ncol, 128).T


def prep_inputs(inp):
    x = np.asarray(inp["L0_attn_x"], np.float32)
    w = {k[8:]: np.asarray(v, np.float32) for k, v in inp.items() if k.startswith("L0_attn_")}
    wf = {k[7:]: np.asarray(v, np.float32) for k, v in inp.items() if k.startswith("L0_ffn_")}

    smalls = np.zeros((128, NSMALL), np.float32)
    for nm in ["ln1_w", "ln1_b", "lns_w", "lns_b", "conv_b", "in_b", "out_b",
               "ln2_w", "ln2_b", "mlp_b1", "mlp_b2", "xp_b", "A_log"]:
        _pack_small(smalls, nm, w[nm])
    for k in range(4):
        _pack_small(smalls, f"cw{k}", w["conv_w"][:, 0, k])
    _pack_small(smalls, "w_ln1", wf["w_ln1"])
    _pack_small(smalls, "w_ln2", wf["w_ln2"])
    _pack_small(smalls, "ebias", wf["expert_bias"])

    tpos = np.arange(S, dtype=np.float32)
    freqs = 1.0 / (100000.0 ** (np.arange(0, 32, 2, dtype=np.float32) / 32))
    ang = np.outer(freqs, tpos)
    cos, sin = np.cos(ang), np.sin(ang)
    csA = np.zeros((32, S), np.float32)
    csB = np.zeros((32, S), np.float32)
    csA[0::2] = cos
    csA[1::2] = cos
    csB[0::2] = -sin
    csB[1::2] = sin

    wg9 = np.ascontiguousarray(np.concatenate([wf["Wg"], wf["sWg"]], 0))
    wu9 = np.ascontiguousarray(np.concatenate([wf["Wu"], wf["sWu"]], 0))
    wd9 = np.ascontiguousarray(np.concatenate([wf["Wd"], wf["sWd"]], 0))
    rows_eff = np.concatenate([np.arange(h * 64, h * 64 + 32) for h in range(16)])
    wo_eff = np.ascontiguousarray(wf["Wo"][rows_eff])

    triu = np.triu(np.ones((128, 128), np.float32))
    zeros = np.zeros((128, 128), np.float32)
    onesm = np.ones((128, 128), np.float32)

    in_maps = []
    for c in range(NCORES):
        b0, par = c // 2, c & 1
        s0, lo = par * T, c * 512
        pad3 = np.zeros((3, D), np.float32)
        xo = np.concatenate([x[b0, s0 - 3:s0] if s0 >= 3 else pad3,
                             x[b0, s0:s0 + T]], 0).T
        x_own = np.zeros((D, 2060), np.float32)
        x_own[:, :2051] = xo
        if lo >= 3:
            xs = x[:, lo - 3:lo + 512]
        else:
            xs = np.concatenate([np.zeros((B, 3, D), np.float32), x[:, 0:512]], 1)
        x_scan = np.ascontiguousarray(xs.transpose(2, 1, 0)).reshape(D, 2060)

        cpw = np.broadcast_to(
            np.array([(c - 1 - j) if j < c else 1e6 for j in range(8)], np.float32),
            (128, 8)).copy()
        bmask = np.ones((128, 16), np.float32)
        if lo == 0:
            bmask[:, 0:12] = 0.0
        if s0 == 0:
            bmask[:, 12:15] = 0.0
        qm = np.empty((128, 32 * 128), np.float32)
        for kb in range(32):
            if par == 0:
                blk = triu if (kb % 2 == 0) else zeros
            else:
                blk = onesm if (kb % 2 == 0) else triu
            qm[:, kb * 128:(kb + 1) * 128] = blk
        own_cols = np.concatenate(
            [np.arange((2 * j + par) * 128, (2 * j + par + 1) * 128) for j in range(16)])

        parsel = np.zeros((128, 2), np.float32)
        parsel[:, par] = 1.0
        in_maps.append(dict(
            parsel=parsel,
            x_own=x_own, x_scan=x_scan,
            in_w=w["in_w"], xp_w=np.ascontiguousarray(w["xp_w"]),
            dmat=np.ascontiguousarray(w["D"]), out_w=w["out_w"],
            mlp_w1=w["mlp_w1"], mlp_w2=w["mlp_w2"],
            wq=wf["Wq"], wkva=np.ascontiguousarray(wf["Wkva"]),
            wkvb=np.ascontiguousarray(wf["Wkvb"]), wo_eff=wo_eff,
            wgate=np.ascontiguousarray(wf["Wgate"]),
            wg9=wg9, wu9=wu9, wd9=wd9,
            smalls=smalls, cpw=cpw, bmask=bmask,
            qmask=qm.astype(BNP),
            csq_a=np.ascontiguousarray(csA[:, own_cols]).astype(BNP),
            csq_b=np.ascontiguousarray(csB[:, own_cols]).astype(BNP),
            csk_a=csA.astype(BNP), csk_b=csB.astype(BNP),
        ))
    return in_maps


class MK:
    ENGS = ("tensor", "vector", "scalar", "gpsimd", "sync")

    def __init__(self):
        self.nc = bass.Bass("TRN2", target_bir_lowering=False,
                            detect_race_conditions=bool(os.environ.get("MKRACE")))
        self.ops = {e: [] for e in self.ENGS}
        self.cnt = {}
        self.sems = {}
        self._semname = []

    def newsem(self, name):
        self.cnt[name] = 0
        self._semname.append(name)

    def c(self, name):
        return (name, self.cnt.get(name, 0))

    ESEM = {"tensor": "ts", "vector": "vs", "scalar": "ss"}

    def op(self, eng, fn, waits=(), incs=()):
        waits = list(waits)
        es = self.ESEM.get(eng)
        if es is not None and self.cnt.get(es, 0) > 0 and fn.__name__ != "<nowait>":
            waits.append((es, self.cnt[es]))
        for s, k in incs:
            self.cnt[s] += k
        waits = tuple((s, v) for (s, v) in waits if v > 0)
        incs = tuple(incs)

        def run(E, fn=fn, waits=waits, incs=incs):
            for s, v in waits:
                E.wait_ge(self.sems[s], v)
            inst = fn(E)
            for s, k in incs:
                inst.then_inc(self.sems[s], k)
        self.ops[eng].append(run)
        return [(s, self.cnt[s]) for s, _ in incs]

    def dma(self, q, out, in_, waits=(), ch=0, **kw):
        eng = {"g": "gpsimd", "s": "sync"}[q]
        sem = f"d{q}{ch}"
        if sem not in self.cnt:
            self.newsem(sem)
        return self.op(eng, lambda E, o=out, i=in_, kw=kw: E.dma_start(out=o, in_=i, **kw),
                       waits=waits, incs=[(sem, 16)])[0]

    def wait(self, eng, waits):
        self.op(eng, lambda E: None, waits=waits)

    def barrier(self, engines=None):
        engines = engines or self.ENGS
        dmas = [(s, self.cnt[s]) for s in self.cnt if s.startswith("d") and self.cnt[s] > 0]
        bs = f"bar{len(self._semname)}"
        self.newsem(bs)
        n = len(engines)
        for e in engines:
            self.op(e, lambda E: E.nop(), waits=dmas, incs=[(bs, 1)])
        for e in engines:
            self.wait(e, [(bs, n)])

    def build(self):
        nc = self.nc
        import contextlib
        stack = contextlib.ExitStack()
        for name in self._semname:
            self.sems[name] = stack.enter_context(nc.semaphore(name))
        blk = stack.enter_context(nc.Block())
        for eng in self.ENGS:
            ops = self.ops[eng]
            getattr(blk, eng)(lambda E, ops=ops: [f(E) for f in ops])
        stack.close()
        return nc


def build_program(stage=STAGE, devtap=None):
    mk = MK()
    nc = mk.nc
    for s in ("ts", "vs", "ss", "cc"):
        mk.newsem(s)

    dt_in = {}
    for name, shape, dt in [
        ("x_own", [D, 2060], F32), ("x_scan", [D, 2060], F32),
        ("in_w", [D, 2048], F32), ("xp_w", [D, 16], F32), ("dmat", [16, D], F32),
        ("out_w", [D, D], F32), ("mlp_w1", [D, 2048], F32), ("mlp_w2", [2048, D], F32),
        ("wq", [D, D], F32), ("wkva", [D, 160], F32), ("wkvb", [128, 512], F32),
        ("wo_eff", [512, D], F32), ("wgate", [D, 8], F32),
        ("wg9", [9, D, 512], F32), ("wu9", [9, D, 512], F32), ("wd9", [9, 512, D], F32),
        ("smalls", [128, NSMALL], F32),
        ("cpw", [128, 8], F32), ("bmask", [128, 16], F32),
        ("parsel", [128, 2], F32),
        ("qmask", [128, 32 * 128], BF16),
        ("csq_a", [32, T], BF16), ("csq_b", [32, T], BF16),
        ("csk_a", [32, S], BF16), ("csk_b", [32, S], BF16),
    ]:
        dt_in[name] = nc.dram_tensor(name, shape, dt, kind="ExternalInput").ap()

    outp = nc.dram_tensor("outp", [128, KC, T], F32, kind="ExternalOutput").ap()

    scr = {}
    for name, shape, dt in [
        ("rows_scr", [8, 4160], F32),
        ("esnd", [128], F32), ("ercv", [8, 128], F32),
        ("xsnd", [128, 16, T], BF16), ("xrcv", [2, 128, 16, T], BF16),
        ("xbsp", [128, KC, T], F32), ("xbscan", [128, KC, T], F32),
        ("xmid", [128, KC, T], F32), ("x1own", [128, KC, T], F32),
        ("x2sp", [128, KC, T], F32), ("bo2sp", [128, KC, T], F32),
        ("cmbs", [8, T], F32),
        ("ysp", [128, 4, T], BF16),
    ]:
        scr[name] = nc.dram_tensor(name, shape, dt).ap()

    AR_A = nc.alloc_sbuf_tensor("AR_A", [128, 10304], F32).ap()
    AR_B = nc.alloc_sbuf_tensor("AR_B", [128, 10240], F32).ap()
    AR_C = nc.alloc_sbuf_tensor("AR_C", [128, 12288], F32).ap()
    WS = nc.alloc_sbuf_tensor("WS", [128, 9216], F32).ap()
    ROWS = nc.alloc_sbuf_tensor("ROWS", [128, 6180], F32).ap()
    SM = nc.alloc_sbuf_tensor("SM", [128, NSMALL], F32).ap()
    CPW = nc.alloc_sbuf_tensor("CPW", [128, 8], F32).ap()
    BMASK = nc.alloc_sbuf_tensor("BMASK", [128, 16], F32).ap()
    PARSEL = nc.alloc_sbuf_tensor("PARSEL", [128, 2], F32).ap()
    MISC = nc.alloc_sbuf_tensor("MISC", [128, 2176], F32).ap()
    SC = nc.alloc_sbuf_tensor("SC", [128, 2048], F32).ap()

    pb = [nc.alloc_psum_tensor(f"pb{i}", [128, 512], F32).ap() for i in range(8)]

    def smcol(name, k=0, rows=128):
        c0, _ = SMCOL[name]
        return SM[0:rows, c0 + k:c0 + k + 1]

    ones_bf = MISC[:, 0:4].bitcast(BF16)[:, 0:1]
    gcol_bf = MISC[:, 4:8].bitcast(BF16)
    bcol_bf = MISC[:, 8:12].bitcast(BF16)
    hw_col = MISC[:, 16:32]
    xpw_bf = MISC[:, 32:96].bitcast(BF16)
    ld64 = MISC[:, 96:97]
    ld512 = MISC[:, 97:98]
    w8t = MISC[:, 98:106]
    carry = MISC[:, 106:107]
    ecol = MISC[:, 107:108]
    ca8 = MISC[:, 108:116]
    xpbq = MISC[:, 116:117]
    eps_col = MISC[:, 117:118]
    eps2_col = MISC[:, 118:119]
    mrow_bf = MISC[0:1, 128:1152].bitcast(BF16)
    gw_bf = MISC[0:1, 1152:2176].bitcast(BF16)
    hsr_bf = MISC[32:48, 128:1152].bitcast(BF16)
    dmat_bf = MISC[32:48, 1152:1664].bitcast(BF16)

    u64q = SC[:, 0:512]
    hsq = SC[:, 512:1024]
    dk_rep = SC[:, 1024:1536]
    p512 = SC[:, 1536:2048]

    def rslot(i, n=2060):
        return ROWS[0:1, i * 2060:i * 2060 + n]

    mk.dma("s", SM[:], dt_in["smalls"], ch=0)
    mk.dma("s", CPW[:], dt_in["cpw"], ch=0)
    mk.dma("s", PARSEL[:], dt_in["parsel"], ch=0)
    t_fix = mk.dma("s", BMASK[:], dt_in["bmask"], ch=0)
    mk.op("vector", lambda E: E.memset(MISC[:, 96:128], 0.0), incs=[("vs", 1)])
    mk.op("vector", lambda E: E.memset(ones_bf, 1.0), incs=[("vs", 1)])
    mk.op("vector", lambda E: E.memset(eps_col, EPS), incs=[("vs", 1)])
    mk.op("vector", lambda E: E.memset(eps2_col, EPS * EPS), incs=[("vs", 1)])
    v_ones = mk.c("vs")

    def stats(src_dram_fn, ncols, with_mean=True, fused2=False, extra_wait=(),
              ship_row0=0, ship_off=0, tbuf=None, tbase=0, psum_wait=None):
        s1, s2, t2 = rslot(0, ncols), rslot(1, ncols), rslot(2, ncols)
        ta = AR_A if tbuf is None else tbuf
        xbf = [ta[:, tbase + 0:tbase + 1030].bitcast(BF16),
               ta[:, tbase + 1030:tbase + 2060].bitcast(BF16)]
        sqb2 = [ta[:, tbase + 2060:tbase + 3090].bitcast(BF16),
                ta[:, tbase + 3090:tbase + 4120].bitcast(BF16)]
        if psum_wait is None:
            psum_wait = [mk.c("vs")]
        if hasattr(mk, "_stats_prev"):
            ent_ts, ent_ss, ent_vs = mk._stats_prev
        else:
            ent_ts, ent_ss, ent_vs = mk.c("ts"), mk.c("ss"), mk.c("vs")
        ranges = [(0, min(ncols, 2048))]
        if ncols > 2048:
            ranges.append((2048, ncols - 2048))
        read_done = {}
        sq_done = {}
        for (c0, cn) in ranges:
            FCH = [(f0, min(512, cn - f0)) for f0 in range(0, cn, 512)]
            for kc in range(KC):
                bb = xbf[kc % 2][:, 0:cn]
                war = ([read_done[kc - 2]] if kc - 2 in read_done else [])
                tok = mk.dma("g", bb, src_dram_fn(kc)[:, c0:c0 + cn], ch=kc % 2,
                             waits=war
                             + ([ent_ts, ent_ss, ent_vs] if (kc <= 1 and c0 == 0) else [])
                             + (list(extra_wait) if (kc <= 1 and c0 == 0) else []))
                sqb = sqb2[kc % 2]
                if with_mean:
                    for fi, (f0, fw) in enumerate(FCH):
                        mk.op("tensor", lambda E, bb=bb, f0=f0, fw=fw, fi=fi, kc=kc:
                              E.matmul(pb[4 + fi][0:1, 0:fw], ones_bf, bb[:, f0:f0 + fw],
                                       start=(kc == 0), stop=(kc == KC - 1)),
                              waits=([tok, v_ones, ent_vs] + psum_wait if kc == 0 else [tok]),
                              incs=[("ts", 1)])
                war_s = ([sq_done[kc - 2]] if kc - 2 in sq_done else [])
                mk.op("scalar", lambda E, bb=bb, cn=cn, sqb=sqb:
                      E.activation(sqb[:, 0:cn], bb, AF.Square),
                      waits=[tok] + war_s, incs=[("ss", 1)])
                s_sq = mk.c("ss")
                for fi, (f0, fw) in enumerate(FCH):
                    mk.op("tensor", lambda E, f0=f0, fw=fw, fi=fi, kc=kc, sqb=sqb:
                          E.matmul(pb[fi][0:1, 0:fw], ones_bf, sqb[:, f0:f0 + fw],
                                   start=(kc == 0), stop=(kc == KC - 1)),
                          waits=([s_sq, v_ones, ent_vs] + psum_wait if kc == 0 else [s_sq]),
                          incs=[("ts", 1)])
                sq_done[kc] = mk.c("ts")
                read_done[kc] = mk.c("ts")
            read_done, sq_done = {}, {}
            t_stats = mk.c("ts")
            prev_ships = getattr(mk, "_stats_ships", [])
            first_evac = True
            for fi, (f0, fw) in enumerate(FCH):
                if with_mean:
                    mk.op("vector", lambda E, f0=f0, fw=fw, fi=fi, c0=c0:
                          E.tensor_copy(s1[:, c0 + f0:c0 + f0 + fw], pb[4 + fi][0:1, 0:fw]),
                          waits=[t_stats] + (list(prev_ships) if first_evac else []),
                          incs=[("vs", 1)])
                    first_evac = False
                mk.op("vector", lambda E, f0=f0, fw=fw, fi=fi, c0=c0:
                      E.tensor_copy(s2[:, c0 + f0:c0 + f0 + fw], pb[fi][0:1, 0:fw]),
                      waits=[t_stats] + (list(prev_ships) if first_evac else []),
                      incs=[("vs", 1)])
                first_evac = False
            if len(ranges) > 1 and c0 == 0:
                mk.wait("gpsimd", [mk.c("vs")])
        c1 = 1.0 / 1024.0
        out = {}
        if with_mean:
            mk.op("vector", lambda E: E.tensor_scalar_mul(s1, s1, c1), incs=[("vs", 1)])
            mk.op("vector", lambda E: E.tensor_mul(t2, s1, s1), incs=[("vs", 1)])
            mk.op("vector", lambda E: E.scalar_tensor_tensor(s2, s2, c1, t2, ALU.mult, ALU.subtract),
                  incs=[("vs", 1)])
        else:
            mk.op("vector", lambda E: E.tensor_scalar_mul(s2, s2, c1), incs=[("vs", 1)])
        v_v = mk.c("vs")
        if fused2:
            mk.op("scalar", lambda E: E.activation(t2, s2, AF.Ln, bias=eps2_col[0:1, :],
                                                   scale=1.0 + EPS),
                  waits=[v_v], incs=[("ss", 1)])
            mk.op("scalar", lambda E: E.activation(t2, t2, AF.Exp, scale=-0.5),
                  incs=[("ss", 1)])
        mk.op("scalar", lambda E: E.activation(s2, s2, AF.Ln, bias=eps_col[0:1, :]),
              waits=[v_v], incs=[("ss", 1)])
        mk.op("scalar", lambda E: E.activation(s2, s2, AF.Exp, scale=-0.5),
              incs=[("ss", 1)])
        s_r = mk.c("ss")
        out["tok_r"] = mk.dma("s", scr["rows_scr"][ship_row0 + 0, ship_off:ship_off + ncols], s2, waits=[s_r], ch=1)
        if fused2:
            out["tok_g"] = mk.dma("s", scr["rows_scr"][ship_row0 + 1, ship_off:ship_off + ncols], t2, waits=[s_r], ch=2)
        if with_mean:
            mk.op("vector", lambda E: E.tensor_mul(s2, s1, s2), waits=[out["tok_r"], s_r],
                  incs=[("vs", 1)])
            v_mr = mk.c("vs")
            out["tok_mr"] = mk.dma("s", scr["rows_scr"][ship_row0 + 2, ship_off:ship_off + ncols], s2, waits=[v_mr], ch=3)
            if fused2:
                mk.op("vector", lambda E: E.tensor_mul(t2, s1, t2),
                      waits=[out["tok_g"]], incs=[("vs", 1)])
                v_mg = mk.c("vs")
                out["tok_mg"] = mk.dma("s", scr["rows_scr"][ship_row0 + 3, ship_off:ship_off + ncols], t2, waits=[v_mg], ch=4)
        mk._stats_prev = (mk.c("ts"), mk.c("ss"), mk.c("vs"))
        mk._stats_ships = list(out.values())
        return out

    def bcast(dst, slot, ncols, tok, q="g", off=0):
        return mk.dma(q, dst, scr["rows_scr"][slot, off:off + ncols].partition_broadcast(128),
                      waits=[tok], ch=2 + slot % 4)

    def load_mrow(tok_mr, ncols=T, slot=2):
        mrl = rslot(2, ncols)
        tok = mk.dma("s", mrl, scr["rows_scr"][slot:slot + 1, 0:ncols], waits=[tok_mr], ch=5)
        mk.op("vector", lambda E: E.tensor_scalar_mul(mrow_bf[:, 0:ncols], mrl, -1.0),
              waits=[tok], incs=[("vs", 1)])
        mk._stats_ships = getattr(mk, "_stats_ships", []) + [tok]
        return mk.c("vs")

    def gw_hw(w_bf, nout, gamma_name, beta_name, bias_name, w_tok):
        c0 = SMCOL[gamma_name][0]
        mk.op("vector", lambda E, c0=c0: E.tensor_copy(gcol_bf, SM[:, c0:c0 + 8]),
              waits=[t_fix], incs=[("vs", 1)])
        c0b = SMCOL[beta_name][0]
        mk.op("vector", lambda E, c0b=c0b: E.tensor_copy(bcol_bf, SM[:, c0b:c0b + 8]),
              incs=[("vs", 1)])
        v_cols = mk.c("vs")
        for f0 in range(0, nout, 512):
            for kc in range(KC):
                mk.op("tensor", lambda E, f0=f0, kc=kc:
                      E.matmul(pb[5][0:1, 0:512], gcol_bf[:, kc:kc + 1],
                               w_bf[:, kc, f0:f0 + 512], start=(kc == 0), stop=(kc == KC - 1)),
                      waits=([v_cols, w_tok, mk.c("vs")] if kc == 0 else []), incs=[("ts", 1)])
            t_gw = mk.c("ts")
            mk.op("vector", lambda E, f0=f0: E.tensor_copy(gw_bf[:, f0:f0 + 512], pb[5][0:1, 0:512]),
                  waits=[t_gw], incs=[("vs", 1)])
        v_gw = mk.c("vs")
        for mc in range(nout // 128):
            for kc in range(KC):
                mk.op("tensor", lambda E, mc=mc, kc=kc:
                      E.matmul(pb[4][:, 0:1], w_bf[:, kc, mc * 128:mc * 128 + 128],
                               bcol_bf[:, kc:kc + 1], start=(kc == 0), stop=(kc == KC - 1)),
                      waits=([v_cols, w_tok, mk.c("vs")] if kc == 0 else []),
                      incs=[("ts", 1)])
            t_hw = mk.c("ts")
            mk.op("vector", lambda E, mc=mc:
                  E.tensor_add(hw_col[:, mc:mc + 1], pb[4][:, 0:1], smcol(bias_name, mc)),
                  waits=[t_hw], incs=[("vs", 1)])
        return v_gw, mk.c("vs")

    def m_chain(which):
        xsrc = dt_in["x_own" if which == "own" else "x_scan"]
        spill = scr["xbsp" if which == "own" else "xbscan"]
        nb = 3 if which == "own" else 12
        shift = 1 if which == "own" else 4
        ncols = T + nb
        bm = BMASK[:, 12:12 + nb] if which == "own" else BMASK[:, 0:nb]

        st = stats(lambda kc: xsrc[kc * 128:(kc + 1) * 128, :], ncols,
                   with_mean=True, fused2=True)
        bc_r = AR_B[:, 0:ncols]
        bc_g = AR_B[:, 2080:2080 + ncols]
        bc_mr = AR_B[:, 4160:4160 + ncols]
        bc_mg = AR_B[:, 6240:6240 + ncols]
        tb_r = bcast(bc_r, 0, ncols, st["tok_r"])
        tb_g = bcast(bc_g, 1, ncols, st["tok_g"])
        tb_mr = bcast(bc_mr, 2, ncols, st["tok_mr"])
        tb_mg = bcast(bc_mg, 3, ncols, st["tok_mg"])

        spill_toks, chain_toks = {}, {}
        spill0 = mk.c("dg6")
        for kc in range(KC):
            xs_c = AR_A[:, (kc % 2) * 2064:(kc % 2) * 2064 + ncols]
            xn_c = AR_A[:, 4128 + (kc % 2) * 2064:4128 + (kc % 2) * 2064 + ncols]
            xb_c = AR_A[:, 8256:8256 + T]
            war = ([chain_toks[kc - 2]] if kc - 2 in chain_toks else [tb_mg])
            tok_x = mk.dma("s", xs_c, xsrc[kc * 128:(kc + 1) * 128, 0:ncols], waits=war, ch=6 + kc % 2)
            mk.op("vector", lambda E, xn_c=xn_c, xs_c=xs_c: E.tensor_mul(xn_c, xs_c, bc_g),
                  waits=[tok_x] + ([tb_g] if kc == 0 else []), incs=[("vs", 1)])
            mk.op("vector", lambda E, xn_c=xn_c: E.tensor_sub(xn_c, xn_c, bc_mg),
                  incs=[("vs", 1)])
            mk.op("vector", lambda E, xn_c=xn_c, kc=kc:
                  E.tensor_scalar(xn_c, xn_c, smcol("ln1_w", kc), smcol("ln1_b", kc),
                                  ALU.mult, ALU.add),
                  waits=([t_fix] if kc == 0 else []), incs=[("vs", 1)])
            mk.op("vector", lambda E, xn_c=xn_c, bm=bm:
                  E.tensor_mul(xn_c[:, 0:nb], xn_c[:, 0:nb], bm), incs=[("vs", 1)])
            warb = ([spill_toks[kc - 1]] if kc - 1 in spill_toks else [spill0])
            mk.op("vector", lambda E, xb_c=xb_c, xs_c=xs_c, nb=nb:
                  E.tensor_mul(xb_c, xs_c[:, nb:nb + T], bc_r[:, nb:nb + T]),
                  waits=([tb_r] if kc == 0 else []) + warb, incs=[("vs", 1)])
            mk.op("vector", lambda E, xb_c=xb_c, nb=nb:
                  E.tensor_sub(xb_c, xb_c, bc_mr[:, nb:nb + T]),
                  waits=([tb_mr] if kc == 0 else []), incs=[("vs", 1)])
            for k in range(4):
                mk.op("vector", lambda E, xb_c=xb_c, xn_c=xn_c, k=k, kc=kc, shift=shift:
                      E.scalar_tensor_tensor(xb_c, xn_c[:, k * shift:k * shift + T],
                                             smcol(f"cw{k}", kc), xb_c, ALU.mult, ALU.add),
                      incs=[("vs", 1)])
            mk.op("vector", lambda E, xb_c=xb_c, kc=kc:
                  E.tensor_scalar_add(xb_c, xb_c, smcol("conv_b", kc)), incs=[("vs", 1)])
            v_xb = mk.c("vs")
            chain_toks[kc] = v_xb
            spill_toks[kc] = mk.dma("g", spill[:, kc, :], xb_c, waits=[v_xb], ch=6)
        return spill_toks[KC - 1]

    tok_xb_own = m_chain("own")
    mk.barrier()
    tok_xb_scan = m_chain("scan")
    mk.barrier()

    stb = stats(lambda kc: scr["xbscan"][:, kc, :], T, extra_wait=[tok_xb_scan])
    bc_r2 = AR_B[:, 0:T]
    tb_r2 = bcast(bc_r2, 0, T, stb["tok_r"])
    v_mrow = load_mrow(stb["tok_mr"])

    inw_bf = WS[:, 0:8192].bitcast(BF16).rearrange("p (k m) -> p k m", k=KC)
    tok_inw = None
    for kc in range(KC):
        tok_inw = mk.dma("g", inw_bf[:, kc, :], dt_in["in_w"][kc * 128:(kc + 1) * 128, :], ch=7)
    v_gw, v_hw = gw_hw(inw_bf, 2048, "lns_w", "lns_b", "in_b", tok_inw)

    sp_bf = AR_B[:, 2048:2048 + 8192].bitcast(BF16).rearrange("p (k m) -> p k m", k=KC)
    z_f16 = AR_A[:, 0:8192].bitcast(F16).rearrange("p (k m) -> p k m", k=KC)
    rhs_b = [AR_C[:, 0:2048].bitcast(BF16).rearrange("p (k m) -> p k m", k=KC),
             AR_C[:, 2048:4096].bitcast(BF16).rearrange("p (k m) -> p k m", k=KC)]
    xb_f = [AR_C[:, 4096:8192], AR_C[:, 8192:12288]]

    evac_toks, rhsuse_toks = {}, {}
    for ft in range(4):
        rb, xf = rhs_b[ft % 2], xb_f[ft % 2]
        war = ([rhsuse_toks[ft - 2]] if ft - 2 in rhsuse_toks else [tok_xb_scan])
        tok_xf = mk.dma("s", xf.rearrange("p (k m) -> p k m", k=KC),
                        scr["xbscan"][:, :, ft * 512:(ft + 1) * 512],
                        waits=war, ch=6 + ft % 2)
        for kc in range(KC):
            mk.op("vector", lambda E, rb=rb, xf=xf, kc=kc, ft=ft:
                  E.scalar_tensor_tensor(rb[:, kc, :], xf[:, kc * 512:(kc + 1) * 512],
                                         smcol("lns_w", kc), bc_r2[:, ft * 512:(ft + 1) * 512],
                                         ALU.mult, ALU.mult),
                  waits=([tok_xf, tb_r2] if kc == 0 else [tok_xf] if kc == 1 else []),
                  incs=[("vs", 1)])
        v_rhs = mk.c("vs")
        for mc in range(16):
            bank = pb[mc % 2]
            key = ft * 16 + mc
            war_t = ([evac_toks[key - 2]] if key - 2 in evac_toks else [])
            for kc in range(KC):
                mk.op("tensor", lambda E, bank=bank, mc=mc, kc=kc, rb=rb:
                      E.matmul(bank[:, 0:512], inw_bf[:, kc, mc * 128:mc * 128 + 128],
                               rb[:, kc, :], start=(kc == 0), stop=False),
                      waits=(([v_rhs] + war_t + ([tok_inw, v_gw, v_mrow, v_hw] if key == 0 else []))
                             if kc == 0 else []),
                      incs=[("ts", 1)])
            mk.op("tensor", lambda E, bank=bank, mc=mc, ft=ft:
                  E.matmul(bank[:, 0:512], gw_bf[:, mc * 128:mc * 128 + 128],
                           mrow_bf[:, ft * 512:(ft + 1) * 512], start=False, stop=True),
                  incs=[("ts", 1)])
            t_mm = mk.c("ts")
            if mc < 8:
                mk.op("scalar", lambda E, bank=bank, mc=mc, ft=ft:
                      E.activation(sp_bf[:, mc, ft * 512:(ft + 1) * 512], bank[:, 0:512],
                                   AF.Silu, bias=hw_col[:, mc:mc + 1]),
                      waits=[t_mm], incs=[("ss", 1)])
            else:
                mk.op("scalar", lambda E, bank=bank, mc=mc, ft=ft:
                      E.activation(z_f16[:, mc - 8, ft * 512:(ft + 1) * 512], bank[:, 0:512],
                                   AF.Sigmoid, bias=hw_col[:, mc:mc + 1]),
                      waits=[t_mm], incs=[("ss", 1)])
            evac_toks[key] = mk.c("ss")
        rhsuse_toks[ft] = mk.c("ts")
    s_inw = mk.c("ss")

    tok_xpw = mk.dma("g", xpw_bf.rearrange("p (k n) -> p k n", k=KC),
                 dt_in["xp_w"].rearrange("(k p) n -> p k n", p=128), ch=9)
    mk.op("vector", lambda E: E.memset(u64q, 0.0), incs=[("vs", 1)])
    v_u0 = mk.c("vs")
    for ft in range(4):
        for kc in range(KC):
            mk.op("tensor", lambda E, ft=ft, kc=kc:
                  E.matmul(pb[2][0:16, 0:512], xpw_bf[:, kc * 16:kc * 16 + 16],
                           sp_bf[:, kc, ft * 512:(ft + 1) * 512],
                           start=(kc == 0), stop=(kc == KC - 1)),
                  waits=([tok_xpw, s_inw, mk.c("vs")] if kc == 0 else []), incs=[("ts", 1)])
        t_u = mk.c("ts")
        for b in range(4):
            mk.op("vector", lambda E, ft=ft, b=b:
                  E.tensor_copy(u64q[32 * b:32 * b + 16, 128 * ft:128 * ft + 128],
                                pb[2][0:16, b:512:4]),
                  waits=([t_u, v_u0] if b == 0 else []), incs=[("vs", 1)])
    v_u = mk.c("vs")
    mk.op("scalar", lambda E: E.activation(ld64[0:16, :], smcol("A_log", rows=16), AF.Exp),
          waits=[t_fix], incs=[("ss", 1)])
    s_ld = mk.c("ss")
    mk.op("vector", lambda E: E.tensor_scalar_mul(ld64[0:16, :], ld64[0:16, :], -1.0),
          waits=[s_ld], incs=[("vs", 1)])
    mk.op("vector", lambda E: E.tensor_copy(xpbq[0:16, :], smcol("xp_b", rows=16)), incs=[("vs", 1)])
    for q in (32, 64, 96):
        mk.op("vector", lambda E, q=q: E.tensor_copy(ld64[q:q + 16, :], ld64[0:16, :]),
              incs=[("vs", 1)])
        mk.op("vector", lambda E, q=q: E.tensor_copy(xpbq[q:q + 16, :], xpbq[0:16, :]),
              incs=[("vs", 1)])
    mk.op("vector", lambda E: E.tensor_scalar_mul(ld512, ld64, 512.0), incs=[("vs", 1)])
    v_ld = mk.c("vs")
    mk.op("vector", lambda E: E.memset(dk_rep, 1.0), incs=[("vs", 1)])
    mk.op("vector", lambda E: E.tensor_tensor_scan(p512, dk_rep, dk_rep, 0.0, ALU.mult, ALU.add),
          incs=[("vs", 1)])
    v_iota = mk.c("vs")
    mk.op("scalar", lambda E: E.activation(dk_rep, p512, AF.Exp, bias=ld64, scale=0.0),
          waits=[v_ld, v_iota], incs=[("ss", 1)])
    mk.op("scalar", lambda E: E.activation(p512, p512, AF.Exp, bias=0.0, scale=ld64),
          waits=[mk.c("ss")], incs=[("ss", 1)])
    s_dk = mk.c("ss")
    mk.op("vector", lambda E: E.tensor_scalar_add(u64q, u64q, xpbq), waits=[v_u, v_ld],
          incs=[("vs", 1)])
    v_u64 = mk.c("vs")
    mk.op("vector", lambda E: E.tensor_tensor_scan(hsq, dk_rep, u64q, 0.0, ALU.mult, ALU.add),
          waits=[v_u64, s_dk], incs=[("vs", 1)])
    mk.op("vector", lambda E: E.tensor_copy(ecol, hsq[:, 511:512]), incs=[("vs", 1)])
    v_e = mk.c("vs")
    tok_e = mk.dma("s", scr["esnd"][0:128], ecol[:, 0:1], waits=[v_e], ch=5)
    mk.op("gpsimd", lambda E: E.collective_compute(
        "AllGather", ALU.bypass, replica_groups=[list(range(8))],
        ins=[scr["esnd"][:].opt()], outs=[scr["ercv"][:].opt()]),
        waits=[tok_e], incs=[("cc", 1)])
    cc_e = mk.c("cc")
    tok_ca = mk.dma("g", ca8[:, :], scr["ercv"].transpose([1, 0]),
                    waits=[cc_e], allow_slow_non_contiguous=True, ch=8)
    mk.op("scalar", lambda E: E.activation(w8t, CPW[:], AF.Exp, bias=0.0, scale=ld512),
          waits=[v_ld, t_fix], incs=[("ss", 1)])
    s_w8 = mk.c("ss")
    mk.op("vector", lambda E: E.tensor_mul(w8t, w8t, ca8), waits=[s_w8, tok_ca], incs=[("vs", 1)])
    mk.op("vector", lambda E: E.reduce_sum(carry, w8t, axis=AXX), incs=[("vs", 1)])
    mk.op("vector", lambda E: E.scalar_tensor_tensor(hsq, p512, carry, hsq, ALU.mult, ALU.add),
          incs=[("vs", 1)])
    v_hs = mk.c("vs")
    for b in range(4):
        mk.op("vector", lambda E, b=b: E.tensor_copy(hsr_bf[:, b::4], hsq[32 * b:32 * b + 16, :]),
              waits=([v_hs] if b == 0 else []), incs=[("vs", 1)])
    v_hsr = mk.c("vs")

    tok_d = mk.dma("g", dmat_bf, dt_in["dmat"], ch=9)
    outw_bf = WS[:, 4608:4608 + 4096].bitcast(BF16).rearrange("p (k m) -> p k m", k=KC)
    tok_ow = None
    t_inw_done = mk.c("ts")
    for kc in range(KC):
        tok_ow = mk.dma("g", outw_bf[:, kc, :], dt_in["out_w"][kc * 128:(kc + 1) * 128, :],
                        waits=([t_inw_done] if kc == 0 else []), ch=7)

    ysd_bf = AR_C[:, 0:8192].bitcast(BF16).rearrange("p (k m) -> p k m", k=KC)
    xbo_f = [AR_C[:, 8192:12288]]
    ysd_toks = {}
    for ft in range(4):
        for mc in range(8):
            bank = pb[mc % 2]
            key = ft * 8 + mc
            war = ([ysd_toks[key - 2]] if key - 2 in ysd_toks else [])
            mk.op("tensor", lambda E, bank=bank, mc=mc, ft=ft:
                  E.matmul(bank[:, 0:512], dmat_bf[:, mc * 128:mc * 128 + 128],
                           hsr_bf[:, ft * 512:(ft + 1) * 512], start=True, stop=True),
                  waits=([v_hsr, tok_d] if key == 0 else []) + war, incs=[("ts", 1)])
            t_ysd = mk.c("ts")
            mk.op("vector", lambda E, bank=bank, mc=mc, ft=ft:
                  E.scalar_tensor_tensor(ysd_bf[:, mc, ft * 512:(ft + 1) * 512],
                                         bank[:, 0:512], 1.0,
                                         z_f16[:, mc, ft * 512:(ft + 1) * 512],
                                         ALU.mult, ALU.mult),
                  waits=[t_ysd], incs=[("vs", 1)])
            ysd_toks[key] = mk.c("vs")
    v_ysd = mk.c("vs")

    xm_toks = {}
    for ft in range(4):
        xbo = xbo_f[0]
        war = ([xm_toks[ft - 1]] if ft - 1 in xm_toks else [])
        tok_xbo = mk.dma("s", xbo.rearrange("p (k m) -> p k m", k=KC),
                         scr["xbsp"][:, :, ft * 512:(ft + 1) * 512],
                         waits=([tok_xb_own] if ft == 0 else []) + war, ch=6)
        xbo_k = xbo.rearrange("p (k m) -> p k m", k=KC)
        for mc in range(8):
            bank = pb[2 + mc % 2]
            key = ft * 8 + mc
            for kc in range(KC):
                mk.op("tensor", lambda E, bank=bank, mc=mc, kc=kc, ft=ft:
                      E.matmul(bank[:, 0:512], outw_bf[:, kc, mc * 128:mc * 128 + 128],
                               ysd_bf[:, kc, ft * 512:(ft + 1) * 512],
                               start=(kc == 0), stop=(kc == KC - 1)),
                      waits=(([v_ysd, tok_ow] if key == 0 else []) +
                             ([mk.c("vs")] if kc == 0 and key >= 2 else []))
                      if kc == 0 else [],
                      incs=[("ts", 1)])
            t_xm = mk.c("ts")
            mk.op("vector", lambda E, bank=bank, mc=mc, xbo_k=xbo_k:
                  E.scalar_tensor_tensor(xbo_k[:, mc, :], bank[:, 0:512], smcol("out_b", mc),
                                         xbo_k[:, mc, :], ALU.add, ALU.add),
                  waits=[t_xm, tok_xbo], incs=[("vs", 1)])
        v_xm = mk.c("vs")
        xm_toks[ft] = mk.dma("g",
                             scr["xmid"][:, :, ft * 512:(ft + 1) * 512],
                             xbo.rearrange("p (k m) -> p k m", k=KC), waits=[v_xm], ch=6)
    tok_xmid = xm_toks[3]

    if stage >= 2:
        mk.barrier()
        stc = stats(lambda kc: scr["xmid"][:, kc, :], T)
        bc_r3 = AR_B[:, 0:T]
        tb_r3 = bcast(bc_r3, 0, T, stc["tok_r"])
        v_mrow3 = load_mrow(stc["tok_mr"])

        w1_bf = WS[:, 0:8192].bitcast(BF16).rearrange("p (k m) -> p k m", k=KC)
        tok_w1 = None
        t_ws_free = mk.c("ts")
        for kc in range(KC):
            tok_w1 = mk.dma("g", w1_bf[:, kc, :], dt_in["mlp_w1"][kc * 128:(kc + 1) * 128, :],
                            ch=7, waits=([t_ws_free] if kc == 0 else []))
        v_gw3, v_hw3 = gw_hw(w1_bf, 2048, "ln2_w", "ln2_b", "mlp_b1", tok_w1)

        w2c_bf = [WS[:, 8192:8704].bitcast(BF16), WS[:, 8704:9216].bitcast(BF16)]
        hdn_bf = AR_A[:, 0:4096].bitcast(BF16).rearrange("p (k m) -> p k m", k=16)
        rhs_c = [AR_C[:, 0:2048].bitcast(BF16).rearrange("p (k m) -> p k m", k=KC),
                 AR_C[:, 2048:4096].bitcast(BF16).rearrange("p (k m) -> p k m", k=KC)]
        xm_f = [AR_C[:, 4096:8192], AR_C[:, 8192:12288]]

        prev_spill, rhsuse = {}, {}
        hdn_war = None
        for ft in range(4):
            rb, xf = rhs_c[ft % 2], xm_f[ft % 2]
            war = ([rhsuse[ft - 2]] if ft - 2 in rhsuse else [])
            warsp = ([prev_spill[ft - 2]] if ft - 2 in prev_spill else [])
            tok_xf = mk.dma("s", xf.rearrange("p (k m) -> p k m", k=KC),
                            scr["xmid"][:, :, ft * 512:(ft + 1) * 512],
                            waits=war + warsp, ch=6 + ft % 2)
            for kc in range(KC):
                mk.op("vector", lambda E, rb=rb, xf=xf, kc=kc, ft=ft:
                      E.scalar_tensor_tensor(rb[:, kc, :], xf[:, kc * 512:(kc + 1) * 512],
                                             smcol("ln2_w", kc), bc_r3[:, ft * 512:(ft + 1) * 512],
                                             ALU.mult, ALU.mult),
                      waits=([tok_xf, tb_r3] if kc == 0 else []), incs=[("vs", 1)])
            v_rhs = mk.c("vs")
            gelu_done = {}
            for mc in range(16):
                bank = pb[mc % 2]
                key = ft * 16 + mc
                war_t = ([gelu_done[mc - 2]] if mc >= 2 else ([hdn_war] if hdn_war else []))
                for kc in range(KC):
                    mk.op("tensor", lambda E, bank=bank, mc=mc, kc=kc, rb=rb:
                          E.matmul(bank[:, 0:512], w1_bf[:, kc, mc * 128:mc * 128 + 128],
                                   rb[:, kc, :], start=(kc == 0), stop=False),
                          waits=(([v_rhs] + war_t +
                                  ([tok_w1, v_gw3, v_mrow3, v_hw3] if key == 0 else []))
                                 if kc == 0 else []),
                          incs=[("ts", 1)])
                mk.op("tensor", lambda E, bank=bank, mc=mc, ft=ft:
                      E.matmul(bank[:, 0:512], gw_bf[:, mc * 128:mc * 128 + 128],
                               mrow_bf[:, ft * 512:(ft + 1) * 512], start=False, stop=True),
                      incs=[("ts", 1)])
                t_mm = mk.c("ts")
                mk.op("scalar", lambda E, bank=bank, mc=mc:
                      E.activation(hdn_bf[:, mc, :], bank[:, 0:512], AF.Gelu,
                                   bias=hw_col[:, mc:mc + 1]),
                      waits=[t_mm], incs=[("ss", 1)])
                gelu_done[mc] = mk.c("ss")
            s_hdn = mk.c("ss")
            tokw2, w2war = {}, {}
            for kc in range(16):
                w2b = w2c_bf[kc % 2]
                war2 = ([w2war[kc - 2]] if kc - 2 in w2war else [mk.c("ts")])
                tokw2[kc] = mk.dma("g", w2b, dt_in["mlp_w2"][kc * 128:(kc + 1) * 128, :],
                                   ch=9 + kc % 2, waits=war2)
                for mc in range(8):
                    mk.op("tensor", lambda E, kc=kc, mc=mc, w2b=w2b:
                          E.matmul(pb[mc][:, 0:512], w2b[:, mc * 128:mc * 128 + 128],
                                   hdn_bf[:, kc, :], start=(kc == 0), stop=(kc == 15)),
                          waits=(([tokw2[kc]] + ([s_hdn] if kc == 0 else [])) if mc == 0 else []),
                          incs=[("ts", 1)])
                w2war[kc] = mk.c("ts")
            t_w2 = mk.c("ts")
            xm_k = xf.rearrange("p (k m) -> p k m", k=KC)
            for mc in range(8):
                mk.op("vector", lambda E, mc=mc, xm_k=xm_k:
                      E.scalar_tensor_tensor(xm_k[:, mc, :], pb[mc][:, 0:512], smcol("mlp_b2", mc),
                                             xm_k[:, mc, :], ALU.add, ALU.add),
                      waits=([t_w2] if mc == 0 else []), incs=[("vs", 1)])
            v_bo = mk.c("vs")
            hdn_war = v_bo
            prev_spill[ft] = mk.dma("g", scr["xmid"][:, :, ft * 512:(ft + 1) * 512],
                                    xm_k, waits=[v_bo], ch=6 + 5 * (ft % 2))
            rhsuse[ft] = mk.c("ts")
        mk.barrier()

        std = stats(lambda kc: scr["xmid"][:, kc, :], T)
        bc_r4 = AR_B[:, 0:T]
        bc_mr4 = AR_B[:, 2080:2080 + T]
        tb_r4 = bcast(bc_r4, 0, T, std["tok_r"])
        tb_mr4 = bcast(bc_mr4, 2, T, std["tok_mr"])
        bo_c = [AR_C[:, 0:4096], AR_C[:, 4096:8192]]
        xo_c = AR_C[:, 8192:12288]
        hic = AR_A[:, 0:2048].bitcast(BF16).rearrange("p (k m) -> p k m", k=KC)
        loc = AR_A[:, 2048:4096].bitcast(BF16).rearrange("p (k m) -> p k m", k=KC)
        hif = AR_A[:, 4096:4608]
        prev = {}
        tok_hi = tok_lo = None
        for ft in range(4):
            boc = bo_c[ft % 2]
            war = ([prev[ft - 2]] if ft - 2 in prev else [])
            warx = ([prev[ft - 1]] if ft - 1 in prev else [])
            tok_bo = mk.dma("s", boc.rearrange("p (k m) -> p k m", k=KC),
                            scr["xmid"][:, :, ft * 512:(ft + 1) * 512],
                            waits=war, ch=6 + ft % 2)
            tok_xo = mk.dma("g", xo_c.rearrange("p (k m) -> p k m", k=KC),
                            dt_in["x_own"].rearrange("(k p) m -> p k m", p=128)[:, :, 3 + ft * 512:3 + (ft + 1) * 512],
                            waits=warx, ch=8)
            boc_k = boc.rearrange("p (k m) -> p k m", k=KC)
            xoc_k = xo_c.rearrange("p (k m) -> p k m", k=KC)
            sl = slice(ft * 512, (ft + 1) * 512)
            war_hl = [tok_hi, tok_lo] if tok_hi else []
            for kc in range(KC):
                mk.op("vector", lambda E, boc_k=boc_k, kc=kc, sl=sl:
                      E.tensor_mul(boc_k[:, kc, :], boc_k[:, kc, :], bc_r4[:, sl]),
                      waits=([tok_bo, tb_r4] if kc == 0 else []), incs=[("vs", 1)])
                mk.op("vector", lambda E, boc_k=boc_k, kc=kc, sl=sl:
                      E.tensor_sub(boc_k[:, kc, :], boc_k[:, kc, :], bc_mr4[:, sl]),
                      waits=([tb_mr4] if kc == 0 else []), incs=[("vs", 1)])
                mk.op("vector", lambda E, boc_k=boc_k, xoc_k=xoc_k, kc=kc:
                      E.tensor_add(boc_k[:, kc, :], boc_k[:, kc, :], xoc_k[:, kc, :]),
                      waits=([tok_xo] if kc == 0 else []), incs=[("vs", 1)])
                mk.op("vector", lambda E, boc_k=boc_k, kc=kc:
                      E.tensor_copy(hic[:, kc, :], boc_k[:, kc, :]),
                      waits=(war_hl if kc == 0 else []), incs=[("vs", 1)])
                mk.op("vector", lambda E, kc=kc:
                      E.tensor_copy(hif, hic[:, kc, :]), incs=[("vs", 1)])
                mk.op("vector", lambda E, boc_k=boc_k, kc=kc:
                      E.tensor_sub(loc[:, kc, :], boc_k[:, kc, :], hif), incs=[("vs", 1)])
            v_x1 = mk.c("vs")
            tok_hi = mk.dma("g", scr["xsnd"][:, 0:8, ft * 512:(ft + 1) * 512], hic,
                            waits=[v_x1], ch=6 + 5 * (ft % 2))
            tok_lo = mk.dma("g", scr["xsnd"][:, 8:16, ft * 512:(ft + 1) * 512], loc,
                            waits=[v_x1], ch=12 + ft % 2)
            prev[ft] = tok_lo
        mk.op("gpsimd", lambda E: E.collective_compute(
            "AllGather", ALU.bypass, replica_groups=[[0, 1], [2, 3], [4, 5], [6, 7]],
            ins=[scr["xsnd"][:].opt()], outs=[scr["xrcv"][:].opt()]),
            waits=[mk.c("dg6"), mk.c("dg11"), mk.c("dg12"), mk.c("dg13")], incs=[("cc", 1)])
        cc_x1 = mk.c("cc")

        srcbuf2 = [AR_A[:, 1024:1024 + 2048].bitcast(BF16).rearrange("p (v k m) -> p v k m", v=4, k=KC),
                   AR_A[:, 5120 + 1024:5120 + 1024 + 2048].bitcast(BF16).rearrange("p (v k m) -> p v k m", v=4, k=KC)]
        x1o_c2 = [AR_A[:, 4096:5120], AR_A[:, 9216:10240]]
        prevass = {}
        selused = {}
        for j in range(16):
            srcbuf, x1o_c = srcbuf2[j % 2], x1o_c2[j % 2]
            g_ev, g_od = 2 * j, 2 * j + 1
            war = ([selused[j - 2]] if j - 2 in selused else [cc_x1])
            toks = []
            for vi, (g, hl) in enumerate([(g_ev, 0), (g_ev, 1), (g_od, 0), (g_od, 1)]):
                half, locc = g // 16, (g % 16) * 128
                toks.append(mk.dma("s", srcbuf[:, vi, :, :],
                                   scr["xrcv"][half, :, hl * 8:hl * 8 + 8, locc:locc + 128],
                                   waits=war, ch=(6, 7, 14, 15)[vi]))
            mk.op("vector", lambda E, srcbuf=srcbuf, x1o_c=x1o_c:
                  E.tensor_scalar(x1o_c, srcbuf[:, 0, :, :].rearrange("p k m -> p (k m)"),
                                  PARSEL[:, 0:1], None, ALU.mult),
                  waits=toks + ([prevass[j - 2]] if j - 2 in prevass else []), incs=[("vs", 1)])
            mk.op("vector", lambda E, srcbuf=srcbuf, x1o_c=x1o_c:
                  E.scalar_tensor_tensor(x1o_c, srcbuf[:, 1, :, :].rearrange("p k m -> p (k m)"),
                                         PARSEL[:, 0:1], x1o_c, ALU.mult, ALU.add),
                  incs=[("vs", 1)])
            mk.op("vector", lambda E, srcbuf=srcbuf, x1o_c=x1o_c:
                  E.scalar_tensor_tensor(x1o_c, srcbuf[:, 2, :, :].rearrange("p k m -> p (k m)"),
                                         PARSEL[:, 1:2], x1o_c, ALU.mult, ALU.add),
                  incs=[("vs", 1)])
            mk.op("vector", lambda E, srcbuf=srcbuf, x1o_c=x1o_c:
                  E.scalar_tensor_tensor(x1o_c, srcbuf[:, 3, :, :].rearrange("p k m -> p (k m)"),
                                         PARSEL[:, 1:2], x1o_c, ALU.mult, ALU.add),
                  incs=[("vs", 1)])
            v_sel = mk.c("vs")
            selused[j] = v_sel
            prevass[j] = mk.dma("g", scr["x1own"][:, :, j * 128:(j + 1) * 128],
                                x1o_c.rearrange("p (k m) -> p k m", k=KC),
                                waits=[v_sel], ch=8 + j % 2)
        tok_x1own = prevass[15]

    if stage >= 3:
        mk.barrier()
        H = 16
        steo = stats(lambda kc: scr["x1own"][:, kc, :], T, fused2=True, ship_row0=4)
        bc_g5o = AR_B[:, 6144:6144 + T]
        tb_g5o = bcast(bc_g5o, 5, T, steo["tok_g"])
        v_mrow5 = load_mrow(steo["tok_mg"], slot=7)

        wq_bf = WS[:, 0:4096].bitcast(BF16).rearrange("p (k m) -> p k m", k=KC)
        kva_bf = WS[:, 4096:4736].bitcast(BF16).rearrange("p (k m) -> p k m", k=KC)
        kvb_bf = WS[:, 4736:4992].bitcast(BF16)
        wo_bf = WS[:, 4992:7040].bitcast(BF16).rearrange("p (k m) -> p k m", k=4)
        wgate_f = WS[:, 7040:7104].rearrange("p (k m) -> p k m", k=KC)
        tok_wq = tok_kva = tok_kvb = tok_wo = tok_wg8 = None
        for kc in range(KC):
            tok_wq = mk.dma("g", wq_bf[:, kc, :], dt_in["wq"][kc * 128:(kc + 1) * 128, :], ch=7)
        for kc in range(KC):
            tok_kva = mk.dma("g", kva_bf[:, kc, :], dt_in["wkva"][kc * 128:(kc + 1) * 128, :], ch=9)
        tok_kvb = mk.dma("g", kvb_bf, dt_in["wkvb"], ch=10)
        for kc in range(4):
            tok_wo = mk.dma("g", wo_bf[:, kc, :], dt_in["wo_eff"][kc * 128:(kc + 1) * 128, :], ch=8)
        tok_wg8 = mk.dma("s", wgate_f, dt_in["wgate"].rearrange("(k p) m -> p k m", p=128), ch=14)
        csqa_v = AR_B[:, 2048:3072].bitcast(BF16)
        csqb_v = AR_B[:, 3072:4096].bitcast(BF16)
        for rb in (32, 96):
            tok_csq = mk.dma("s", csqa_v[rb:rb + 32, :], dt_in["csq_a"], ch=15)
            tok_csq = mk.dma("s", csqb_v[rb:rb + 32, :], dt_in["csq_b"], ch=15)
        cska_bf = AR_B[:, 6144:8192].bitcast(BF16)[32:64, :]
        cskb_bf = AR_B[:, 8192:10240].bitcast(BF16)[32:64, :]
        maskq_bf = AR_B[:, 0:2048].bitcast(BF16)
        tok_mask = mk.dma("s", maskq_bf, dt_in["qmask"], ch=14)

        c0w = SMCOL["w_ln1"][0]
        mk.op("vector", lambda E: E.tensor_copy(gcol_bf, SM[:, c0w:c0w + 8]), incs=[("vs", 1)])
        v_g1 = mk.c("vs")
        for i, f0 in enumerate(range(0, 1024, 512)):
            for kc in range(KC):
                mk.op("tensor", lambda E, f0=f0, kc=kc:
                      E.matmul(pb[5][0:1, 0:512], gcol_bf[:, kc:kc + 1],
                               wq_bf[:, kc, f0:f0 + 512], start=(kc == 0), stop=(kc == KC - 1)),
                      waits=([v_g1, tok_wq, mk.c("vs")] if kc == 0 else []), incs=[("ts", 1)])
            t_gw = mk.c("ts")
            mk.op("vector", lambda E, f0=f0: E.tensor_copy(gw_bf[:, f0:f0 + 512], pb[5][0:1, 0:512]),
                  waits=[t_gw], incs=[("vs", 1)])
        for kc in range(KC):
            mk.op("tensor", lambda E, kc=kc:
                  E.matmul(pb[5][0:1, 0:160], gcol_bf[:, kc:kc + 1],
                           kva_bf[:, kc, :], start=(kc == 0), stop=(kc == KC - 1)),
                  waits=([tok_kva, mk.c("vs")] if kc == 0 else []), incs=[("ts", 1)])
        t_gk = mk.c("ts")
        mk.op("vector", lambda E: E.tensor_copy(gw_bf[:, 1024:1184], pb[5][0:1, 0:160]),
              waits=[t_gk], incs=[("vs", 1)])
        v_gwq = mk.c("vs")

        qp_bf = AR_A[:, 0:8192].bitcast(BF16)
        lat_bf = AR_A[:, 8192:10240].bitcast(BF16)
        xoc2 = [AR_C[:, 0:4096], AR_C[:, 4096:8192]]
        rhsq = AR_C[:, 8448:10496].bitcast(BF16).rearrange("p (k m) -> p k m", k=KC)
        ev_tok = {}
        for ft in range(4):
            xoc = xoc2[ft % 2]
            war = ([ev_tok[ft - 2]] if ft - 2 in ev_tok else [])
            tok_xo = mk.dma("s", xoc.rearrange("p (k m) -> p k m", k=KC),
                            scr["x1own"][:, :, ft * 512:(ft + 1) * 512],
                            waits=war, ch=6 + ft % 2)
            xok = xoc.rearrange("p (k m) -> p k m", k=KC)
            for kc in range(KC):
                mk.op("vector", lambda E, xok=xok, kc=kc, ft=ft:
                      E.scalar_tensor_tensor(rhsq[:, kc, :], xok[:, kc, :],
                                             smcol("w_ln1", kc), bc_g5o[:, ft * 512:(ft + 1) * 512],
                                             ALU.mult, ALU.mult),
                      waits=([tok_xo, tb_g5o, mk.c("ts")] if kc == 0 else []), incs=[("vs", 1)])
            v_rq = mk.c("vs")
            for pr in range(8):
                bank = pb[pr % 2]
                key = ft * 8 + pr
                war_t = ([ev_tok.get(("q", key - 2), ("ss", 0))] if key >= 2 else [])
                for kc in range(KC):
                    mk.op("tensor", lambda E, bank=bank, pr=pr, kc=kc:
                          E.matmul(bank[:, 0:512], wq_bf[:, kc, pr * 128:pr * 128 + 128],
                                   rhsq[:, kc, :], start=(kc == 0), stop=False),
                          waits=(([v_rq] + war_t + ([v_gwq, v_mrow5] if key == 0 else []))
                                 if kc == 0 else []),
                          incs=[("ts", 1)])
                mk.op("tensor", lambda E, bank=bank, pr=pr, ft=ft:
                      E.matmul(bank[:, 0:512], gw_bf[:, pr * 128:pr * 128 + 128],
                               mrow_bf[:, ft * 512:(ft + 1) * 512], start=False, stop=True),
                      incs=[("ts", 1)])
                t_q = mk.c("ts")
                mk.op("scalar", lambda E, bank=bank, pr=pr, ft=ft:
                      E.activation(qp_bf[:, pr * 2048 + ft * 512:pr * 2048 + (ft + 1) * 512],
                                   bank[:, 0:512], AF.Copy, scale=0.125),
                      waits=[t_q], incs=[("ss", 1)])
                ev_tok[("q", key)] = mk.c("ss")
            ev_tok[ft] = mk.c("ts")
        s_qdone = mk.c("ss")
        v_qbuild = mk.c("vs")

        rtmp = AR_B[:, 4096:5120].bitcast(BF16)
        for pr in range(8):
            cols = slice(pr * 2048, (pr + 1) * 2048)
            for base in (32, 96):
                rt = rtmp[base:base + 32, :]
                mk.op("vector", lambda E, cols=cols, base=base, rt=rt:
                      E.stream_shuffle(rt, qp_bf[base:base + 32, cols],
                                       [i ^ 1 for i in range(32)]),
                      waits=([s_qdone, tok_csq] if (pr == 0 and base == 32) else []),
                      incs=[("vs", 1)])
                mk.op("vector", lambda E, rt=rt, base=base:
                      E.tensor_mul(rt, rt, csqb_v[base:base + 32, :]),
                      incs=[("vs", 1)])
                mk.op("vector", lambda E, cols=cols, base=base:
                      E.tensor_mul(qp_bf[base:base + 32, cols], qp_bf[base:base + 32, cols],
                                   csqa_v[base:base + 32, :]),
                      incs=[("vs", 1)])
                mk.op("vector", lambda E, cols=cols, base=base, rt=rt:
                      E.tensor_add(qp_bf[base:base + 32, cols], qp_bf[base:base + 32, cols], rt),
                      incs=[("vs", 1)])
        v_qrope = mk.c("vs")

        stg = {}
        for half in range(2):
            stg[half] = stats(lambda kc, half=half: scr["xrcv"][half, :, kc, :], T,
                              fused2=True, ship_off=half * T, tbuf=AR_C, tbase=4096,
                              extra_wait=[v_qbuild] if half == 0 else (),
                              psum_wait=[v_gwq] if half == 0 else [])
        bc_g5g = [AR_B[:, 6144:6144 + T], AR_B[:, 8192:8192 + T]]
        tb_g5g = [bcast(bc_g5g[0], 1, T, stg[0]["tok_g"], off=0),
                  bcast(bc_g5g[1], 1, T, stg[1]["tok_g"], off=T)]
        mrowG_bf = ROWS[0:1, 0:2048].bitcast(BF16)
        mg_f = rslot(2, T)
        for half in range(2):
            tokm = mk.dma("s", mg_f, scr["rows_scr"][3:4, half * T:(half + 1) * T],
                          waits=[stg[half]["tok_mg"], mk.c("vs")], ch=5)
            mk.op("vector", lambda E, half=half:
                  E.tensor_scalar_mul(mrowG_bf[:, half * T:(half + 1) * T], mg_f, -1.0),
                  waits=[tokm], incs=[("vs", 1)])
            mk._stats_ships = getattr(mk, "_stats_ships", []) + [tokm]
        v_mrowG = mk.c("vs")

        krp_bf = ROWS[32:64, 0:2048].bitcast(BF16)
        khc = [AR_C[:, 0:2048].bitcast(BF16).rearrange("p (k m) -> p k m", k=KC),
               AR_C[:, 2048:4096].bitcast(BF16).rearrange("p (k m) -> p k m", k=KC)]
        rhk = AR_C[:, 8448:10496].bitcast(BF16).rearrange("p (k m) -> p k m", k=KC)
        kv_tok = {}
        for ftg in range(8):
            half, ftl = ftg // 4, ftg % 4
            hc = khc[ftg % 2]
            war = ([kv_tok[ftg - 2]] if ftg - 2 in kv_tok else [v_qrope])
            tok_h = mk.dma("s", hc, scr["xrcv"][half, :, 0:8, ftl * 512:(ftl + 1) * 512],
                           waits=war, ch=6 + ftg % 2)
            for kc in range(KC):
                mk.op("vector", lambda E, hc=hc, kc=kc, ftg=ftg:
                      E.scalar_tensor_tensor(rhk[:, kc, :], hc[:, kc, :],
                                             smcol("w_ln1", kc),
                                             bc_g5g[ftg // 4][:, (ftg % 4) * 512:(ftg % 4 + 1) * 512],
                                             ALU.mult, ALU.mult),
                      waits=([tok_h] + ([tb_g5g[0], tb_g5g[1], mk.c("ts")] if kc == 0 else []))
                      if kc <= 0 else [],
                      incs=[("vs", 1)])
            v_rk = mk.c("vs")
            for kc in range(KC):
                mk.op("tensor", lambda E, kc=kc, ftg=ftg:
                      E.matmul(pb[ftg % 2][:, 0:512], kva_bf[:, kc, 0:128],
                               rhk[:, kc, :], start=(kc == 0), stop=False),
                      waits=([v_rk, v_mrowG] if kc == 0 else []), incs=[("ts", 1)])
            mk.op("tensor", lambda E, ftg=ftg:
                  E.matmul(pb[ftg % 2][:, 0:512], gw_bf[:, 1024:1152],
                           mrowG_bf[:, ftg * 512:(ftg + 1) * 512], start=False, stop=True),
                  incs=[("ts", 1)])
            t_lat = mk.c("ts")
            for kc in range(KC):
                mk.op("tensor", lambda E, kc=kc, ftg=ftg:
                      E.matmul(pb[2 + ftg % 2][0:32, 0:512], kva_bf[:, kc, 128:160],
                               rhk[:, kc, :], start=(kc == 0), stop=False),
                      incs=[("ts", 1)])
            mk.op("tensor", lambda E, ftg=ftg:
                  E.matmul(pb[2 + ftg % 2][0:32, 0:512], gw_bf[:, 1152:1184],
                           mrowG_bf[:, ftg * 512:(ftg + 1) * 512], start=False, stop=True),
                  incs=[("ts", 1)])
            t_krp = mk.c("ts")
            mk.op("scalar", lambda E, ftg=ftg:
                  E.activation(lat_bf[:, ftg * 512:(ftg + 1) * 512], pb[ftg % 2][:, 0:512], AF.Copy),
                  waits=[t_lat], incs=[("ss", 1)])
            mk.op("scalar", lambda E, ftg=ftg:
                  E.activation(krp_bf[:, ftg * 512:(ftg + 1) * 512], pb[2 + ftg % 2][0:32, 0:512], AF.Copy),
                  waits=[t_krp], incs=[("ss", 1)])
            kv_tok[ftg] = mk.c("ss")
        s_ckv = mk.c("ss")

        tok_csk = mk.dma("s", cska_bf, dt_in["csk_a"], waits=[s_ckv], ch=15)
        tok_csk = mk.dma("s", cskb_bf, dt_in["csk_b"], ch=15)
        rtm2 = AR_B[:, 4096:6144].bitcast(BF16)[32:64, :]
        mk.op("vector", lambda E: E.stream_shuffle(rtm2[:, 0:2048], krp_bf[:, 0:2048],
                                                   [i ^ 1 for i in range(32)]),
              waits=[s_ckv, tok_csk], incs=[("vs", 1)])
        mk.op("vector", lambda E: E.stream_shuffle(rtm2[:, 2048:4096], krp_bf[:, 2048:4096],
                                                   [i ^ 1 for i in range(32)]),
              incs=[("vs", 1)])
        mk.op("vector", lambda E: E.tensor_mul(krp_bf, krp_bf, cska_bf), incs=[("vs", 1)])
        mk.op("vector", lambda E: E.tensor_mul(rtm2, rtm2, cskb_bf), incs=[("vs", 1)])
        mk.op("vector", lambda E: E.tensor_add(krp_bf, krp_bf, rtm2), incs=[("vs", 1)])
        v_rotk = mk.c("vs")

        vp_bf = AR_C[:, 0:8448].bitcast(BF16)
        mk.op("vector", lambda E: E.memset(vp_bf, 1.0), waits=[mk.c("ts")], incs=[("vs", 1)])
        v_vp0 = mk.c("vs")
        for kb in range(32):
            bank = pb[kb % 2]
            war = [mk.c("vs")] if kb >= 2 else [v_vp0]
            mk.op("tensor", lambda E, kb=kb, bank=bank:
                  E.matmul(bank[:, 0:512], lat_bf[:, kb * 128:(kb + 1) * 128],
                           kvb_bf, start=True, stop=True),
                  waits=[s_ckv, tok_kvb] + war, incs=[("ts", 1)])
            t_v = mk.c("ts")
            mk.op("vector", lambda E, kb=kb, bank=bank:
                  E.tensor_copy(vp_bf[:, kb * 528:kb * 528 + 528].rearrange("p (h v) -> p h v", h=16)[:, :, 0:32],
                                bank[:, 0:512].rearrange("p (h v) -> p h v", h=16)),
                  waits=[t_v], incs=[("vs", 1)])
        v_vp = mk.c("vs")

        kb_buf = AR_C[:, 8448:10496].bitcast(BF16)
        p_buf = AR_B[:, 0:0]
        p_buf = AR_C[:, 10240 - 2048 - 0:0] if False else None
        pB = AR_B[:, 4096:5120].bitcast(BF16)
        pB_b = AR_B[:, 6400:7424].bitcast(BF16)
        yh_bf = AR_B[0:32, 5120:6144].bitcast(BF16)
        lb_sb = AR_B[0:32, 6144:6400].bitcast(BF16)
        linv_bf = ROWS[0:1, 0:256].bitcast(BF16)
        ones32_bf = MISC[0:1, 2048:2064].bitcast(BF16)
        mk.op("vector", lambda E: E.memset(ones32_bf, 1.0), incs=[("vs", 1)])
        mk.op("vector", lambda E: E.tensor_copy(kb_buf[32:64, :], krp_bf),
              waits=[v_rotk], incs=[("vs", 1)])
        mk.op("vector", lambda E: E.tensor_copy(kb_buf[96:128, :], krp_bf),
              incs=[("vs", 1)])
        v_o32 = mk.c("vs")

        kbuild_tok = {}
        att_done = {}
        yspill = {}
        prev_tlb = None
        probe1 = isinstance(devtap, str) and devtap.startswith("dbg_attk")
        att_H = 1 if probe1 else H
        att_KB = int(devtap[8:]) if probe1 else 32

        def att_fc0(kb):
            return (kb // 2) // 4
        if probe1:
            mk.op("vector", lambda E: E.memset(SC[:, 0:2048], 0.0), incs=[("vs", 1)])
            vz = mk.c("vs")
            ysflat = scr["ysp"].rearrange("p k t -> p (k t)")
            mk.dma("g", ysflat[:, 0:4096], SC[:, 0:2048].bitcast(BF16), waits=[vz], ch=6)
            mk.dma("g", ysflat[:, 4096:8192], SC[:, 0:2048].bitcast(BF16), ch=6)
        for h in range(att_H):
            pbase = 64 * (h % 2)
            war_kb = ([att_done[h - 2]] if h - 2 in att_done else [])
            cp_done = {}
            for ftk in range(8):
                bank = pb[6 if ftk % 2 == 0 else 7]
                war_cp = ([cp_done[ftk - 2]] if ftk - 2 in cp_done else [])
                mk.op("tensor", lambda E, h=h, ftk=ftk, bank=bank:
                      E.matmul(bank[0:32, 0:512], kvb_bf[:, 32 * h:32 * h + 32],
                               lat_bf[:, ftk * 512:(ftk + 1) * 512], start=True, stop=True),
                      waits=([s_ckv, v_rotk, mk.c("vs")] + war_kb if ftk == 0 else war_cp),
                      incs=[("ts", 1)])
                t_kn = mk.c("ts")
                mk.op("vector", lambda E, h=h, ftk=ftk, bank=bank, pbase=pbase:
                      E.tensor_copy(kb_buf[pbase:pbase + 32, ftk * 512:(ftk + 1) * 512],
                                    bank[0:32, 0:512]),
                      waits=[t_kn], incs=[("vs", 1)])
                cp_done[ftk] = mk.c("vs")
            kbuild_tok[h] = mk.c("vs")

            pr = h // 2
            qcols0 = pr * 2048
            exp_done = {}
            avused = {}
            bank_free = {0: None, 1: None}
            pend = []

            def emit_qk(kb, fc):
                jmin = kb // 2
                off0 = min(max(jmin - 4 * fc, 0), 4) * 128
                bksel = (kb + fc) % 2
                qk_bank = pb[bksel]
                war_bank = ([bank_free[bksel]] if bank_free[bksel] is not None else [])
                first = kb == 0 and fc == jmin // 4
                mk.op("tensor", lambda E, h=h, kb=kb, fc=fc, qk_bank=qk_bank, pbase=pbase,
                      qcols0=qcols0, off0=off0:
                      E.matmul(qk_bank[:, off0:512],
                               kb_buf[pbase:pbase + 64, kb * 128:(kb + 1) * 128],
                               qp_bf[pbase:pbase + 64,
                                     qcols0 + fc * 512 + off0:qcols0 + (fc + 1) * 512],
                               start=True, stop=True),
                      waits=(([kbuild_tok[h], v_qrope, v_vp, v_o32, tok_mask] if first else []) +
                             war_bank +
                             ([mk.c("vs")] if first and h >= 1 else [])),
                      incs=[("ts", 1)])
                t_s = mk.c("ts")
                pc = (pB if kb % 2 == 0 else pB_b)[:, fc * 512:(fc + 1) * 512]
                war_p = ([avused[(kb - 2, fc)]] if (kb - 2, fc) in avused else [])
                mk.op("scalar", lambda E, pc=pc, qk_bank=qk_bank, off0=off0:
                      E.activation(pc[:, off0:512], qk_bank[:, off0:512], AF.Exp),
                      waits=[t_s] + war_p, incs=[("ss", 1)])
                s_p = mk.c("ss")
                bank_free[bksel] = s_p
                avw = [s_p]
                if 4 * fc <= jmin < 4 * fc + 4:
                    mk.op("vector", lambda E, pc=pc, off0=off0, kb=kb:
                          E.tensor_mul(pc[:, off0:off0 + 128], pc[:, off0:off0 + 128],
                                       maskq_bf[:, kb * 128:(kb + 1) * 128]),
                          waits=[s_p], incs=[("vs", 1)])
                    avw = [mk.c("vs")]
                return (kb, fc, off0, pc, avw)

            def emit_av(st):
                kb, fc, off0, pc, avw = st
                mk.op("tensor", lambda E, h=h, kb=kb, fc=fc, pc=pc, off0=off0:
                      E.matmul(pb[2 + fc][0:33, off0:512],
                               vp_bf[:, kb * 528 + 33 * h:kb * 528 + 33 * h + 33],
                               pc[:, off0:512], start=(kb == 0),
                               stop=(kb == (att_KB - 1 if probe1 and att_KB < 32
                                            else 8 * fc + 7))),
                      waits=avw, incs=[("ts", 1)])
                avused[(kb, fc)] = mk.c("ts")
                exp_done[(kb, fc)] = mk.c("ts")

            lrow = ROWS[0:1, 256:768]

            def emit_headend(fc):
                nonlocal prev_tlb
                t_av = avused[(att_KB - 1 if probe1 and att_KB < 32 else 8 * fc + 7, fc)]
                mk.op("scalar", lambda E, fc=fc:
                      E.activation(lrow, pb[2 + fc][32:33, 0:512], AF.Ln),
                      waits=[t_av] + ([prev_tlb] if prev_tlb is not None else []),
                      incs=[("ss", 1)])
                mk.op("scalar", lambda E:
                      E.activation(linv_bf[:, 0:512], lrow, AF.Exp, scale=-1.0),
                      incs=[("ss", 1)])
                s_li = mk.c("ss")
                mk.op("tensor", lambda E:
                      E.matmul(pb[6][0:32, 0:512], ones32_bf, linv_bf[:, 0:512],
                               start=True, stop=True),
                      waits=[s_li, mk.c("vs")], incs=[("ts", 1)])
                t_lb = mk.c("ts")
                prev_tlb = t_lb
                mk.op("vector", lambda E:
                      E.tensor_copy(lb_sb, pb[6][0:32, 0:512]),
                      waits=[t_lb], incs=[("vs", 1)])
                mk.op("vector", lambda E, fc=fc:
                      E.tensor_mul(yh_bf[:, fc * 512:(fc + 1) * 512], pb[2 + fc][0:32, 0:512], lb_sb),
                      waits=([yspill[h - 1]] if fc == 0 and h >= 1 else []),
                      incs=[("vs", 1)])

            he_q = []

            def flush_av():
                st = pend.pop(0)
                emit_av(st)
                kb_, fc_ = st[0], st[1]
                if kb_ == (att_KB - 1 if probe1 and att_KB < 32 else 8 * fc_ + 7):
                    he_q.append([fc_, 0])

            for kb in range(att_KB):
                for fc in range(att_fc0(kb), 4):
                    for e in list(he_q):
                        e[1] += 1
                        if e[1] >= 3:
                            emit_headend(e[0])
                            he_q.remove(e)
                    pend.append(emit_qk(kb, fc))
                    if len(pend) > 2:
                        flush_av()
            while pend:
                flush_av()
            for e in he_q:
                emit_headend(e[0])
            v_y = mk.c("vs")
            yspill[h] = mk.dma("g", scr["ysp"][32 * (h % 4):32 * (h % 4) + 32, h // 4, :],
                               yh_bf, waits=[v_y], ch=6 + h % 2)
            att_done[h] = mk.c("ts")
        mk.barrier()
        if probe1:
            mk.op("vector", lambda E: E.tensor_copy(AR_B[0:33, 6912:7424], pb[2][0:33, 0:512]),
                  incs=[("vs", 1)])
            mk.barrier()
            mk.dma("g", scr["xmid"][:, 5, 0:1024], AR_B[:, 4096:5120], ch=6)
            mk.dma("g", scr["xmid"][0:32, 6, 0:1024], AR_B[0:32, 5120:6144], ch=6)
            mk.dma("g", scr["xmid"][0:33, 7, 0:512], AR_B[0:33, 6912:7424], ch=6)
            mk.dma("g", scr["xbsp"].rearrange("p k t -> p (k t)")[:, 0:10240],
                   AR_A[:, 0:10240], ch=6)
            mk.dma("g", scr["xbscan"].rearrange("p k t -> p (k t)")[:, 0:8448],
                   AR_C[:, 0:8448], ch=6)
            mk.dma("g", scr["xmid"][0:32, 0, :], ROWS[32:64, 0:2048], ch=6)
            mk.dma("g", scr["xmid"][:, 3, :], AR_B[:, 0:2048], ch=6)
            mk.barrier()
        if devtap == "dbg_att":
            mk.dma("g", scr["xbsp"].rearrange("p k t -> p (k t)")[:, 0:10240],
                   AR_A[:, 0:10240], ch=6)
            mk.dma("g", scr["xbscan"].rearrange("p k t -> p (k t)")[:, 0:8448],
                   AR_C[:, 0:8448], ch=6)
            mk.dma("g", scr["xmid"][0:32, 0, :], ROWS[32:64, 0:2048], ch=6)
            mk.dma("g", scr["xmid"][:, 2, :], bc_g5o, ch=6)
            mk.dma("g", scr["xmid"][:, 3, :], AR_B[:, 0:2048], ch=6)
            mk.dma("g", scr["xmid"][:, 4, :], AR_C[:, 8448:10496], ch=6)
            mk.barrier()

        bc_r5o = AR_B[:, 6144:6144 + T]
        bc_mr5o = AR_B[:, 8192:8192 + T]
        tb_r5o = bcast(bc_r5o, 4, T, steo["tok_r"])
        tb_mr5o = bcast(bc_mr5o, 6, T, steo["tok_mr"])
        yb = [AR_C[:, 0:1024].bitcast(BF16).rearrange("p (k m) -> p k m", k=4),
              AR_C[:, 1024:2048].bitcast(BF16).rearrange("p (k m) -> p k m", k=4)]
        x1c2 = [AR_C[:, 2048:6144], AR_C[:, 6144:10240]]
        x2c = AR_A[:, 0:4096]
        x2tok = {}
        for ft in range(4):
            ybf, x1c = yb[ft % 2], x1c2[ft % 2]
            war = ([x2tok[ft - 2]] if ft - 2 in x2tok else [])
            tok_y = mk.dma("s", ybf, scr["ysp"][:, :, ft * 512:(ft + 1) * 512],
                           waits=war, ch=6 + ft % 2)
            tok_x1 = mk.dma("s", x1c.rearrange("p (k m) -> p k m", k=KC),
                            scr["x1own"][:, :, ft * 512:(ft + 1) * 512],
                            waits=war, ch=14 + ft % 2)
            x1k = x1c.rearrange("p (k m) -> p k m", k=KC)
            x2k = x2c.rearrange("p (k m) -> p k m", k=KC)
            for mc in range(8):
                bank = pb[mc % 2]
                for kc in range(4):
                    mk.op("tensor", lambda E, bank=bank, mc=mc, kc=kc, ybf=ybf:
                          E.matmul(bank[:, 0:512], wo_bf[:, kc, mc * 128:mc * 128 + 128],
                                   ybf[:, kc, :], start=(kc == 0), stop=(kc == 3)),
                          waits=(([tok_y, tok_wo] + ([mk.c("vs")] if ft + mc > 0 else []))
                                 if kc == 0 else []),
                          incs=[("ts", 1)])
                t_wo = mk.c("ts")
                sl = slice(ft * 512, (ft + 1) * 512)
                mk.op("vector", lambda E, x1k=x1k, x2k=x2k, mc=mc, sl=sl:
                      E.tensor_mul(x2k[:, mc, :], x1k[:, mc, :], bc_r5o[:, sl]),
                      waits=[tok_x1] + ([tb_r5o, tb_mr5o] if ft == 0 and mc == 0 else [])
                      + ([x2tok[ft - 1]] if mc == 0 and ft - 1 in x2tok else []),
                      incs=[("vs", 1)])
                mk.op("vector", lambda E, x2k=x2k, mc=mc, sl=sl:
                      E.tensor_sub(x2k[:, mc, :], x2k[:, mc, :], bc_mr5o[:, sl]),
                      incs=[("vs", 1)])
                mk.op("vector", lambda E, x2k=x2k, mc=mc, bank=bank:
                      E.tensor_add(x2k[:, mc, :], x2k[:, mc, :], bank[:, 0:512]),
                      waits=[t_wo], incs=[("vs", 1)])
            v_x2 = mk.c("vs")
            x2tok[ft] = mk.dma("g", scr["x2sp"][:, :, ft * 512:(ft + 1) * 512], x2k,
                               waits=[v_x2], ch=11 + ft % 2)
        mk.barrier()

        stf = stats(lambda kc: scr["x2sp"][:, kc, :], T, with_mean=False)
        bc_rr = AR_C[:, 0:2048]
        tb_rr = bcast(bc_rr, 0, T, stf["tok_r"])
        gate8 = MISC[0:8, 128:2176]
        rhsm = AR_B[:, 0:8192].bitcast(BF16).rearrange("p (k m) -> p k m", k=KC)
        x2l = [AR_C[:, 4096:8192], AR_C[:, 8192:12288]]
        gtok = {}
        for ft in range(4):
            x2b = x2l[ft % 2]
            war = ([gtok[ft - 2]] if ft - 2 in gtok else [])
            tok_x2 = mk.dma("s", x2b.rearrange("p (k m) -> p k m", k=KC),
                            scr["x2sp"][:, :, ft * 512:(ft + 1) * 512],
                            waits=war, ch=6 + ft % 2)
            x2bk = x2b.rearrange("p (k m) -> p k m", k=KC)
            for kc in range(KC):
                mk.op("vector", lambda E, x2bk=x2bk, kc=kc:
                      E.tensor_scalar(x2bk[:, kc, :], x2bk[:, kc, :], smcol("w_ln2", kc), None,
                                      ALU.mult),
                      waits=([tok_x2] if kc == 0 else []), incs=[("vs", 1)])
            v_tp = mk.c("vs")
            for kc in range(KC):
                mk.op("tensor", lambda E, x2bk=x2bk, kc=kc, ft=ft:
                      E.matmul(pb[4][0:8, 0:512], wgate_f[:, kc, :], x2bk[:, kc, :],
                               start=(kc == 0), stop=(kc == KC - 1)),
                      waits=([v_tp, tok_wg8, mk.c("vs")] if kc == 0 else []), incs=[("ts", 1)])
            t_g8 = mk.c("ts")
            mk.op("vector", lambda E, ft=ft:
                  E.tensor_copy(gate8[:, ft * 512:(ft + 1) * 512], pb[4][0:8, 0:512]),
                  waits=[t_g8], incs=[("vs", 1)])
            for kc in range(KC):
                mk.op("vector", lambda E, x2bk=x2bk, kc=kc, ft=ft:
                      E.tensor_mul(rhsm[:, kc, ft * 512:(ft + 1) * 512], x2bk[:, kc, :],
                                   bc_rr[:, ft * 512:(ft + 1) * 512]),
                      waits=([tb_rr] if kc == 0 and ft == 0 else []), incs=[("vs", 1)])
            gtok[ft] = mk.c("vs")
        v_rhsm = mk.c("vs")

        sG = AR_A[0:8, 0:2048]
        sT8, sT32 = AR_A[0:8, 2048:4096], AR_A[0:32, 2048:4096]
        sM8, sM32 = AR_A[0:8, 4096:6144], AR_A[0:32, 4096:6144]
        sK8 = AR_A[0:8, 6144:8192]
        sC8, sC32 = AR_A[0:8, 8192:10240], AR_A[0:32, 8192:10240]
        sF1 = AR_C[0:8, 8448:10496]
        sE8 = AR_B[0:8, 8192:10240]
        ebcol = smcol("ebias", rows=8)
        mk.op("vector", lambda E: E.tensor_mul(sG, gate8, bc_rr[0:8, :]), waits=[v_rhsm, mk.c("ts")],
              incs=[("vs", 1)])
        mk.op("vector", lambda E: E.tensor_scalar_add(sG, sG, ebcol), incs=[("vs", 1)])
        mk.op("scalar", lambda E: E.activation(sG, sG, AF.Exp), waits=[mk.c("vs")], incs=[("ss", 1)])
        s_e8 = mk.c("ss")
        mk.wait("vector", [s_e8])
        shufs = {k: [((i + k) % 8) if i < 8 else i for i in range(32)] for k in (1, 2, 4)}

        def fold(op):
            for k in (4, 2, 1):
                mk.op("vector", lambda E, k=k: E.stream_shuffle(sT32, sM32, shufs[k]),
                      incs=[("vs", 1)])
                mk.op("vector", lambda E, op=op: E.tensor_tensor(sM8, sM8, sT8, op),
                      incs=[("vs", 1)])

        def cumsum8():
            for k in (1, 2, 4):
                mask = [((i - k) % 8) if i < 8 else i for i in range(32)]
                mk.op("vector", lambda E, mask=mask: E.stream_shuffle(sT32, sC32, mask),
                      incs=[("vs", 1)])
                mk.op("vector", lambda E, k=k: E.memset(sT8[0:k, :], 0.0), incs=[("vs", 1)])
                mk.op("vector", lambda E: E.tensor_add(sC8, sC8, sT8), incs=[("vs", 1)])
        mk.op("vector", lambda E: E.tensor_copy(sM8, sG), incs=[("vs", 1)])
        fold(ALU.max)
        mk.op("vector", lambda E: E.tensor_tensor(sK8, sG, sM8, ALU.is_equal), incs=[("vs", 1)])
        mk.op("vector", lambda E: E.tensor_copy(sC8, sK8), incs=[("vs", 1)])
        cumsum8()
        mk.op("vector", lambda E: E.tensor_single_scalar(sF1, sC8, 1.0, ALU.is_equal),
              incs=[("vs", 1)])
        mk.op("vector", lambda E: E.tensor_mul(sF1, sF1, sK8), incs=[("vs", 1)])
        mk.op("vector", lambda E: E.tensor_scalar(sE8, sF1, -1.0, 1.0, ALU.mult, ALU.add),
              incs=[("vs", 1)])
        mk.op("vector", lambda E: E.tensor_mul(sE8, sE8, sG), incs=[("vs", 1)])
        mk.op("vector", lambda E: E.tensor_copy(sM8, sE8), incs=[("vs", 1)])
        fold(ALU.max)
        mk.op("vector", lambda E: E.tensor_tensor(sK8, sE8, sM8, ALU.is_equal), incs=[("vs", 1)])
        mk.op("vector", lambda E: E.tensor_copy(sC8, sK8), incs=[("vs", 1)])
        cumsum8()
        mk.op("vector", lambda E: E.tensor_single_scalar(sE8, sC8, 1.0, ALU.is_equal),
              incs=[("vs", 1)])
        mk.op("vector", lambda E: E.tensor_mul(sE8, sE8, sK8), incs=[("vs", 1)])
        mk.op("vector", lambda E: E.tensor_add(sK8, sF1, sE8), incs=[("vs", 1)])
        mk.op("vector", lambda E: E.tensor_mul(sK8, sK8, sG), incs=[("vs", 1)])
        mk.op("vector", lambda E: E.tensor_copy(sM8, sK8), incs=[("vs", 1)])
        fold(ALU.add)
        mk.op("scalar", lambda E: E.activation(sM8, sM8, AF.Ln), waits=[mk.c("vs")],
              incs=[("ss", 1)])
        mk.op("scalar", lambda E: E.activation(sM8, sM8, AF.Exp, scale=-1.0),
              incs=[("ss", 1)])
        mk.op("vector", lambda E: E.tensor_mul(sK8, sK8, sM8), waits=[mk.c("ss")],
              incs=[("vs", 1)])
        v_comb = mk.c("vs")
        tok_comb = mk.dma("s", scr["cmbs"][:], sK8, waits=[v_comb], ch=5)

        wd_flat = dt_in["wd9"].rearrange("e i o -> (e i) o")
        h_all = AR_A[:, 0:9216].bitcast(BF16).rearrange("p (c m) -> p c m", c=36)
        gsb2 = [AR_C[:, 2048:3072].bitcast(BF16).rearrange("p (c m) -> p c m", c=4),
                AR_C[:, 3072:4096].bitcast(BF16).rearrange("p (c m) -> p c m", c=4)]
        gu_bf = AR_C[:, 0:256].bitcast(BF16)
        cmb_b = [AR_C[:, 256:512].bitcast(BF16), AR_C[:, 512:768].bitcast(BF16)]
        wg_b = [WS[:, 0:2048].bitcast(BF16).rearrange("p (k m) -> p k m", k=KC),
                WS[:, 4096:6144].bitcast(BF16).rearrange("p (k m) -> p k m", k=KC)]
        wu_b = [WS[:, 2048:4096].bitcast(BF16).rearrange("p (k m) -> p k m", k=KC),
                WS[:, 6144:8192].bitcast(BF16).rearrange("p (k m) -> p k m", k=KC)]
        wd_b = [WS[:, 8192:8704].bitcast(BF16), WS[:, 8704:9216].bitcast(BF16)]
        bo2c = AR_C[:, 8192:12288]
        x2rl = AR_C[:, 4096:8192]
        bo2tok = {}
        wtok = {}
        wuse = {}
        cb_use = {}
        sg_done = {}
        su_done = {}
        g_banks = (pb[4], pb[6])
        u_banks = (pb[5], pb[7])
        for tq in range(4):
            tsl = slice(tq * 512, (tq + 1) * 512)
            for e in range(9):
                bi = (tq * 9 + e) % 2
                war_w = ([wuse[(tq * 9 + e) - 2]] if (tq * 9 + e) - 2 in wuse else [mk.c("ts")])
                tg = tu = None
                if tq == 0 or True:
                    for kc in range(KC):
                        tg = mk.dma("g", wg_b[bi][:, kc, :],
                                    dt_in["wg9"][e, kc * 128:(kc + 1) * 128, :],
                                    waits=(war_w if kc == 0 else []), ch=7 + bi)
                    for kc in range(KC):
                        tu = mk.dma("g", wu_b[bi][:, kc, :],
                                    dt_in["wu9"][e, kc * 128:(kc + 1) * 128, :], ch=9 + bi)
                tok_cb = None
                if e < 8:
                    war_cb = ([cb_use[bi]] if bi in cb_use else
                              ([tok_comb] if tq == 0 and e == 0 else []))
                    tok_cb = mk.dma("g", cmb_b[bi], scr["cmbs"][e, tsl].partition_broadcast(128),
                                    waits=war_cb, ch=12 + bi)
                ei = tq * 9 + e
                gsb = gsb2[ei % 2]
                for mc in range(4):
                    sg = ei * 4 + mc
                    bank = g_banks[sg % 2]
                    war_bank = ([sg_done[sg - 2]] if sg - 2 in sg_done else [])
                    for kc in range(KC):
                        mk.op("tensor", lambda E, mc=mc, kc=kc, bi=bi, tsl=tsl, bank=bank:
                              E.matmul(bank[:, 0:512], wg_b[bi][:, kc, mc * 128:mc * 128 + 128],
                                       rhsm[:, kc, tsl], start=(kc == 0), stop=(kc == KC - 1)),
                              waits=(([tg] + war_bank
                                      + ([v_rhsm] if ei == 0 and mc == 0 else [])
                                      + ([mk.c("vs")] if e == 0 and mc == 0 else []))
                                     if kc == 0 else []),
                              incs=[("ts", 1)])
                    t_gm = mk.c("ts")
                    war_gsb = ([su_done[(ei - 2) * 4 + mc]] if ei >= 2 else [])
                    mk.op("scalar", lambda E, mc=mc, bank=bank, gsb=gsb:
                          E.activation(gsb[:, mc, :], bank[:, 0:512], AF.Silu),
                          waits=[t_gm] + war_gsb, incs=[("ss", 1)])
                    sg_done[sg] = mk.c("ss")
                for mc in range(4):
                    su = ei * 4 + mc
                    bank = u_banks[su % 2]
                    war_bank = ([su_done[su - 2]] if su - 2 in su_done else [])
                    for kc in range(KC):
                        mk.op("tensor", lambda E, mc=mc, kc=kc, bi=bi, tsl=tsl, bank=bank:
                              E.matmul(bank[:, 0:512], wu_b[bi][:, kc, mc * 128:mc * 128 + 128],
                                       rhsm[:, kc, tsl], start=(kc == 0), stop=(kc == KC - 1)),
                              waits=(([tu] + war_bank
                                      + ([mk.c("vs")] if e == 0 and mc == 0 else []))
                                     if kc == 0 else []),
                              incs=[("ts", 1)])
                    t_um = mk.c("ts")
                    mk.op("vector", lambda E, mc=mc, bank=bank, gsb=gsb:
                          E.scalar_tensor_tensor(gu_bf, bank[:, 0:512], 1.0, gsb[:, mc, :],
                                                 ALU.mult, ALU.mult),
                          waits=[t_um, sg_done[ei * 4 + mc]], incs=[("vs", 1)])
                    su_done[su] = mk.c("vs")
                    ch = 4 * e + mc
                    if e < 8:
                        mk.op("vector", lambda E, ch=ch, bi=bi:
                              E.tensor_mul(h_all[:, ch, :], gu_bf, cmb_b[bi]),
                              waits=([tok_cb] if mc == 0 else []), incs=[("vs", 1)])
                    else:
                        mk.op("vector", lambda E, ch=ch:
                              E.tensor_copy(h_all[:, ch, :], gu_bf), incs=[("vs", 1)])
                if e < 8:
                    cb_use[bi] = mk.c("vs")
                wuse[tq * 9 + e] = mk.c("ts")
            v_hall = mk.c("vs")
            wdtok = {}
            for kc in range(36):
                wdb = wd_b[kc % 2]
                war_d = ([wdtok[kc - 2]] if kc - 2 in wdtok else [mk.c("ts")])
                td = mk.dma("g", wdb, wd_flat[kc * 128:(kc + 1) * 128, :],
                            waits=war_d, ch=14 + kc % 2)
                for mc in range(8):
                    mk.op("tensor", lambda E, kc=kc, mc=mc, wdb=wdb:
                          E.matmul(pb[mc][:, 0:512], wdb[:, mc * 128:mc * 128 + 128],
                                   h_all[:, kc, :], start=(kc == 0), stop=(kc == 35)),
                          waits=(([td] + ([v_hall, mk.c("ss")] if kc == 0 else []))
                                 if mc == 0 else []),
                          incs=[("ts", 1)])
                wdtok[kc] = mk.c("ts")
            t_rt = mk.c("ts")
            war_x2 = ([bo2tok[tq - 1]] if tq - 1 in bo2tok else [])
            tok_x2r = mk.dma("s", x2rl.rearrange("p (k m) -> p k m", k=KC),
                             scr["x2sp"][:, :, tsl], waits=war_x2, ch=6)
            bo2k = bo2c.rearrange("p (k m) -> p k m", k=KC)
            x2rk = x2rl.rearrange("p (k m) -> p k m", k=KC)
            for mc in range(8):
                mk.op("vector", lambda E, mc=mc, bo2k=bo2k, x2rk=x2rk:
                      E.tensor_add(bo2k[:, mc, :], pb[mc][:, 0:512], x2rk[:, mc, :]),
                      waits=[t_rt] + ([tok_x2r] if mc == 0 else []), incs=[("vs", 1)])
            v_bo2 = mk.c("vs")
            bo2tok[tq] = mk.dma("g", scr["bo2sp"][:, :, tsl], bo2k, waits=[v_bo2], ch=11)
        mk.barrier()

        stgf = stats(lambda kc: scr["bo2sp"][:, kc, :], T)
        bc_r7 = AR_B[:, 0:T]
        bc_mr7 = AR_B[:, 2080:2080 + T]
        tb_r7 = bcast(bc_r7, 0, T, stgf["tok_r"])
        tb_mr7 = bcast(bc_mr7, 2, T, stgf["tok_mr"])
        boc2 = [AR_C[:, 0:4096], AR_C[:, 4096:8192]]
        x1f = AR_C[:, 8192:12288]
        ftok = {}
        for ft in range(4):
            bc2 = boc2[ft % 2]
            war = ([ftok[ft - 2]] if ft - 2 in ftok else [])
            warx = ([ftok[ft - 1]] if ft - 1 in ftok else [])
            tok_b = mk.dma("s", bc2.rearrange("p (k m) -> p k m", k=KC),
                           scr["bo2sp"][:, :, ft * 512:(ft + 1) * 512],
                           waits=war, ch=6 + ft % 2)
            tok_x1f = mk.dma("s", x1f.rearrange("p (k m) -> p k m", k=KC),
                             scr["x1own"][:, :, ft * 512:(ft + 1) * 512],
                             waits=warx, ch=14)
            b2k = bc2.rearrange("p (k m) -> p k m", k=KC)
            x1fk = x1f.rearrange("p (k m) -> p k m", k=KC)
            sl = slice(ft * 512, (ft + 1) * 512)
            for kc in range(KC):
                mk.op("vector", lambda E, b2k=b2k, kc=kc, sl=sl:
                      E.tensor_mul(b2k[:, kc, :], b2k[:, kc, :], bc_r7[:, sl]),
                      waits=([tok_b, tb_r7] if kc == 0 else []), incs=[("vs", 1)])
                mk.op("vector", lambda E, b2k=b2k, kc=kc, sl=sl:
                      E.tensor_sub(b2k[:, kc, :], b2k[:, kc, :], bc_mr7[:, sl]),
                      waits=([tb_mr7] if kc == 0 else []), incs=[("vs", 1)])
                mk.op("vector", lambda E, b2k=b2k, x1fk=x1fk, kc=kc:
                      E.tensor_add(b2k[:, kc, :], b2k[:, kc, :], x1fk[:, kc, :]),
                      waits=([tok_x1f] if kc == 0 else []), incs=[("vs", 1)])
            v_f = mk.c("vs")
            ftok[ft] = mk.dma("s", outp[:, :, ft * 512:(ft + 1) * 512], b2k,
                              waits=[v_f], ch=0)

    if devtap == "dbg_misc":
        mk.barrier()
        mk.dma("g", scr["x2sp"][:, 0, :], MISC[:, 0:2048], ch=6)
        mk.dma("g", scr["x2sp"][:, 1, :], AR_B[:, 0:2048], ch=6)
        devtap = "x2sp"
    if devtap in ("dbg_sp", "dbg_z", "dbg_scan", "dbg_ysd"):
        mk.barrier()
        if devtap == "dbg_sp":
            for kc in range(KC):
                mk.dma("g", scr["x2sp"][:, kc, :], sp_bf[:, kc, :], ch=6)
        elif devtap == "dbg_z":
            for kc in range(KC):
                mk.dma("g", scr["x2sp"][:, kc, :], z_f16[:, kc, :], ch=6)
        elif devtap == "dbg_scan":
            mk.dma("g", scr["x2sp"][:, 0, 0:512], u64q, ch=6)
            mk.dma("g", scr["x2sp"][:, 1, 0:512], hsq, ch=6)
            mk.dma("g", scr["x2sp"][:, 2, 0:512], dk_rep, ch=6)
            mk.dma("g", scr["x2sp"][:, 3, 0:512], p512, ch=6)
            mk.dma("g", scr["x2sp"][:, 4, 0:1024], hsr_bf, ch=6)
        elif devtap == "dbg_ysd":
            for kc in range(KC):
                mk.dma("g", scr["x2sp"][:, kc, :], ysd_bf[:, kc, :], ch=6)
        devtap = "x2sp"

    mk.barrier()
    if devtap == "dbg_att" or (isinstance(devtap, str) and devtap.startswith("dbg_attk")):
        devtap = None
    if devtap or stage < 3:
        src = scr[devtap] if devtap else (scr["xmid"] if stage == 1 else scr["x1own"])
        for kc in range(KC):
            mk.dma("s", outp[:, kc, :], src[:, kc, :], ch=0)
    mk.wait("sync", [mk.c("ds0")])
    mk.wait("gpsimd", [mk.c("ds0")])
    return mk.build()


_PROG_CACHE = {}


def get_program(stage=STAGE, devtap=None):
    key = (stage, devtap)
    if key not in _PROG_CACHE:
        _PROG_CACHE[key] = build_program(stage, devtap)
    return _PROG_CACHE[key]


def kernel(**inputs):
    in_maps = prep_inputs(inputs)
    nc = get_program()
    res = run_bass_kernel_spmd(nc, in_maps, core_ids=list(range(NCORES)))
    out = np.zeros((B, S, D), np.float32)
    for c in range(NCORES):
        b0, par = c // 2, c & 1
        o = np.asarray(res.results[c]["outp"], np.float32)
        o = o.transpose(1, 0, 2).reshape(D, T)
        for j in range(16):
            g = 2 * j + par
            out[b0, g * 128:(g + 1) * 128, :] = o[:, j * 128:(j + 1) * 128].T
    return out
